# version 01
# baseline (speedup 1.0000x reference)
import dataclasses
import math
from contextlib import ExitStack
import numpy as np
import ml_dtypes
import concourse.bass as bass
import concourse.mybir as mybir
from concourse.bass_utils import run_bass_kernel_spmd

F32 = mybir.dt.float32
BF16 = mybir.dt.bfloat16
I32 = mybir.dt.int32
AF = mybir.ActivationFunctionType
ALU = mybir.AluOpType
AX = mybir.AxisListType.X
BF = ml_dtypes.bfloat16

B, S, D = 2, 4096, 2048
NCORE = 8
TOW = 1024
NTT = TOW // 128
WIN = 32
NJT = S // 128
NSWT = 32 + 9
STAGE = 5
import os as _os
SKIP = set((_os.environ.get('K_SKIP') or '').split(','))
DBG_A = bool(_os.environ.get('K_DBG_A'))
DBG_B = bool(_os.environ.get('K_DBG_B'))
DBG_D = bool(_os.environ.get('K_DBG_D'))
CCUT = int(_os.environ.get('K_CCUT') or 0)


class _CutDone(Exception):
    pass


def apm(ap, off=0, dims=None):
    return dataclasses.replace(ap, offset=ap.offset + off,
                               ap=[list(d) for d in dims] if dims is not None else ap.ap)


class Ctx:
    def __init__(self, nc, stack):
        self.nc = nc
        self.stack = stack
        self.sb = {}
        self.ps = {}
        self.dram = {}
        self.sems = {}

    def sem(self, *names):
        for name in names:
            if name not in self.sems:
                self.sems[name] = self.stack.enter_context(self.nc.semaphore(name))

    def sbuf(self, name, shape, dtype):
        self.sb[name] = self.stack.enter_context(self.nc.sbuf_tensor(name, shape, dtype))
        return self.sb[name]

    def sbuf_at(self, name, shape, dtype, host, off_bytes):
        base = self.nc.lookup_mloc(self.sb[host]).addr
        h = self.nc.alloc_sbuf_tensor_at(name, shape, dtype, offset=base + off_bytes)
        self.sb[name] = h
        return h

    def psum(self, name, shape, dtype=F32):
        self.ps[name] = self.stack.enter_context(self.nc.psum_tensor(name, shape, dtype))
        return self.ps[name]

    def dscr(self, name, shape, dtype, kind="Internal"):
        if name not in self.dram:
            self.dram[name] = self.nc.dram_tensor(name, shape, dtype, kind=kind)
        return self.dram[name]


class PhaseA:

    def __init__(self, ctx: Ctx, io, out_kind="Internal"):
        self.ctx = ctx
        self.io = io
        c = ctx
        self.xinb = [c.sbuf(f"A_xinb{i}", [128, D], BF16) for i in range(2)]
        self.xres = c.sbuf("A_xres", [128, D], F32)
        self.xT = c.sbuf("A_xT", [128, D], BF16)
        self.wkv = c.sbuf("A_wkv", [128, 16 * 64], BF16)
        self.wq = c.sbuf("A_wq", [128, 16 * 128], BF16)
        self.kfm = c.sbuf("A_kfm", [32, WIN - 1 + S], BF16)
        self.vfull = c.sbuf("A_vfull", [32, S + 128], BF16)
        self.qfm = c.sbuf("A_qfm", [128, 1152], BF16)
        self.vs = c.sbuf("A_vs", [128, NJT * 36], BF16)
        self.kw = [c.sbuf(f"A_kw{i}", [128, S], BF16) for i in range(2)]
        self.qw = [c.sbuf(f"A_qw{i}", [128, 4 * 512], BF16) for i in range(2)]
        self.ebias = c.sbuf("A_ebias", [128, NJT * 512], BF16)
        self.ebarg = [c.sbuf(f"A_ebarg{i}", [128, 512], BF16) for i in range(2)]
        self.Usb = c.sbuf("A_U", [128, 512], BF16)
        self.e1 = [c.sbuf(f"A_e1_{i}", [128, 512], BF16) for i in range(2)]
        self.e2 = [c.sbuf(f"A_e2_{i}", [128, 512], BF16) for i in range(2)]
        self.stg = c.sbuf("A_stg", [9, 512], F32)
        self.attn = c.sbuf("A_attn", [128, TOW], F32)
        self.attnb = c.sbuf("A_attnb", [128, TOW], BF16)
        self.den = c.sbuf("A_den", [16, TOW], F32)
        self.denb = c.sbuf("A_denb", [16, TOW], BF16)
        self.recip = c.sbuf("A_recip", [128, TOW], F32)
        self.wo = c.sbuf("A_wo", [128, D], BF16)
        self.wop = c.sbuf("A_wop", [128, D], BF16)
        self.v0c = c.sbuf("A_v0c", [128, 1], F32)
        self.v1c = c.sbuf("A_v1c", [128, 1], F32)
        self.v0cb = c.sbuf("A_v0cb", [128, 1], BF16)
        self.dvc = c.sbuf("A_dvc", [128, 1], F32)
        self.v0wo = c.sbuf("A_v0wo", [1, D], BF16)
        self.v0woe = c.sbuf("A_v0woe", [128, D], F32)
        self.km = c.sbuf("A_km", [128, 8], F32)
        self.sc1 = c.sbuf("A_sc1", [1, 8], F32)
        self.onesf = c.sbuf("A_onesf", [1, 128], F32)
        self.onesb = c.sbuf("A_onesb", [1, 128], BF16)
        self.E16 = c.sbuf("A_E16", [16, 128], BF16)
        self.maskm = c.sbuf("A_maskm", [128, 5632], BF16)
        self.invi = c.sbuf("A_invi", [1, TOW], F32)
        self.jga = c.sbuf("A_jga", [128, NJT], F32)
        self.jp = c.sbuf("A_jp", [128, 1], F32)
        self.identb = c.sbuf("A_identb", [128, 128], BF16)
        self.xr = c.sbuf("A_xr", [128, D], F32)
        self.ln = c.sbuf("A_ln", [128, 8], F32)
        self.x1f = c.sbuf("A_x1f", [128, D], F32)
        self.x1b = c.sbuf("A_x1b", [128, D], BF16)
        self.p_sc = [c.psum(f"A_psc{i}", [128, 512], F32) for i in range(2)]
        self.p_pv = c.psum("A_ppv", [9, 512], F32)
        self.p_tr = [c.psum(f"A_ptr{i}", [128, 1024], BF16) for i in range(2)]
        self.p_kv = c.psum("A_pkv", [64, 512], F32)
        self.p_q = c.psum("A_pq", [128, 512], F32)
        self.p_x = c.psum("A_px", [128, 512], F32)
        self.k_dram = c.dscr("A_kdram", [32, WIN - 1 + S], BF16)
        self.q_dram = c.dscr("A_qdram", [128, 1152], BF16)
        self.x1f_d = c.dscr("x1f_d", [TOW, D], F32, kind=out_kind)
        self.x1b_d = c.dscr("x1b_d", [TOW, D], BF16, kind=out_kind)
        c.sem("aP", "aXT", "aXC", "aTW", "aTE", "aPJ", "aAC", "aKDR", "aQDR",
              "aQWB", "aKWB", "aVT", "aVE", "aUE", "aEA", "aEE", "aSC", "aE1",
              "aE2", "aPV", "aSG", "aASM", "aDN", "aDXP", "aRC", "aAB",
              "aWOE", "aRCP", "aWO", "aLV", "aLNV", "aLNS", "aX1", "aOUT",
              "aPS1", "aPV1", "aKM", "aSW", "aPg", "aDBG", "aVSf")

    NPS = 6
    NPG = 7

    def em_sync(self, e):
        sm, io = self.ctx.sems, self.io
        for name, dst in [("invi", self.invi), ("jp", self.jp),
                          ("onesf", self.onesf), ("v_emb0", self.v0c),
                          ("v_emb1", self.v1c)]:
            e.dma_start(out=dst[:, :], in_=io[name]).then_inc(sm["aP"], 16)
        e.dma_start(out=self.sc1[0:1, 0:1], in_=io["tau"]).then_inc(sm["aP"], 16)
        if STAGE < 2:
            return
        e.wait_ge(sm["aAC"], 32)
        e.dma_start(out=self.k_dram.ap(), in_=self.kfm[:, :]).then_inc(sm["aKDR"], 16)
        e.wait_ge(sm["aAC"], NSWT)
        e.dma_start(out=self.q_dram.ap(), in_=self.qfm[:, :]).then_inc(sm["aQDR"], 16)
        if STAGE < 3:
            return
        e.wait_ge(sm["aQDR"], 16)
        e.wait_ge(sm["aKDR"], 16)
        for rb in range(2):
            for g4 in range(4):
                ng = rb * 4 + g4
                if ng > 0:
                    e.wait_ge(sm["aSC"], 128 * ng)
                for w in range(WIN):
                    half, wl = w // 16, w % 16
                    dst = apm(self.qw[half][:, :], (wl * 8) * (4 * 512),
                              [[4 * 512, 8], [512, 4], [1, 512]])
                    src = apm(self.q_dram.ap(), 97 + 512 * rb + w + (4 * g4) * 8 * 1152,
                              [[1152, 8], [8 * 1152, 4], [1, 512]])
                    e.dma_start(out=dst, in_=src).then_inc(sm["aQWB"], 16)
                for w in range(WIN):
                    half, wl = w // 16, w % 16
                    dst = apm(self.kw[half][:, :], (wl * 8) * S, [[S, 8], [1, S]])
                    src = apm(self.k_dram.ap(), g4 * 8 * (WIN - 1 + S) + w,
                              [[WIN - 1 + S, 8], [1, S]])
                    e.dma_start(out=dst, in_=src).then_inc(sm["aKWB"], 16)
        if DBG_A:
            e.wait_ge(sm["aAB"], 1)
            for nm, buf in [("dbg_attn", self.attn), ("dbg_attnb", self.attnb)]:
                e.dma_start(out=self.ctx.dram[nm].ap(), in_=buf[:, :]).then_inc(sm["aDBG"], 16)
            e.dma_start(out=self.ctx.dram["dbg_den"].ap(), in_=self.den[:, :]).then_inc(sm["aDBG"], 16)
            e.dma_start(out=self.ctx.dram["dbg_qfm"].ap(), in_=self.qfm[:, :]).then_inc(sm["aDBG"], 16)
            e.dma_start(out=self.ctx.dram["dbg_kfm"].ap(), in_=self.kfm[:, :]).then_inc(sm["aDBG"], 16)
            e.dma_start(out=self.ctx.dram["dbg_vs"].ap(), in_=self.vs[:, :]).then_inc(sm["aDBG"], 16)
            e.dma_start(out=self.ctx.dram["dbg_km"].ap(), in_=self.km[:, :]).then_inc(sm["aDBG"], 16)
            e.dma_start(out=self.ctx.dram["dbg_eb"].ap(), in_=self.ebias[:, :]).then_inc(sm["aDBG"], 16)
            e.wait_ge(sm["aDBG"], 16 * 8)
        if STAGE < 5:
            return
        for t in range(NTT):
            if t >= 1:
                e.wait_ge(sm["aXC"], t)
            e.dma_start(out=self.xres[:, :],
                        in_=apm(io["x_own"], t * 128 * D, [[D, 128], [1, D]])
                        ).then_inc(sm["aXT"], 16)
            if t >= 1:
                e.wait_ge(sm["aX1"], 2 * t)
                e.dma_start(out=apm(self.x1f_d.ap(), (t - 1) * 128 * D, [[D, 128], [1, D]]),
                            in_=self.x1f[:, :]).then_inc(sm["aOUT"], 16)
                e.dma_start(out=apm(self.x1b_d.ap(), (t - 1) * 128 * D, [[D, 128], [1, D]]),
                            in_=self.x1b[:, :]).then_inc(sm["aOUT"], 16)
        e.wait_ge(sm["aX1"], 2 * NTT)
        e.dma_start(out=apm(self.x1f_d.ap(), (NTT - 1) * 128 * D, [[D, 128], [1, D]]),
                    in_=self.x1f[:, :]).then_inc(sm["aOUT"], 16)
        e.dma_start(out=apm(self.x1b_d.ap(), (NTT - 1) * 128 * D, [[D, 128], [1, D]]),
                    in_=self.x1b[:, :]).then_inc(sm["aOUT"], 16)
        e.wait_ge(sm["aOUT"], 32 * NTT)

    def em_gpsimd(self, e):
        sm, io = self.ctx.sems, self.io
        e.dma_start(out=self.wkv[:, :], in_=io["wkv_t"]).then_inc(sm["aPg"], 16)
        e.dma_start(out=self.wq[:, :], in_=io["wq_t"]).then_inc(sm["aPg"], 16)
        e.dma_start(out=self.wo[:, :], in_=io["wo_t"]).then_inc(sm["aPg"], 16)
        e.dma_start(out=self.E16[:, :], in_=io["E16f"]).then_inc(sm["aPg"], 16)
        e.dma_start(out=self.identb[:, :], in_=io["identf"]).then_inc(sm["aPg"], 16)
        e.dma_start(out=self.maskm[:, :], in_=io["maskm"]).then_inc(sm["aPg"], 16)
        e.dma_start(out=self.onesb[:, :], in_=io["onesf"]).then_inc(sm["aPg"], 16)
        if STAGE < 2:
            return
        for g in range(NSWT):
            if g >= 2:
                e.wait_ge(sm["aTW"], 16 * (g - 1))
            if g < 32:
                srcap = apm(io["x_pref"], g * 128 * D, [[D, 128], [1, D]])
            else:
                srcap = apm(io["x_qext"], (g - 32) * 128 * D, [[D, 128], [1, D]])
            e.dma_start(out=self.xinb[g % 2][:, :], in_=srcap).then_inc(sm["aSW"], 16)
        if STAGE < 4:
            return
        for hx in range(32):
            rb, h = hx // 16, hx % 16
            e.wait_ge(sm["aSG"], hx + 1)
            e.dma_start(out=apm(self.attn[:, :], 8 * h * TOW + rb * 512,
                                [[TOW, 8], [1, 512]]),
                        in_=self.stg[0:8, :]).then_inc(sm["aASM"], 16)
            e.dma_start(out=apm(self.den[:, :], h * TOW + rb * 512,
                                [[TOW, 1], [1, 512]]),
                        in_=self.stg[8:9, :]).then_inc(sm["aASM"], 16)

    def em_tensor(self, e):
        sm = self.ctx.sems
        nc = self.ctx.nc
        if STAGE < 1:
            return
        e.wait_ge(sm["aP"], 16 * self.NPS)
        e.wait_ge(sm["aPg"], 16 * self.NPG)
        e.wait_ge(sm["aPV1"], 1)
        if 'expand' not in SKIP:
            for i, col in enumerate([4, 5, 1]):
                mm = e.matmul(self.p_x[:, i:i + 1], self.onesf[0:1, :],
                              self.sc1[0:1, col:col + 1], start=True, stop=True)
            mm.then_inc(sm["aWOE"], 1)
        else:
            nc.tensor.nop().then_inc(sm["aWOE"], 1)
        if STAGE < 2:
            return
        for t in range(NSWT):
            e.wait_ge(sm["aSW"], 16 * (t + 1))
            e.wait_ge(sm["aAC"], t)
            for dt in range(16):
                g = t * 16 + dt
                if g >= 2:
                    e.wait_ge(sm["aTE"], g - 1)
                e.transpose(self.p_tr[g % 2][:, 0:128],
                            self.xinb[t % 2][:, 128 * dt:128 * dt + 128],
                            self.identb[:, :]).then_inc(sm["aTW"], 1)
            e.wait_ge(sm["aTE"], 16 * (t + 1))
            if t < 32:
                for dt in range(16):
                    mm = e.matmul(self.p_kv[:, 0:128], self.wkv[:, 64 * dt:64 * dt + 64],
                                  self.xT[:, 128 * dt:128 * dt + 128],
                                  start=(dt == 0), stop=(dt == 15))
            else:
                for dt in range(16):
                    mm = e.matmul(self.p_q[:, 0:128], self.wq[:, 128 * dt:128 * dt + 128],
                                  self.xT[:, 128 * dt:128 * dt + 128],
                                  start=(dt == 0), stop=(dt == 15))
            mm.then_inc(sm["aPJ"], 1)
        if STAGE < 3:
            return
        e.wait_ge(sm["aAC"], NSWT)
        e.wait_ge(sm["aPV1"], 2)
        for jt in range(NJT):
            if jt >= 2:
                e.wait_ge(sm["aVE"], jt - 1)
            e.transpose(self.p_tr[jt % 2][:, 0:32],
                        self.vfull[:, 128 * jt + 1:128 * jt + 129],
                        self.identb[0:32, 0:32]).then_inc(sm["aVT"], 1)
        e.wait_ge(sm["aWO"], 1)
        for ch in range(4):
            if ch > 0:
                e.wait_ge(sm["aRCP"], ch)
            e.matmul(self.p_x[0:1, :], self.v0cb[:, 0:1],
                     self.wo[:, 512 * ch:512 * ch + 512], start=True,
                     stop=True).then_inc(sm["aWOE"], 1)
        for ch in range(4):
            e.wait_ge(sm["aRCP"], 4 + ch)
            e.matmul(self.p_x[:, :], self.onesb[0:1, :],
                     self.v0wo[0:1, 512 * ch:512 * ch + 512], start=True,
                     stop=True).then_inc(sm["aWOE"], 1)
        e.wait_ge(sm["aRCP"], 8)
        if STAGE < 4:
            return
        for rb in range(2):
            e.wait_ge(sm["aPV1"], 1)
            e.wait_ge(sm["aAC"], NSWT)
            if rb > 0:
                e.wait_ge(sm["aEA"], NJT * rb)
            e.matmul(self.p_q[:, :], self.onesf[0:1, :],
                     self.invi[0:1, 512 * rb:512 * rb + 512], start=True,
                     stop=True).then_inc(sm["aUE"], 1)
            for g4 in range(4):
                ng = rb * 4 + g4
                e.wait_ge(sm["aQWB"], 16 * WIN * (ng + 1))
                e.wait_ge(sm["aKWB"], 16 * WIN * (ng + 1))
                for hh in range(4):
                    h = 4 * g4 + hh
                    hx = rb * 16 + h
                    q0 = hx * NJT
                    e.wait_ge(sm["aSG"], hx)
                    e.wait_ge(sm["aVE"], NJT)
                    e.wait_ge(sm["aEE"], NJT * (rb + 1))
                    for jt in range(NJT):
                        q = q0 + jt
                        if q >= 2:
                            e.wait_ge(sm["aE1"], q - 1)
                        pb = self.p_sc[jt % 2]
                        e.matmul(pb[:, :], self.kw[0][:, 128 * jt:128 * jt + 128],
                                 self.qw[0][:, 512 * hh:512 * hh + 512],
                                 start=True, stop=False)
                        e.matmul(pb[:, :], self.kw[1][:, 128 * jt:128 * jt + 128],
                                 self.qw[1][:, 512 * hh:512 * hh + 512],
                                 start=False, stop=True).then_inc(sm["aSC"], 1)
                        if jt >= 1:
                            qq = q - 1
                            e.wait_ge(sm["aE2"], qq + 1)
                            jq = qq % NJT
                            e.matmul(self.p_pv[:, :],
                                     self.vs[:, 36 * jq + 9 * g4:36 * jq + 9 * g4 + 9],
                                     self.e2[qq % 2][:, :],
                                     start=(jq == 0), stop=False).then_inc(sm["aPV"], 1)
                    qq = q0 + NJT - 1
                    e.wait_ge(sm["aE2"], qq + 1)
                    e.matmul(self.p_pv[:, :],
                             self.vs[:, 36 * (NJT - 1) + 9 * g4:36 * (NJT - 1) + 9 * g4 + 9],
                             self.e2[qq % 2][:, :], start=False,
                             stop=True).then_inc(sm["aPV"], 1)
        if STAGE < 5:
            return
        e.wait_ge(sm["aDN"], 1)
        for ch in range(2):
            e.matmul(self.p_sc[ch][:, :], self.E16[:, :],
                     self.denb[:, 512 * ch:512 * ch + 512], start=True,
                     stop=True).then_inc(sm["aDXP"], 1)
        e.wait_ge(sm["aAB"], 1)
        for t in range(NTT):
            for ch in range(4):
                k = t * 4 + ch
                pb = self.p_x if ch % 2 == 0 else self.p_q
                if k >= 2:
                    e.wait_ge(sm["aLV"], k - 1)
                e.matmul(pb[:, :], self.attnb[:, 128 * t:128 * t + 128],
                         self.wop[:, 512 * ch:512 * ch + 512], start=True,
                         stop=True).then_inc(sm["aWO"], 1)

    def em_vector(self, e):
        sm = self.ctx.sems
        self._vsn = 0

        def W(ins):
            self._vsn += 1
            ins.then_inc(sm["aVSf"], 1)
            e.wait_ge(sm["aVSf"], self._vsn)
        if STAGE < 1:
            return
        e.wait_ge(sm["aP"], 16 * self.NPS)
        e.wait_ge(sm["aPg"], 16 * self.NPG)
        W(e.reciprocal(self.sc1[0:1, 1:2], self.sc1[0:1, 0:1]))
        W(e.tensor_scalar_mul(self.sc1[0:1, 6:7], self.sc1[0:1, 0:1], -1.0))
        W(e.tensor_max(self.sc1[0:1, 2:3], self.sc1[0:1, 0:1], self.sc1[0:1, 6:7]))
        W(e.reciprocal(self.sc1[0:1, 3:4], self.sc1[0:1, 2:3]))
        W(e.tensor_scalar_mul(self.sc1[0:1, 4:5], self.sc1[0:1, 3:4], -16.0))
        W(e.tensor_scalar_mul(self.sc1[0:1, 5:6], self.sc1[0:1, 1:2], 1.0 / 16.0))
        e.tensor_scalar(self.invi[0:1, :], self.invi[0:1, :], self.sc1[0:1, 1:2],
                        None, ALU.mult).then_inc(sm["aPV1"], 1)
        if 'jga' not in SKIP:
            for jt in range(NJT):
                e.tensor_scalar_add(self.jga[:, jt:jt + 1], self.jp[:, :], float(128 * jt))
        if 'memset' not in SKIP:
            e.memset(self.kfm[:, 0:WIN - 1], 0.0)
            e.memset(self.vfull[:, S:S + 128], 0.0)
            e.memset(self.vs[:, :], 1.0)
            e.memset(self.km[:, 3:4], 1e-5).then_inc(sm["aPV1"], 1)
        else:
            e.engine_nop().then_inc(sm["aPV1"], 1)
        e.wait_ge(sm["aWOE"], 1)
        if 'km' not in SKIP:
            e.tensor_copy(self.km[:, 0:3], self.p_x[:, 0:3]).then_inc(sm["aKM"], 1)
        else:
            e.engine_nop().then_inc(sm["aKM"], 1)
        W(e.tensor_sub(self.dvc[:, :], self.v1c[:, :], self.v0c[:, :]))
        e.tensor_copy(self.v0cb[:, :], self.v0c[:, :])
        e.tensor_scalar(self.wop[:, :], self.wo[:, :], self.dvc[:, 0:1], None,
                        ALU.mult).then_inc(sm["aWO"], 1)
        if STAGE < 3:
            return
        for jt in range(NJT):
            e.wait_ge(sm["aVT"], jt + 1)
            for g4 in range(4):
                cp = e.tensor_copy(self.vs[:, 36 * jt + 9 * g4:36 * jt + 9 * g4 + 8],
                                   self.p_tr[jt % 2][:, 8 * g4:8 * g4 + 8])
            cp.then_inc(sm["aVE"], 1)
        for ch in range(4):
            e.wait_ge(sm["aWOE"], 2 + ch)
            e.tensor_copy(self.v0wo[0:1, 512 * ch:512 * ch + 512],
                          self.p_x[0:1, :]).then_inc(sm["aRCP"], 1)
        for ch in range(4):
            e.wait_ge(sm["aWOE"], 6 + ch)
            e.tensor_copy(self.v0woe[:, 512 * ch:512 * ch + 512],
                          self.p_x[:, :]).then_inc(sm["aRCP"], 1)
        if STAGE < 4:
            return
        for rb in range(2):
            e.wait_ge(sm["aUE"], rb + 1)
            e.tensor_copy(self.Usb[:, :], self.p_q[:, :])
            for jt in range(NJT):
                idx = rb * NJT + jt
                if idx >= 2:
                    e.wait_ge(sm["aEE"], idx - 1)
                ofs = 4480 + 512 * rb - 128 * jt
                e.scalar_tensor_tensor(self.ebarg[idx % 2][:, :], self.Usb[:, :],
                                       self.jga[:, jt:jt + 1],
                                       self.maskm[:, ofs:ofs + 512], ALU.mult,
                                       ALU.add).then_inc(sm["aEA"], 1)
            for h in range(16):
                hx = rb * 16 + h
                q0 = hx * NJT
                for jt in range(NJT):
                    q = q0 + jt
                    e.wait_ge(sm["aE1"], q + 1)
                    if q >= 2:
                        e.wait_ge(sm["aPV"], q - 1)
                    e.tensor_mul(self.e2[q % 2][:, :], self.e1[q % 2][:, :],
                                 self.ebias[:, 512 * jt:512 * jt + 512]
                                 ).then_inc(sm["aE2"], 1)
                e.wait_ge(sm["aPV"], NJT * (hx + 1))
                e.wait_ge(sm["aASM"], 32 * hx)
                e.tensor_copy(self.stg[:, :], self.p_pv[:, :]).then_inc(sm["aSG"], 1)
        if STAGE < 5:
            return
        e.wait_ge(sm["aASM"], 32 * 32)
        e.tensor_scalar_max(self.den[:, :], self.den[:, :], 1e-30)
        e.tensor_copy(self.denb[:, :], self.den[:, :]).then_inc(sm["aDN"], 1)
        for ch in range(2):
            e.wait_ge(sm["aDXP"], ch + 1)
            e.reciprocal(self.recip[:, 512 * ch:512 * ch + 512], self.p_sc[ch][:, :])
        for ch in range(2):
            ab = e.tensor_mul(self.attnb[:, 512 * ch:512 * ch + 512],
                              self.attn[:, 512 * ch:512 * ch + 512],
                              self.recip[:, 512 * ch:512 * ch + 512])
        ab.then_inc(sm["aAB"], 1)
        for t in range(NTT):
            for ch in range(4):
                k = t * 4 + ch
                pb = self.p_x if ch % 2 == 0 else self.p_q
                e.wait_ge(sm["aWO"], 2 + k)
                e.tensor_add(self.xr[:, 512 * ch:512 * ch + 512], pb[:, :],
                             self.v0woe[:, 512 * ch:512 * ch + 512]
                             ).then_inc(sm["aLV"], 1)
            e.wait_ge(sm["aXT"], 16 * (t + 1))
            e.tensor_add(self.xr[:, :], self.xr[:, :],
                         self.xres[:, :]).then_inc(sm["aXC"], 1)
            W(e.reduce_sum(self.ln[:, 0:1], self.xr[:, :], AX))
            W(e.tensor_scalar_mul(self.ln[:, 1:2], self.ln[:, 0:1], 1.0 / D))
            e.tensor_scalar(self.xr[:, :], self.xr[:, :], self.ln[:, 1:2], None,
                            ALU.subtract)
            if t >= 1:
                e.wait_ge(sm["aOUT"], 32 * t)
            e.tensor_mul(self.x1b[:, :], self.xr[:, :], self.xr[:, :])
            e.reduce_sum(self.ln[:, 2:3], self.x1b[:, :], AX).then_inc(sm["aLNV"], 1)
            e.wait_ge(sm["aLNS"], t + 1)
            W(e.reciprocal(self.ln[:, 4:5], self.ln[:, 3:4]))
            e.tensor_scalar(self.x1f[:, :], self.xr[:, :], self.ln[:, 4:5], None,
                            ALU.mult).then_inc(sm["aX1"], 1)
            e.tensor_scalar(self.x1b[:, :], self.xr[:, :], self.ln[:, 4:5], None,
                            ALU.mult).then_inc(sm["aX1"], 1)

    def em_scalar(self, e):
        sm = self.ctx.sems
        nop = lambda: self.ctx.nc.scalar.nop()
        if STAGE < 1:
            return
        e.wait_ge(sm["aP"], 16 * self.NPS)
        e.wait_ge(sm["aPg"], 16 * self.NPG)

        e.wait_ge(sm["aKM"], 1)
        if STAGE < 2:
            return
        for t in range(NSWT):
            for dt in range(16):
                g = t * 16 + dt
                e.wait_ge(sm["aTW"], g + 1)
                if t >= 1 and dt == 0:
                    e.wait_ge(sm["aPJ"], t)
                e.copy(self.xT[:, 128 * dt:128 * dt + 128],
                       self.p_tr[g % 2][:, 0:128]).then_inc(sm["aTE"], 1)
            e.wait_ge(sm["aPJ"], t + 1)
            if t < 32:
                e.activation(self.kfm[:, WIN - 1 + 128 * t:WIN - 1 + 128 * t + 128],
                             self.p_kv[0:32, 0:128], AF.Tanh, scale=self.km[0:32, 2:3])
                e.activation(self.vfull[:, 128 * t:128 * t + 128],
                             self.p_kv[32:64, 0:128], AF.Sigmoid,
                             scale=self.km[32:64, 2:3]).then_inc(sm["aAC"], 1)
            else:
                tq = t - 32
                e.activation(self.qfm[:, 128 * tq:128 * tq + 128],
                             self.p_q[:, 0:128], AF.Tanh,
                             scale=self.km[:, 2:3]).then_inc(sm["aAC"], 1)
        if STAGE < 4:
            return
        for rb in range(2):
            for jt in range(NJT):
                idx = rb * NJT + jt
                e.wait_ge(sm["aEA"], idx + 1)
                e.activation(self.ebias[:, 512 * jt:512 * jt + 512],
                             self.ebarg[idx % 2][:, :], AF.Exp).then_inc(sm["aEE"], 1)
            for h in range(16):
                hx = rb * 16 + h
                q0 = hx * NJT
                for jt in range(NJT):
                    q = q0 + jt
                    e.wait_ge(sm["aSC"], q + 1)
                    if q >= 2:
                        e.wait_ge(sm["aE2"], q - 1)
                    e.activation(self.e1[q % 2][:, :], self.p_sc[jt % 2][:, :], AF.Exp,
                                 bias=self.km[:, 0:1],
                                 scale=self.km[:, 1:2]).then_inc(sm["aE1"], 1)
        if STAGE < 5:
            return
        for t in range(NTT):
            e.wait_ge(sm["aLNV"], t + 1)
            e.activation(self.ln[:, 3:4], self.ln[:, 2:3], AF.Sqrt,
                         scale=1.0 / D, bias=self.km[:, 3:4]).then_inc(sm["aLNS"], 1)


def _hostprep_A(inputs, core):
    beta, seg = core // 4, core % 4
    own0 = seg * TOW
    x = np.asarray(inputs["L0_attn_hidden_states"], dtype=np.float32)
    xb = x[beta]
    x_own = np.ascontiguousarray(xb[own0:own0 + TOW])
    x_pref = np.ascontiguousarray(xb)
    lo = own0 - 128
    x_qext = np.zeros((1152, D), np.float32)
    s0 = max(0, lo)
    x_qext[s0 - lo:, :] = xb[s0:own0 + TOW]
    wq = np.asarray(inputs["L0_attn_Wq"], np.float32)
    wk = np.asarray(inputs["L0_attn_Wk"], np.float32)
    wv = np.asarray(inputs["L0_attn_Wv"], np.float32)
    wo = np.asarray(inputs["L0_attn_Wo"], np.float32)
    wq_t = np.ascontiguousarray(wq.T.reshape(16, 128, 128).transpose(1, 0, 2).reshape(128, 2048))
    wkv_t = np.ascontiguousarray(np.concatenate([wk, wv], axis=0).T
                                 .reshape(16, 128, 64).transpose(1, 0, 2).reshape(128, 1024))
    wo_t = np.ascontiguousarray(wo.T)
    v0 = np.asarray(inputs["L0_attn_v_emb0"], np.float32).reshape(128, 1)
    v1 = np.asarray(inputs["L0_attn_v_emb1"], np.float32).reshape(128, 1)
    tau = np.asarray(inputs["L0_attn_tau"], np.float32).reshape(1, 1)
    jl = np.arange(128)[:, None]
    vv = np.arange(5632)[None, :]
    maskm = np.where(vv >= (4481 - own0) + jl, 0.0, -1e30).astype(np.float32)
    invi = (1.0 / (np.arange(own0, own0 + TOW, dtype=np.float64) + 1.0)).astype(np.float32).reshape(1, TOW)
    jp = np.arange(128, dtype=np.float32).reshape(128, 1)
    onesf = np.ones((1, 128), np.float32)
    E16f = np.zeros((16, 128), np.float32)
    for hh in range(16):
        E16f[hh, 8 * hh:8 * hh + 8] = 1.0
    identf = np.eye(128, dtype=np.float32)
    return dict(x_own=x_own, x_pref=x_pref, x_qext=x_qext, wq_t=wq_t, wkv_t=wkv_t,
                wo_t=wo_t, v_emb0=v0, v_emb1=v1, tau=tau, maskm=maskm, invi=invi,
                jp=jp, onesf=onesf, E16f=E16f, identf=identf)


def _declare_A(nc):
    io = {}
    decl = [("x_own", [TOW, D], F32), ("x_pref", [S, D], F32), ("x_qext", [1152, D], F32),
            ("wq_t", [128, 2048], F32), ("wkv_t", [128, 1024], F32), ("wo_t", [128, D], F32),
            ("v_emb0", [128, 1], F32), ("v_emb1", [128, 1], F32), ("tau", [1, 1], F32),
            ("maskm", [128, 5632], F32), ("invi", [1, TOW], F32), ("jp", [128, 1], F32),
            ("onesf", [1, 128], F32), ("E16f", [16, 128], F32), ("identf", [128, 128], F32)]
    for n, sh, dt in decl:
        io[n] = nc.dram_tensor(n, sh, dt, kind="ExternalInput").ap()
    return io


def build_program_A():
    nc = bass.Bass("TRN2", target_bir_lowering=False)
    stack = ExitStack()
    ctx = Ctx(nc, stack)
    io = _declare_A(nc)
    with stack:
        ph = PhaseA(ctx, io, out_kind="ExternalOutput")
        if DBG_A:
            ctx.dscr("dbg_attn", [128, TOW], F32, kind="ExternalOutput")
            ctx.dscr("dbg_attnb", [128, TOW], BF16, kind="ExternalOutput")
            ctx.dscr("dbg_den", [16, TOW], F32, kind="ExternalOutput")
            ctx.dscr("dbg_qfm", [128, 1152], BF16, kind="ExternalOutput")
            ctx.dscr("dbg_kfm", [32, WIN - 1 + S], BF16, kind="ExternalOutput")
            ctx.dscr("dbg_vs", [128, NJT * 36], BF16, kind="ExternalOutput")
            ctx.dscr("dbg_km", [128, 8], F32, kind="ExternalOutput")
            ctx.dscr("dbg_eb", [128, NJT * 512], BF16, kind="ExternalOutput")
        with nc.Block() as block:
            @block.sync
            def _(e):
                ph.em_sync(e)
                if STAGE >= 5:
                    e.wait_ge(ctx.sems["aOUT"], 32 * NTT)
                elif STAGE >= 2:
                    e.wait_ge(ctx.sems["aKDR"], 16)

            @block.gpsimd
            def _(e):
                ph.em_gpsimd(e)

            @block.tensor
            def _(e):
                ph.em_tensor(e)

            @block.vector
            def _(e):
                ph.em_vector(e)

            @block.scalar
            def _(e):
                ph.em_scalar(e)
    return nc


def run_phase_A(inputs):
    nc = build_program_A()
    in_maps = [_hostprep_A(inputs, c) for c in range(NCORE)]
    res = run_bass_kernel_spmd(nc, in_maps, core_ids=list(range(NCORE)))
    outs = []
    for c in range(NCORE):
        outs.append(np.asarray(res.results[c]["x1f_d"]))
    return outs


def kernel(**inputs):
    cores = list(range(NCORE))
    resA = run_bass_kernel_spmd(build_program_A(),
                                [_hostprep_A(inputs, c) for c in cores],
                                core_ids=cores)
    imB = []
    for c in cores:
        im = _hostprep_B(inputs, c)
        im["x1f_d"] = np.asarray(resA.results[c]["x1f_d"])
        im["x1b_d"] = np.asarray(resA.results[c]["x1b_d"])
        imB.append(im)
    resB = run_bass_kernel_spmd(build_program_B(), imB, core_ids=cores)
    imC = []
    for c in cores:
        im = _hostprep_C(inputs, c)
        im["x2f_d"] = np.asarray(resB.results[c]["x2f_d"])
        im["x2b_d"] = np.asarray(resB.results[c]["x2b_d"])
        imC.append(im)
    resC = run_bass_kernel_spmd(build_program_C(), imC, core_ids=cores)
    imD = []
    for c in cores:
        im = _hostprep_D(inputs, c)
        im["x3f_d"] = np.asarray(resC.results[c]["x3f_d"])
        im["x3b_d"] = np.asarray(resC.results[c]["x3b_d"])
        imD.append(im)
    resD = run_bass_kernel_spmd(build_program_D(), imD, core_ids=cores)
    out = np.zeros((B, S, D), np.float32)
    for c in cores:
        b, s = c // 4, c % 4
        out[b, s * 1024:(s + 1) * 1024] = np.asarray(resD.results[c]["x4f_d"])
    return out


NSEG = 16
ETS = 4
DE = 8192


class PhaseB:

    def __init__(self, ctx: Ctx, io, out_kind="Internal"):
        self.ctx = ctx
        self.io = io
        c = ctx
        self.xT = c.sbuf("B_xT", [128, 16 * 1024], BF16)
        self.win = [c.sbuf(f"B_win{i}", [128, 2048], F32) for i in range(2)]
        self.sgn = c.sbuf("B_sgn", [128, 2048], BF16)
        self.q2 = [c.sbuf(f"B_q2_{i}", [128, 2048], BF16) for i in range(4)]
        self.h8 = c.sbuf("B_h8", [128, ETS * 1024], BF16)
        self.wdin = [c.sbuf(f"B_wdin{i}", [128, 2048], F32) for i in range(2)]
        self.sgd = c.sbuf("B_sgd", [128, 2048], BF16)
        self.q2d = [c.sbuf(f"B_q2d{i}", [128, 2048], BF16) for i in range(ETS)]
        self.dacc = c.sbuf("B_dacc", [128, NTT * 2048], F32)
        self.x1res = c.sbuf("B_x1res", [128, 2048], F32)
        self.x2f = c.sbuf("B_x2f", [128, 2048], F32)
        self.x2b = c.sbuf("B_x2b", [128, 2048], BF16)
        self.racc = c.sbuf("B_racc", [128, 64], F32)
        self.acc3 = c.sbuf("B_acc3", [128, 4], F32)
        self.tot3 = c.sbuf("B_tot3", [3, 8], F32)
        self.scC = c.sbuf("B_scC", [128, 12], F32)
        self.onesf = c.sbuf("B_onesf", [1, 128], F32)
        self.onescol = c.sbuf("B_onescol", [128, 1], F32)
        self.sel3 = c.sbuf("B_sel3", [3, 384], F32)
        self.silu = c.sbuf("B_silu", [128, 1024], BF16)
        self.ln = c.sbuf("B_ln", [128, 8], F32)
        self.p_up = c.psum("B_pup", [128, 1024], F32)
        self.p_g = c.psum("B_pg", [128, 1024], F32)
        self.p_dn = c.psum("B_pdn", [128, 2048], F32)
        self.part_d = c.dscr("B_part", [3, 1], F32)
        self.red_d = c.dscr("B_red", [3, 1], F32)
        self.x2f_d = c.dscr("x2f_d", [TOW, D], F32, kind=out_kind)
        self.x2b_d = c.dscr("x2b_d", [TOW, D], BF16, kind=out_kind)
        self.x1f_d = c.dram["x1f_d"] if "x1f_d" in c.dram else c.dscr("x1f_d", [TOW, D], F32)
        self.x1b_d = c.dram["x1b_d"] if "x1b_d" in c.dram else c.dscr("x1b_d", [TOW, D], BF16)
        c.sem("bSL", "bRD", "bACC", "bTOT", "bCC", "bCCL", "bSCL", "bXT", "bLU",
              "bSGp", "bQ", "bMM", "bSILs", "bGP", "bH", "bLD", "bSGd", "bQD",
              "bDN", "bFL", "bX1R", "bLNV", "bLNS", "bX2", "bOUT", "bVS", "bUP")

    def em_sync(self, e):
        sm, io = self.ctx.sems, self.io
        e.dma_start(out=self.onesf[:, :], in_=io["onesf"]).then_inc(sm["bSL"], 16)
        e.dma_start(out=self.sel3[:, :], in_=io["sel3"]).then_inc(sm["bSL"], 16)
        for m, nm in enumerate(["wu_sl", "wg_sl", "wd_sl"]):
            for st in range(16):
                idx = m * 16 + st
                if idx >= 2:
                    e.wait_ge(sm["bRD"], idx - 1)
                e.dma_start(out=self.win[idx % 2][:, 0:1024],
                            in_=apm(io[nm], st * 1024, [[16384, 128], [1, 1024]])
                            ).then_inc(sm["bSL"], 16)
        for dt in range(16):
            e.dma_start(out=self.xT[:, 1024 * dt:1024 * dt + 1024],
                        in_=apm(self.x1b_d.ap(), 128 * dt, [[D, 1024], [1, 128]]),
                        transpose=True).then_inc(sm["bXT"], 16)
        for seg in range(NSEG):
            for etl in range(ETS):
                etg = seg * ETS + etl
                for w, nm in enumerate(["wu_t", "wg_t"]):
                    li = 2 * etg + w
                    if li >= 2:
                        e.wait_ge(sm["bSGp"], li - 1)
                    else:
                        e.wait_ge(sm["bACC"], 1)
                    e.dma_start(out=self.win[li % 2][:, :],
                                in_=apm(io[nm], 128 * etg,
                                        [[DE, 128], [128 * DE, 16], [1, 128]])
                                ).then_inc(sm["bLU"], 16)
            for etl in range(ETS):
                eg = seg * ETS + etl
                if eg >= 2:
                    e.wait_ge(sm["bSGd"], eg - 1)
                e.dma_start(out=self.wdin[eg % 2][:, :],
                            in_=apm(io["wd_t"], 128 * eg * D, [[D, 128], [1, D]])
                            ).then_inc(sm["bLD"], 16)
        for t in range(NTT):
            if t >= 1:
                e.wait_ge(sm["bX1R"], t)
            e.dma_start(out=self.x1res[:, :],
                        in_=apm(self.x1f_d.ap(), t * 128 * D, [[D, 128], [1, D]])
                        ).then_inc(sm["bXT"], 16)
            if t >= 1:
                e.wait_ge(sm["bX2"], 2 * t)
                e.dma_start(out=apm(self.x2f_d.ap(), (t - 1) * 128 * D, [[D, 128], [1, D]]),
                            in_=self.x2f[:, :]).then_inc(sm["bOUT"], 16)
                e.dma_start(out=apm(self.x2b_d.ap(), (t - 1) * 128 * D, [[D, 128], [1, D]]),
                            in_=self.x2b[:, :]).then_inc(sm["bOUT"], 16)
        e.wait_ge(sm["bX2"], 2 * NTT)
        e.dma_start(out=apm(self.x2f_d.ap(), (NTT - 1) * 128 * D, [[D, 128], [1, D]]),
                    in_=self.x2f[:, :]).then_inc(sm["bOUT"], 16)
        e.dma_start(out=apm(self.x2b_d.ap(), (NTT - 1) * 128 * D, [[D, 128], [1, D]]),
                    in_=self.x2b[:, :]).then_inc(sm["bOUT"], 16)
        if DBG_B:
            e.wait_ge(sm["bSCL"], 2)
            e.dma_start(out=self.ctx.dram["dbgB_scC"].ap(), in_=self.scC[:, :]).then_inc(sm["bVS"] if False else sm["bOUT"], 16)
            e.dma_start(out=self.ctx.dram["dbgB_tot"].ap(), in_=self.tot3[:, :]).then_inc(sm["bOUT"], 16)
            e.dma_start(out=self.ctx.dram["dbgB_acc"].ap(), in_=self.acc3[:, :]).then_inc(sm["bOUT"], 16)
            e.wait_ge(sm["bFL"], NSEG * NTT)
            e.dma_start(out=self.ctx.dram["dbgB_dacc"].ap(), in_=self.dacc[:, :]).then_inc(sm["bOUT"], 16)
        e.wait_ge(sm["bOUT"], 32 * NTT + (64 if DBG_B else 0))

    def em_gpsimd(self, e):
        sm = self.ctx.sems
        e.wait_ge(sm["bACC"], 2)
        e.dma_start(out=self.part_d.ap(), in_=self.tot3[:, 0:1]).then_inc(sm["bCC"], 16)
        e.wait_ge(sm["bCC"], 16)
        e.collective_compute("AllReduce", ALU.add, replica_groups=[list(range(8))],
                             ins=[self.part_d.ap()], outs=[self.red_d.ap()]
                             ).then_inc(sm["bCC"], 1)
        e.wait_ge(sm["bCC"], 17)
        e.dma_start(out=self.tot3[:, 0:1], in_=self.red_d.ap()).then_inc(sm["bCCL"], 16)

    def em_tensor(self, e):
        sm = self.ctx.sems
        e.wait_ge(sm["bACC"], 1)
        e.matmul(self.p_up[0:3, 0:1], self.acc3[:, 0:3], self.onescol[:, 0:1],
                 start=True, stop=True).then_inc(sm["bTOT"], 1)
        e.wait_ge(sm["bSCL"], 1)
        for i, (m, col) in enumerate([(0, 2), (1, 2), (2, 2), (0, 3), (1, 3), (2, 3)]):
            mm = e.matmul(self.p_g[:, i:i + 1], self.sel3[:, 128 * m:128 * m + 128],
                          self.tot3[:, col:col + 1], start=True, stop=True)
        mm.then_inc(sm["bTOT"], 1)
        e.wait_ge(sm["bXT"], 16 * 16)
        e.wait_ge(sm["bSCL"], 2)
        for seg in range(NSEG):
            for etl in range(ETS):
                etg = seg * ETS + etl
                for w in range(2):
                    li = 2 * etg + w
                    qi = li % 4
                    e.wait_ge(sm["bQ"], li + 1)
                    if w == 0 and etg >= 1:
                        e.wait_ge(sm["bSILs"], etg)
                    if w == 1 and etg >= 1:
                        e.wait_ge(sm["bH"], etg)
                    pb = self.p_up if w == 0 else self.p_g
                    for c2 in range(2):
                        for dt in range(16):
                            mm = e.matmul(pb[:, 512 * c2:512 * c2 + 512],
                                          self.q2[qi][:, 128 * dt:128 * dt + 128],
                                          self.xT[:, 1024 * dt + 512 * c2:1024 * dt + 512 * c2 + 512],
                                          start=(dt == 0), stop=(dt == 15))
                    mm.then_inc(sm["bMM"], 1)
            e.wait_ge(sm["bH"], ETS * (seg + 1))
            e.wait_ge(sm["bQD"], ETS * (seg + 1))
            for tt in range(NTT):
                it = seg * NTT + tt
                if it >= 1:
                    e.wait_ge(sm["bFL"], it)
                for etl in range(ETS):
                    for c4 in range(4):
                        mm = e.matmul(self.p_dn[:, 512 * c4:512 * c4 + 512],
                                      self.h8[:, 1024 * etl + 128 * tt:1024 * etl + 128 * tt + 128],
                                      self.q2d[etl][:, 512 * c4:512 * c4 + 512],
                                      start=(etl == 0), stop=(etl == ETS - 1))
                mm.then_inc(sm["bDN"], 1)

    def em_vector(self, e):
        sm = self.ctx.sems
        self._vsn = 0

        def W(ins):
            self._vsn += 1
            ins.then_inc(sm["bVS"], 1)
            e.wait_ge(sm["bVS"], self._vsn)
        e.memset(self.onescol[:, :], 1.0)
        e.memset(self.scC[:, 6:7], 1e-5)
        e.memset(self.scC[:, 7:8], -0.5)
        e.memset(self.scC[:, 8:9], 0.5)
        for idx in range(48):
            e.wait_ge(sm["bSL"], 16 * (idx + 3))
            e.reduce_sum(self.racc[:, idx:idx + 1], self.win[idx % 2][:, 0:1024], AX,
                         apply_absolute_value=True).then_inc(sm["bRD"], 1)
        for m in range(3):
            W(e.reduce_sum(self.acc3[:, m:m + 1], self.racc[:, 16 * m:16 * m + 16], AX))
        e.engine_nop().then_inc(sm["bACC"], 1)
        e.wait_ge(sm["bTOT"], 1)
        e.tensor_copy(self.tot3[:, 0:1], self.p_up[0:3, 0:1]).then_inc(sm["bACC"], 1)
        e.wait_ge(sm["bCCL"], 16)
        W(e.tensor_scalar(self.tot3[:, 1:2], self.tot3[:, 0:1],
                          1.0 / (2048.0 * 8192.0), 1e-8, ALU.mult, ALU.add))
        W(e.reciprocal(self.tot3[:, 2:3], self.tot3[:, 1:2]))
        W(e.tensor_scalar_mul(self.tot3[:, 3:4], self.tot3[:, 1:2], 0.5))
        e.engine_nop().then_inc(sm["bSCL"], 1)
        e.wait_ge(sm["bTOT"], 2)
        e.tensor_copy(self.scC[:, 0:6], self.p_g[:, 0:6])
        e.engine_nop().then_inc(sm["bSCL"], 1)
        for seg in range(NSEG):
            for etl in range(ETS):
                etg = seg * ETS + etl
                for w in range(2):
                    li = 2 * etg + w
                    qi = li % 4
                    e.wait_ge(sm["bSGp"], li + 1)
                    if li >= 4:
                        e.wait_ge(sm["bMM"], li - 3)
                    e.tensor_add(self.q2[qi][:, :], self.q2[qi][:, :],
                                 self.sgn[:, :]).then_inc(sm["bQ"], 1)
                e.wait_ge(sm["bMM"], 2 * etg + 2)
                e.wait_ge(sm["bSILs"], etg + 1)
                e.scalar_tensor_tensor(self.h8[:, 1024 * etl:1024 * etl + 1024],
                                       self.p_g[:, :], self.scC[:, 4:5],
                                       self.silu[:, :], ALU.mult,
                                       ALU.mult).then_inc(sm["bH"], 1)
            for etl in range(ETS):
                eg = seg * ETS + etl
                e.wait_ge(sm["bSGd"], eg + 1)
                if seg >= 1:
                    e.wait_ge(sm["bDN"], NTT * seg)
                e.tensor_add(self.q2d[etl][:, :], self.q2d[etl][:, :],
                             self.sgd[:, :]).then_inc(sm["bQD"], 1)
            for tt in range(NTT):
                it = seg * NTT + tt
                e.wait_ge(sm["bDN"], it + 1)
                if seg == 0:
                    e.tensor_copy(self.dacc[:, 2048 * tt:2048 * tt + 2048],
                                  self.p_dn[:, :]).then_inc(sm["bFL"], 1)
                else:
                    e.tensor_add(self.dacc[:, 2048 * tt:2048 * tt + 2048],
                                 self.dacc[:, 2048 * tt:2048 * tt + 2048],
                                 self.p_dn[:, :]).then_inc(sm["bFL"], 1)
        e.wait_ge(sm["bFL"], NSEG * NTT)
        for t in range(NTT):
            e.wait_ge(sm["bXT"], 16 * (16 + t + 1))
            dx = self.dacc[:, 2048 * t:2048 * t + 2048]
            e.scalar_tensor_tensor(dx, dx, self.scC[:, 5:6], self.x1res[:, :],
                                   ALU.mult, ALU.add).then_inc(sm["bX1R"], 1)
            W(e.reduce_sum(self.ln[:, 0:1], dx, AX))
            W(e.tensor_scalar_mul(self.ln[:, 1:2], self.ln[:, 0:1], 1.0 / D))
            e.tensor_scalar(dx, dx, self.ln[:, 1:2], None, ALU.subtract)
            if t >= 1:
                e.wait_ge(sm["bOUT"], 32 * t)
            e.tensor_mul(self.x2b[:, :], dx, dx)
            e.reduce_sum(self.ln[:, 2:3], self.x2b[:, :], AX).then_inc(sm["bLNV"], 1)
            e.wait_ge(sm["bLNS"], t + 1)
            W(e.reciprocal(self.ln[:, 4:5], self.ln[:, 3:4]))
            e.tensor_scalar(self.x2f[:, :], dx, self.ln[:, 4:5], None,
                            ALU.mult).then_inc(sm["bX2"], 1)
            e.tensor_scalar(self.x2b[:, :], dx, self.ln[:, 4:5], None,
                            ALU.mult).then_inc(sm["bX2"], 1)

    def em_scalar(self, e):
        sm = self.ctx.sems
        e.wait_ge(sm["bSCL"], 2)
        for seg in range(NSEG):
            for etl in range(ETS):
                etg = seg * ETS + etl
                for w in range(2):
                    li = 2 * etg + w
                    qi = li % 4
                    e.wait_ge(sm["bLU"], 16 * (li + 1))
                    if li >= 4:
                        e.wait_ge(sm["bMM"], li - 3)
                    if li >= 1:
                        e.wait_ge(sm["bQ"], li)
                    sc = self.scC[:, w:w + 1]
                    e.activation(self.q2[qi][:, :], self.win[li % 2][:, :], AF.Sign,
                                 scale=sc, bias=self.scC[:, 7:8])
                    e.activation(self.sgn[:, :], self.win[li % 2][:, :], AF.Sign,
                                 scale=sc, bias=self.scC[:, 8:9]).then_inc(sm["bSGp"], 1)
                e.wait_ge(sm["bMM"], 2 * etg + 1)
                if etg >= 1:
                    e.wait_ge(sm["bH"], etg)
                e.activation(self.silu[:, :], self.p_up[:, :], AF.Silu,
                             scale=self.scC[:, 3:4]).then_inc(sm["bSILs"], 1)
            for etl in range(ETS):
                eg = seg * ETS + etl
                e.wait_ge(sm["bLD"], 16 * (eg + 1))
                if seg >= 1:
                    e.wait_ge(sm["bDN"], NTT * seg)
                if eg >= 1:
                    e.wait_ge(sm["bQD"], eg)
                e.activation(self.q2d[etl][:, :], self.wdin[eg % 2][:, :], AF.Sign,
                             scale=self.scC[:, 2:3], bias=self.scC[:, 7:8])
                e.activation(self.sgd[:, :], self.wdin[eg % 2][:, :], AF.Sign,
                             scale=self.scC[:, 2:3],
                             bias=self.scC[:, 8:9]).then_inc(sm["bSGd"], 1)
        for t in range(NTT):
            e.wait_ge(sm["bLNV"], t + 1)
            e.activation(self.ln[:, 3:4], self.ln[:, 2:3], AF.Sqrt,
                         scale=1.0 / D, bias=self.scC[:, 6:7]).then_inc(sm["bLNS"], 1)


def _hostprep_B(inputs, core):
    wu = np.asarray(inputs["L0_ffn_W_up"], np.float32)
    wg = np.asarray(inputs["L0_ffn_W_gate"], np.float32)
    wd = np.asarray(inputs["L0_ffn_W_down"], np.float32)
    wu_t = np.ascontiguousarray(wu.T)
    wg_t = np.ascontiguousarray(wg.T)
    wd_t = np.ascontiguousarray(wd.T)
    def til(a):
        return np.ascontiguousarray(a.reshape(16, 128, 1024).transpose(1, 0, 2).reshape(128, 16384))
    wu_sl = til(wu_t[:, 1024 * core:1024 * core + 1024])
    wg_sl = til(wg_t[:, 1024 * core:1024 * core + 1024])
    wd_sl = np.ascontiguousarray(
        wd_t[1024 * core:1024 * core + 1024].reshape(8, 128, 2048).transpose(1, 0, 2).reshape(128, 16384))
    sel3 = np.zeros((3, 384), np.float32)
    for m in range(3):
        sel3[m, 128 * m:128 * m + 128] = 1.0
    return dict(wu_t=wu_t, wg_t=wg_t, wd_t=wd_t, wu_sl=wu_sl, wg_sl=wg_sl,
                wd_sl=wd_sl, sel3=sel3, onesf=np.ones((1, 128), np.float32))


def _declare_B(nc, standalone=True):
    io = {}
    decl = [("wu_t", [D, DE], F32), ("wg_t", [D, DE], F32), ("wd_t", [DE, D], F32),
            ("wu_sl", [128, 16384], F32), ("wg_sl", [128, 16384], F32),
            ("wd_sl", [128, 16384], F32), ("sel3", [3, 384], F32)]
    if standalone:
        decl += [("onesf", [1, 128], F32)]
    for n, sh, dt in decl:
        io[n] = nc.dram_tensor(n, sh, dt, kind="ExternalInput").ap()
    return io


def build_program_B():
    nc = bass.Bass("TRN2", target_bir_lowering=False)
    stack = ExitStack()
    ctx = Ctx(nc, stack)
    io = _declare_B(nc)
    ctx.dram["x1f_d"] = nc.dram_tensor("x1f_d", [TOW, D], F32, kind="ExternalInput")
    ctx.dram["x1b_d"] = nc.dram_tensor("x1b_d", [TOW, D], BF16, kind="ExternalInput")
    with stack:
        ph = PhaseB(ctx, io, out_kind="ExternalOutput")
        if DBG_B:
            ctx.dscr("dbgB_scC", [128, 12], F32, kind="ExternalOutput")
            ctx.dscr("dbgB_tot", [3, 8], F32, kind="ExternalOutput")
            ctx.dscr("dbgB_acc", [128, 4], F32, kind="ExternalOutput")
            ctx.dscr("dbgB_dacc", [128, NTT * 2048], F32, kind="ExternalOutput")
        with nc.Block() as block:
            @block.sync
            def _(e):
                ph.em_sync(e)

            @block.gpsimd
            def _(e):
                ph.em_gpsimd(e)

            @block.tensor
            def _(e):
                ph.em_tensor(e)

            @block.vector
            def _(e):
                ph.em_vector(e)

            @block.scalar
            def _(e):
                ph.em_scalar(e)
    return nc


NH = 64
HP = 64
NST = 128
CH = 64
NCH = 16
NMT = 67
MT_ORDER = [66, 64, 65] + list(range(32, 64)) + list(range(32))


class PhaseC:

    def __init__(self, ctx: Ctx, io, out_kind="Internal"):
        self.ctx = ctx
        self.io = io
        c = ctx
        self.xT = c.sbuf("C_xT", [128, 16 * 1024], BF16)
        self.win = [c.sbuf(f"C_win{i}", [128, 2048], BF16) for i in range(2)]
        self.zstg = [c.sbuf(f"C_zstg{i}", [128, 1024], BF16) for i in range(2)]
        self.xbc = [c.sbuf(f"C_xbc{i}", [128, 1032], BF16) for i in range(2)]
        self.xssm = c.sbuf("C_xssm", [128, 1024], BF16)
        self.Bfm = c.sbuf("C_Bfm", [128, 1024], BF16)
        self.Cfm = c.sbuf("C_Cfm", [128, 1024], BF16)
        self.btok = c.sbuf("C_btok", [64, 2048], BF16)
        self.xstg = c.sbuf("C_xstg", [128, 1024], BF16)
        self.dtraw = c.sbuf("C_dtraw", [64, 1024], F32)
        self.dt = c.sbuf("C_dt", [64, 1024], F32)
        self.acs = [c.sbuf(f"C_acs{i}", [64, 1024], F32) for i in range(2)]
        self.acsT = c.sbuf("C_acsT", [64, 1024], F32)
        self.dtT = c.sbuf("C_dtT", [64, 1024], F32)
        self.EA = c.sbuf("C_EA", [64, 1024], F32)
        self.dec = c.sbuf("C_dec", [64, 1024], BF16)
        self.decT = c.sbuf("C_decT", [64, 1024], BF16)
        self.plog = c.sbuf("C_plog", [64, 20], F32)
        self.hout = c.sbuf("C_hout", [128, 102], BF16)
        self.hgb = c.sbuf("C_hgb", [128, 816], BF16)
        self.hsel = c.sbuf("C_hsel", [128, 102], BF16)
        self.xch = c.sbuf("C_xch", [64, 4096], BF16)
        self.xdt = c.sbuf("C_xdt", [64, 4096], BF16)
        self.G = c.sbuf("C_G", [64, 64], BF16)
        self.argf = c.sbuf("C_argf", [64, 1024], F32)
        self.lgcol = c.sbuf("C_lgcol", [64, 1], F32)
        self.M = c.sbuf("C_M", [64, 4096], BF16)
        self.bdec = c.sbuf("C_bdec", [64, 1024], BF16)
        self.run = c.sbuf("C_run", [128, 4096], F32)
        self.runbf = c.sbuf("C_runbf", [128, 4096], BF16)
        self.csc = c.sbuf("C_csc", [128, 4096], BF16)
        self.ybuf = c.sbuf("C_ybuf", [64, 4096], F32)
        self.afl = [c.sbuf(f"C_afl{i}", [1, 1024], F32) for i in range(2)]
        self.afl2 = [c.sbuf(f"C_afl2{i}", [1, 1024], F32) for i in range(2)]
        self.gch = c.sbuf("C_gch", [128, 1024], F32)
        self.pde = c.sbuf("C_pde", [64, 16], F32)
        self.drow = c.sbuf("C_drow", [1, 64], F32)
        self.cdrow = c.sbuf("C_cdrow", [1, 64], F32)
        self.pdrow = c.sbuf("C_pdrow", [1, 64], F32)
        self.wrow = c.sbuf("C_wrow", [1, 64], F32)
        self.cacc = c.sbuf("C_cacc", [128, 1024], BF16)
        self.nwb = c.sbuf("C_nwb", [128, 4096], BF16)
        self.cde = c.sbuf("C_cde", [128, 64], F32)
        self.De = c.sbuf("C_De", [128, 64], F32)
        self.wexp = c.sbuf("C_wexp", [128, 64], F32)
        self.crow = c.sbuf("C_crow", [1, 64], F32)
        self.lgp = c.sbuf("C_lgp", [8, 64], F32)
        self.wT = c.sbuf("C_wT", [64, 8], F32)
        self.mbias = c.sbuf("C_mbias", [64, 8], F32)
        self.mBt = c.sbuf("C_mBt", [8, 8], F32)
        self.trilU = c.sbuf("C_trilU", [64, 64], BF16)
        self.acol = c.sbuf("C_acol", [64, 8], F32)
        self.convw = c.sbuf("C_convw", [128, 136], F32)
        self.convb = c.sbuf("C_convb", [128, 34], F32)
        self.onesf = c.sbuf("C_onesf", [1, 128], F32)
        self.identb = c.sbuf("C_identb", [128, 128], BF16)
        self.identf = c.sbuf("C_identf", [128, 128], F32)
        self.selw = c.sbuf("C_selw", [128, 8], F32)
        self.ln = c.sbuf("C_ln", [128, 8], F32)
        self.zt = c.sbuf_at("C_zt", [128, 4096], BF16, "C_xT", 0)
        self.zsil = c.sbuf_at("C_zsil", [128, 4096], BF16, "C_xT", 8192)
        self.yt = c.sbuf_at("C_yt", [128, 4096], F32, "C_xT", 16384)
        self.ynb = c.sbuf_at("C_ynb", [128, 4096], BF16, "C_M", 0)
        self.ynT = [c.sbuf_at(f"C_ynT{i}", [128, 1024], BF16, "C_xch", 2048 * i) for i in range(4)]
        self.wo = [c.sbuf_at(f"C_wo{i}", [128, 2048], BF16, "C_run", 4096 * i) for i in range(4)]
        self.ostg = c.sbuf_at("C_ostg", [128, 2048], F32, "C_xdt", 0)
        self.x2r = c.sbuf_at("C_x2r", [128, 2048], F32, "C_runbf", 0)
        self.x3f = c.sbuf_at("C_x3f", [128, 2048], F32, "C_ybuf", 0)
        self.x3b = c.sbuf_at("C_x3b", [128, 2048], BF16, "C_ybuf", 8192)
        self.p_a = c.psum("C_pa", [128, 512], F32)
        self.p_b = c.psum("C_pb", [128, 512], F32)
        self.p_G = c.psum("C_pG", [64, 512], F32)
        self.p_S = c.psum("C_pS", [128, 512], F32)
        self.p_x = c.psum("C_px", [128, 512], F32)
        self.p_t = [c.psum(f"C_pt{i}", [128, 512], BF16) for i in range(2)]
        self.p_o = c.psum("C_po", [128, 512], F32)
        self.x_d = c.dscr("C_x_d", [TOW, 4096], BF16)
        self.z_d = c.dscr("C_z_d", [4096, TOW], BF16)
        self.y_d = c.dscr("C_y_d", [TOW, 4096], F32)
        self.acs_d = c.dscr("C_acs_d", [64, 1024], F32)
        self.ea_d = c.dscr("C_ea_d", [64, 1024], F32)
        self.pde_d = c.dscr("C_pde_d", [64, 16], F32)
        self.hout_d = c.dscr("C_hout_d", [128, 102], BF16)
        self.hg_d = c.dscr("C_hg_d", [1024, 102], BF16)
        self.sum_d = c.dscr("C_sum_d", [128, 4096], F32)
        self.sumg_d = c.dscr("C_sumg_d", [1024, 4096], F32)
        self.lg_d = c.dscr("C_lg_d", [64, 1], F32)
        self.lgg_d = c.dscr("C_lgg_d", [512, 1], F32)
        self.wt_d = c.dscr("C_wt_d", [64, 8], F32)
        self.yn_d = c.dscr("C_yn_d", [TOW, 4096], BF16)
        self.o_d = c.dscr("C_o_d", [8 * TOW, D], F32)
        self.prt = c.sbuf_at("C_prt", [128, 2048], F32, "C_xT", 24576)
        self.x3f_d = c.dscr("x3f_d", [TOW, D], F32, kind=out_kind)
        self.x3b_d = c.dscr("x3b_d", [TOW, D], BF16, kind=out_kind)
        self.x2f_d = c.dram["x2f_d"] if "x2f_d" in c.dram else c.dscr("x2f_d", [TOW, D], F32)
        self.x2b_d = c.dram["x2b_d"] if "x2b_d" in c.dram else c.dscr("x2b_d", [TOW, D], BF16)
        c.sem("cP", "cPg", "cXT", "cHM", "cHL", "cHLd", "cHG", "cHS", "cWL",
              "cPJ", "cEVp", "cCVt", "cSILa", "cTR", "cTRe", "cXSD",
              "cZDd", "cTF", "cTFe", "cTB", "cTBe", "cACD", "cACSv", "cEAs",
              "cXC", "cAF", "cBAm", "cBAv", "cARGv", "cMe", "cMv", "cGm", "cGV",
              "cCD", "cCDm", "cCDE", "cEF", "cEFm", "cCSCv", "cBD", "cXDT",
              "cST", "cRU", "cYM", "cYE", "cYD", "cSU", "cAG", "cLG",
              "cWC", "cWE", "cWEd", "cWR", "cWXm", "cCW", "cGC", "cGA",
              "cPD", "cPDm", "cPDE", "cCSCc", "cCSC2v", "cEF2", "cEF2m",
              "cYB2", "cYM2", "cYE2", "cP2d", "cZT", "cYN", "cYND", "cWO",
              "cYNT", "cOM", "cOS", "cOA", "cXRl", "cLNS", "cX3", "cOUT",
              "cVS", "cDE", "cP5a", "cP5b", "cRBv", "cXRc")

    NPS = 11
    NPG = 1

    def bar(self):
        import concourse.mybir as _mb
        ET = _mb.EngineType
        self.ctx.nc.multi_engine_barrier([ET.PE, ET.DVE, ET.Activation, ET.SP, ET.Pool])
        self._bn = getattr(self, "_bn", 0) + 1
        if CCUT and self._bn >= CCUT:
            raise _CutDone()


    XSSM = list(range(32, 64))

    def em_sync(self, e):
        self._bn = 0
        try:
            self._em_sync(e)
        except _CutDone:
            pass

    def _em_sync(self, e):
        sm, io = self.ctx.sems, self.io
        for name, dst in [("onesf", self.onesf), ("identf", self.identf),
                          ("identb", self.identb), ("trilU", self.trilU),
                          ("convw", self.convw), ("convb", self.convb),
                          ("mBt", self.mBt), ("mbias", self.mbias),
                          ("selw", self.selw), ("drow", self.drow)]:
            e.dma_start(out=dst[:, :], in_=io[name]).then_inc(sm["cP"], 16)
        e.dma_start(out=self.acol[:, 0:4], in_=io["acolin"]).then_inc(sm["cP"], 16)
        for dt_ in range(16):
            e.dma_start(out=self.xT[:, 1024 * dt_:1024 * dt_ + 1024],
                        in_=apm(self.x2b_d.ap(), 128 * dt_, [[D, 1024], [1, 128]]),
                        transpose=True).then_inc(sm["cXT"], 16)
        e.wait_ge(sm["cP"], 16 * self.NPS)
        e.wait_ge(sm["cXT"], 16 * 16)
        self.bar()
        e.wait_ge(sm["cHL"], 34)
        e.dma_start(out=self.hout_d.ap(), in_=self.hout[:, :]).then_inc(sm["cHLd"], 16)
        e.wait_ge(sm["cHLd"], 16)
        self.bar()
        self.bar()
        for p in range(8):
            e.dma_start(out=self.hgb[:, 102 * p:102 * p + 102],
                        in_=apm(self.hg_d.ap(), 128 * p * 102, [[102, 128], [1, 102]])
                        ).then_inc(sm["cHS"], 16)
        for mi, m in enumerate(MT_ORDER):
            if m < 32:
                zi = mi - 35
                e.wait_ge(sm["cEVp"], 2 * (mi + 1))
                e.dma_start(out=apm(self.z_d.ap(), 128 * m * TOW, [[TOW, 128], [1, TOW]]),
                            in_=self.zstg[mi % 2][:, :]).then_inc(sm["cZDd"], 16)
            elif 32 <= m < 64:
                sxi = m - 32
                e.wait_ge(sm["cTRe"], 8 * (sxi + 1))
                e.dma_start(out=apm(self.x_d.ap(), 128 * sxi,
                                    [[4096, 128], [128 * 4096, 8], [1, 128]]),
                            in_=apm(self.xstg[:, :], 0, [[1024, 128], [128, 8], [1, 128]])
                            ).then_inc(sm["cXSD"], 16)
        self.bar()
        self.bar()
        self.bar()
        self.bar()
        self.bar()
        e.wait_ge(sm["cACSv"], 1)
        e.dma_start(out=self.acs_d.ap(), in_=self.acs[1][:, :]).then_inc(sm["cACD"], 16)
        e.wait_ge(sm["cEAs"], 1)
        e.dma_start(out=self.ea_d.ap(), in_=self.EA[:, :]).then_inc(sm["cACD"], 16)
        e.wait_ge(sm["cEAs"], 2)
        e.dma_start(out=self.pde_d.ap(), in_=self.pde[:, :]).then_inc(sm["cACD"], 16)
        e.wait_ge(sm["cACD"], 48)
        self.bar()
        for c in range(NCH):
            e.dma_start(out=self.xch[:, :],
                        in_=apm(self.x_d.ap(), 64 * c * 4096, [[4096, 64], [1, 4096]])
                        ).then_inc(sm["cXC"], 16)
            for q in range(4):
                aq = 4 * c + q
                if q >= 2:
                    e.wait_ge(sm["cBAm"], 2 * (aq - 1))
                e.dma_start(out=self.afl[aq % 2][:, :],
                            in_=apm(self.acs_d.ap(), 64 * c + 1024 * 16 * q,
                                    [[1024, 16], [1, 64]])).then_inc(sm["cAF"], 16)
            e.dma_start(out=self.cdrow[:, :],
                        in_=apm(self.ea_d.ap(), 64 * c + 63, [[1024, 64]]),
                        allow_slow_non_contiguous=True).then_inc(sm["cCD"], 16)
            for q in range(4):
                aq = 4 * c + q
                if q >= 2:
                    e.wait_ge(sm["cEFm"], 2 * (aq - 1))
                e.dma_start(out=self.afl2[aq % 2][:, :],
                            in_=apm(self.ea_d.ap(), 64 * c + 1024 * 16 * q,
                                    [[1024, 16], [1, 64]])).then_inc(sm["cEF"], 16)
            e.wait_ge(sm["cYE"], 8 * (c + 1))
            e.dma_start(out=apm(self.y_d.ap(), 64 * c * 4096, [[4096, 64], [1, 4096]]),
                        in_=self.ybuf[:, :]).then_inc(sm["cYD"], 16)
            self.bar()
        e.dma_start(out=self.sum_d.ap(), in_=self.run[:, :]).then_inc(sm["cSU"], 16)
        e.dma_start(out=self.lg_d.ap(), in_=self.lgcol[:, :]).then_inc(sm["cSU"], 16)
        e.wait_ge(sm["cSU"], 32)
        self.bar()
        self.bar()
        e.dma_start(out=self.lgp[:, :],
                    in_=apm(self.lgg_d.ap(), 0, [[64, 8], [1, 64]])).then_inc(sm["cLG"], 16)
        e.wait_ge(sm["cWE"], 2)
        e.dma_start(out=self.wt_d.ap(), in_=self.wT[:, :]).then_inc(sm["cWEd"], 16)
        e.wait_ge(sm["cWEd"], 16)
        self.bar()
        for p in range(8):
            if p >= 1:
                e.wait_ge(sm["cCW"], p)
            e.dma_start(out=self.wrow[:, :],
                        in_=apm(self.wt_d.ap(), p, [[8, 64]]),
                        allow_slow_non_contiguous=True).then_inc(sm["cWR"], 16)
            for q in range(4):
                gi = 4 * p + q
                if gi >= 1:
                    e.wait_ge(sm["cGA"], gi)
                e.dma_start(out=self.gch[:, :],
                            in_=apm(self.sumg_d.ap(), 128 * p * 4096 + 1024 * q,
                                    [[4096, 128], [1, 1024]])).then_inc(sm["cGC"], 16)
        self.bar()
        for c in range(NCH):
            e.dma_start(out=self.ybuf[:, :],
                        in_=apm(self.y_d.ap(), 64 * c * 4096, [[4096, 64], [1, 4096]])
                        ).then_inc(sm["cYB2"], 16)
            e.dma_start(out=self.pdrow[:, :],
                        in_=apm(self.pde_d.ap(), c, [[16, 64]]),
                        allow_slow_non_contiguous=True).then_inc(sm["cPD"], 16)
            for q in range(4):
                aq = 4 * c + q
                if q >= 2:
                    e.wait_ge(sm["cEF2m"], 2 * (aq - 1))
                e.dma_start(out=self.afl2[aq % 2][:, :],
                            in_=apm(self.ea_d.ap(), 64 * c + 1024 * 16 * q,
                                    [[1024, 16], [1, 64]])).then_inc(sm["cEF2"], 16)
            e.wait_ge(sm["cYE2"], 8 * (c + 1))
            e.dma_start(out=apm(self.y_d.ap(), 64 * c * 4096, [[4096, 64], [1, 4096]]),
                        in_=self.ybuf[:, :]).then_inc(sm["cP2d"], 16)
            self.bar()
        for t in range(NTT):
            e.dma_start(out=self.zt[:, :],
                        in_=apm(self.z_d.ap(), 128 * t, [[TOW, 4096], [1, 128]]),
                        transpose=True).then_inc(sm["cZT"], 16)
            e.dma_start(out=self.yt[:, :],
                        in_=apm(self.y_d.ap(), 128 * t * 4096, [[4096, 128], [1, 4096]])
                        ).then_inc(sm["cZT"], 16)
            e.wait_ge(sm["cYN"], t + 1)
            e.dma_start(out=apm(self.yn_d.ap(), 128 * t * 4096, [[4096, 128], [1, 4096]]),
                        in_=self.ynb[:, :]).then_inc(sm["cYND"], 16)
            e.wait_ge(sm["cYND"], 16 * (t + 1))
            self.bar()
        for dseg in range(8):
            for j in range(4):
                di = 4 * dseg + j
                if dseg >= 1:
                    e.wait_ge(sm["cOM"], 32 * dseg)
                e.dma_start(out=self.ynT[j][:, :],
                            in_=apm(self.yn_d.ap(), 128 * di, [[4096, TOW], [1, 128]]),
                            transpose=True).then_inc(sm["cYNT"], 16)
        self.bar()
        for t in range(NTT):
            e.dma_start(out=self.x2r[:, :],
                        in_=apm(self.x2f_d.ap(), 128 * t * D, [[D, 128], [1, D]])
                        ).then_inc(sm["cXRl"], 16)
            for p in range(8):
                pi = 8 * t + p
                buf = self.ostg if p % 2 == 0 else self.prt
                if p >= 2:
                    e.wait_ge(sm["cXRc"], pi - 1)
                e.dma_start(out=buf[:, :],
                            in_=apm(self.o_d.ap(), (p * TOW + 128 * t) * D,
                                    [[D, 128], [1, D]])).then_inc(sm["cXRl"], 16)
            e.wait_ge(sm["cX3"], 2 * (t + 1))
            e.dma_start(out=apm(self.x3f_d.ap(), t * 128 * D, [[D, 128], [1, D]]),
                        in_=self.x3f[:, :]).then_inc(sm["cOUT"], 16)
            e.dma_start(out=apm(self.x3b_d.ap(), t * 128 * D, [[D, 128], [1, D]]),
                        in_=self.x3b[:, :]).then_inc(sm["cOUT"], 16)
            e.wait_ge(sm["cOUT"], 32 * (t + 1))
            self.bar()

    def em_gpsimd(self, e):
        self._bn = 0
        try:
            self._em_gpsimd(e)
        except _CutDone:
            pass

    def _em_gpsimd(self, e):
        sm, io = self.ctx.sems, self.io
        e.dma_start(out=self.nwb[:, :], in_=io["normw_rep"]).then_inc(sm["cPg"], 16)
        e.wait_ge(sm["cPg"], 16)
        self.bar()
        for hi in range(34):
            if hi >= 2:
                e.wait_ge(sm["cHL"], hi - 1)
            m = 32 + hi
            e.dma_start(out=self.win[hi % 2][:, :],
                        in_=apm(io["wip_t"], 128 * m,
                                [[8512, 128], [128 * 8512, 16], [1, 128]])
                        ).then_inc(sm["cWL"], 16)
        self.bar()
        e.collective_compute("AllGather", ALU.bypass, replica_groups=[list(range(8))],
                             ins=[self.hout_d.ap()], outs=[self.hg_d.ap()]
                             ).then_inc(sm["cHG"], 1)
        e.wait_ge(sm["cHG"], 1)
        self.bar()
        for mi, m in enumerate(MT_ORDER):
            li = 34 + mi
            if mi >= 2:
                e.wait_ge(sm["cEVp"], 2 * (mi - 1))
            if m < 66:
                e.dma_start(out=self.win[li % 2][:, :],
                            in_=apm(io["wip_t"], 128 * m,
                                    [[8512, 128], [128 * 8512, 16], [1, 128]])
                            ).then_inc(sm["cWL"], 16)
            else:
                e.dma_start(out=apm(self.win[li % 2][:, :], 0,
                                    [[2048, 128], [64, 16], [1, 64]]),
                            in_=apm(io["wip_t"], 128 * m,
                                    [[8512, 128], [128 * 8512, 16], [1, 64]])
                            ).then_inc(sm["cWL"], 16)
        for b in range(6 + NCH + 1):
            self.bar()
        e.collective_compute("AllGather", ALU.bypass, replica_groups=[list(range(8))],
                             ins=[self.sum_d.ap()], outs=[self.sumg_d.ap()]
                             ).then_inc(sm["cAG"], 1)
        e.collective_compute("AllGather", ALU.bypass, replica_groups=[list(range(8))],
                             ins=[self.lg_d.ap()], outs=[self.lgg_d.ap()]
                             ).then_inc(sm["cAG"], 1)
        e.wait_ge(sm["cAG"], 2)
        self.bar()
        self.bar()
        self.bar()
        for c in range(NCH):
            self.bar()
        for t in range(NTT):
            self.bar()
        for dseg in range(8):
            for j in range(4):
                di = 4 * dseg + j
                if dseg >= 1:
                    e.wait_ge(sm["cOM"], 32 * dseg)
                e.dma_start(out=self.wo[j][:, :],
                            in_=apm(io["wout_t"], 128 * di * D, [[D, 128], [1, D]])
                            ).then_inc(sm["cWO"], 16)
            for tt in range(NTT):
                it = dseg * NTT + tt
                e.wait_ge(sm["cOS"], 4 * (it + 1))
                e.dma_start(out=apm(self.o_d.ap(), (dseg * TOW + 128 * tt) * D,
                                    [[D, 128], [1, D]]),
                            in_=self.ostg[:, :]).then_inc(sm["cOA"], 16)
        self.bar()
        for t in range(NTT):
            self.bar()

    def em_tensor(self, e):
        self._bn = 0
        try:
            self._em_tensor(e)
        except _CutDone:
            pass

    def _em_tensor(self, e):
        sm = self.ctx.sems
        self.bar()
        e.matmul(self.p_x[:, 0:64], self.onesf[0:1, :], self.drow[0:1, :],
                 start=True, stop=True).then_inc(sm["cDE"], 1)
        for hi in range(34):
            e.wait_ge(sm["cWL"], 16 * (hi + 1))
            if hi >= 2:
                e.wait_ge(sm["cHL"], hi - 1)
            pb = self.p_a if hi % 2 == 0 else self.p_b
            for dt_ in range(16):
                mm = e.matmul(pb[:, 0:3], self.win[hi % 2][:, 128 * dt_:128 * dt_ + 128],
                              self.xT[:, 1024 * dt_ + 1021:1024 * dt_ + 1024],
                              start=(dt_ == 0), stop=(dt_ == 15))
            mm.then_inc(sm["cHM"], 1)
        self.bar()
        self.bar()
        for mi, m in enumerate(MT_ORDER):
            li = 34 + mi
            e.wait_ge(sm["cWL"], 16 * (li + 1))
            for nc2 in range(2):
                pidx = 2 * mi + nc2
                pb = self.p_a if pidx % 2 == 0 else self.p_b
                if pidx >= 2:
                    e.wait_ge(sm["cEVp"], pidx - 1)
                for dt_ in range(16):
                    if m < 66:
                        mm = e.matmul(pb[:, :], self.win[li % 2][:, 128 * dt_:128 * dt_ + 128],
                                      self.xT[:, 1024 * dt_ + 512 * nc2:1024 * dt_ + 512 * nc2 + 512],
                                      start=(dt_ == 0), stop=(dt_ == 15))
                    else:
                        mm = e.matmul(pb[0:64, :], self.win[li % 2][:, 64 * dt_:64 * dt_ + 64],
                                      self.xT[:, 1024 * dt_ + 512 * nc2:1024 * dt_ + 512 * nc2 + 512],
                                      start=(dt_ == 0), stop=(dt_ == 15))
                mm.then_inc(sm["cPJ"], 1)
            if 32 <= m < 64:
                sxi = m - 32
                e.wait_ge(sm["cSILa"], sxi + 3)
                for tt in range(8):
                    g = 8 * sxi + tt
                    if g >= 2:
                        e.wait_ge(sm["cTRe"], g - 1)
                    e.transpose(self.p_t[g % 2][:, 0:128], self.xssm[:, 128 * tt:128 * tt + 128],
                                self.identb[:, :]).then_inc(sm["cTR"], 1)
        self.bar()
        self.bar()
        self.bar()
        self.bar()
        self.bar()
        for i in range(16):
            if i >= 1:
                e.wait_ge(sm["cTFe"], i)
            e.transpose(self.p_x[0:64, 0:64], self.acs[1][:, 64 * i:64 * i + 64],
                        self.identf[0:64, 0:64]).then_inc(sm["cTF"], 1)
        for i in range(16):
            e.wait_ge(sm["cTFe"], 16 + i)
            e.transpose(self.p_x[0:64, 0:64], self.dt[:, 64 * i:64 * i + 64],
                        self.identf[0:64, 0:64]).then_inc(sm["cTF"], 1)
        for i in range(16):
            if i >= 2:
                e.wait_ge(sm["cTBe"], i - 1)
            e.transpose(self.p_t[i % 2][0:64, 0:64], self.dec[:, 64 * i:64 * i + 64],
                        self.identb[0:64, 0:64]).then_inc(sm["cTB"], 1)
        for i in range(16):
            e.wait_ge(sm["cTBe"], 15 + i)
            e.transpose(self.p_t[i % 2][0:64, 0:128], self.Bfm[:, 64 * i:64 * i + 64],
                        self.identb[:, :]).then_inc(sm["cTB"], 1)
        self.bar()
        for c in range(NCH):
            e.matmul(self.p_G[:, 0:64], self.Bfm[:, 64 * c:64 * c + 64],
                     self.Cfm[:, 64 * c:64 * c + 64], start=True,
                     stop=True).then_inc(sm["cGm"], 1)
            for q in range(4):
                aq = 4 * c + q
                e.wait_ge(sm["cAF"], 16 * (aq + 1))
                if q >= 1:
                    e.wait_ge(sm["cBAv"], 8 * c + 2 * q)
                e.matmul(self.p_a[0:64, :], self.onesf[0:1, 0:64],
                         self.afl[aq % 2][0:1, 0:512], start=True,
                         stop=True).then_inc(sm["cBAm"], 1)
                e.matmul(self.p_b[0:64, :], self.onesf[0:1, 0:64],
                         self.afl[aq % 2][0:1, 512:1024], start=True,
                         stop=True).then_inc(sm["cBAm"], 1)
            e.wait_ge(sm["cCD"], 16 * (c + 1))
            if c >= 1:
                e.wait_ge(sm["cCDE"], c)
            e.matmul(self.p_x[:, 0:64], self.onesf[0:1, :], self.cdrow[0:1, :],
                     start=True, stop=True).then_inc(sm["cCDm"], 1)
            e.wait_ge(sm["cBAv"], 8 * (c + 1))
            for q in range(4):
                aq = 4 * c + q
                e.wait_ge(sm["cEF"], 16 * (aq + 1))
                if q >= 1:
                    e.wait_ge(sm["cCSCv"], 8 * c + 2 * q)
                e.matmul(self.p_a[:, :], self.onesf[0:1, :],
                         self.afl2[aq % 2][0:1, 0:512], start=True,
                         stop=True).then_inc(sm["cEFm"], 1)
                e.matmul(self.p_b[:, :], self.onesf[0:1, :],
                         self.afl2[aq % 2][0:1, 512:1024], start=True,
                         stop=True).then_inc(sm["cEFm"], 1)
            for hg in range(8):
                ih = 8 * c + hg
                e.wait_ge(sm["cBD"], ih + 1)
                e.wait_ge(sm["cXDT"], c + 1)
                if hg >= 1:
                    e.wait_ge(sm["cRU"], ih)
                for hl in range(8):
                    h = 8 * hg + hl
                    mm = e.matmul(self.p_S[:, 64 * hl:64 * hl + 64],
                                  self.bdec[:, 128 * hl:128 * hl + 128],
                                  self.xdt[:, 64 * h:64 * h + 64],
                                  start=True, stop=True)
                mm.then_inc(sm["cST"], 1)
            for hg in range(8):
                ih = 8 * c + hg
                e.wait_ge(sm["cMv"], 4 * (c + 1))
                e.wait_ge(sm["cCSCv"], 8 * (c + 1))
                e.wait_ge(sm["cRBv"], c + 1)
                e.wait_ge(sm["cGV"], c + 1)
                if hg >= 1:
                    e.wait_ge(sm["cYE"], ih)
                for hl in range(8):
                    h = 8 * hg + hl
                    e.matmul(self.p_G[:, 64 * hl:64 * hl + 64],
                             self.M[:, 64 * h:64 * h + 64],
                             self.xdt[:, 64 * h:64 * h + 64], start=True, stop=False)
                    mm = e.matmul(self.p_G[:, 64 * hl:64 * hl + 64],
                                  self.csc[:, 64 * h:64 * h + 64],
                                  self.runbf[:, 64 * h:64 * h + 64], start=False, stop=True)
                mm.then_inc(sm["cYM"], 1)
            self.bar()
        self.bar()
        self.bar()
        e.wait_ge(sm["cLG"], 16)
        e.matmul(self.p_x[0:64, 0:8], self.lgp[:, :], self.mBt[:, :],
                 start=True, stop=True).then_inc(sm["cWC"], 1)
        self.bar()
        for p in range(8):
            e.wait_ge(sm["cWR"], 16 * (p + 1))
            if p >= 1:
                e.wait_ge(sm["cCW"], p)
            e.matmul(self.p_x[:, 0:64], self.onesf[0:1, :], self.wrow[0:1, :],
                     start=True, stop=True).then_inc(sm["cWXm"], 1)
        self.bar()
        for c in range(NCH):
            e.wait_ge(sm["cPD"], 16 * (c + 1))
            if c >= 1:
                e.wait_ge(sm["cPDE"], c)
            e.matmul(self.p_x[:, 0:64], self.onesf[0:1, :], self.pdrow[0:1, :],
                     start=True, stop=True).then_inc(sm["cPDm"], 1)
            for q in range(4):
                aq = 4 * c + q
                e.wait_ge(sm["cEF2"], 16 * (aq + 1))
                if q >= 1:
                    e.wait_ge(sm["cCSC2v"], 8 * c + 2 * q)
                e.matmul(self.p_a[:, :], self.onesf[0:1, :],
                         self.afl2[aq % 2][0:1, 0:512], start=True,
                         stop=True).then_inc(sm["cEF2m"], 1)
                e.matmul(self.p_b[:, :], self.onesf[0:1, :],
                         self.afl2[aq % 2][0:1, 512:1024], start=True,
                         stop=True).then_inc(sm["cEF2m"], 1)
            for hg in range(8):
                ih = 8 * c + hg
                e.wait_ge(sm["cCSC2v"], 8 * (c + 1))
                e.wait_ge(sm["cCSCc"], c + 1)
                e.wait_ge(sm["cYB2"], 16 * (c + 1))
                if hg >= 1:
                    e.wait_ge(sm["cYE2"], ih)
                for hl in range(8):
                    h = 8 * hg + hl
                    mm = e.matmul(self.p_G[:, 64 * hl:64 * hl + 64],
                                  self.csc[:, 64 * h:64 * h + 64],
                                  self.runbf[:, 64 * h:64 * h + 64],
                                  start=True, stop=True)
                mm.then_inc(sm["cYM2"], 1)
            self.bar()
        for t in range(NTT):
            self.bar()
        for dseg in range(8):
            e.wait_ge(sm["cWO"], 64 * (dseg + 1))
            e.wait_ge(sm["cYNT"], 64 * (dseg + 1))
            for tt in range(NTT):
                it = dseg * NTT + tt
                for q4 in range(4):
                    pq = 4 * it + q4
                    po = self.p_o if q4 % 2 == 0 else self.p_S
                    if pq >= 2:
                        e.wait_ge(sm["cOS"], pq - 1)
                    for j in range(4):
                        mm = e.matmul(po[:, 0:512],
                                      self.ynT[j][:, 128 * tt:128 * tt + 128],
                                      self.wo[j][:, 512 * q4:512 * q4 + 512],
                                      start=(j == 0), stop=(j == 3))
                    mm.then_inc(sm["cOM"], 1)
        self.bar()
        for t in range(NTT):
            self.bar()

    def em_vector(self, e):
        self._bn = 0
        try:
            self._em_vector(e)
        except _CutDone:
            pass

    def _em_vector(self, e):
        sm = self.ctx.sems
        self._vsn = 0

        def W(ins):
            self._vsn += 1
            ins.then_inc(sm["cVS"], 1)
            e.wait_ge(sm["cVS"], self._vsn)
        e.memset(self.ln[:, 6:7], 1e-5)
        self.bar()
        e.wait_ge(sm["cDE"], 1)
        W(e.tensor_copy(self.De[:, :], self.p_x[:, 0:64]))
        e.memset(self.run[:, :], 0.0)
        self.bar()
        self.bar()
        e.wait_ge(sm["cHS"], 128)
        W(e.tensor_scalar(self.hsel[:, :], self.hgb[:, 0:102], self.selw[:, 0:1],
                          None, ALU.mult))
        for p in range(1, 8):
            W(e.scalar_tensor_tensor(self.hsel[:, :], self.hgb[:, 102 * p:102 * p + 102],
                                     self.selw[:, p:p + 1], self.hsel[:, :],
                                     ALU.mult, ALU.add))
        xbi = 0
        for mi, m in enumerate(MT_ORDER):
            if not (32 <= m < 66):
                continue
            ct = (m - 32) if m < 64 else (32 if m == 64 else 33)
            e.wait_ge(sm["cEVp"], 2 * (mi + 1))
            if xbi >= 1:
                e.wait_ge(sm["cSILa"], xbi)
            W(e.tensor_copy(self.xbc[mi % 2][:, 1:4], self.hsel[:, 3 * ct:3 * ct + 3]))
            b = self.xbc[mi % 2]
            e.tensor_scalar(self.cacc[:, :], b[:, 1:1025], self.convw[:, 4 * ct:4 * ct + 1],
                            None, ALU.mult)
            for k in range(1, 4):
                ins = e.scalar_tensor_tensor(self.cacc[:, :], b[:, 1 + k:1025 + k],
                                             self.convw[:, 4 * ct + k:4 * ct + k + 1],
                                             self.cacc[:, :], ALU.mult, ALU.add)
            ins.then_inc(sm["cCVt"], 1)
            xbi += 1
        self.bar()
        e.tensor_scalar(self.dtraw[:, :], self.dtraw[:, :], self.acol[:, 1:2],
                        None, ALU.add)
        self.bar()
        self.bar()
        e.tensor_scalar_max(self.dt[:, :], self.dtraw[:, :], 0.0)
        e.tensor_add(self.dt[:, :], self.dt[:, :], self.acs[0][:, :])
        e.tensor_scalar(self.acs[1][:, :], self.dt[:, :], self.acol[:, 2:3], -1.0,
                        ALU.mult, ALU.mult)
        srcb, dstb = 1, 0
        for k in [1, 2, 4, 8, 16, 32]:
            for c in range(NCH):
                c0 = 64 * c
                e.tensor_copy(self.acs[dstb][:, c0:c0 + k], self.acs[srcb][:, c0:c0 + k])
                ins = e.tensor_add(self.acs[dstb][:, c0 + k:c0 + 64],
                                   self.acs[srcb][:, c0 + k:c0 + 64],
                                   self.acs[srcb][:, c0:c0 + 64 - k])
            W(ins)
            srcb, dstb = dstb, srcb
        W(e.memset(self.plog[:, 0:1], 0.0))
        for c in range(NCH):
            W(e.tensor_add(self.plog[:, c + 1:c + 2], self.plog[:, c:c + 1],
                           self.acs[1][:, 64 * c + 63:64 * c + 64]))
        lastap = apm(self.acs[1][:, :], 63, [[1024, 64], [64, 16]])
        W(e.reduce_sum(self.lgcol[:, :], lastap, AX))
        bc = apm(self.acs[1][:, :], 63, [[1024, 64], [64, 16], [0, 64]])
        e.tensor_tensor(self.dtraw[:, :], bc, self.acs[1][:, :],
                        ALU.subtract).then_inc(sm["cACSv"], 1)
        self.bar()
        self.bar()
        self.bar()
        for c in range(NCH):
            e.tensor_copy(self.runbf[:, :], self.run[:, :]).then_inc(sm["cRBv"], 1)
            e.wait_ge(sm["cGm"], c + 1)
            e.tensor_mul(self.G[:, :], self.p_G[0:64, 0:64], self.trilU[:, :]
                         ).then_inc(sm["cGV"], 1)
            e.wait_ge(sm["cGV"], c + 1)
            e.wait_ge(sm["cXC"], 16 * (c + 1))
            dtap = dataclasses.replace(self.dtT[:, :], offset=self.dtT[:, :].offset + 64 * c,
                                       ap=[[1024, 64], [1, 64], [0, 64]])
            e.tensor_tensor(self.xdt[:, :], self.xch[:, :], dtap,
                            ALU.mult).then_inc(sm["cXDT"], 1)
            for q in range(4):
                aq = 4 * c + q
                e.wait_ge(sm["cBAm"], 2 * (aq + 1))
                for piece, pb in [(0, self.p_a), (1, self.p_b)]:
                    hoff = 16 * q + 8 * piece
                    aap = dataclasses.replace(self.acsT[:, :],
                                              offset=self.acsT[:, :].offset + 64 * c + hoff,
                                              ap=[[1024, 64], [1, 8], [0, 64]])
                    ins = e.tensor_tensor(self.argf[:, 512 * piece:512 * piece + 512],
                                          pb[0:64, :], aap, ALU.subtract)
                    ins.then_inc(sm["cBAv"], 1)
                e.tensor_scalar_min(self.argf[:, :], self.argf[:, :],
                                    0.0).then_inc(sm["cARGv"], 1)
                e.wait_ge(sm["cMe"], aq + 1)
                gap = dataclasses.replace(self.G[:, :], ap=[[64, 64], [0, 16], [1, 64]])
                e.tensor_mul(self.M[:, 1024 * q:1024 * q + 1024],
                             self.M[:, 1024 * q:1024 * q + 1024], gap
                             ).then_inc(sm["cMv"], 1)
            e.wait_ge(sm["cCDm"], c + 1)
            e.tensor_copy(self.cde[:, :], self.p_x[:, 0:64]).then_inc(sm["cCDE"], 1)
            e.wait_ge(sm["cCDE"], c + 1)
            for q in range(4):
                aq = 4 * c + q
                e.wait_ge(sm["cEFm"], 2 * (aq + 1))
                cap = dataclasses.replace(self.Cfm[:, 64 * c:64 * c + 64],
                                          ap=[[1024, 128], [0, 8], [1, 64]])
                for piece, pb in [(0, self.p_a), (1, self.p_b)]:
                    ins = e.tensor_tensor(self.csc[:, 1024 * q + 512 * piece:1024 * q + 512 * piece + 512],
                                          cap, pb[:, :], ALU.mult)
                    ins.then_inc(sm["cCSCv"], 1)
            for hg in range(8):
                ih = 8 * c + hg
                btap = dataclasses.replace(self.btok[:, :],
                                           offset=self.btok[:, :].offset + 128 * c,
                                           ap=[[2048, 64], [0, 8], [1, 128]])
                dcap = dataclasses.replace(self.decT[:, :],
                                           offset=self.decT[:, :].offset + 64 * c + 8 * hg,
                                           ap=[[1024, 64], [1, 8], [0, 128]])
                e.tensor_tensor(self.bdec[:, :], btap, dcap, ALU.mult).then_inc(sm["cBD"], 1)
            for hg in range(8):
                ih = 8 * c + hg
                e.wait_ge(sm["cST"], ih + 1)
                cdap = dataclasses.replace(self.cde[:, :], offset=self.cde[:, :].offset + 8 * hg,
                                           ap=[[64, 128], [1, 8], [0, 64]])
                e.tensor_tensor(self.run[:, 512 * hg:512 * hg + 512],
                                self.run[:, 512 * hg:512 * hg + 512], cdap, ALU.mult)
                e.tensor_add(self.run[:, 512 * hg:512 * hg + 512],
                             self.run[:, 512 * hg:512 * hg + 512],
                             self.p_S[:, :]).then_inc(sm["cRU"], 1)
            for hg in range(8):
                ih = 8 * c + hg
                e.wait_ge(sm["cYM"], ih + 1)
                dap = dataclasses.replace(self.De[:, :], offset=self.De[:, :].offset + 8 * hg,
                                          ap=[[64, 64], [1, 8], [0, 64]])
                e.tensor_tensor(self.ybuf[:, 512 * hg:512 * hg + 512],
                                self.xch[:, 512 * hg:512 * hg + 512], dap, ALU.mult)
                e.tensor_add(self.ybuf[:, 512 * hg:512 * hg + 512],
                             self.ybuf[:, 512 * hg:512 * hg + 512],
                             self.p_G[:, :]).then_inc(sm["cYE"], 1)
            self.bar()
        self.bar()
        self.bar()
        e.wait_ge(sm["cWC"], 1)
        W(e.tensor_copy(self.wT[:, :], self.p_x[0:64, 0:8]))
        e.tensor_add(self.wT[:, :], self.wT[:, :], self.mbias[:, :]).then_inc(sm["cWE"], 1)
        self.bar()
        e.memset(self.run[:, :], 0.0)
        for p in range(8):
            e.wait_ge(sm["cWXm"], p + 1)
            e.tensor_copy(self.wexp[:, :], self.p_x[:, 0:64]).then_inc(sm["cCW"], 1)
            e.wait_ge(sm["cCW"], p + 1)
            for q in range(4):
                gi = 4 * p + q
                e.wait_ge(sm["cGC"], 16 * (gi + 1))
                wap = dataclasses.replace(self.wexp[:, :],
                                          offset=self.wexp[:, :].offset + 16 * q,
                                          ap=[[64, 128], [1, 16], [0, 64]])
                e.tensor_mul(self.gch[:, :], self.gch[:, :], wap)
                e.tensor_add(self.run[:, 1024 * q:1024 * q + 1024],
                             self.run[:, 1024 * q:1024 * q + 1024],
                             self.gch[:, :]).then_inc(sm["cGA"], 1)
        self.bar()
        for c in range(NCH):
            e.wait_ge(sm["cPDm"], c + 1)
            e.tensor_copy(self.cde[:, :], self.p_x[:, 0:64]).then_inc(sm["cPDE"], 1)
            e.wait_ge(sm["cPDE"], c + 1)
            cdap = dataclasses.replace(self.cde[:, :], ap=[[64, 128], [1, 64], [0, 64]])
            e.tensor_tensor(self.runbf[:, :], self.run[:, :], cdap,
                            ALU.mult).then_inc(sm["cCSCc"], 1)
            for q in range(4):
                aq = 4 * c + q
                e.wait_ge(sm["cEF2m"], 2 * (aq + 1))
                cap = dataclasses.replace(self.Cfm[:, 64 * c:64 * c + 64],
                                          ap=[[1024, 128], [0, 8], [1, 64]])
                for piece, pb in [(0, self.p_a), (1, self.p_b)]:
                    ins = e.tensor_tensor(self.csc[:, 1024 * q + 512 * piece:1024 * q + 512 * piece + 512],
                                          cap, pb[:, :], ALU.mult)
                    ins.then_inc(sm["cCSC2v"], 1)
            for hg in range(8):
                ih = 8 * c + hg
                e.wait_ge(sm["cYM2"], ih + 1)
                e.tensor_add(self.ybuf[:, 512 * hg:512 * hg + 512],
                             self.ybuf[:, 512 * hg:512 * hg + 512],
                             self.p_G[:, :]).then_inc(sm["cYE2"], 1)
            self.bar()
        for t in range(NTT):
            e.wait_ge(sm["cP5a"], t + 1)
            e.tensor_mul(self.yt[:, :], self.yt[:, :], self.zsil[:, :])
            e.tensor_mul(self.ynb[:, :], self.yt[:, :], self.yt[:, :])
            e.reduce_sum(self.ln[:, 2:3], self.ynb[:, :], AX).then_inc(sm["cP5b"], 1)
            e.wait_ge(sm["cLNS"], t + 1)
            W(e.reciprocal(self.ln[:, 4:5], self.ln[:, 3:4]))
            e.scalar_tensor_tensor(self.ynb[:, :], self.yt[:, :], self.ln[:, 4:5],
                                   self.nwb[:, :], ALU.mult,
                                   ALU.mult).then_inc(sm["cYN"], 1)
            self.bar()
        for dseg in range(8):
            for tt in range(NTT):
                it = dseg * NTT + tt
                for q4 in range(4):
                    pq = 4 * it + q4
                    po = self.p_o if q4 % 2 == 0 else self.p_S
                    e.wait_ge(sm["cOM"], pq + 1)
                    if it >= 1 and q4 == 0:
                        e.wait_ge(sm["cOA"], 16 * it)
                    e.tensor_copy(self.ostg[:, 512 * q4:512 * q4 + 512],
                                  po[:, 0:512]).then_inc(sm["cOS"], 1)
        self.bar()
        for t in range(NTT):
            dx = self.x2r[:, :]
            for p in range(8):
                pi = 8 * t + p
                buf = self.ostg if p % 2 == 0 else self.prt
                e.wait_ge(sm["cXRl"], 16 * (9 * t + p + 2))
                e.tensor_add(dx, dx, buf[:, :]).then_inc(sm["cXRc"], 1)
            W(e.reduce_sum(self.ln[:, 0:1], dx, AX))
            W(e.tensor_scalar_mul(self.ln[:, 1:2], self.ln[:, 0:1], 1.0 / D))
            e.tensor_scalar(dx, dx, self.ln[:, 1:2], None, ALU.subtract)
            e.tensor_mul(self.x3b[:, :], dx, dx)
            e.reduce_sum(self.ln[:, 2:3], self.x3b[:, :], AX).then_inc(sm["cP5b"], 1)
            e.wait_ge(sm["cLNS"], NTT + t + 1)
            W(e.reciprocal(self.ln[:, 4:5], self.ln[:, 3:4]))
            e.tensor_scalar(self.x3f[:, :], dx, self.ln[:, 4:5], None,
                            ALU.mult).then_inc(sm["cX3"], 1)
            e.tensor_scalar(self.x3b[:, :], dx, self.ln[:, 4:5], None,
                            ALU.mult).then_inc(sm["cX3"], 1)
            self.bar()

    def em_scalar(self, e):
        self._bn = 0
        try:
            self._em_scalar(e)
        except _CutDone:
            pass

    def _em_scalar(self, e):
        sm = self.ctx.sems
        self.bar()
        for hi in range(34):
            e.wait_ge(sm["cHM"], hi + 1)
            e.copy(self.hout[:, 3 * hi:3 * hi + 3],
                   (self.p_a if hi % 2 == 0 else self.p_b)[:, 0:3]).then_inc(sm["cHL"], 1)
        self.bar()
        self.bar()
        xbi = 0
        sxi = 0
        zi = 0
        for mi, m in enumerate(MT_ORDER):
            for nc2 in range(2):
                pidx = 2 * mi + nc2
                pb = self.p_a if pidx % 2 == 0 else self.p_b
                e.wait_ge(sm["cPJ"], pidx + 1)
                if m == 66:
                    e.copy(self.dtraw[:, 512 * nc2:512 * nc2 + 512],
                           pb[0:64, :]).then_inc(sm["cEVp"], 1)
                elif m >= 32:
                    if xbi >= 2 and nc2 == 0:
                        e.wait_ge(sm["cCVt"], xbi - 1)
                    e.copy(self.xbc[mi % 2][:, 4 + 512 * nc2:4 + 512 * nc2 + 512],
                           pb[:, :]).then_inc(sm["cEVp"], 1)
                else:
                    if zi >= 2 and nc2 == 0:
                        e.wait_ge(sm["cZDd"], 16 * (zi - 1))
                    e.copy(self.zstg[mi % 2][:, 512 * nc2:512 * nc2 + 512],
                           pb[:, :]).then_inc(sm["cEVp"], 1)
            if 32 <= m < 66:
                ct = (m - 32) if m < 64 else (32 if m == 64 else 33)
                e.wait_ge(sm["cCVt"], xbi + 1)
                if m < 64:
                    if sxi >= 1:
                        e.wait_ge(sm["cTR"], 8 * sxi)
                    dst = self.xssm[:, :]
                elif m == 64:
                    dst = self.Bfm[:, :]
                else:
                    dst = self.Cfm[:, :]
                e.activation(dst, self.cacc[:, :], AF.Silu,
                             bias=self.convb[:, ct:ct + 1]).then_inc(sm["cSILa"], 1)
                if m < 64:
                    for tt in range(8):
                        g = 8 * sxi + tt
                        e.wait_ge(sm["cTR"], g + 1)
                        e.copy(self.xstg[:, 128 * tt:128 * tt + 128],
                               self.p_t[g % 2][:, 0:128]).then_inc(sm["cTRe"], 1)
                    sxi += 1
                xbi += 1
            if m < 32:
                zi += 1
        self.bar()
        self.bar()
        e.activation(self.acs[1][:, :], self.dtraw[:, :], AF.Abs)
        e.activation(self.EA[:, :], self.acs[1][:, :], AF.Exp, scale=-1.0)
        e.activation(self.acs[0][:, :], self.EA[:, :], AF.Ln,
                     bias=self.acol[:, 3:4])
        e.activation(self.acol[:, 2:3], self.acol[:, 0:1], AF.Exp)
        self.bar()
        self.bar()
        e.activation(self.EA[:, :], self.acs[1][:, :], AF.Exp).then_inc(sm["cEAs"], 1)
        e.activation(self.dec[:, :], self.dtraw[:, :], AF.Exp)
        e.activation(self.pde[:, :], self.plog[:, 0:16], AF.Exp).then_inc(sm["cEAs"], 1)
        self.bar()
        for i in range(16):
            e.wait_ge(sm["cTF"], i + 1)
            e.copy(self.acsT[:, 64 * i:64 * i + 64], self.p_x[0:64, 0:64]).then_inc(sm["cTFe"], 1)
        for i in range(16):
            e.wait_ge(sm["cTF"], 16 + i + 1)
            e.copy(self.dtT[:, 64 * i:64 * i + 64], self.p_x[0:64, 0:64]).then_inc(sm["cTFe"], 1)
        for i in range(16):
            e.wait_ge(sm["cTB"], i + 1)
            e.copy(self.decT[:, 64 * i:64 * i + 64], self.p_t[i % 2][0:64, 0:64]).then_inc(sm["cTBe"], 1)
        for i in range(16):
            e.wait_ge(sm["cTB"], 16 + i + 1)
            e.copy(self.btok[:, 128 * i:128 * i + 128],
                   self.p_t[i % 2][0:64, 0:128]).then_inc(sm["cTBe"], 1)
        self.bar()
        for c in range(NCH):
            for q in range(4):
                aq = 4 * c + q
                e.wait_ge(sm["cARGv"], aq + 1)
                e.activation(self.M[:, 1024 * q:1024 * q + 1024], self.argf[:, :],
                             AF.Exp).then_inc(sm["cMe"], 1)
            self.bar()
        self.bar()
        self.bar()
        e.wait_ge(sm["cWE"], 1)
        e.activation(self.wT[:, :], self.wT[:, :], AF.Exp).then_inc(sm["cWE"], 1)
        self.bar()
        self.bar()
        for c in range(NCH):
            self.bar()
        for t in range(NTT):
            e.wait_ge(sm["cZT"], 32 * (t + 1))
            e.activation(self.zsil[:, :], self.zt[:, :], AF.Silu).then_inc(sm["cP5a"], 1)
            e.wait_ge(sm["cP5b"], t + 1)
            e.activation(self.ln[:, 3:4], self.ln[:, 2:3], AF.Sqrt,
                         scale=1.0 / 4096.0, bias=self.ln[:, 6:7]).then_inc(sm["cLNS"], 1)
            self.bar()
        self.bar()
        for t in range(NTT):
            e.wait_ge(sm["cP5b"], NTT + t + 1)
            e.activation(self.ln[:, 3:4], self.ln[:, 2:3], AF.Sqrt,
                         scale=1.0 / D, bias=self.ln[:, 6:7]).then_inc(sm["cLNS"], 1)
            self.bar()


def _hostprep_C(inputs, core):
    wip = np.asarray(inputs["L1_attn_in_proj_w"], np.float32)
    wip_t = np.ascontiguousarray(wip.T)
    wout = np.asarray(inputs["L1_attn_out_proj_w"], np.float32)
    wout_t = np.ascontiguousarray(wout.T)
    cw = np.asarray(inputs["L1_attn_conv_w"], np.float32).reshape(4352, 4)
    convw = np.ascontiguousarray(cw.reshape(34, 128, 4).transpose(1, 0, 2).reshape(128, 136))
    cb = np.asarray(inputs["L1_attn_conv_b"], np.float32)
    convb = np.ascontiguousarray(cb.reshape(34, 128).T)
    acolin = np.zeros((64, 4), np.float32)
    acolin[:, 0] = np.asarray(inputs["L1_attn_A_log"], np.float32)
    acolin[:, 1] = np.asarray(inputs["L1_attn_dt_bias"], np.float32)
    acolin[:, 3] = 1.0
    drow = np.asarray(inputs["L1_attn_D"], np.float32).reshape(1, 64)
    normw_rep = np.ascontiguousarray(
        np.broadcast_to(np.asarray(inputs["L1_attn_norm_w"], np.float32)[None, :], (128, 4096)))
    trilU = np.triu(np.ones((64, 64), np.float32)).astype(BF)
    beta, sig = core // 4, core % 4
    mBt = np.zeros((8, 8), np.float32)
    mbias = np.full((64, 8), -1e30, np.float32)
    for p in range(8):
        pb_, ps = p // 4, p % 4
        if pb_ == beta and ps < sig:
            mbias[:, p] = 0.0
            for qq in range(8):
                qb, qs = qq // 4, qq % 4
                if qb == beta and ps < qs < sig:
                    mBt[qq, p] = 1.0
    selw = np.zeros((128, 8), np.float32)
    if sig > 0:
        selw[:, core - 1] = 1.0
    return dict(wip_t=wip_t, wout_t=wout_t, convw=convw, convb=convb,
                acolin=acolin, drow=drow, normw_rep=normw_rep,
                trilU=np.asarray(trilU), mBt=mBt, mbias=mbias, selw=selw,
                onesf=np.ones((1, 128), np.float32),
                identf=np.eye(128, dtype=np.float32),
                identb=np.eye(128, dtype=BF))


def _declare_C(nc):
    io = {}
    decl = [("wip_t", [D, 8512], F32), ("wout_t", [4096, D], F32),
            ("convw", [128, 136], F32), ("convb", [128, 34], F32),
            ("acolin", [64, 4], F32), ("drow", [1, 64], F32),
            ("normw_rep", [128, 4096], F32), ("trilU", [64, 64], BF16),
            ("mBt", [8, 8], F32), ("mbias", [64, 8], F32), ("selw", [128, 8], F32),
            ("onesf", [1, 128], F32), ("identf", [128, 128], F32),
            ("identb", [128, 128], BF16)]
    for n, sh, dt in decl:
        io[n] = nc.dram_tensor(n, sh, dt, kind="ExternalInput").ap()
    return io


def build_program_C():
    nc = bass.Bass("TRN2", target_bir_lowering=False)
    stack = ExitStack()
    ctx = Ctx(nc, stack)
    io = _declare_C(nc)
    ctx.dram["x2f_d"] = nc.dram_tensor("x2f_d", [TOW, D], F32, kind="ExternalInput")
    ctx.dram["x2b_d"] = nc.dram_tensor("x2b_d", [TOW, D], BF16, kind="ExternalInput")
    with stack:
        ph = PhaseC(ctx, io, out_kind="ExternalOutput")
        with nc.Block() as block:
            @block.sync
            def _(e):
                ph.em_sync(e)

            @block.gpsimd
            def _(e):
                ph.em_gpsimd(e)

            @block.tensor
            def _(e):
                ph.em_tensor(e)

            @block.vector
            def _(e):
                ph.em_vector(e)

            @block.scalar
            def _(e):
                ph.em_scalar(e)
    return nc



DSEG = 8
DETS = 8


class PhaseD:

    def __init__(self, ctx: Ctx, io, out_kind="Internal"):
        self.ctx = ctx
        self.io = io
        c = ctx
        self.xT = c.sbuf("D_xT", [128, 16 * 1024], BF16)
        self.win = [c.sbuf(f"D_win{i}", [128, 2048], BF16) for i in range(2)]
        self.tg = [c.sbuf(f"D_tg{i}", [128, 1024], F32) for i in range(2)]
        self.silu = c.sbuf("D_silu", [128, 1024], BF16)
        self.h8 = c.sbuf("D_h8", [128, DETS * 1024], BF16)
        self.wdn = [c.sbuf(f"D_wdn{i}", [128, 2048], BF16) for i in range(DETS)]
        self.dacc = c.sbuf("D_dacc", [128, NTT * 2048], F32)
        self.rsx = c.sbuf("D_rsx", [128, 1024], F32)
        self.rsrow = c.sbuf("D_rsrow", [1, 1024], F32)
        self.rstg = c.sbuf("D_rstg", [8, 128], F32)
        self.rcol = c.sbuf("D_rcol", [128, 8], F32)
        self.hs = c.sbuf("D_hs", [128, 8], F32)
        self.dsc = c.sbuf("D_dsc", [128, 2048], F32)
        self.gsc = c.sbuf("D_gsc", [128, 64], F32)
        self.usc = c.sbuf("D_usc", [128, 64], F32)
        self.x3res = c.sbuf("D_x3res", [128, 2048], F32)
        self.x4f = c.sbuf("D_x4f", [128, 2048], F32)
        self.x4b = c.sbuf("D_x4b", [128, 2048], BF16)
        self.negf = c.sbuf("D_negf", [1, 128], F32)
        self.onescb = c.sbuf("D_onescb", [128, 1], BF16)
        self.identf = c.sbuf("D_identf", [128, 128], F32)
        self.ln = c.sbuf("D_ln", [128, 8], F32)
        self.p_up = c.psum("D_pup", [128, 1024], F32)
        self.p_g = c.psum("D_pg", [128, 1024], F32)
        self.p_dn = c.psum("D_pdn", [128, 1024], F32)
        self.p_hs = c.psum("D_phs", [128, 512], F32)
        self.rs_d = c.dscr("D_rs_d", [1, 1024], F32)
        self.x4f_d = c.dscr("x4f_d", [TOW, D], F32, kind=out_kind)
        self.x3f_d = c.dram["x3f_d"] if "x3f_d" in c.dram else c.dscr("x3f_d", [TOW, D], F32)
        self.x3b_d = c.dram["x3b_d"] if "x3b_d" in c.dram else c.dscr("x3b_d", [TOW, D], BF16)
        c.sem("dP", "dONE", "dXT", "dRS", "dRSt", "dRSv", "dRSd", "dRSx", "dWL",
              "dMM", "dSIL", "dH", "dTG", "dLD", "dDN", "dFL", "dHS", "dX3R",
              "dX3c", "dHSe", "dLNV", "dLNS", "dX4", "dOUT", "dVS")

    def em_sync(self, e):
        sm, io = self.ctx.sems, self.io
        for name, dst in [("onescb_h", None), ("negf", self.negf),
                          ("identf", self.identf), ("dsc_rep", self.dsc),
                          ("gsc_t", self.gsc), ("usc_t", self.usc)]:
            if dst is None:
                continue
            e.dma_start(out=dst[:, :], in_=io[name]).then_inc(sm["dP"], 16)
        for dt_ in range(16):
            e.dma_start(out=self.xT[:, 1024 * dt_:1024 * dt_ + 1024],
                        in_=apm(self.x3b_d.ap(), 128 * dt_, [[D, 1024], [1, 128]]),
                        transpose=True).then_inc(sm["dXT"], 16)
        for t in range(NTT):
            if t >= 1:
                e.wait_ge(sm["dRS"], t)
            e.dma_start(out=self.x3res[:, :],
                        in_=apm(self.x3f_d.ap(), t * 128 * D, [[D, 128], [1, D]])
                        ).then_inc(sm["dX3R"], 16)
        e.wait_ge(sm["dRSv"], 1)
        e.dma_start(out=apm(self.rs_d.ap(), 0, [[128, 8], [1, 128]]),
                    in_=self.rstg[:, :]).then_inc(sm["dRSd"], 16)
        e.wait_ge(sm["dRSd"], 16)
        e.dma_start(out=self.rsrow[:, :],
                    in_=apm(self.rs_d.ap(), 0, [[1024, 1], [1, 1024]])
                    ).then_inc(sm["dRSd"], 16)
        for t in range(NTT):
            if t == 0:
                e.wait_ge(sm["dRS"], NTT)
            else:
                e.wait_ge(sm["dX3c"], t)
            e.dma_start(out=self.x3res[:, :],
                        in_=apm(self.x3f_d.ap(), t * 128 * D, [[D, 128], [1, D]])
                        ).then_inc(sm["dX3R"], 16)
            e.wait_ge(sm["dX4"], t + 1)
            e.dma_start(out=apm(self.x4f_d.ap(), t * 128 * D, [[D, 128], [1, D]]),
                        in_=self.x4f[:, :]).then_inc(sm["dOUT"], 16)
        if DBG_D:
            e.dma_start(out=self.ctx.dram["dbgD_rsx"].ap(), in_=self.rsx[:, :]).then_inc(sm["dOUT"], 16)
            e.dma_start(out=self.ctx.dram["dbgD_hs"].ap(), in_=self.hs[:, :]).then_inc(sm["dOUT"], 16)
            e.dma_start(out=self.ctx.dram["dbgD_h8"].ap(), in_=self.h8[:, :]).then_inc(sm["dOUT"], 16)
            e.wait_ge(sm["dOUT"], 16 * NTT + 48)
        else:
            e.wait_ge(sm["dOUT"], 16 * NTT)

    def em_gpsimd(self, e):
        sm, io = self.ctx.sems, self.io
        e.dma_start(out=self.onescb[:, :], in_=io["onescb_h"]).then_inc(sm["dONE"], 16)
        for seg in range(DSEG):
            for etl in range(DETS):
                etg = seg * DETS + etl
                for w, nm in enumerate(["uw_t", "gw_t"]):
                    li = 2 * etg + w
                    if li >= 2:
                        e.wait_ge(sm["dMM"], li - 1)
                    e.dma_start(out=self.win[li % 2][:, :],
                                in_=apm(io[nm], 128 * etg,
                                        [[DE, 128], [128 * DE, 16], [1, 128]])
                                ).then_inc(sm["dWL"], 16)
            for etl in range(DETS):
                eg = seg * DETS + etl
                if seg >= 1:
                    e.wait_ge(sm["dDN"], 16 * seg)
                e.dma_start(out=self.wdn[etl][:, :],
                            in_=apm(io["dw_t"], 128 * eg * D, [[D, 128], [1, D]])
                            ).then_inc(sm["dLD"], 16)

    def em_tensor(self, e):
        sm = self.ctx.sems
        e.wait_ge(sm["dP"], 16 * 5)
        e.wait_ge(sm["dRS"], NTT)
        e.transpose(self.p_hs[0:8, 0:128], self.rcol[:, 0:8],
                    self.identf[:, :]).then_inc(sm["dRSt"], 1)
        e.wait_ge(sm["dRSd"], 32)
        for ch in range(2):
            e.matmul(self.p_up[:, 512 * ch:512 * ch + 512], self.negf[0:1, 0:128],
                     self.rsrow[0:1, 512 * ch:512 * ch + 512],
                     start=True, stop=True).then_inc(sm["dRSx"], 1)
        e.wait_ge(sm["dXT"], 16 * 16)
        e.wait_ge(sm["dRSx"], 3)
        e.wait_ge(sm["dONE"], 16)
        for seg in range(DSEG):
            for etl in range(DETS):
                etg = seg * DETS + etl
                for w in range(2):
                    li = 2 * etg + w
                    e.wait_ge(sm["dWL"], 16 * (li + 1))
                    if etg >= 1:
                        e.wait_ge(sm["dTG"], 2 * etg if w == 0 else 2 * etg - 1)
                    pb = self.p_up if w == 0 else self.p_g
                    for c2 in range(2):
                        for dt_ in range(16):
                            mm = e.matmul(pb[:, 512 * c2:512 * c2 + 512],
                                          self.win[li % 2][:, 128 * dt_:128 * dt_ + 128],
                                          self.xT[:, 1024 * dt_ + 512 * c2:1024 * dt_ + 512 * c2 + 512],
                                          start=(dt_ == 0), stop=(dt_ == 15))
                    mm.then_inc(sm["dMM"], 1)
            e.wait_ge(sm["dH"], DETS * (seg + 1))
            e.wait_ge(sm["dLD"], 16 * DETS * (seg + 1))
            for half in range(2):
                for tt in range(NTT):
                    it = (seg * 2 + half) * NTT + tt
                    if it >= 1:
                        e.wait_ge(sm["dFL"], it)
                    for etl in range(DETS):
                        for c2 in range(2):
                            mm = e.matmul(self.p_dn[:, 512 * c2:512 * c2 + 512],
                                          self.h8[:, 1024 * etl + 128 * tt:1024 * etl + 128 * tt + 128],
                                          self.wdn[etl][:, 1024 * half + 512 * c2:1024 * half + 512 * c2 + 512],
                                          start=(etl == 0), stop=(etl == DETS - 1))
                    mm.then_inc(sm["dDN"], 1)
            if seg >= 1:
                e.wait_ge(sm["dHSe"], seg)
            for tt in range(NTT):
                for etl in range(DETS):
                    mm = e.matmul(self.p_hs[:, tt:tt + 1],
                                  self.h8[:, 1024 * etl + 128 * tt:1024 * etl + 128 * tt + 128],
                                  self.onescb[:, 0:1],
                                  start=(etl == 0), stop=(etl == DETS - 1))
                mm.then_inc(sm["dHS"], 1)

    def em_vector(self, e):
        sm = self.ctx.sems
        self._vsn = 0

        def W(ins):
            self._vsn += 1
            ins.then_inc(sm["dVS"], 1)
            e.wait_ge(sm["dVS"], self._vsn)
        e.memset(self.ln[:, 6:7], 1e-5)
        for t in range(NTT):
            e.wait_ge(sm["dX3R"], 16 * (t + 1))
            e.reduce_sum(self.rcol[:, t:t + 1], self.x3res[:, :], AX).then_inc(sm["dRS"], 1)
        e.wait_ge(sm["dRSt"], 1)
        e.tensor_copy(self.rstg[:, :], self.p_hs[0:8, 0:128]).then_inc(sm["dRSv"], 1)
        for ch in range(2):
            e.wait_ge(sm["dRSx"], ch + 1)
            cp = e.tensor_copy(self.rsx[:, 512 * ch:512 * ch + 512],
                               self.p_up[:, 512 * ch:512 * ch + 512])
        cp.then_inc(sm["dRSx"], 1)
        for seg in range(DSEG):
            for etl in range(DETS):
                etg = seg * DETS + etl
                e.wait_ge(sm["dMM"], 2 * etg + 2)
                if etl == 0 and seg >= 1:
                    e.wait_ge(sm["dDN"], 16 * seg)
                    e.wait_ge(sm["dHS"], NTT * seg)
                if etg >= 1:
                    e.wait_ge(sm["dSIL"], etg)
                e.tensor_add(self.tg[1][:, :], self.p_g[:, :],
                             self.rsx[:, :]).then_inc(sm["dTG"], 1)
                e.tensor_add(self.tg[0][:, :], self.p_up[:, :],
                             self.rsx[:, :]).then_inc(sm["dTG"], 1)
                e.wait_ge(sm["dSIL"], etg + 1)
                e.scalar_tensor_tensor(self.h8[:, 1024 * etl:1024 * etl + 1024],
                                       self.tg[0][:, :], self.usc[:, etg:etg + 1],
                                       self.silu[:, :], ALU.mult,
                                       ALU.mult).then_inc(sm["dH"], 1)
            for half in range(2):
                for tt in range(NTT):
                    it = (seg * 2 + half) * NTT + tt
                    e.wait_ge(sm["dDN"], it + 1)
                    dx = self.dacc[:, 2048 * tt + 1024 * half:2048 * tt + 1024 * half + 1024]
                    if seg == 0:
                        e.tensor_copy(dx, self.p_dn[:, :]).then_inc(sm["dFL"], 1)
                    else:
                        e.tensor_add(dx, dx, self.p_dn[:, :]).then_inc(sm["dFL"], 1)
            e.wait_ge(sm["dHS"], NTT * (seg + 1))
            if seg == 0:
                e.tensor_copy(self.hs[:, 0:8], self.p_hs[:, 0:8]).then_inc(sm["dHSe"], 1)
            else:
                e.tensor_add(self.hs[:, 0:8], self.hs[:, 0:8],
                             self.p_hs[:, 0:8]).then_inc(sm["dHSe"], 1)
        e.wait_ge(sm["dHSe"], DSEG)
        W(e.tensor_scalar_mul(self.hs[:, 0:8], self.hs[:, 0:8], -128.0))
        for t in range(NTT):
            e.wait_ge(sm["dX3R"], 16 * (NTT + t + 1))
            dx = self.dacc[:, 2048 * t:2048 * t + 2048]
            e.tensor_scalar(dx, dx, self.hs[:, t:t + 1], None, ALU.add)
            e.tensor_mul(dx, dx, self.dsc[:, :])
            e.tensor_add(dx, dx, self.x3res[:, :]).then_inc(sm["dX3c"], 1)
            W(e.reduce_sum(self.ln[:, 0:1], dx, AX))
            W(e.tensor_scalar_mul(self.ln[:, 1:2], self.ln[:, 0:1], 1.0 / D))
            e.tensor_scalar(dx, dx, self.ln[:, 1:2], None, ALU.subtract)
            if t >= 1:
                e.wait_ge(sm["dOUT"], 16 * t)
            e.tensor_mul(self.x4b[:, :], dx, dx)
            e.reduce_sum(self.ln[:, 2:3], self.x4b[:, :], AX).then_inc(sm["dLNV"], 1)
            e.wait_ge(sm["dLNS"], t + 1)
            W(e.reciprocal(self.ln[:, 4:5], self.ln[:, 3:4]))
            e.tensor_scalar(self.x4f[:, :], dx, self.ln[:, 4:5], None,
                            ALU.mult).then_inc(sm["dX4"], 1)

    def em_scalar(self, e):
        sm = self.ctx.sems
        e.wait_ge(sm["dP"], 16 * 5)
        for seg in range(DSEG):
            for etl in range(DETS):
                etg = seg * DETS + etl
                e.wait_ge(sm["dTG"], 2 * etg + 1)
                if etg >= 1:
                    e.wait_ge(sm["dH"], etg)
                e.activation(self.silu[:, :], self.tg[1][:, :], AF.Silu,
                             scale=self.gsc[:, etg:etg + 1]).then_inc(sm["dSIL"], 1)
        for t in range(NTT):
            e.wait_ge(sm["dLNV"], t + 1)
            e.activation(self.ln[:, 3:4], self.ln[:, 2:3], AF.Sqrt,
                         scale=1.0 / D, bias=self.ln[:, 6:7]).then_inc(sm["dLNS"], 1)


def _hostprep_D(inputs, core):
    io = {}
    io["uw_t"] = np.ascontiguousarray(inputs["L1_ffn_up_w"].T).astype(np.int32)
    io["gw_t"] = np.ascontiguousarray(inputs["L1_ffn_gate_w"].T).astype(np.int32)
    io["dw_t"] = np.ascontiguousarray(inputs["L1_ffn_down_w"].T).astype(np.int32)
    io["gsc_t"] = np.ascontiguousarray(
        inputs["L1_ffn_gate_scale"].astype(np.float32).reshape(64, 128).T)
    io["usc_t"] = np.ascontiguousarray(
        inputs["L1_ffn_up_scale"].astype(np.float32).reshape(64, 128).T)
    io["dsc_rep"] = np.ascontiguousarray(
        np.broadcast_to(inputs["L1_ffn_down_scale"].astype(np.float32), (128, D)))
    io["negf"] = np.full((1, 128), -128.0, np.float32)
    io["identf"] = np.eye(128, dtype=np.float32)
    io["onescb_h"] = np.ones((128, 1), np.float32)
    return io


def _declare_D(nc):
    io = {}
    decl = [("uw_t", [D, DE], I32), ("gw_t", [D, DE], I32), ("dw_t", [DE, D], I32),
            ("gsc_t", [128, 64], F32), ("usc_t", [128, 64], F32),
            ("dsc_rep", [128, D], F32), ("negf", [1, 128], F32),
            ("identf", [128, 128], F32), ("onescb_h", [128, 1], F32)]
    for n, sh, dt in decl:
        io[n] = nc.dram_tensor(n, sh, dt, kind="ExternalInput").ap()
    return io


def build_program_D():
    nc = bass.Bass("TRN2", target_bir_lowering=False)
    stack = ExitStack()
    ctx = Ctx(nc, stack)
    io = _declare_D(nc)
    ctx.dram["x3f_d"] = nc.dram_tensor("x3f_d", [TOW, D], F32, kind="ExternalInput")
    ctx.dram["x3b_d"] = nc.dram_tensor("x3b_d", [TOW, D], BF16, kind="ExternalInput")
    with stack:
        if DBG_D:
            ctx.dscr("dbgD_rsx", [128, 1024], F32, kind="ExternalOutput")
            ctx.dscr("dbgD_hs", [128, 8], F32, kind="ExternalOutput")
            ctx.dscr("dbgD_h8", [128, DETS * 1024], BF16, kind="ExternalOutput")
        ph = PhaseD(ctx, io, out_kind="ExternalOutput")
        with nc.Block() as block:
            @block.sync
            def _(e):
                ph.em_sync(e)

            @block.gpsimd
            def _(e):
                ph.em_gpsimd(e)

            @block.tensor
            def _(e):
                ph.em_tensor(e)

            @block.vector
            def _(e):
                ph.em_vector(e)

            @block.scalar
            def _(e):
                ph.em_scalar(e)
    return nc


# version 02
# speedup vs baseline: 1.0002x; 1.0002x over previous
import dataclasses
import math
from contextlib import ExitStack
import numpy as np
import ml_dtypes
import concourse.bass as bass
import concourse.mybir as mybir
from concourse.bass_utils import run_bass_kernel_spmd

F32 = mybir.dt.float32
BF16 = mybir.dt.bfloat16
I32 = mybir.dt.int32
AF = mybir.ActivationFunctionType
ALU = mybir.AluOpType
AX = mybir.AxisListType.X
BF = ml_dtypes.bfloat16

B, S, D = 2, 4096, 2048
NCORE = 8
TOW = 1024
NTT = TOW // 128
WIN = 32
NJT = S // 128
NSWT = 32 + 9
STAGE = 5
import os as _os
SKIP = set((_os.environ.get('K_SKIP') or '').split(','))
DBG_A = bool(_os.environ.get('K_DBG_A'))
DBG_B = bool(_os.environ.get('K_DBG_B'))
DBG_D = bool(_os.environ.get('K_DBG_D'))
CCUT = int(_os.environ.get('K_CCUT') or 0)


class _CutDone(Exception):
    pass


def apm(ap, off=0, dims=None):
    return dataclasses.replace(ap, offset=ap.offset + off,
                               ap=[list(d) for d in dims] if dims is not None else ap.ap)


class Ctx:
    def __init__(self, nc, stack):
        self.nc = nc
        self.stack = stack
        self.sb = {}
        self.ps = {}
        self.dram = {}
        self.sems = {}

    def sem(self, *names):
        for name in names:
            if name not in self.sems:
                self.sems[name] = self.stack.enter_context(self.nc.semaphore(name))

    def sbuf(self, name, shape, dtype):
        self.sb[name] = self.stack.enter_context(self.nc.sbuf_tensor(name, shape, dtype))
        return self.sb[name]

    def sbuf_at(self, name, shape, dtype, host, off_bytes):
        base = self.nc.lookup_mloc(self.sb[host]).addr
        h = self.nc.alloc_sbuf_tensor_at(name, shape, dtype, offset=base + off_bytes)
        self.sb[name] = h
        return h

    def psum(self, name, shape, dtype=F32):
        self.ps[name] = self.stack.enter_context(self.nc.psum_tensor(name, shape, dtype))
        return self.ps[name]

    def dscr(self, name, shape, dtype, kind="Internal"):
        if name not in self.dram:
            self.dram[name] = self.nc.dram_tensor(name, shape, dtype, kind=kind)
        return self.dram[name]


class PhaseA:

    def __init__(self, ctx: Ctx, io, out_kind="Internal"):
        self.ctx = ctx
        self.io = io
        c = ctx
        self.xinb = [c.sbuf(f"A_xinb{i}", [128, D], BF16) for i in range(2)]
        self.xres = c.sbuf("A_xres", [128, D], F32)
        self.xT = c.sbuf("A_xT", [128, D], BF16)
        self.wkv = c.sbuf("A_wkv", [128, 16 * 64], BF16)
        self.wq = c.sbuf("A_wq", [128, 16 * 128], BF16)
        self.kfm = c.sbuf("A_kfm", [32, WIN - 1 + S], BF16)
        self.vfull = c.sbuf("A_vfull", [32, S + 128], BF16)
        self.qfm = c.sbuf("A_qfm", [128, 1152], BF16)
        self.vs = c.sbuf("A_vs", [128, NJT * 36], BF16)
        self.kw = [c.sbuf(f"A_kw{i}", [128, S], BF16) for i in range(2)]
        self.qw = [c.sbuf(f"A_qw{i}", [128, 4 * 512], BF16) for i in range(2)]
        self.ebias = c.sbuf("A_ebias", [128, NJT * 512], BF16)
        self.ebarg = [c.sbuf(f"A_ebarg{i}", [128, 512], BF16) for i in range(2)]
        self.Usb = c.sbuf("A_U", [128, 512], BF16)
        self.e1 = [c.sbuf(f"A_e1_{i}", [128, 512], BF16) for i in range(2)]
        self.e2 = [c.sbuf(f"A_e2_{i}", [128, 512], BF16) for i in range(2)]
        self.stg = c.sbuf("A_stg", [9, 512], F32)
        self.attn = c.sbuf("A_attn", [128, TOW], F32)
        self.attnb = c.sbuf("A_attnb", [128, TOW], BF16)
        self.den = c.sbuf("A_den", [16, TOW], F32)
        self.denb = c.sbuf("A_denb", [16, TOW], BF16)
        self.recip = c.sbuf("A_recip", [128, TOW], F32)
        self.wo = c.sbuf("A_wo", [128, D], BF16)
        self.wop = c.sbuf("A_wop", [128, D], BF16)
        self.v0c = c.sbuf("A_v0c", [128, 1], F32)
        self.v1c = c.sbuf("A_v1c", [128, 1], F32)
        self.v0cb = c.sbuf("A_v0cb", [128, 1], BF16)
        self.dvc = c.sbuf("A_dvc", [128, 1], F32)
        self.v0wo = c.sbuf("A_v0wo", [1, D], BF16)
        self.v0woe = c.sbuf("A_v0woe", [128, D], F32)
        self.km = c.sbuf("A_km", [128, 8], F32)
        self.sc1 = c.sbuf("A_sc1", [1, 8], F32)
        self.onesf = c.sbuf("A_onesf", [1, 128], F32)
        self.onesb = c.sbuf("A_onesb", [1, 128], BF16)
        self.E16 = c.sbuf("A_E16", [16, 128], BF16)
        self.maskm = c.sbuf("A_maskm", [128, 5632], BF16)
        self.invi = c.sbuf("A_invi", [1, TOW], F32)
        self.jga = c.sbuf("A_jga", [128, NJT], F32)
        self.jp = c.sbuf("A_jp", [128, 1], F32)
        self.identb = c.sbuf("A_identb", [128, 128], BF16)
        self.xr = c.sbuf("A_xr", [128, D], F32)
        self.ln = c.sbuf("A_ln", [128, 8], F32)
        self.x1f = c.sbuf("A_x1f", [128, D], F32)
        self.x1b = c.sbuf("A_x1b", [128, D], BF16)
        self.p_sc = [c.psum(f"A_psc{i}", [128, 512], F32) for i in range(2)]
        self.p_pv = c.psum("A_ppv", [9, 512], F32)
        self.p_tr = [c.psum(f"A_ptr{i}", [128, 1024], BF16) for i in range(2)]
        self.p_kv = c.psum("A_pkv", [64, 512], F32)
        self.p_q = c.psum("A_pq", [128, 512], F32)
        self.p_x = c.psum("A_px", [128, 512], F32)
        self.k_dram = c.dscr("A_kdram", [32, WIN - 1 + S], BF16)
        self.q_dram = c.dscr("A_qdram", [128, 1152], BF16)
        self.x1f_d = c.dscr("x1f_d", [TOW, D], F32, kind=out_kind)
        self.x1b_d = c.dscr("x1b_d", [TOW, D], BF16, kind=out_kind)
        c.sem("aP", "aXT", "aXC", "aTW", "aTE", "aPJ", "aAC", "aKDR", "aQDR",
              "aQWB", "aKWB", "aVT", "aVE", "aUE", "aEA", "aEE", "aSC", "aE1",
              "aE2", "aPV", "aSG", "aASM", "aDN", "aDXP", "aRC", "aAB",
              "aWOE", "aRCP", "aWO", "aLV", "aLNV", "aLNS", "aX1", "aOUT",
              "aPS1", "aPV1", "aKM", "aSW", "aPg", "aDBG", "aVSf")

    NPS = 6
    NPG = 7

    def em_sync(self, e):
        sm, io = self.ctx.sems, self.io
        for name, dst in [("invi", self.invi), ("jp", self.jp),
                          ("onesf", self.onesf), ("v_emb0", self.v0c),
                          ("v_emb1", self.v1c)]:
            e.dma_start(out=dst[:, :], in_=io[name]).then_inc(sm["aP"], 16)
        e.dma_start(out=self.sc1[0:1, 0:1], in_=io["tau"]).then_inc(sm["aP"], 16)
        if STAGE < 2:
            return
        e.wait_ge(sm["aAC"], 32)
        e.dma_start(out=self.k_dram.ap(), in_=self.kfm[:, :]).then_inc(sm["aKDR"], 16)
        e.wait_ge(sm["aAC"], NSWT)
        e.dma_start(out=self.q_dram.ap(), in_=self.qfm[:, :]).then_inc(sm["aQDR"], 16)
        if STAGE < 3:
            return
        e.wait_ge(sm["aQDR"], 16)
        e.wait_ge(sm["aKDR"], 16)
        for rb in range(2):
            for g4 in range(4):
                ng = rb * 4 + g4
                if ng > 0:
                    e.wait_ge(sm["aSC"], 128 * ng)
                for w in range(WIN):
                    half, wl = w // 16, w % 16
                    dst = apm(self.qw[half][:, :], (wl * 8) * (4 * 512),
                              [[4 * 512, 8], [512, 4], [1, 512]])
                    src = apm(self.q_dram.ap(), 97 + 512 * rb + w + (4 * g4) * 8 * 1152,
                              [[1152, 8], [8 * 1152, 4], [1, 512]])
                    e.dma_start(out=dst, in_=src).then_inc(sm["aQWB"], 16)
                for w in range(WIN):
                    half, wl = w // 16, w % 16
                    dst = apm(self.kw[half][:, :], (wl * 8) * S, [[S, 8], [1, S]])
                    src = apm(self.k_dram.ap(), g4 * 8 * (WIN - 1 + S) + w,
                              [[WIN - 1 + S, 8], [1, S]])
                    e.dma_start(out=dst, in_=src).then_inc(sm["aKWB"], 16)
        if DBG_A:
            e.wait_ge(sm["aAB"], 1)
            for nm, buf in [("dbg_attn", self.attn), ("dbg_attnb", self.attnb)]:
                e.dma_start(out=self.ctx.dram[nm].ap(), in_=buf[:, :]).then_inc(sm["aDBG"], 16)
            e.dma_start(out=self.ctx.dram["dbg_den"].ap(), in_=self.den[:, :]).then_inc(sm["aDBG"], 16)
            e.dma_start(out=self.ctx.dram["dbg_qfm"].ap(), in_=self.qfm[:, :]).then_inc(sm["aDBG"], 16)
            e.dma_start(out=self.ctx.dram["dbg_kfm"].ap(), in_=self.kfm[:, :]).then_inc(sm["aDBG"], 16)
            e.dma_start(out=self.ctx.dram["dbg_vs"].ap(), in_=self.vs[:, :]).then_inc(sm["aDBG"], 16)
            e.dma_start(out=self.ctx.dram["dbg_km"].ap(), in_=self.km[:, :]).then_inc(sm["aDBG"], 16)
            e.dma_start(out=self.ctx.dram["dbg_eb"].ap(), in_=self.ebias[:, :]).then_inc(sm["aDBG"], 16)
            e.wait_ge(sm["aDBG"], 16 * 8)
        if STAGE < 5:
            return
        for t in range(NTT):
            if t >= 1:
                e.wait_ge(sm["aXC"], t)
            e.dma_start(out=self.xres[:, :],
                        in_=apm(io["x_own"], t * 128 * D, [[D, 128], [1, D]])
                        ).then_inc(sm["aXT"], 16)
            if t >= 1:
                e.wait_ge(sm["aX1"], 2 * t)
                e.dma_start(out=apm(self.x1f_d.ap(), (t - 1) * 128 * D, [[D, 128], [1, D]]),
                            in_=self.x1f[:, :]).then_inc(sm["aOUT"], 16)
                e.dma_start(out=apm(self.x1b_d.ap(), (t - 1) * 128 * D, [[D, 128], [1, D]]),
                            in_=self.x1b[:, :]).then_inc(sm["aOUT"], 16)
        e.wait_ge(sm["aX1"], 2 * NTT)
        e.dma_start(out=apm(self.x1f_d.ap(), (NTT - 1) * 128 * D, [[D, 128], [1, D]]),
                    in_=self.x1f[:, :]).then_inc(sm["aOUT"], 16)
        e.dma_start(out=apm(self.x1b_d.ap(), (NTT - 1) * 128 * D, [[D, 128], [1, D]]),
                    in_=self.x1b[:, :]).then_inc(sm["aOUT"], 16)
        e.wait_ge(sm["aOUT"], 32 * NTT)

    def em_gpsimd(self, e):
        sm, io = self.ctx.sems, self.io
        e.dma_start(out=self.wkv[:, :], in_=io["wkv_t"]).then_inc(sm["aPg"], 16)
        e.dma_start(out=self.wq[:, :], in_=io["wq_t"]).then_inc(sm["aPg"], 16)
        e.dma_start(out=self.wo[:, :], in_=io["wo_t"]).then_inc(sm["aPg"], 16)
        e.dma_start(out=self.E16[:, :], in_=io["E16f"]).then_inc(sm["aPg"], 16)
        e.dma_start(out=self.identb[:, :], in_=io["identf"]).then_inc(sm["aPg"], 16)
        e.dma_start(out=self.maskm[:, :], in_=io["maskm"]).then_inc(sm["aPg"], 16)
        e.dma_start(out=self.onesb[:, :], in_=io["onesf"]).then_inc(sm["aPg"], 16)
        if STAGE < 2:
            return
        for g in range(NSWT):
            if g >= 2:
                e.wait_ge(sm["aTW"], 16 * (g - 1))
            if g < 32:
                srcap = apm(io["x_pref"], g * 128 * D, [[D, 128], [1, D]])
            else:
                srcap = apm(io["x_qext"], (g - 32) * 128 * D, [[D, 128], [1, D]])
            e.dma_start(out=self.xinb[g % 2][:, :], in_=srcap).then_inc(sm["aSW"], 16)
        if STAGE < 4:
            return
        for hx in range(32):
            rb, h = hx // 16, hx % 16
            e.wait_ge(sm["aSG"], hx + 1)
            e.dma_start(out=apm(self.attn[:, :], 8 * h * TOW + rb * 512,
                                [[TOW, 8], [1, 512]]),
                        in_=self.stg[0:8, :]).then_inc(sm["aASM"], 16)
            e.dma_start(out=apm(self.den[:, :], h * TOW + rb * 512,
                                [[TOW, 1], [1, 512]]),
                        in_=self.stg[8:9, :]).then_inc(sm["aASM"], 16)

    def em_tensor(self, e):
        sm = self.ctx.sems
        nc = self.ctx.nc
        if STAGE < 1:
            return
        e.wait_ge(sm["aP"], 16 * self.NPS)
        e.wait_ge(sm["aPg"], 16 * self.NPG)
        e.wait_ge(sm["aPV1"], 1)
        if 'expand' not in SKIP:
            for i, col in enumerate([4, 5, 1]):
                mm = e.matmul(self.p_x[:, i:i + 1], self.onesf[0:1, :],
                              self.sc1[0:1, col:col + 1], start=True, stop=True)
            mm.then_inc(sm["aWOE"], 1)
        else:
            nc.tensor.nop().then_inc(sm["aWOE"], 1)
        if STAGE < 2:
            return
        for t in range(NSWT):
            e.wait_ge(sm["aSW"], 16 * (t + 1))
            e.wait_ge(sm["aAC"], t)
            for dt in range(16):
                g = t * 16 + dt
                if g >= 2:
                    e.wait_ge(sm["aTE"], g - 1)
                e.transpose(self.p_tr[g % 2][:, 0:128],
                            self.xinb[t % 2][:, 128 * dt:128 * dt + 128],
                            self.identb[:, :]).then_inc(sm["aTW"], 1)
            e.wait_ge(sm["aTE"], 16 * (t + 1))
            if t < 32:
                for dt in range(16):
                    mm = e.matmul(self.p_kv[:, 0:128], self.wkv[:, 64 * dt:64 * dt + 64],
                                  self.xT[:, 128 * dt:128 * dt + 128],
                                  start=(dt == 0), stop=(dt == 15))
            else:
                for dt in range(16):
                    mm = e.matmul(self.p_q[:, 0:128], self.wq[:, 128 * dt:128 * dt + 128],
                                  self.xT[:, 128 * dt:128 * dt + 128],
                                  start=(dt == 0), stop=(dt == 15))
            mm.then_inc(sm["aPJ"], 1)
        if STAGE < 3:
            return
        e.wait_ge(sm["aAC"], NSWT)
        e.wait_ge(sm["aPV1"], 2)
        for jt in range(NJT):
            if jt >= 2:
                e.wait_ge(sm["aVE"], jt - 1)
            e.transpose(self.p_tr[jt % 2][:, 0:32],
                        self.vfull[:, 128 * jt + 1:128 * jt + 129],
                        self.identb[0:32, 0:32]).then_inc(sm["aVT"], 1)
        e.wait_ge(sm["aWO"], 1)
        for ch in range(4):
            if ch > 0:
                e.wait_ge(sm["aRCP"], ch)
            e.matmul(self.p_x[0:1, :], self.v0cb[:, 0:1],
                     self.wo[:, 512 * ch:512 * ch + 512], start=True,
                     stop=True).then_inc(sm["aWOE"], 1)
        for ch in range(4):
            e.wait_ge(sm["aRCP"], 4 + ch)
            e.matmul(self.p_x[:, :], self.onesb[0:1, :],
                     self.v0wo[0:1, 512 * ch:512 * ch + 512], start=True,
                     stop=True).then_inc(sm["aWOE"], 1)
        e.wait_ge(sm["aRCP"], 8)
        if STAGE < 4:
            return
        for rb in range(2):
            e.wait_ge(sm["aPV1"], 1)
            e.wait_ge(sm["aAC"], NSWT)
            if rb > 0:
                e.wait_ge(sm["aEA"], NJT * rb)
            e.matmul(self.p_q[:, :], self.onesf[0:1, :],
                     self.invi[0:1, 512 * rb:512 * rb + 512], start=True,
                     stop=True).then_inc(sm["aUE"], 1)
            for g4 in range(4):
                ng = rb * 4 + g4
                e.wait_ge(sm["aQWB"], 16 * WIN * (ng + 1))
                e.wait_ge(sm["aKWB"], 16 * WIN * (ng + 1))
                for hh in range(4):
                    h = 4 * g4 + hh
                    hx = rb * 16 + h
                    q0 = hx * NJT
                    e.wait_ge(sm["aSG"], hx)
                    e.wait_ge(sm["aVE"], NJT)
                    e.wait_ge(sm["aEE"], NJT * (rb + 1))
                    for jt in range(NJT):
                        q = q0 + jt
                        if q >= 2:
                            e.wait_ge(sm["aE1"], q - 1)
                        pb = self.p_sc[jt % 2]
                        e.matmul(pb[:, :], self.kw[0][:, 128 * jt:128 * jt + 128],
                                 self.qw[0][:, 512 * hh:512 * hh + 512],
                                 start=True, stop=False)
                        e.matmul(pb[:, :], self.kw[1][:, 128 * jt:128 * jt + 128],
                                 self.qw[1][:, 512 * hh:512 * hh + 512],
                                 start=False, stop=True).then_inc(sm["aSC"], 1)
                        if jt >= 1:
                            qq = q - 1
                            e.wait_ge(sm["aE2"], qq + 1)
                            jq = qq % NJT
                            e.matmul(self.p_pv[:, :],
                                     self.vs[:, 36 * jq + 9 * g4:36 * jq + 9 * g4 + 9],
                                     self.e2[qq % 2][:, :],
                                     start=(jq == 0), stop=False).then_inc(sm["aPV"], 1)
                    qq = q0 + NJT - 1
                    e.wait_ge(sm["aE2"], qq + 1)
                    e.matmul(self.p_pv[:, :],
                             self.vs[:, 36 * (NJT - 1) + 9 * g4:36 * (NJT - 1) + 9 * g4 + 9],
                             self.e2[qq % 2][:, :], start=False,
                             stop=True).then_inc(sm["aPV"], 1)
        if STAGE < 5:
            return
        e.wait_ge(sm["aDN"], 1)
        for ch in range(2):
            e.matmul(self.p_sc[ch][:, :], self.E16[:, :],
                     self.denb[:, 512 * ch:512 * ch + 512], start=True,
                     stop=True).then_inc(sm["aDXP"], 1)
        e.wait_ge(sm["aAB"], 1)
        for t in range(NTT):
            for ch in range(4):
                k = t * 4 + ch
                pb = self.p_x if ch % 2 == 0 else self.p_q
                if k >= 2:
                    e.wait_ge(sm["aLV"], k - 1)
                e.matmul(pb[:, :], self.attnb[:, 128 * t:128 * t + 128],
                         self.wop[:, 512 * ch:512 * ch + 512], start=True,
                         stop=True).then_inc(sm["aWO"], 1)

    def em_vector(self, e):
        sm = self.ctx.sems
        self._vsn = 0

        def W(ins):
            self._vsn += 1
            ins.then_inc(sm["aVSf"], 1)
            e.wait_ge(sm["aVSf"], self._vsn)
        if STAGE < 1:
            return
        e.wait_ge(sm["aP"], 16 * self.NPS)
        e.wait_ge(sm["aPg"], 16 * self.NPG)
        W(e.reciprocal(self.sc1[0:1, 1:2], self.sc1[0:1, 0:1]))
        W(e.tensor_scalar_mul(self.sc1[0:1, 6:7], self.sc1[0:1, 0:1], -1.0))
        W(e.tensor_max(self.sc1[0:1, 2:3], self.sc1[0:1, 0:1], self.sc1[0:1, 6:7]))
        W(e.reciprocal(self.sc1[0:1, 3:4], self.sc1[0:1, 2:3]))
        W(e.tensor_scalar_mul(self.sc1[0:1, 4:5], self.sc1[0:1, 3:4], -16.0))
        W(e.tensor_scalar_mul(self.sc1[0:1, 5:6], self.sc1[0:1, 1:2], 1.0 / 16.0))
        e.tensor_scalar(self.invi[0:1, :], self.invi[0:1, :], self.sc1[0:1, 1:2],
                        None, ALU.mult).then_inc(sm["aPV1"], 1)
        if 'jga' not in SKIP:
            for jt in range(NJT):
                e.tensor_scalar_add(self.jga[:, jt:jt + 1], self.jp[:, :], float(128 * jt))
        if 'memset' not in SKIP:
            e.memset(self.kfm[:, 0:WIN - 1], 0.0)
            e.memset(self.vfull[:, S:S + 128], 0.0)
            e.memset(self.vs[:, :], 1.0)
            e.memset(self.km[:, 3:4], 1e-5).then_inc(sm["aPV1"], 1)
        else:
            e.engine_nop().then_inc(sm["aPV1"], 1)
        e.wait_ge(sm["aWOE"], 1)
        if 'km' not in SKIP:
            e.tensor_copy(self.km[:, 0:3], self.p_x[:, 0:3]).then_inc(sm["aKM"], 1)
        else:
            e.engine_nop().then_inc(sm["aKM"], 1)
        W(e.tensor_sub(self.dvc[:, :], self.v1c[:, :], self.v0c[:, :]))
        e.tensor_copy(self.v0cb[:, :], self.v0c[:, :])
        e.tensor_scalar(self.wop[:, :], self.wo[:, :], self.dvc[:, 0:1], None,
                        ALU.mult).then_inc(sm["aWO"], 1)
        if STAGE < 3:
            return
        for jt in range(NJT):
            e.wait_ge(sm["aVT"], jt + 1)
            for g4 in range(4):
                cp = e.tensor_copy(self.vs[:, 36 * jt + 9 * g4:36 * jt + 9 * g4 + 8],
                                   self.p_tr[jt % 2][:, 8 * g4:8 * g4 + 8])
            cp.then_inc(sm["aVE"], 1)
        for ch in range(4):
            e.wait_ge(sm["aWOE"], 2 + ch)
            e.tensor_copy(self.v0wo[0:1, 512 * ch:512 * ch + 512],
                          self.p_x[0:1, :]).then_inc(sm["aRCP"], 1)
        for ch in range(4):
            e.wait_ge(sm["aWOE"], 6 + ch)
            e.tensor_copy(self.v0woe[:, 512 * ch:512 * ch + 512],
                          self.p_x[:, :]).then_inc(sm["aRCP"], 1)
        if STAGE < 4:
            return
        for rb in range(2):
            e.wait_ge(sm["aUE"], rb + 1)
            e.tensor_copy(self.Usb[:, :], self.p_q[:, :])
            for jt in range(NJT):
                idx = rb * NJT + jt
                if idx >= 2:
                    e.wait_ge(sm["aEE"], idx - 1)
                ofs = 4480 + 512 * rb - 128 * jt
                e.scalar_tensor_tensor(self.ebarg[idx % 2][:, :], self.Usb[:, :],
                                       self.jga[:, jt:jt + 1],
                                       self.maskm[:, ofs:ofs + 512], ALU.mult,
                                       ALU.add).then_inc(sm["aEA"], 1)
            for h in range(16):
                hx = rb * 16 + h
                q0 = hx * NJT
                for jt in range(NJT):
                    q = q0 + jt
                    e.wait_ge(sm["aE1"], q + 1)
                    if q >= 2:
                        e.wait_ge(sm["aPV"], q - 1)
                    e.tensor_mul(self.e2[q % 2][:, :], self.e1[q % 2][:, :],
                                 self.ebias[:, 512 * jt:512 * jt + 512]
                                 ).then_inc(sm["aE2"], 1)
                e.wait_ge(sm["aPV"], NJT * (hx + 1))
                e.wait_ge(sm["aASM"], 32 * hx)
                e.tensor_copy(self.stg[:, :], self.p_pv[:, :]).then_inc(sm["aSG"], 1)
        if STAGE < 5:
            return
        e.wait_ge(sm["aASM"], 32 * 32)
        e.tensor_scalar_max(self.den[:, :], self.den[:, :], 1e-30)
        e.tensor_copy(self.denb[:, :], self.den[:, :]).then_inc(sm["aDN"], 1)
        for ch in range(2):
            e.wait_ge(sm["aDXP"], ch + 1)
            e.reciprocal(self.recip[:, 512 * ch:512 * ch + 512], self.p_sc[ch][:, :])
        for ch in range(2):
            ab = e.tensor_mul(self.attnb[:, 512 * ch:512 * ch + 512],
                              self.attn[:, 512 * ch:512 * ch + 512],
                              self.recip[:, 512 * ch:512 * ch + 512])
        ab.then_inc(sm["aAB"], 1)
        for t in range(NTT):
            for ch in range(4):
                k = t * 4 + ch
                pb = self.p_x if ch % 2 == 0 else self.p_q
                e.wait_ge(sm["aWO"], 2 + k)
                e.tensor_add(self.xr[:, 512 * ch:512 * ch + 512], pb[:, :],
                             self.v0woe[:, 512 * ch:512 * ch + 512]
                             ).then_inc(sm["aLV"], 1)
            e.wait_ge(sm["aXT"], 16 * (t + 1))
            e.tensor_add(self.xr[:, :], self.xr[:, :],
                         self.xres[:, :]).then_inc(sm["aXC"], 1)
            W(e.reduce_sum(self.ln[:, 0:1], self.xr[:, :], AX))
            W(e.tensor_scalar_mul(self.ln[:, 1:2], self.ln[:, 0:1], 1.0 / D))
            e.tensor_scalar(self.xr[:, :], self.xr[:, :], self.ln[:, 1:2], None,
                            ALU.subtract)
            if t >= 1:
                e.wait_ge(sm["aOUT"], 32 * t)
            e.tensor_mul(self.x1b[:, :], self.xr[:, :], self.xr[:, :])
            e.reduce_sum(self.ln[:, 2:3], self.x1b[:, :], AX).then_inc(sm["aLNV"], 1)
            e.wait_ge(sm["aLNS"], t + 1)
            W(e.reciprocal(self.ln[:, 4:5], self.ln[:, 3:4]))
            e.tensor_scalar(self.x1f[:, :], self.xr[:, :], self.ln[:, 4:5], None,
                            ALU.mult).then_inc(sm["aX1"], 1)
            e.tensor_scalar(self.x1b[:, :], self.xr[:, :], self.ln[:, 4:5], None,
                            ALU.mult).then_inc(sm["aX1"], 1)

    def em_scalar(self, e):
        sm = self.ctx.sems
        nop = lambda: self.ctx.nc.scalar.nop()
        if STAGE < 1:
            return
        e.wait_ge(sm["aP"], 16 * self.NPS)
        e.wait_ge(sm["aPg"], 16 * self.NPG)

        e.wait_ge(sm["aKM"], 1)
        if STAGE < 2:
            return
        for t in range(NSWT):
            for dt in range(16):
                g = t * 16 + dt
                e.wait_ge(sm["aTW"], g + 1)
                if t >= 1 and dt == 0:
                    e.wait_ge(sm["aPJ"], t)
                e.copy(self.xT[:, 128 * dt:128 * dt + 128],
                       self.p_tr[g % 2][:, 0:128]).then_inc(sm["aTE"], 1)
            e.wait_ge(sm["aPJ"], t + 1)
            if t < 32:
                e.activation(self.kfm[:, WIN - 1 + 128 * t:WIN - 1 + 128 * t + 128],
                             self.p_kv[0:32, 0:128], AF.Tanh, scale=self.km[0:32, 2:3])
                e.activation(self.vfull[:, 128 * t:128 * t + 128],
                             self.p_kv[32:64, 0:128], AF.Sigmoid,
                             scale=self.km[32:64, 2:3]).then_inc(sm["aAC"], 1)
            else:
                tq = t - 32
                e.activation(self.qfm[:, 128 * tq:128 * tq + 128],
                             self.p_q[:, 0:128], AF.Tanh,
                             scale=self.km[:, 2:3]).then_inc(sm["aAC"], 1)
        if STAGE < 4:
            return
        for rb in range(2):
            for jt in range(NJT):
                idx = rb * NJT + jt
                e.wait_ge(sm["aEA"], idx + 1)
                e.activation(self.ebias[:, 512 * jt:512 * jt + 512],
                             self.ebarg[idx % 2][:, :], AF.Exp).then_inc(sm["aEE"], 1)
            for h in range(16):
                hx = rb * 16 + h
                q0 = hx * NJT
                for jt in range(NJT):
                    q = q0 + jt
                    e.wait_ge(sm["aSC"], q + 1)
                    if q >= 2:
                        e.wait_ge(sm["aE2"], q - 1)
                    e.activation(self.e1[q % 2][:, :], self.p_sc[jt % 2][:, :], AF.Exp,
                                 bias=self.km[:, 0:1],
                                 scale=self.km[:, 1:2]).then_inc(sm["aE1"], 1)
        if STAGE < 5:
            return
        for t in range(NTT):
            e.wait_ge(sm["aLNV"], t + 1)
            e.activation(self.ln[:, 3:4], self.ln[:, 2:3], AF.Sqrt,
                         scale=1.0 / D, bias=self.km[:, 3:4]).then_inc(sm["aLNS"], 1)


def _hostprep_A(inputs, core):
    beta, seg = core // 4, core % 4
    own0 = seg * TOW
    x = np.asarray(inputs["L0_attn_hidden_states"], dtype=np.float32)
    xb = x[beta]
    x_own = np.ascontiguousarray(xb[own0:own0 + TOW])
    x_pref = np.ascontiguousarray(xb)
    lo = own0 - 128
    x_qext = np.zeros((1152, D), np.float32)
    s0 = max(0, lo)
    x_qext[s0 - lo:, :] = xb[s0:own0 + TOW]
    wq = np.asarray(inputs["L0_attn_Wq"], np.float32)
    wk = np.asarray(inputs["L0_attn_Wk"], np.float32)
    wv = np.asarray(inputs["L0_attn_Wv"], np.float32)
    wo = np.asarray(inputs["L0_attn_Wo"], np.float32)
    wq_t = np.ascontiguousarray(wq.T.reshape(16, 128, 128).transpose(1, 0, 2).reshape(128, 2048))
    wkv_t = np.ascontiguousarray(np.concatenate([wk, wv], axis=0).T
                                 .reshape(16, 128, 64).transpose(1, 0, 2).reshape(128, 1024))
    wo_t = np.ascontiguousarray(wo.T)
    v0 = np.asarray(inputs["L0_attn_v_emb0"], np.float32).reshape(128, 1)
    v1 = np.asarray(inputs["L0_attn_v_emb1"], np.float32).reshape(128, 1)
    tau = np.asarray(inputs["L0_attn_tau"], np.float32).reshape(1, 1)
    jl = np.arange(128)[:, None]
    vv = np.arange(5632)[None, :]
    maskm = np.where(vv >= (4481 - own0) + jl, 0.0, -1e30).astype(np.float32)
    invi = (1.0 / (np.arange(own0, own0 + TOW, dtype=np.float64) + 1.0)).astype(np.float32).reshape(1, TOW)
    jp = np.arange(128, dtype=np.float32).reshape(128, 1)
    onesf = np.ones((1, 128), np.float32)
    E16f = np.zeros((16, 128), np.float32)
    for hh in range(16):
        E16f[hh, 8 * hh:8 * hh + 8] = 1.0
    identf = np.eye(128, dtype=np.float32)
    return dict(x_own=x_own, x_pref=x_pref, x_qext=x_qext, wq_t=wq_t, wkv_t=wkv_t,
                wo_t=wo_t, v_emb0=v0, v_emb1=v1, tau=tau, maskm=maskm, invi=invi,
                jp=jp, onesf=onesf, E16f=E16f, identf=identf)


def _declare_A(nc):
    io = {}
    decl = [("x_own", [TOW, D], F32), ("x_pref", [S, D], F32), ("x_qext", [1152, D], F32),
            ("wq_t", [128, 2048], F32), ("wkv_t", [128, 1024], F32), ("wo_t", [128, D], F32),
            ("v_emb0", [128, 1], F32), ("v_emb1", [128, 1], F32), ("tau", [1, 1], F32),
            ("maskm", [128, 5632], F32), ("invi", [1, TOW], F32), ("jp", [128, 1], F32),
            ("onesf", [1, 128], F32), ("E16f", [16, 128], F32), ("identf", [128, 128], F32)]
    for n, sh, dt in decl:
        io[n] = nc.dram_tensor(n, sh, dt, kind="ExternalInput").ap()
    return io


def build_program_A():
    nc = bass.Bass("TRN2", target_bir_lowering=False)
    stack = ExitStack()
    ctx = Ctx(nc, stack)
    io = _declare_A(nc)
    with stack:
        ph = PhaseA(ctx, io, out_kind="ExternalOutput")
        if DBG_A:
            ctx.dscr("dbg_attn", [128, TOW], F32, kind="ExternalOutput")
            ctx.dscr("dbg_attnb", [128, TOW], BF16, kind="ExternalOutput")
            ctx.dscr("dbg_den", [16, TOW], F32, kind="ExternalOutput")
            ctx.dscr("dbg_qfm", [128, 1152], BF16, kind="ExternalOutput")
            ctx.dscr("dbg_kfm", [32, WIN - 1 + S], BF16, kind="ExternalOutput")
            ctx.dscr("dbg_vs", [128, NJT * 36], BF16, kind="ExternalOutput")
            ctx.dscr("dbg_km", [128, 8], F32, kind="ExternalOutput")
            ctx.dscr("dbg_eb", [128, NJT * 512], BF16, kind="ExternalOutput")
        with nc.Block() as block:
            @block.sync
            def _(e):
                ph.em_sync(e)
                if STAGE >= 5:
                    e.wait_ge(ctx.sems["aOUT"], 32 * NTT)
                elif STAGE >= 2:
                    e.wait_ge(ctx.sems["aKDR"], 16)

            @block.gpsimd
            def _(e):
                ph.em_gpsimd(e)

            @block.tensor
            def _(e):
                ph.em_tensor(e)

            @block.vector
            def _(e):
                ph.em_vector(e)

            @block.scalar
            def _(e):
                ph.em_scalar(e)
    return nc


def run_phase_A(inputs):
    nc = build_program_A()
    in_maps = [_hostprep_A(inputs, c) for c in range(NCORE)]
    res = run_bass_kernel_spmd(nc, in_maps, core_ids=list(range(NCORE)))
    outs = []
    for c in range(NCORE):
        outs.append(np.asarray(res.results[c]["x1f_d"]))
    return outs


def kernel_multi(**inputs):
    cores = list(range(NCORE))
    resA = run_bass_kernel_spmd(build_program_A(),
                                [_hostprep_A(inputs, c) for c in cores],
                                core_ids=cores)
    imB = []
    for c in cores:
        im = _hostprep_B(inputs, c)
        im["x1f_d"] = np.asarray(resA.results[c]["x1f_d"])
        im["x1b_d"] = np.asarray(resA.results[c]["x1b_d"])
        imB.append(im)
    resB = run_bass_kernel_spmd(build_program_B(), imB, core_ids=cores)
    imC = []
    for c in cores:
        im = _hostprep_C(inputs, c)
        im["x2f_d"] = np.asarray(resB.results[c]["x2f_d"])
        im["x2b_d"] = np.asarray(resB.results[c]["x2b_d"])
        imC.append(im)
    resC = run_bass_kernel_spmd(build_program_C(), imC, core_ids=cores)
    imD = []
    for c in cores:
        im = _hostprep_D(inputs, c)
        im["x3f_d"] = np.asarray(resC.results[c]["x3f_d"])
        im["x3b_d"] = np.asarray(resC.results[c]["x3b_d"])
        imD.append(im)
    resD = run_bass_kernel_spmd(build_program_D(), imD, core_ids=cores)
    out = np.zeros((B, S, D), np.float32)
    for c in cores:
        b, s = c // 4, c % 4
        out[b, s * 1024:(s + 1) * 1024] = np.asarray(resD.results[c]["x4f_d"])
    return out


NSEG = 16
ETS = 4
DE = 8192


class PhaseB:

    def __init__(self, ctx: Ctx, io, out_kind="Internal"):
        self.ctx = ctx
        self.io = io
        c = ctx
        self.xT = c.sbuf("B_xT", [128, 16 * 1024], BF16)
        self.win = [c.sbuf(f"B_win{i}", [128, 2048], F32) for i in range(2)]
        self.sgn = c.sbuf("B_sgn", [128, 2048], BF16)
        self.q2 = [c.sbuf(f"B_q2_{i}", [128, 2048], BF16) for i in range(4)]
        self.h8 = c.sbuf("B_h8", [128, ETS * 1024], BF16)
        self.wdin = [c.sbuf(f"B_wdin{i}", [128, 2048], F32) for i in range(2)]
        self.sgd = c.sbuf("B_sgd", [128, 2048], BF16)
        self.q2d = [c.sbuf(f"B_q2d{i}", [128, 2048], BF16) for i in range(ETS)]
        self.dacc = c.sbuf("B_dacc", [128, NTT * 2048], F32)
        self.x1res = c.sbuf("B_x1res", [128, 2048], F32)
        self.x2f = c.sbuf("B_x2f", [128, 2048], F32)
        self.x2b = c.sbuf("B_x2b", [128, 2048], BF16)
        self.racc = c.sbuf("B_racc", [128, 64], F32)
        self.acc3 = c.sbuf("B_acc3", [128, 4], F32)
        self.tot3 = c.sbuf("B_tot3", [3, 8], F32)
        self.scC = c.sbuf("B_scC", [128, 12], F32)
        self.onesf = c.sbuf("B_onesf", [1, 128], F32)
        self.onescol = c.sbuf("B_onescol", [128, 1], F32)
        self.sel3 = c.sbuf("B_sel3", [3, 384], F32)
        self.silu = c.sbuf("B_silu", [128, 1024], BF16)
        self.ln = c.sbuf("B_ln", [128, 8], F32)
        self.p_up = c.psum("B_pup", [128, 1024], F32)
        self.p_g = c.psum("B_pg", [128, 1024], F32)
        self.p_dn = c.psum("B_pdn", [128, 2048], F32)
        self.part_d = c.dscr("B_part", [3, 1], F32)
        self.red_d = c.dscr("B_red", [3, 1], F32)
        self.x2f_d = c.dscr("x2f_d", [TOW, D], F32, kind=out_kind)
        self.x2b_d = c.dscr("x2b_d", [TOW, D], BF16, kind=out_kind)
        self.x1f_d = c.dram["x1f_d"] if "x1f_d" in c.dram else c.dscr("x1f_d", [TOW, D], F32)
        self.x1b_d = c.dram["x1b_d"] if "x1b_d" in c.dram else c.dscr("x1b_d", [TOW, D], BF16)
        c.sem("bSL", "bRD", "bACC", "bTOT", "bCC", "bCCL", "bSCL", "bXT", "bLU",
              "bSGp", "bQ", "bMM", "bSILs", "bGP", "bH", "bLD", "bSGd", "bQD",
              "bDN", "bFL", "bX1R", "bLNV", "bLNS", "bX2", "bOUT", "bVS", "bUP")

    def em_sync(self, e):
        sm, io = self.ctx.sems, self.io
        e.dma_start(out=self.onesf[:, :], in_=io["onesf"]).then_inc(sm["bSL"], 16)
        e.dma_start(out=self.sel3[:, :], in_=io["sel3"]).then_inc(sm["bSL"], 16)
        for m, nm in enumerate(["wu_sl", "wg_sl", "wd_sl"]):
            for st in range(16):
                idx = m * 16 + st
                if idx >= 2:
                    e.wait_ge(sm["bRD"], idx - 1)
                e.dma_start(out=self.win[idx % 2][:, 0:1024],
                            in_=apm(io[nm], st * 1024, [[16384, 128], [1, 1024]])
                            ).then_inc(sm["bSL"], 16)
        for dt in range(16):
            e.dma_start(out=self.xT[:, 1024 * dt:1024 * dt + 1024],
                        in_=apm(self.x1b_d.ap(), 128 * dt, [[D, 1024], [1, 128]]),
                        transpose=True).then_inc(sm["bXT"], 16)
        for seg in range(NSEG):
            for etl in range(ETS):
                etg = seg * ETS + etl
                for w, nm in enumerate(["wu_t", "wg_t"]):
                    li = 2 * etg + w
                    if li >= 2:
                        e.wait_ge(sm["bSGp"], li - 1)
                    else:
                        e.wait_ge(sm["bACC"], 1)
                    e.dma_start(out=self.win[li % 2][:, :],
                                in_=apm(io[nm], 128 * etg,
                                        [[DE, 128], [128 * DE, 16], [1, 128]])
                                ).then_inc(sm["bLU"], 16)
            for etl in range(ETS):
                eg = seg * ETS + etl
                if eg >= 2:
                    e.wait_ge(sm["bSGd"], eg - 1)
                e.dma_start(out=self.wdin[eg % 2][:, :],
                            in_=apm(io["wd_t"], 128 * eg * D, [[D, 128], [1, D]])
                            ).then_inc(sm["bLD"], 16)
        for t in range(NTT):
            if t >= 1:
                e.wait_ge(sm["bX1R"], t)
            e.dma_start(out=self.x1res[:, :],
                        in_=apm(self.x1f_d.ap(), t * 128 * D, [[D, 128], [1, D]])
                        ).then_inc(sm["bXT"], 16)
            if t >= 1:
                e.wait_ge(sm["bX2"], 2 * t)
                e.dma_start(out=apm(self.x2f_d.ap(), (t - 1) * 128 * D, [[D, 128], [1, D]]),
                            in_=self.x2f[:, :]).then_inc(sm["bOUT"], 16)
                e.dma_start(out=apm(self.x2b_d.ap(), (t - 1) * 128 * D, [[D, 128], [1, D]]),
                            in_=self.x2b[:, :]).then_inc(sm["bOUT"], 16)
        e.wait_ge(sm["bX2"], 2 * NTT)
        e.dma_start(out=apm(self.x2f_d.ap(), (NTT - 1) * 128 * D, [[D, 128], [1, D]]),
                    in_=self.x2f[:, :]).then_inc(sm["bOUT"], 16)
        e.dma_start(out=apm(self.x2b_d.ap(), (NTT - 1) * 128 * D, [[D, 128], [1, D]]),
                    in_=self.x2b[:, :]).then_inc(sm["bOUT"], 16)
        if DBG_B:
            e.wait_ge(sm["bSCL"], 2)
            e.dma_start(out=self.ctx.dram["dbgB_scC"].ap(), in_=self.scC[:, :]).then_inc(sm["bVS"] if False else sm["bOUT"], 16)
            e.dma_start(out=self.ctx.dram["dbgB_tot"].ap(), in_=self.tot3[:, :]).then_inc(sm["bOUT"], 16)
            e.dma_start(out=self.ctx.dram["dbgB_acc"].ap(), in_=self.acc3[:, :]).then_inc(sm["bOUT"], 16)
            e.wait_ge(sm["bFL"], NSEG * NTT)
            e.dma_start(out=self.ctx.dram["dbgB_dacc"].ap(), in_=self.dacc[:, :]).then_inc(sm["bOUT"], 16)
        e.wait_ge(sm["bOUT"], 32 * NTT + (64 if DBG_B else 0))

    def em_gpsimd(self, e):
        sm = self.ctx.sems
        e.wait_ge(sm["bACC"], 2)
        e.dma_start(out=self.part_d.ap(), in_=self.tot3[:, 0:1]).then_inc(sm["bCC"], 16)
        e.wait_ge(sm["bCC"], 16)
        e.collective_compute("AllReduce", ALU.add, replica_groups=[list(range(8))],
                             ins=[self.part_d.ap()], outs=[self.red_d.ap()]
                             ).then_inc(sm["bCC"], 1)
        e.wait_ge(sm["bCC"], 17)
        e.dma_start(out=self.tot3[:, 0:1], in_=self.red_d.ap()).then_inc(sm["bCCL"], 16)

    def em_tensor(self, e):
        sm = self.ctx.sems
        e.wait_ge(sm["bACC"], 1)
        e.matmul(self.p_up[0:3, 0:1], self.acc3[:, 0:3], self.onescol[:, 0:1],
                 start=True, stop=True).then_inc(sm["bTOT"], 1)
        e.wait_ge(sm["bSCL"], 1)
        for i, (m, col) in enumerate([(0, 2), (1, 2), (2, 2), (0, 3), (1, 3), (2, 3)]):
            mm = e.matmul(self.p_g[:, i:i + 1], self.sel3[:, 128 * m:128 * m + 128],
                          self.tot3[:, col:col + 1], start=True, stop=True)
        mm.then_inc(sm["bTOT"], 1)
        e.wait_ge(sm["bXT"], 16 * 16)
        e.wait_ge(sm["bSCL"], 2)
        for seg in range(NSEG):
            for etl in range(ETS):
                etg = seg * ETS + etl
                for w in range(2):
                    li = 2 * etg + w
                    qi = li % 4
                    e.wait_ge(sm["bQ"], li + 1)
                    if w == 0 and etg >= 1:
                        e.wait_ge(sm["bSILs"], etg)
                    if w == 1 and etg >= 1:
                        e.wait_ge(sm["bH"], etg)
                    pb = self.p_up if w == 0 else self.p_g
                    for c2 in range(2):
                        for dt in range(16):
                            mm = e.matmul(pb[:, 512 * c2:512 * c2 + 512],
                                          self.q2[qi][:, 128 * dt:128 * dt + 128],
                                          self.xT[:, 1024 * dt + 512 * c2:1024 * dt + 512 * c2 + 512],
                                          start=(dt == 0), stop=(dt == 15))
                    mm.then_inc(sm["bMM"], 1)
            e.wait_ge(sm["bH"], ETS * (seg + 1))
            e.wait_ge(sm["bQD"], ETS * (seg + 1))
            for tt in range(NTT):
                it = seg * NTT + tt
                if it >= 1:
                    e.wait_ge(sm["bFL"], it)
                for etl in range(ETS):
                    for c4 in range(4):
                        mm = e.matmul(self.p_dn[:, 512 * c4:512 * c4 + 512],
                                      self.h8[:, 1024 * etl + 128 * tt:1024 * etl + 128 * tt + 128],
                                      self.q2d[etl][:, 512 * c4:512 * c4 + 512],
                                      start=(etl == 0), stop=(etl == ETS - 1))
                mm.then_inc(sm["bDN"], 1)

    def em_vector(self, e):
        sm = self.ctx.sems
        self._vsn = 0

        def W(ins):
            self._vsn += 1
            ins.then_inc(sm["bVS"], 1)
            e.wait_ge(sm["bVS"], self._vsn)
        e.memset(self.onescol[:, :], 1.0)
        e.memset(self.scC[:, 6:7], 1e-5)
        e.memset(self.scC[:, 7:8], -0.5)
        e.memset(self.scC[:, 8:9], 0.5)
        for idx in range(48):
            e.wait_ge(sm["bSL"], 16 * (idx + 3))
            e.reduce_sum(self.racc[:, idx:idx + 1], self.win[idx % 2][:, 0:1024], AX,
                         apply_absolute_value=True).then_inc(sm["bRD"], 1)
        for m in range(3):
            W(e.reduce_sum(self.acc3[:, m:m + 1], self.racc[:, 16 * m:16 * m + 16], AX))
        e.engine_nop().then_inc(sm["bACC"], 1)
        e.wait_ge(sm["bTOT"], 1)
        e.tensor_copy(self.tot3[:, 0:1], self.p_up[0:3, 0:1]).then_inc(sm["bACC"], 1)
        e.wait_ge(sm["bCCL"], 16)
        W(e.tensor_scalar(self.tot3[:, 1:2], self.tot3[:, 0:1],
                          1.0 / (2048.0 * 8192.0), 1e-8, ALU.mult, ALU.add))
        W(e.reciprocal(self.tot3[:, 2:3], self.tot3[:, 1:2]))
        W(e.tensor_scalar_mul(self.tot3[:, 3:4], self.tot3[:, 1:2], 0.5))
        e.engine_nop().then_inc(sm["bSCL"], 1)
        e.wait_ge(sm["bTOT"], 2)
        e.tensor_copy(self.scC[:, 0:6], self.p_g[:, 0:6])
        e.engine_nop().then_inc(sm["bSCL"], 1)
        for seg in range(NSEG):
            for etl in range(ETS):
                etg = seg * ETS + etl
                for w in range(2):
                    li = 2 * etg + w
                    qi = li % 4
                    e.wait_ge(sm["bSGp"], li + 1)
                    if li >= 4:
                        e.wait_ge(sm["bMM"], li - 3)
                    e.tensor_add(self.q2[qi][:, :], self.q2[qi][:, :],
                                 self.sgn[:, :]).then_inc(sm["bQ"], 1)
                e.wait_ge(sm["bMM"], 2 * etg + 2)
                e.wait_ge(sm["bSILs"], etg + 1)
                e.scalar_tensor_tensor(self.h8[:, 1024 * etl:1024 * etl + 1024],
                                       self.p_g[:, :], self.scC[:, 4:5],
                                       self.silu[:, :], ALU.mult,
                                       ALU.mult).then_inc(sm["bH"], 1)
            for etl in range(ETS):
                eg = seg * ETS + etl
                e.wait_ge(sm["bSGd"], eg + 1)
                if seg >= 1:
                    e.wait_ge(sm["bDN"], NTT * seg)
                e.tensor_add(self.q2d[etl][:, :], self.q2d[etl][:, :],
                             self.sgd[:, :]).then_inc(sm["bQD"], 1)
            for tt in range(NTT):
                it = seg * NTT + tt
                e.wait_ge(sm["bDN"], it + 1)
                if seg == 0:
                    e.tensor_copy(self.dacc[:, 2048 * tt:2048 * tt + 2048],
                                  self.p_dn[:, :]).then_inc(sm["bFL"], 1)
                else:
                    e.tensor_add(self.dacc[:, 2048 * tt:2048 * tt + 2048],
                                 self.dacc[:, 2048 * tt:2048 * tt + 2048],
                                 self.p_dn[:, :]).then_inc(sm["bFL"], 1)
        e.wait_ge(sm["bFL"], NSEG * NTT)
        for t in range(NTT):
            e.wait_ge(sm["bXT"], 16 * (16 + t + 1))
            dx = self.dacc[:, 2048 * t:2048 * t + 2048]
            e.scalar_tensor_tensor(dx, dx, self.scC[:, 5:6], self.x1res[:, :],
                                   ALU.mult, ALU.add).then_inc(sm["bX1R"], 1)
            W(e.reduce_sum(self.ln[:, 0:1], dx, AX))
            W(e.tensor_scalar_mul(self.ln[:, 1:2], self.ln[:, 0:1], 1.0 / D))
            e.tensor_scalar(dx, dx, self.ln[:, 1:2], None, ALU.subtract)
            if t >= 1:
                e.wait_ge(sm["bOUT"], 32 * t)
            e.tensor_mul(self.x2b[:, :], dx, dx)
            e.reduce_sum(self.ln[:, 2:3], self.x2b[:, :], AX).then_inc(sm["bLNV"], 1)
            e.wait_ge(sm["bLNS"], t + 1)
            W(e.reciprocal(self.ln[:, 4:5], self.ln[:, 3:4]))
            e.tensor_scalar(self.x2f[:, :], dx, self.ln[:, 4:5], None,
                            ALU.mult).then_inc(sm["bX2"], 1)
            e.tensor_scalar(self.x2b[:, :], dx, self.ln[:, 4:5], None,
                            ALU.mult).then_inc(sm["bX2"], 1)

    def em_scalar(self, e):
        sm = self.ctx.sems
        e.wait_ge(sm["bSCL"], 2)
        for seg in range(NSEG):
            for etl in range(ETS):
                etg = seg * ETS + etl
                for w in range(2):
                    li = 2 * etg + w
                    qi = li % 4
                    e.wait_ge(sm["bLU"], 16 * (li + 1))
                    if li >= 4:
                        e.wait_ge(sm["bMM"], li - 3)
                    if li >= 1:
                        e.wait_ge(sm["bQ"], li)
                    sc = self.scC[:, w:w + 1]
                    e.activation(self.q2[qi][:, :], self.win[li % 2][:, :], AF.Sign,
                                 scale=sc, bias=self.scC[:, 7:8])
                    e.activation(self.sgn[:, :], self.win[li % 2][:, :], AF.Sign,
                                 scale=sc, bias=self.scC[:, 8:9]).then_inc(sm["bSGp"], 1)
                e.wait_ge(sm["bMM"], 2 * etg + 1)
                if etg >= 1:
                    e.wait_ge(sm["bH"], etg)
                e.activation(self.silu[:, :], self.p_up[:, :], AF.Silu,
                             scale=self.scC[:, 3:4]).then_inc(sm["bSILs"], 1)
            for etl in range(ETS):
                eg = seg * ETS + etl
                e.wait_ge(sm["bLD"], 16 * (eg + 1))
                if seg >= 1:
                    e.wait_ge(sm["bDN"], NTT * seg)
                if eg >= 1:
                    e.wait_ge(sm["bQD"], eg)
                e.activation(self.q2d[etl][:, :], self.wdin[eg % 2][:, :], AF.Sign,
                             scale=self.scC[:, 2:3], bias=self.scC[:, 7:8])
                e.activation(self.sgd[:, :], self.wdin[eg % 2][:, :], AF.Sign,
                             scale=self.scC[:, 2:3],
                             bias=self.scC[:, 8:9]).then_inc(sm["bSGd"], 1)
        for t in range(NTT):
            e.wait_ge(sm["bLNV"], t + 1)
            e.activation(self.ln[:, 3:4], self.ln[:, 2:3], AF.Sqrt,
                         scale=1.0 / D, bias=self.scC[:, 6:7]).then_inc(sm["bLNS"], 1)


def _hostprep_B(inputs, core):
    wu = np.asarray(inputs["L0_ffn_W_up"], np.float32)
    wg = np.asarray(inputs["L0_ffn_W_gate"], np.float32)
    wd = np.asarray(inputs["L0_ffn_W_down"], np.float32)
    wu_t = np.ascontiguousarray(wu.T)
    wg_t = np.ascontiguousarray(wg.T)
    wd_t = np.ascontiguousarray(wd.T)
    def til(a):
        return np.ascontiguousarray(a.reshape(16, 128, 1024).transpose(1, 0, 2).reshape(128, 16384))
    wu_sl = til(wu_t[:, 1024 * core:1024 * core + 1024])
    wg_sl = til(wg_t[:, 1024 * core:1024 * core + 1024])
    wd_sl = np.ascontiguousarray(
        wd_t[1024 * core:1024 * core + 1024].reshape(8, 128, 2048).transpose(1, 0, 2).reshape(128, 16384))
    sel3 = np.zeros((3, 384), np.float32)
    for m in range(3):
        sel3[m, 128 * m:128 * m + 128] = 1.0
    return dict(wu_t=wu_t, wg_t=wg_t, wd_t=wd_t, wu_sl=wu_sl, wg_sl=wg_sl,
                wd_sl=wd_sl, sel3=sel3, onesf=np.ones((1, 128), np.float32))


def _declare_B(nc, standalone=True):
    io = {}
    decl = [("wu_t", [D, DE], F32), ("wg_t", [D, DE], F32), ("wd_t", [DE, D], F32),
            ("wu_sl", [128, 16384], F32), ("wg_sl", [128, 16384], F32),
            ("wd_sl", [128, 16384], F32), ("sel3", [3, 384], F32)]
    if standalone:
        decl += [("onesf", [1, 128], F32)]
    for n, sh, dt in decl:
        io[n] = nc.dram_tensor(n, sh, dt, kind="ExternalInput").ap()
    return io


def build_program_B():
    nc = bass.Bass("TRN2", target_bir_lowering=False)
    stack = ExitStack()
    ctx = Ctx(nc, stack)
    io = _declare_B(nc)
    ctx.dram["x1f_d"] = nc.dram_tensor("x1f_d", [TOW, D], F32, kind="ExternalInput")
    ctx.dram["x1b_d"] = nc.dram_tensor("x1b_d", [TOW, D], BF16, kind="ExternalInput")
    with stack:
        ph = PhaseB(ctx, io, out_kind="ExternalOutput")
        if DBG_B:
            ctx.dscr("dbgB_scC", [128, 12], F32, kind="ExternalOutput")
            ctx.dscr("dbgB_tot", [3, 8], F32, kind="ExternalOutput")
            ctx.dscr("dbgB_acc", [128, 4], F32, kind="ExternalOutput")
            ctx.dscr("dbgB_dacc", [128, NTT * 2048], F32, kind="ExternalOutput")
        with nc.Block() as block:
            @block.sync
            def _(e):
                ph.em_sync(e)

            @block.gpsimd
            def _(e):
                ph.em_gpsimd(e)

            @block.tensor
            def _(e):
                ph.em_tensor(e)

            @block.vector
            def _(e):
                ph.em_vector(e)

            @block.scalar
            def _(e):
                ph.em_scalar(e)
    return nc


NH = 64
HP = 64
NST = 128
CH = 64
NCH = 16
NMT = 67
MT_ORDER = [66, 64, 65] + list(range(32, 64)) + list(range(32))


class PhaseC:

    def __init__(self, ctx: Ctx, io, out_kind="Internal"):
        self.ctx = ctx
        self.io = io
        c = ctx
        self.xT = c.sbuf("C_xT", [128, 16 * 1024], BF16)
        self.win = [c.sbuf(f"C_win{i}", [128, 2048], BF16) for i in range(2)]
        self.zstg = [c.sbuf(f"C_zstg{i}", [128, 1024], BF16) for i in range(2)]
        self.xbc = [c.sbuf(f"C_xbc{i}", [128, 1032], BF16) for i in range(2)]
        self.xssm = c.sbuf("C_xssm", [128, 1024], BF16)
        self.Bfm = c.sbuf("C_Bfm", [128, 1024], BF16)
        self.Cfm = c.sbuf("C_Cfm", [128, 1024], BF16)
        self.btok = c.sbuf("C_btok", [64, 2048], BF16)
        self.xstg = c.sbuf("C_xstg", [128, 1024], BF16)
        self.dtraw = c.sbuf("C_dtraw", [64, 1024], F32)
        self.dt = c.sbuf("C_dt", [64, 1024], F32)
        self.acs = [c.sbuf(f"C_acs{i}", [64, 1024], F32) for i in range(2)]
        self.acsT = c.sbuf("C_acsT", [64, 1024], F32)
        self.dtT = c.sbuf("C_dtT", [64, 1024], F32)
        self.EA = c.sbuf("C_EA", [64, 1024], F32)
        self.dec = c.sbuf("C_dec", [64, 1024], BF16)
        self.decT = c.sbuf("C_decT", [64, 1024], BF16)
        self.plog = c.sbuf("C_plog", [64, 20], F32)
        self.hout = c.sbuf("C_hout", [128, 102], BF16)
        self.hgb = c.sbuf("C_hgb", [128, 816], BF16)
        self.hsel = c.sbuf("C_hsel", [128, 102], BF16)
        self.xch = c.sbuf("C_xch", [64, 4096], BF16)
        self.xdt = c.sbuf("C_xdt", [64, 4096], BF16)
        self.G = c.sbuf("C_G", [64, 64], BF16)
        self.argf = c.sbuf("C_argf", [64, 1024], F32)
        self.lgcol = c.sbuf("C_lgcol", [64, 1], F32)
        self.M = c.sbuf("C_M", [64, 4096], BF16)
        self.bdec = c.sbuf("C_bdec", [64, 1024], BF16)
        self.run = c.sbuf("C_run", [128, 4096], F32)
        self.runbf = c.sbuf("C_runbf", [128, 4096], BF16)
        self.csc = c.sbuf("C_csc", [128, 4096], BF16)
        self.ybuf = c.sbuf("C_ybuf", [64, 4096], F32)
        self.afl = [c.sbuf(f"C_afl{i}", [1, 1024], F32) for i in range(2)]
        self.afl2 = [c.sbuf(f"C_afl2{i}", [1, 1024], F32) for i in range(2)]
        self.gch = c.sbuf("C_gch", [128, 1024], F32)
        self.pde = c.sbuf("C_pde", [64, 16], F32)
        self.drow = c.sbuf("C_drow", [1, 64], F32)
        self.cdrow = c.sbuf("C_cdrow", [1, 64], F32)
        self.pdrow = c.sbuf("C_pdrow", [1, 64], F32)
        self.wrow = c.sbuf("C_wrow", [1, 64], F32)
        self.cacc = c.sbuf("C_cacc", [128, 1024], BF16)
        self.nwb = c.sbuf("C_nwb", [128, 4096], BF16)
        self.cde = c.sbuf("C_cde", [128, 64], F32)
        self.De = c.sbuf("C_De", [128, 64], F32)
        self.wexp = c.sbuf("C_wexp", [128, 64], F32)
        self.crow = c.sbuf("C_crow", [1, 64], F32)
        self.lgp = c.sbuf("C_lgp", [8, 64], F32)
        self.wT = c.sbuf("C_wT", [64, 8], F32)
        self.mbias = c.sbuf("C_mbias", [64, 8], F32)
        self.mBt = c.sbuf("C_mBt", [8, 8], F32)
        self.trilU = c.sbuf("C_trilU", [64, 64], BF16)
        self.acol = c.sbuf("C_acol", [64, 8], F32)
        self.convw = c.sbuf("C_convw", [128, 136], F32)
        self.convb = c.sbuf("C_convb", [128, 34], F32)
        self.onesf = c.sbuf("C_onesf", [1, 128], F32)
        self.identb = c.sbuf("C_identb", [128, 128], BF16)
        self.identf = c.sbuf("C_identf", [128, 128], F32)
        self.selw = c.sbuf("C_selw", [128, 8], F32)
        self.ln = c.sbuf("C_ln", [128, 8], F32)
        self.zt = c.sbuf_at("C_zt", [128, 4096], BF16, "C_xT", 0)
        self.zsil = c.sbuf_at("C_zsil", [128, 4096], BF16, "C_xT", 8192)
        self.yt = c.sbuf_at("C_yt", [128, 4096], F32, "C_xT", 16384)
        self.ynb = c.sbuf_at("C_ynb", [128, 4096], BF16, "C_M", 0)
        self.ynT = [c.sbuf_at(f"C_ynT{i}", [128, 1024], BF16, "C_xch", 2048 * i) for i in range(4)]
        self.wo = [c.sbuf_at(f"C_wo{i}", [128, 2048], BF16, "C_run", 4096 * i) for i in range(4)]
        self.ostg = c.sbuf_at("C_ostg", [128, 2048], F32, "C_xdt", 0)
        self.x2r = c.sbuf_at("C_x2r", [128, 2048], F32, "C_runbf", 0)
        self.x3f = c.sbuf_at("C_x3f", [128, 2048], F32, "C_ybuf", 0)
        self.x3b = c.sbuf_at("C_x3b", [128, 2048], BF16, "C_ybuf", 8192)
        self.p_a = c.psum("C_pa", [128, 512], F32)
        self.p_b = c.psum("C_pb", [128, 512], F32)
        self.p_G = c.psum("C_pG", [64, 512], F32)
        self.p_S = c.psum("C_pS", [128, 512], F32)
        self.p_x = c.psum("C_px", [128, 512], F32)
        self.p_t = [c.psum(f"C_pt{i}", [128, 512], BF16) for i in range(2)]
        self.p_o = c.psum("C_po", [128, 512], F32)
        self.x_d = c.dscr("C_x_d", [TOW, 4096], BF16)
        self.z_d = c.dscr("C_z_d", [4096, TOW], BF16)
        self.y_d = c.dscr("C_y_d", [TOW, 4096], F32)
        self.acs_d = c.dscr("C_acs_d", [64, 1024], F32)
        self.ea_d = c.dscr("C_ea_d", [64, 1024], F32)
        self.pde_d = c.dscr("C_pde_d", [64, 16], F32)
        self.hout_d = c.dscr("C_hout_d", [128, 102], BF16)
        self.hg_d = c.dscr("C_hg_d", [1024, 102], BF16)
        self.sum_d = c.dscr("C_sum_d", [128, 4096], F32)
        self.sumg_d = c.dscr("C_sumg_d", [1024, 4096], F32)
        self.lg_d = c.dscr("C_lg_d", [64, 1], F32)
        self.lgg_d = c.dscr("C_lgg_d", [512, 1], F32)
        self.wt_d = c.dscr("C_wt_d", [64, 8], F32)
        self.yn_d = c.dscr("C_yn_d", [TOW, 4096], BF16)
        self.o_d = c.dscr("C_o_d", [8 * TOW, D], F32)
        self.prt = c.sbuf_at("C_prt", [128, 2048], F32, "C_xT", 24576)
        self.x3f_d = c.dscr("x3f_d", [TOW, D], F32, kind=out_kind)
        self.x3b_d = c.dscr("x3b_d", [TOW, D], BF16, kind=out_kind)
        self.x2f_d = c.dram["x2f_d"] if "x2f_d" in c.dram else c.dscr("x2f_d", [TOW, D], F32)
        self.x2b_d = c.dram["x2b_d"] if "x2b_d" in c.dram else c.dscr("x2b_d", [TOW, D], BF16)
        c.sem("cP", "cPg", "cXT", "cHM", "cHL", "cHLd", "cHG", "cHS", "cWL",
              "cPJ", "cEVp", "cCVt", "cSILa", "cTR", "cTRe", "cXSD",
              "cZDd", "cTF", "cTFe", "cTB", "cTBe", "cACD", "cACSv", "cEAs",
              "cXC", "cAF", "cBAm", "cBAv", "cARGv", "cMe", "cMv", "cGm", "cGV",
              "cCD", "cCDm", "cCDE", "cEF", "cEFm", "cCSCv", "cBD", "cXDT",
              "cST", "cRU", "cYM", "cYE", "cYD", "cSU", "cAG", "cLG",
              "cWC", "cWE", "cWEd", "cWR", "cWXm", "cCW", "cGC", "cGA",
              "cPD", "cPDm", "cPDE", "cCSCc", "cCSC2v", "cEF2", "cEF2m",
              "cYB2", "cYM2", "cYE2", "cP2d", "cZT", "cYN", "cYND", "cWO",
              "cYNT", "cOM", "cOS", "cOA", "cXRl", "cLNS", "cX3", "cOUT",
              "cVS", "cDE", "cP5a", "cP5b", "cRBv", "cXRc")

    NPS = 11
    NPG = 1

    def bar(self):
        import concourse.mybir as _mb
        ET = _mb.EngineType
        self.ctx.nc.multi_engine_barrier([ET.PE, ET.DVE, ET.Activation, ET.SP, ET.Pool])
        self._bn = getattr(self, "_bn", 0) + 1
        if CCUT and self._bn >= CCUT:
            raise _CutDone()


    XSSM = list(range(32, 64))

    def em_sync(self, e):
        self._bn = 0
        try:
            self._em_sync(e)
        except _CutDone:
            pass

    def _em_sync(self, e):
        sm, io = self.ctx.sems, self.io
        for name, dst in [("onesf", self.onesf), ("identf", self.identf),
                          ("identb", self.identb), ("trilU", self.trilU),
                          ("convw", self.convw), ("convb", self.convb),
                          ("mBt", self.mBt), ("mbias", self.mbias),
                          ("selw", self.selw), ("drow", self.drow)]:
            e.dma_start(out=dst[:, :], in_=io[name]).then_inc(sm["cP"], 16)
        e.dma_start(out=self.acol[:, 0:4], in_=io["acolin"]).then_inc(sm["cP"], 16)
        for dt_ in range(16):
            e.dma_start(out=self.xT[:, 1024 * dt_:1024 * dt_ + 1024],
                        in_=apm(self.x2b_d.ap(), 128 * dt_, [[D, 1024], [1, 128]]),
                        transpose=True).then_inc(sm["cXT"], 16)
        e.wait_ge(sm["cP"], 16 * self.NPS)
        e.wait_ge(sm["cXT"], 16 * 16)
        self.bar()
        e.wait_ge(sm["cHL"], 34)
        e.dma_start(out=self.hout_d.ap(), in_=self.hout[:, :]).then_inc(sm["cHLd"], 16)
        e.wait_ge(sm["cHLd"], 16)
        self.bar()
        self.bar()
        for p in range(8):
            e.dma_start(out=self.hgb[:, 102 * p:102 * p + 102],
                        in_=apm(self.hg_d.ap(), 128 * p * 102, [[102, 128], [1, 102]])
                        ).then_inc(sm["cHS"], 16)
        for mi, m in enumerate(MT_ORDER):
            if m < 32:
                zi = mi - 35
                e.wait_ge(sm["cEVp"], 2 * (mi + 1))
                e.dma_start(out=apm(self.z_d.ap(), 128 * m * TOW, [[TOW, 128], [1, TOW]]),
                            in_=self.zstg[mi % 2][:, :]).then_inc(sm["cZDd"], 16)
            elif 32 <= m < 64:
                sxi = m - 32
                e.wait_ge(sm["cTRe"], 8 * (sxi + 1))
                e.dma_start(out=apm(self.x_d.ap(), 128 * sxi,
                                    [[4096, 128], [128 * 4096, 8], [1, 128]]),
                            in_=apm(self.xstg[:, :], 0, [[1024, 128], [128, 8], [1, 128]])
                            ).then_inc(sm["cXSD"], 16)
        self.bar()
        self.bar()
        self.bar()
        self.bar()
        self.bar()
        e.wait_ge(sm["cACSv"], 1)
        e.dma_start(out=self.acs_d.ap(), in_=self.acs[1][:, :]).then_inc(sm["cACD"], 16)
        e.wait_ge(sm["cEAs"], 1)
        e.dma_start(out=self.ea_d.ap(), in_=self.EA[:, :]).then_inc(sm["cACD"], 16)
        e.wait_ge(sm["cEAs"], 2)
        e.dma_start(out=self.pde_d.ap(), in_=self.pde[:, :]).then_inc(sm["cACD"], 16)
        e.wait_ge(sm["cACD"], 48)
        self.bar()
        for c in range(NCH):
            e.dma_start(out=self.xch[:, :],
                        in_=apm(self.x_d.ap(), 64 * c * 4096, [[4096, 64], [1, 4096]])
                        ).then_inc(sm["cXC"], 16)
            for q in range(4):
                aq = 4 * c + q
                if q >= 2:
                    e.wait_ge(sm["cBAm"], 2 * (aq - 1))
                e.dma_start(out=self.afl[aq % 2][:, :],
                            in_=apm(self.acs_d.ap(), 64 * c + 1024 * 16 * q,
                                    [[1024, 16], [1, 64]])).then_inc(sm["cAF"], 16)
            e.dma_start(out=self.cdrow[:, :],
                        in_=apm(self.ea_d.ap(), 64 * c + 63, [[1024, 64]]),
                        allow_slow_non_contiguous=True).then_inc(sm["cCD"], 16)
            for q in range(4):
                aq = 4 * c + q
                if q >= 2:
                    e.wait_ge(sm["cEFm"], 2 * (aq - 1))
                e.dma_start(out=self.afl2[aq % 2][:, :],
                            in_=apm(self.ea_d.ap(), 64 * c + 1024 * 16 * q,
                                    [[1024, 16], [1, 64]])).then_inc(sm["cEF"], 16)
            e.wait_ge(sm["cYE"], 8 * (c + 1))
            e.dma_start(out=apm(self.y_d.ap(), 64 * c * 4096, [[4096, 64], [1, 4096]]),
                        in_=self.ybuf[:, :]).then_inc(sm["cYD"], 16)
            self.bar()
        e.dma_start(out=self.sum_d.ap(), in_=self.run[:, :]).then_inc(sm["cSU"], 16)
        e.dma_start(out=self.lg_d.ap(), in_=self.lgcol[:, :]).then_inc(sm["cSU"], 16)
        e.wait_ge(sm["cSU"], 32)
        self.bar()
        self.bar()
        e.dma_start(out=self.lgp[:, :],
                    in_=apm(self.lgg_d.ap(), 0, [[64, 8], [1, 64]])).then_inc(sm["cLG"], 16)
        e.wait_ge(sm["cWE"], 2)
        e.dma_start(out=self.wt_d.ap(), in_=self.wT[:, :]).then_inc(sm["cWEd"], 16)
        e.wait_ge(sm["cWEd"], 16)
        self.bar()
        for p in range(8):
            if p >= 1:
                e.wait_ge(sm["cCW"], p)
            e.dma_start(out=self.wrow[:, :],
                        in_=apm(self.wt_d.ap(), p, [[8, 64]]),
                        allow_slow_non_contiguous=True).then_inc(sm["cWR"], 16)
            for q in range(4):
                gi = 4 * p + q
                if gi >= 1:
                    e.wait_ge(sm["cGA"], gi)
                e.dma_start(out=self.gch[:, :],
                            in_=apm(self.sumg_d.ap(), 128 * p * 4096 + 1024 * q,
                                    [[4096, 128], [1, 1024]])).then_inc(sm["cGC"], 16)
        self.bar()
        for c in range(NCH):
            e.dma_start(out=self.ybuf[:, :],
                        in_=apm(self.y_d.ap(), 64 * c * 4096, [[4096, 64], [1, 4096]])
                        ).then_inc(sm["cYB2"], 16)
            e.dma_start(out=self.pdrow[:, :],
                        in_=apm(self.pde_d.ap(), c, [[16, 64]]),
                        allow_slow_non_contiguous=True).then_inc(sm["cPD"], 16)
            for q in range(4):
                aq = 4 * c + q
                if q >= 2:
                    e.wait_ge(sm["cEF2m"], 2 * (aq - 1))
                e.dma_start(out=self.afl2[aq % 2][:, :],
                            in_=apm(self.ea_d.ap(), 64 * c + 1024 * 16 * q,
                                    [[1024, 16], [1, 64]])).then_inc(sm["cEF2"], 16)
            e.wait_ge(sm["cYE2"], 8 * (c + 1))
            e.dma_start(out=apm(self.y_d.ap(), 64 * c * 4096, [[4096, 64], [1, 4096]]),
                        in_=self.ybuf[:, :]).then_inc(sm["cP2d"], 16)
            self.bar()
        for t in range(NTT):
            e.dma_start(out=self.zt[:, :],
                        in_=apm(self.z_d.ap(), 128 * t, [[TOW, 4096], [1, 128]]),
                        transpose=True).then_inc(sm["cZT"], 16)
            e.dma_start(out=self.yt[:, :],
                        in_=apm(self.y_d.ap(), 128 * t * 4096, [[4096, 128], [1, 4096]])
                        ).then_inc(sm["cZT"], 16)
            e.wait_ge(sm["cYN"], t + 1)
            e.dma_start(out=apm(self.yn_d.ap(), 128 * t * 4096, [[4096, 128], [1, 4096]]),
                        in_=self.ynb[:, :]).then_inc(sm["cYND"], 16)
            e.wait_ge(sm["cYND"], 16 * (t + 1))
            self.bar()
        for dseg in range(8):
            for j in range(4):
                di = 4 * dseg + j
                if dseg >= 1:
                    e.wait_ge(sm["cOM"], 32 * dseg)
                e.dma_start(out=self.ynT[j][:, :],
                            in_=apm(self.yn_d.ap(), 128 * di, [[4096, TOW], [1, 128]]),
                            transpose=True).then_inc(sm["cYNT"], 16)
        self.bar()
        for t in range(NTT):
            e.dma_start(out=self.x2r[:, :],
                        in_=apm(self.x2f_d.ap(), 128 * t * D, [[D, 128], [1, D]])
                        ).then_inc(sm["cXRl"], 16)
            for p in range(8):
                pi = 8 * t + p
                buf = self.ostg if p % 2 == 0 else self.prt
                if p >= 2:
                    e.wait_ge(sm["cXRc"], pi - 1)
                e.dma_start(out=buf[:, :],
                            in_=apm(self.o_d.ap(), (p * TOW + 128 * t) * D,
                                    [[D, 128], [1, D]])).then_inc(sm["cXRl"], 16)
            e.wait_ge(sm["cX3"], 2 * (t + 1))
            e.dma_start(out=apm(self.x3f_d.ap(), t * 128 * D, [[D, 128], [1, D]]),
                        in_=self.x3f[:, :]).then_inc(sm["cOUT"], 16)
            e.dma_start(out=apm(self.x3b_d.ap(), t * 128 * D, [[D, 128], [1, D]]),
                        in_=self.x3b[:, :]).then_inc(sm["cOUT"], 16)
            e.wait_ge(sm["cOUT"], 32 * (t + 1))
            self.bar()

    def em_gpsimd(self, e):
        self._bn = 0
        try:
            self._em_gpsimd(e)
        except _CutDone:
            pass

    def _em_gpsimd(self, e):
        sm, io = self.ctx.sems, self.io
        e.dma_start(out=self.nwb[:, :], in_=io["normw_rep"]).then_inc(sm["cPg"], 16)
        e.wait_ge(sm["cPg"], 16)
        self.bar()
        for hi in range(34):
            if hi >= 2:
                e.wait_ge(sm["cHL"], hi - 1)
            m = 32 + hi
            e.dma_start(out=self.win[hi % 2][:, :],
                        in_=apm(io["wip_t"], 128 * m,
                                [[8512, 128], [128 * 8512, 16], [1, 128]])
                        ).then_inc(sm["cWL"], 16)
        self.bar()
        e.collective_compute("AllGather", ALU.bypass, replica_groups=[list(range(8))],
                             ins=[self.hout_d.ap()], outs=[self.hg_d.ap()]
                             ).then_inc(sm["cHG"], 1)
        e.wait_ge(sm["cHG"], 1)
        self.bar()
        for mi, m in enumerate(MT_ORDER):
            li = 34 + mi
            if mi >= 2:
                e.wait_ge(sm["cEVp"], 2 * (mi - 1))
            if m < 66:
                e.dma_start(out=self.win[li % 2][:, :],
                            in_=apm(io["wip_t"], 128 * m,
                                    [[8512, 128], [128 * 8512, 16], [1, 128]])
                            ).then_inc(sm["cWL"], 16)
            else:
                e.dma_start(out=apm(self.win[li % 2][:, :], 0,
                                    [[2048, 128], [64, 16], [1, 64]]),
                            in_=apm(io["wip_t"], 128 * m,
                                    [[8512, 128], [128 * 8512, 16], [1, 64]])
                            ).then_inc(sm["cWL"], 16)
        for b in range(6 + NCH + 1):
            self.bar()
        e.collective_compute("AllGather", ALU.bypass, replica_groups=[list(range(8))],
                             ins=[self.sum_d.ap()], outs=[self.sumg_d.ap()]
                             ).then_inc(sm["cAG"], 1)
        e.collective_compute("AllGather", ALU.bypass, replica_groups=[list(range(8))],
                             ins=[self.lg_d.ap()], outs=[self.lgg_d.ap()]
                             ).then_inc(sm["cAG"], 1)
        e.wait_ge(sm["cAG"], 2)
        self.bar()
        self.bar()
        self.bar()
        for c in range(NCH):
            self.bar()
        for t in range(NTT):
            self.bar()
        for dseg in range(8):
            for j in range(4):
                di = 4 * dseg + j
                if dseg >= 1:
                    e.wait_ge(sm["cOM"], 32 * dseg)
                e.dma_start(out=self.wo[j][:, :],
                            in_=apm(io["wout_t"], 128 * di * D, [[D, 128], [1, D]])
                            ).then_inc(sm["cWO"], 16)
            for tt in range(NTT):
                it = dseg * NTT + tt
                e.wait_ge(sm["cOS"], 4 * (it + 1))
                e.dma_start(out=apm(self.o_d.ap(), (dseg * TOW + 128 * tt) * D,
                                    [[D, 128], [1, D]]),
                            in_=self.ostg[:, :]).then_inc(sm["cOA"], 16)
        self.bar()
        for t in range(NTT):
            self.bar()

    def em_tensor(self, e):
        self._bn = 0
        try:
            self._em_tensor(e)
        except _CutDone:
            pass

    def _em_tensor(self, e):
        sm = self.ctx.sems
        self.bar()
        e.matmul(self.p_x[:, 0:64], self.onesf[0:1, :], self.drow[0:1, :],
                 start=True, stop=True).then_inc(sm["cDE"], 1)
        for hi in range(34):
            e.wait_ge(sm["cWL"], 16 * (hi + 1))
            if hi >= 2:
                e.wait_ge(sm["cHL"], hi - 1)
            pb = self.p_a if hi % 2 == 0 else self.p_b
            for dt_ in range(16):
                mm = e.matmul(pb[:, 0:3], self.win[hi % 2][:, 128 * dt_:128 * dt_ + 128],
                              self.xT[:, 1024 * dt_ + 1021:1024 * dt_ + 1024],
                              start=(dt_ == 0), stop=(dt_ == 15))
            mm.then_inc(sm["cHM"], 1)
        self.bar()
        self.bar()
        for mi, m in enumerate(MT_ORDER):
            li = 34 + mi
            e.wait_ge(sm["cWL"], 16 * (li + 1))
            for nc2 in range(2):
                pidx = 2 * mi + nc2
                pb = self.p_a if pidx % 2 == 0 else self.p_b
                if pidx >= 2:
                    e.wait_ge(sm["cEVp"], pidx - 1)
                for dt_ in range(16):
                    if m < 66:
                        mm = e.matmul(pb[:, :], self.win[li % 2][:, 128 * dt_:128 * dt_ + 128],
                                      self.xT[:, 1024 * dt_ + 512 * nc2:1024 * dt_ + 512 * nc2 + 512],
                                      start=(dt_ == 0), stop=(dt_ == 15))
                    else:
                        mm = e.matmul(pb[0:64, :], self.win[li % 2][:, 64 * dt_:64 * dt_ + 64],
                                      self.xT[:, 1024 * dt_ + 512 * nc2:1024 * dt_ + 512 * nc2 + 512],
                                      start=(dt_ == 0), stop=(dt_ == 15))
                mm.then_inc(sm["cPJ"], 1)
            if 32 <= m < 64:
                sxi = m - 32
                e.wait_ge(sm["cSILa"], sxi + 3)
                for tt in range(8):
                    g = 8 * sxi + tt
                    if g >= 2:
                        e.wait_ge(sm["cTRe"], g - 1)
                    e.transpose(self.p_t[g % 2][:, 0:128], self.xssm[:, 128 * tt:128 * tt + 128],
                                self.identb[:, :]).then_inc(sm["cTR"], 1)
        self.bar()
        self.bar()
        self.bar()
        self.bar()
        self.bar()
        for i in range(16):
            if i >= 1:
                e.wait_ge(sm["cTFe"], i)
            e.transpose(self.p_x[0:64, 0:64], self.acs[1][:, 64 * i:64 * i + 64],
                        self.identf[0:64, 0:64]).then_inc(sm["cTF"], 1)
        for i in range(16):
            e.wait_ge(sm["cTFe"], 16 + i)
            e.transpose(self.p_x[0:64, 0:64], self.dt[:, 64 * i:64 * i + 64],
                        self.identf[0:64, 0:64]).then_inc(sm["cTF"], 1)
        for i in range(16):
            if i >= 2:
                e.wait_ge(sm["cTBe"], i - 1)
            e.transpose(self.p_t[i % 2][0:64, 0:64], self.dec[:, 64 * i:64 * i + 64],
                        self.identb[0:64, 0:64]).then_inc(sm["cTB"], 1)
        for i in range(16):
            e.wait_ge(sm["cTBe"], 15 + i)
            e.transpose(self.p_t[i % 2][0:64, 0:128], self.Bfm[:, 64 * i:64 * i + 64],
                        self.identb[:, :]).then_inc(sm["cTB"], 1)
        self.bar()
        for c in range(NCH):
            e.matmul(self.p_G[:, 0:64], self.Bfm[:, 64 * c:64 * c + 64],
                     self.Cfm[:, 64 * c:64 * c + 64], start=True,
                     stop=True).then_inc(sm["cGm"], 1)
            for q in range(4):
                aq = 4 * c + q
                e.wait_ge(sm["cAF"], 16 * (aq + 1))
                if q >= 1:
                    e.wait_ge(sm["cBAv"], 8 * c + 2 * q)
                e.matmul(self.p_a[0:64, :], self.onesf[0:1, 0:64],
                         self.afl[aq % 2][0:1, 0:512], start=True,
                         stop=True).then_inc(sm["cBAm"], 1)
                e.matmul(self.p_b[0:64, :], self.onesf[0:1, 0:64],
                         self.afl[aq % 2][0:1, 512:1024], start=True,
                         stop=True).then_inc(sm["cBAm"], 1)
            e.wait_ge(sm["cCD"], 16 * (c + 1))
            if c >= 1:
                e.wait_ge(sm["cCDE"], c)
            e.matmul(self.p_x[:, 0:64], self.onesf[0:1, :], self.cdrow[0:1, :],
                     start=True, stop=True).then_inc(sm["cCDm"], 1)
            e.wait_ge(sm["cBAv"], 8 * (c + 1))
            for q in range(4):
                aq = 4 * c + q
                e.wait_ge(sm["cEF"], 16 * (aq + 1))
                if q >= 1:
                    e.wait_ge(sm["cCSCv"], 8 * c + 2 * q)
                e.matmul(self.p_a[:, :], self.onesf[0:1, :],
                         self.afl2[aq % 2][0:1, 0:512], start=True,
                         stop=True).then_inc(sm["cEFm"], 1)
                e.matmul(self.p_b[:, :], self.onesf[0:1, :],
                         self.afl2[aq % 2][0:1, 512:1024], start=True,
                         stop=True).then_inc(sm["cEFm"], 1)
            for hg in range(8):
                ih = 8 * c + hg
                e.wait_ge(sm["cBD"], ih + 1)
                e.wait_ge(sm["cXDT"], c + 1)
                if hg >= 1:
                    e.wait_ge(sm["cRU"], ih)
                for hl in range(8):
                    h = 8 * hg + hl
                    mm = e.matmul(self.p_S[:, 64 * hl:64 * hl + 64],
                                  self.bdec[:, 128 * hl:128 * hl + 128],
                                  self.xdt[:, 64 * h:64 * h + 64],
                                  start=True, stop=True)
                mm.then_inc(sm["cST"], 1)
            for hg in range(8):
                ih = 8 * c + hg
                e.wait_ge(sm["cMv"], 4 * (c + 1))
                e.wait_ge(sm["cCSCv"], 8 * (c + 1))
                e.wait_ge(sm["cRBv"], c + 1)
                e.wait_ge(sm["cGV"], c + 1)
                if hg >= 1:
                    e.wait_ge(sm["cYE"], ih)
                for hl in range(8):
                    h = 8 * hg + hl
                    e.matmul(self.p_G[:, 64 * hl:64 * hl + 64],
                             self.M[:, 64 * h:64 * h + 64],
                             self.xdt[:, 64 * h:64 * h + 64], start=True, stop=False)
                    mm = e.matmul(self.p_G[:, 64 * hl:64 * hl + 64],
                                  self.csc[:, 64 * h:64 * h + 64],
                                  self.runbf[:, 64 * h:64 * h + 64], start=False, stop=True)
                mm.then_inc(sm["cYM"], 1)
            self.bar()
        self.bar()
        self.bar()
        e.wait_ge(sm["cLG"], 16)
        e.matmul(self.p_x[0:64, 0:8], self.lgp[:, :], self.mBt[:, :],
                 start=True, stop=True).then_inc(sm["cWC"], 1)
        self.bar()
        for p in range(8):
            e.wait_ge(sm["cWR"], 16 * (p + 1))
            if p >= 1:
                e.wait_ge(sm["cCW"], p)
            e.matmul(self.p_x[:, 0:64], self.onesf[0:1, :], self.wrow[0:1, :],
                     start=True, stop=True).then_inc(sm["cWXm"], 1)
        self.bar()
        for c in range(NCH):
            e.wait_ge(sm["cPD"], 16 * (c + 1))
            if c >= 1:
                e.wait_ge(sm["cPDE"], c)
            e.matmul(self.p_x[:, 0:64], self.onesf[0:1, :], self.pdrow[0:1, :],
                     start=True, stop=True).then_inc(sm["cPDm"], 1)
            for q in range(4):
                aq = 4 * c + q
                e.wait_ge(sm["cEF2"], 16 * (aq + 1))
                if q >= 1:
                    e.wait_ge(sm["cCSC2v"], 8 * c + 2 * q)
                e.matmul(self.p_a[:, :], self.onesf[0:1, :],
                         self.afl2[aq % 2][0:1, 0:512], start=True,
                         stop=True).then_inc(sm["cEF2m"], 1)
                e.matmul(self.p_b[:, :], self.onesf[0:1, :],
                         self.afl2[aq % 2][0:1, 512:1024], start=True,
                         stop=True).then_inc(sm["cEF2m"], 1)
            for hg in range(8):
                ih = 8 * c + hg
                e.wait_ge(sm["cCSC2v"], 8 * (c + 1))
                e.wait_ge(sm["cCSCc"], c + 1)
                e.wait_ge(sm["cYB2"], 16 * (c + 1))
                if hg >= 1:
                    e.wait_ge(sm["cYE2"], ih)
                for hl in range(8):
                    h = 8 * hg + hl
                    mm = e.matmul(self.p_G[:, 64 * hl:64 * hl + 64],
                                  self.csc[:, 64 * h:64 * h + 64],
                                  self.runbf[:, 64 * h:64 * h + 64],
                                  start=True, stop=True)
                mm.then_inc(sm["cYM2"], 1)
            self.bar()
        for t in range(NTT):
            self.bar()
        for dseg in range(8):
            e.wait_ge(sm["cWO"], 64 * (dseg + 1))
            e.wait_ge(sm["cYNT"], 64 * (dseg + 1))
            for tt in range(NTT):
                it = dseg * NTT + tt
                for q4 in range(4):
                    pq = 4 * it + q4
                    po = self.p_o if q4 % 2 == 0 else self.p_S
                    if pq >= 2:
                        e.wait_ge(sm["cOS"], pq - 1)
                    for j in range(4):
                        mm = e.matmul(po[:, 0:512],
                                      self.ynT[j][:, 128 * tt:128 * tt + 128],
                                      self.wo[j][:, 512 * q4:512 * q4 + 512],
                                      start=(j == 0), stop=(j == 3))
                    mm.then_inc(sm["cOM"], 1)
        self.bar()
        for t in range(NTT):
            self.bar()

    def em_vector(self, e):
        self._bn = 0
        try:
            self._em_vector(e)
        except _CutDone:
            pass

    def _em_vector(self, e):
        sm = self.ctx.sems
        self._vsn = 0

        def W(ins):
            self._vsn += 1
            ins.then_inc(sm["cVS"], 1)
            e.wait_ge(sm["cVS"], self._vsn)
        e.memset(self.ln[:, 6:7], 1e-5)
        self.bar()
        e.wait_ge(sm["cDE"], 1)
        W(e.tensor_copy(self.De[:, :], self.p_x[:, 0:64]))
        e.memset(self.run[:, :], 0.0)
        self.bar()
        self.bar()
        e.wait_ge(sm["cHS"], 128)
        W(e.tensor_scalar(self.hsel[:, :], self.hgb[:, 0:102], self.selw[:, 0:1],
                          None, ALU.mult))
        for p in range(1, 8):
            W(e.scalar_tensor_tensor(self.hsel[:, :], self.hgb[:, 102 * p:102 * p + 102],
                                     self.selw[:, p:p + 1], self.hsel[:, :],
                                     ALU.mult, ALU.add))
        xbi = 0
        for mi, m in enumerate(MT_ORDER):
            if not (32 <= m < 66):
                continue
            ct = (m - 32) if m < 64 else (32 if m == 64 else 33)
            e.wait_ge(sm["cEVp"], 2 * (mi + 1))
            if xbi >= 1:
                e.wait_ge(sm["cSILa"], xbi)
            W(e.tensor_copy(self.xbc[mi % 2][:, 1:4], self.hsel[:, 3 * ct:3 * ct + 3]))
            b = self.xbc[mi % 2]
            e.tensor_scalar(self.cacc[:, :], b[:, 1:1025], self.convw[:, 4 * ct:4 * ct + 1],
                            None, ALU.mult)
            for k in range(1, 4):
                ins = e.scalar_tensor_tensor(self.cacc[:, :], b[:, 1 + k:1025 + k],
                                             self.convw[:, 4 * ct + k:4 * ct + k + 1],
                                             self.cacc[:, :], ALU.mult, ALU.add)
            ins.then_inc(sm["cCVt"], 1)
            xbi += 1
        self.bar()
        e.tensor_scalar(self.dtraw[:, :], self.dtraw[:, :], self.acol[:, 1:2],
                        None, ALU.add)
        self.bar()
        self.bar()
        e.tensor_scalar_max(self.dt[:, :], self.dtraw[:, :], 0.0)
        e.tensor_add(self.dt[:, :], self.dt[:, :], self.acs[0][:, :])
        e.tensor_scalar(self.acs[1][:, :], self.dt[:, :], self.acol[:, 2:3], -1.0,
                        ALU.mult, ALU.mult)
        srcb, dstb = 1, 0
        for k in [1, 2, 4, 8, 16, 32]:
            for c in range(NCH):
                c0 = 64 * c
                e.tensor_copy(self.acs[dstb][:, c0:c0 + k], self.acs[srcb][:, c0:c0 + k])
                ins = e.tensor_add(self.acs[dstb][:, c0 + k:c0 + 64],
                                   self.acs[srcb][:, c0 + k:c0 + 64],
                                   self.acs[srcb][:, c0:c0 + 64 - k])
            W(ins)
            srcb, dstb = dstb, srcb
        W(e.memset(self.plog[:, 0:1], 0.0))
        for c in range(NCH):
            W(e.tensor_add(self.plog[:, c + 1:c + 2], self.plog[:, c:c + 1],
                           self.acs[1][:, 64 * c + 63:64 * c + 64]))
        lastap = apm(self.acs[1][:, :], 63, [[1024, 64], [64, 16]])
        W(e.reduce_sum(self.lgcol[:, :], lastap, AX))
        bc = apm(self.acs[1][:, :], 63, [[1024, 64], [64, 16], [0, 64]])
        e.tensor_tensor(self.dtraw[:, :], bc, self.acs[1][:, :],
                        ALU.subtract).then_inc(sm["cACSv"], 1)
        self.bar()
        self.bar()
        self.bar()
        for c in range(NCH):
            e.tensor_copy(self.runbf[:, :], self.run[:, :]).then_inc(sm["cRBv"], 1)
            e.wait_ge(sm["cGm"], c + 1)
            e.tensor_mul(self.G[:, :], self.p_G[0:64, 0:64], self.trilU[:, :]
                         ).then_inc(sm["cGV"], 1)
            e.wait_ge(sm["cGV"], c + 1)
            e.wait_ge(sm["cXC"], 16 * (c + 1))
            dtap = dataclasses.replace(self.dtT[:, :], offset=self.dtT[:, :].offset + 64 * c,
                                       ap=[[1024, 64], [1, 64], [0, 64]])
            e.tensor_tensor(self.xdt[:, :], self.xch[:, :], dtap,
                            ALU.mult).then_inc(sm["cXDT"], 1)
            for q in range(4):
                aq = 4 * c + q
                e.wait_ge(sm["cBAm"], 2 * (aq + 1))
                for piece, pb in [(0, self.p_a), (1, self.p_b)]:
                    hoff = 16 * q + 8 * piece
                    aap = dataclasses.replace(self.acsT[:, :],
                                              offset=self.acsT[:, :].offset + 64 * c + hoff,
                                              ap=[[1024, 64], [1, 8], [0, 64]])
                    ins = e.tensor_tensor(self.argf[:, 512 * piece:512 * piece + 512],
                                          pb[0:64, :], aap, ALU.subtract)
                    ins.then_inc(sm["cBAv"], 1)
                e.tensor_scalar_min(self.argf[:, :], self.argf[:, :],
                                    0.0).then_inc(sm["cARGv"], 1)
                e.wait_ge(sm["cMe"], aq + 1)
                gap = dataclasses.replace(self.G[:, :], ap=[[64, 64], [0, 16], [1, 64]])
                e.tensor_mul(self.M[:, 1024 * q:1024 * q + 1024],
                             self.M[:, 1024 * q:1024 * q + 1024], gap
                             ).then_inc(sm["cMv"], 1)
            e.wait_ge(sm["cCDm"], c + 1)
            e.tensor_copy(self.cde[:, :], self.p_x[:, 0:64]).then_inc(sm["cCDE"], 1)
            e.wait_ge(sm["cCDE"], c + 1)
            for q in range(4):
                aq = 4 * c + q
                e.wait_ge(sm["cEFm"], 2 * (aq + 1))
                cap = dataclasses.replace(self.Cfm[:, 64 * c:64 * c + 64],
                                          ap=[[1024, 128], [0, 8], [1, 64]])
                for piece, pb in [(0, self.p_a), (1, self.p_b)]:
                    ins = e.tensor_tensor(self.csc[:, 1024 * q + 512 * piece:1024 * q + 512 * piece + 512],
                                          cap, pb[:, :], ALU.mult)
                    ins.then_inc(sm["cCSCv"], 1)
            for hg in range(8):
                ih = 8 * c + hg
                btap = dataclasses.replace(self.btok[:, :],
                                           offset=self.btok[:, :].offset + 128 * c,
                                           ap=[[2048, 64], [0, 8], [1, 128]])
                dcap = dataclasses.replace(self.decT[:, :],
                                           offset=self.decT[:, :].offset + 64 * c + 8 * hg,
                                           ap=[[1024, 64], [1, 8], [0, 128]])
                e.tensor_tensor(self.bdec[:, :], btap, dcap, ALU.mult).then_inc(sm["cBD"], 1)
            for hg in range(8):
                ih = 8 * c + hg
                e.wait_ge(sm["cST"], ih + 1)
                cdap = dataclasses.replace(self.cde[:, :], offset=self.cde[:, :].offset + 8 * hg,
                                           ap=[[64, 128], [1, 8], [0, 64]])
                e.tensor_tensor(self.run[:, 512 * hg:512 * hg + 512],
                                self.run[:, 512 * hg:512 * hg + 512], cdap, ALU.mult)
                e.tensor_add(self.run[:, 512 * hg:512 * hg + 512],
                             self.run[:, 512 * hg:512 * hg + 512],
                             self.p_S[:, :]).then_inc(sm["cRU"], 1)
            for hg in range(8):
                ih = 8 * c + hg
                e.wait_ge(sm["cYM"], ih + 1)
                dap = dataclasses.replace(self.De[:, :], offset=self.De[:, :].offset + 8 * hg,
                                          ap=[[64, 64], [1, 8], [0, 64]])
                e.tensor_tensor(self.ybuf[:, 512 * hg:512 * hg + 512],
                                self.xch[:, 512 * hg:512 * hg + 512], dap, ALU.mult)
                e.tensor_add(self.ybuf[:, 512 * hg:512 * hg + 512],
                             self.ybuf[:, 512 * hg:512 * hg + 512],
                             self.p_G[:, :]).then_inc(sm["cYE"], 1)
            self.bar()
        self.bar()
        self.bar()
        e.wait_ge(sm["cWC"], 1)
        W(e.tensor_copy(self.wT[:, :], self.p_x[0:64, 0:8]))
        e.tensor_add(self.wT[:, :], self.wT[:, :], self.mbias[:, :]).then_inc(sm["cWE"], 1)
        self.bar()
        e.memset(self.run[:, :], 0.0)
        for p in range(8):
            e.wait_ge(sm["cWXm"], p + 1)
            e.tensor_copy(self.wexp[:, :], self.p_x[:, 0:64]).then_inc(sm["cCW"], 1)
            e.wait_ge(sm["cCW"], p + 1)
            for q in range(4):
                gi = 4 * p + q
                e.wait_ge(sm["cGC"], 16 * (gi + 1))
                wap = dataclasses.replace(self.wexp[:, :],
                                          offset=self.wexp[:, :].offset + 16 * q,
                                          ap=[[64, 128], [1, 16], [0, 64]])
                e.tensor_mul(self.gch[:, :], self.gch[:, :], wap)
                e.tensor_add(self.run[:, 1024 * q:1024 * q + 1024],
                             self.run[:, 1024 * q:1024 * q + 1024],
                             self.gch[:, :]).then_inc(sm["cGA"], 1)
        self.bar()
        for c in range(NCH):
            e.wait_ge(sm["cPDm"], c + 1)
            e.tensor_copy(self.cde[:, :], self.p_x[:, 0:64]).then_inc(sm["cPDE"], 1)
            e.wait_ge(sm["cPDE"], c + 1)
            cdap = dataclasses.replace(self.cde[:, :], ap=[[64, 128], [1, 64], [0, 64]])
            e.tensor_tensor(self.runbf[:, :], self.run[:, :], cdap,
                            ALU.mult).then_inc(sm["cCSCc"], 1)
            for q in range(4):
                aq = 4 * c + q
                e.wait_ge(sm["cEF2m"], 2 * (aq + 1))
                cap = dataclasses.replace(self.Cfm[:, 64 * c:64 * c + 64],
                                          ap=[[1024, 128], [0, 8], [1, 64]])
                for piece, pb in [(0, self.p_a), (1, self.p_b)]:
                    ins = e.tensor_tensor(self.csc[:, 1024 * q + 512 * piece:1024 * q + 512 * piece + 512],
                                          cap, pb[:, :], ALU.mult)
                    ins.then_inc(sm["cCSC2v"], 1)
            for hg in range(8):
                ih = 8 * c + hg
                e.wait_ge(sm["cYM2"], ih + 1)
                e.tensor_add(self.ybuf[:, 512 * hg:512 * hg + 512],
                             self.ybuf[:, 512 * hg:512 * hg + 512],
                             self.p_G[:, :]).then_inc(sm["cYE2"], 1)
            self.bar()
        for t in range(NTT):
            e.wait_ge(sm["cP5a"], t + 1)
            e.tensor_mul(self.yt[:, :], self.yt[:, :], self.zsil[:, :])
            e.tensor_mul(self.ynb[:, :], self.yt[:, :], self.yt[:, :])
            e.reduce_sum(self.ln[:, 2:3], self.ynb[:, :], AX).then_inc(sm["cP5b"], 1)
            e.wait_ge(sm["cLNS"], t + 1)
            W(e.reciprocal(self.ln[:, 4:5], self.ln[:, 3:4]))
            e.scalar_tensor_tensor(self.ynb[:, :], self.yt[:, :], self.ln[:, 4:5],
                                   self.nwb[:, :], ALU.mult,
                                   ALU.mult).then_inc(sm["cYN"], 1)
            self.bar()
        for dseg in range(8):
            for tt in range(NTT):
                it = dseg * NTT + tt
                for q4 in range(4):
                    pq = 4 * it + q4
                    po = self.p_o if q4 % 2 == 0 else self.p_S
                    e.wait_ge(sm["cOM"], pq + 1)
                    if it >= 1 and q4 == 0:
                        e.wait_ge(sm["cOA"], 16 * it)
                    e.tensor_copy(self.ostg[:, 512 * q4:512 * q4 + 512],
                                  po[:, 0:512]).then_inc(sm["cOS"], 1)
        self.bar()
        for t in range(NTT):
            dx = self.x2r[:, :]
            for p in range(8):
                pi = 8 * t + p
                buf = self.ostg if p % 2 == 0 else self.prt
                e.wait_ge(sm["cXRl"], 16 * (9 * t + p + 2))
                e.tensor_add(dx, dx, buf[:, :]).then_inc(sm["cXRc"], 1)
            W(e.reduce_sum(self.ln[:, 0:1], dx, AX))
            W(e.tensor_scalar_mul(self.ln[:, 1:2], self.ln[:, 0:1], 1.0 / D))
            e.tensor_scalar(dx, dx, self.ln[:, 1:2], None, ALU.subtract)
            e.tensor_mul(self.x3b[:, :], dx, dx)
            e.reduce_sum(self.ln[:, 2:3], self.x3b[:, :], AX).then_inc(sm["cP5b"], 1)
            e.wait_ge(sm["cLNS"], NTT + t + 1)
            W(e.reciprocal(self.ln[:, 4:5], self.ln[:, 3:4]))
            e.tensor_scalar(self.x3f[:, :], dx, self.ln[:, 4:5], None,
                            ALU.mult).then_inc(sm["cX3"], 1)
            e.tensor_scalar(self.x3b[:, :], dx, self.ln[:, 4:5], None,
                            ALU.mult).then_inc(sm["cX3"], 1)
            self.bar()

    def em_scalar(self, e):
        self._bn = 0
        try:
            self._em_scalar(e)
        except _CutDone:
            pass

    def _em_scalar(self, e):
        sm = self.ctx.sems
        self.bar()
        for hi in range(34):
            e.wait_ge(sm["cHM"], hi + 1)
            e.copy(self.hout[:, 3 * hi:3 * hi + 3],
                   (self.p_a if hi % 2 == 0 else self.p_b)[:, 0:3]).then_inc(sm["cHL"], 1)
        self.bar()
        self.bar()
        xbi = 0
        sxi = 0
        zi = 0
        for mi, m in enumerate(MT_ORDER):
            for nc2 in range(2):
                pidx = 2 * mi + nc2
                pb = self.p_a if pidx % 2 == 0 else self.p_b
                e.wait_ge(sm["cPJ"], pidx + 1)
                if m == 66:
                    e.copy(self.dtraw[:, 512 * nc2:512 * nc2 + 512],
                           pb[0:64, :]).then_inc(sm["cEVp"], 1)
                elif m >= 32:
                    if xbi >= 2 and nc2 == 0:
                        e.wait_ge(sm["cCVt"], xbi - 1)
                    e.copy(self.xbc[mi % 2][:, 4 + 512 * nc2:4 + 512 * nc2 + 512],
                           pb[:, :]).then_inc(sm["cEVp"], 1)
                else:
                    if zi >= 2 and nc2 == 0:
                        e.wait_ge(sm["cZDd"], 16 * (zi - 1))
                    e.copy(self.zstg[mi % 2][:, 512 * nc2:512 * nc2 + 512],
                           pb[:, :]).then_inc(sm["cEVp"], 1)
            if 32 <= m < 66:
                ct = (m - 32) if m < 64 else (32 if m == 64 else 33)
                e.wait_ge(sm["cCVt"], xbi + 1)
                if m < 64:
                    if sxi >= 1:
                        e.wait_ge(sm["cTR"], 8 * sxi)
                    dst = self.xssm[:, :]
                elif m == 64:
                    dst = self.Bfm[:, :]
                else:
                    dst = self.Cfm[:, :]
                e.activation(dst, self.cacc[:, :], AF.Silu,
                             bias=self.convb[:, ct:ct + 1]).then_inc(sm["cSILa"], 1)
                if m < 64:
                    for tt in range(8):
                        g = 8 * sxi + tt
                        e.wait_ge(sm["cTR"], g + 1)
                        e.copy(self.xstg[:, 128 * tt:128 * tt + 128],
                               self.p_t[g % 2][:, 0:128]).then_inc(sm["cTRe"], 1)
                    sxi += 1
                xbi += 1
            if m < 32:
                zi += 1
        self.bar()
        self.bar()
        e.activation(self.acs[1][:, :], self.dtraw[:, :], AF.Abs)
        e.activation(self.EA[:, :], self.acs[1][:, :], AF.Exp, scale=-1.0)
        e.activation(self.acs[0][:, :], self.EA[:, :], AF.Ln,
                     bias=self.acol[:, 3:4])
        e.activation(self.acol[:, 2:3], self.acol[:, 0:1], AF.Exp)
        self.bar()
        self.bar()
        e.activation(self.EA[:, :], self.acs[1][:, :], AF.Exp).then_inc(sm["cEAs"], 1)
        e.activation(self.dec[:, :], self.dtraw[:, :], AF.Exp)
        e.activation(self.pde[:, :], self.plog[:, 0:16], AF.Exp).then_inc(sm["cEAs"], 1)
        self.bar()
        for i in range(16):
            e.wait_ge(sm["cTF"], i + 1)
            e.copy(self.acsT[:, 64 * i:64 * i + 64], self.p_x[0:64, 0:64]).then_inc(sm["cTFe"], 1)
        for i in range(16):
            e.wait_ge(sm["cTF"], 16 + i + 1)
            e.copy(self.dtT[:, 64 * i:64 * i + 64], self.p_x[0:64, 0:64]).then_inc(sm["cTFe"], 1)
        for i in range(16):
            e.wait_ge(sm["cTB"], i + 1)
            e.copy(self.decT[:, 64 * i:64 * i + 64], self.p_t[i % 2][0:64, 0:64]).then_inc(sm["cTBe"], 1)
        for i in range(16):
            e.wait_ge(sm["cTB"], 16 + i + 1)
            e.copy(self.btok[:, 128 * i:128 * i + 128],
                   self.p_t[i % 2][0:64, 0:128]).then_inc(sm["cTBe"], 1)
        self.bar()
        for c in range(NCH):
            for q in range(4):
                aq = 4 * c + q
                e.wait_ge(sm["cARGv"], aq + 1)
                e.activation(self.M[:, 1024 * q:1024 * q + 1024], self.argf[:, :],
                             AF.Exp).then_inc(sm["cMe"], 1)
            self.bar()
        self.bar()
        self.bar()
        e.wait_ge(sm["cWE"], 1)
        e.activation(self.wT[:, :], self.wT[:, :], AF.Exp).then_inc(sm["cWE"], 1)
        self.bar()
        self.bar()
        for c in range(NCH):
            self.bar()
        for t in range(NTT):
            e.wait_ge(sm["cZT"], 32 * (t + 1))
            e.activation(self.zsil[:, :], self.zt[:, :], AF.Silu).then_inc(sm["cP5a"], 1)
            e.wait_ge(sm["cP5b"], t + 1)
            e.activation(self.ln[:, 3:4], self.ln[:, 2:3], AF.Sqrt,
                         scale=1.0 / 4096.0, bias=self.ln[:, 6:7]).then_inc(sm["cLNS"], 1)
            self.bar()
        self.bar()
        for t in range(NTT):
            e.wait_ge(sm["cP5b"], NTT + t + 1)
            e.activation(self.ln[:, 3:4], self.ln[:, 2:3], AF.Sqrt,
                         scale=1.0 / D, bias=self.ln[:, 6:7]).then_inc(sm["cLNS"], 1)
            self.bar()


def _hostprep_C(inputs, core):
    wip = np.asarray(inputs["L1_attn_in_proj_w"], np.float32)
    wip_t = np.ascontiguousarray(wip.T)
    wout = np.asarray(inputs["L1_attn_out_proj_w"], np.float32)
    wout_t = np.ascontiguousarray(wout.T)
    cw = np.asarray(inputs["L1_attn_conv_w"], np.float32).reshape(4352, 4)
    convw = np.ascontiguousarray(cw.reshape(34, 128, 4).transpose(1, 0, 2).reshape(128, 136))
    cb = np.asarray(inputs["L1_attn_conv_b"], np.float32)
    convb = np.ascontiguousarray(cb.reshape(34, 128).T)
    acolin = np.zeros((64, 4), np.float32)
    acolin[:, 0] = np.asarray(inputs["L1_attn_A_log"], np.float32)
    acolin[:, 1] = np.asarray(inputs["L1_attn_dt_bias"], np.float32)
    acolin[:, 3] = 1.0
    drow = np.asarray(inputs["L1_attn_D"], np.float32).reshape(1, 64)
    normw_rep = np.ascontiguousarray(
        np.broadcast_to(np.asarray(inputs["L1_attn_norm_w"], np.float32)[None, :], (128, 4096)))
    trilU = np.triu(np.ones((64, 64), np.float32)).astype(BF)
    beta, sig = core // 4, core % 4
    mBt = np.zeros((8, 8), np.float32)
    mbias = np.full((64, 8), -1e30, np.float32)
    for p in range(8):
        pb_, ps = p // 4, p % 4
        if pb_ == beta and ps < sig:
            mbias[:, p] = 0.0
            for qq in range(8):
                qb, qs = qq // 4, qq % 4
                if qb == beta and ps < qs < sig:
                    mBt[qq, p] = 1.0
    selw = np.zeros((128, 8), np.float32)
    if sig > 0:
        selw[:, core - 1] = 1.0
    return dict(wip_t=wip_t, wout_t=wout_t, convw=convw, convb=convb,
                acolin=acolin, drow=drow, normw_rep=normw_rep,
                trilU=np.asarray(trilU), mBt=mBt, mbias=mbias, selw=selw,
                onesf=np.ones((1, 128), np.float32),
                identf=np.eye(128, dtype=np.float32),
                identb=np.eye(128, dtype=BF))


def _declare_C(nc):
    io = {}
    decl = [("wip_t", [D, 8512], F32), ("wout_t", [4096, D], F32),
            ("convw", [128, 136], F32), ("convb", [128, 34], F32),
            ("acolin", [64, 4], F32), ("drow", [1, 64], F32),
            ("normw_rep", [128, 4096], F32), ("trilU", [64, 64], BF16),
            ("mBt", [8, 8], F32), ("mbias", [64, 8], F32), ("selw", [128, 8], F32),
            ("onesf", [1, 128], F32), ("identf", [128, 128], F32),
            ("identb", [128, 128], BF16)]
    for n, sh, dt in decl:
        io[n] = nc.dram_tensor(n, sh, dt, kind="ExternalInput").ap()
    return io


def build_program_C():
    nc = bass.Bass("TRN2", target_bir_lowering=False)
    stack = ExitStack()
    ctx = Ctx(nc, stack)
    io = _declare_C(nc)
    ctx.dram["x2f_d"] = nc.dram_tensor("x2f_d", [TOW, D], F32, kind="ExternalInput")
    ctx.dram["x2b_d"] = nc.dram_tensor("x2b_d", [TOW, D], BF16, kind="ExternalInput")
    with stack:
        ph = PhaseC(ctx, io, out_kind="ExternalOutput")
        with nc.Block() as block:
            @block.sync
            def _(e):
                ph.em_sync(e)

            @block.gpsimd
            def _(e):
                ph.em_gpsimd(e)

            @block.tensor
            def _(e):
                ph.em_tensor(e)

            @block.vector
            def _(e):
                ph.em_vector(e)

            @block.scalar
            def _(e):
                ph.em_scalar(e)
    return nc



DSEG = 8
DETS = 8


class PhaseD:

    def __init__(self, ctx: Ctx, io, out_kind="Internal"):
        self.ctx = ctx
        self.io = io
        c = ctx
        self.xT = c.sbuf("D_xT", [128, 16 * 1024], BF16)
        self.win = [c.sbuf(f"D_win{i}", [128, 2048], BF16) for i in range(2)]
        self.tg = [c.sbuf(f"D_tg{i}", [128, 1024], F32) for i in range(2)]
        self.silu = c.sbuf("D_silu", [128, 1024], BF16)
        self.h8 = c.sbuf("D_h8", [128, DETS * 1024], BF16)
        self.wdn = [c.sbuf(f"D_wdn{i}", [128, 2048], BF16) for i in range(DETS)]
        self.dacc = c.sbuf("D_dacc", [128, NTT * 2048], F32)
        self.rsx = c.sbuf("D_rsx", [128, 1024], F32)
        self.rsrow = c.sbuf("D_rsrow", [1, 1024], F32)
        self.rstg = c.sbuf("D_rstg", [8, 128], F32)
        self.rcol = c.sbuf("D_rcol", [128, 8], F32)
        self.hs = c.sbuf("D_hs", [128, 8], F32)
        self.dsc = c.sbuf("D_dsc", [128, 2048], F32)
        self.gsc = c.sbuf("D_gsc", [128, 64], F32)
        self.usc = c.sbuf("D_usc", [128, 64], F32)
        self.x3res = c.sbuf("D_x3res", [128, 2048], F32)
        self.x4f = c.sbuf("D_x4f", [128, 2048], F32)
        self.x4b = c.sbuf("D_x4b", [128, 2048], BF16)
        self.negf = c.sbuf("D_negf", [1, 128], F32)
        self.onescb = c.sbuf("D_onescb", [128, 1], BF16)
        self.identf = c.sbuf("D_identf", [128, 128], F32)
        self.ln = c.sbuf("D_ln", [128, 8], F32)
        self.p_up = c.psum("D_pup", [128, 1024], F32)
        self.p_g = c.psum("D_pg", [128, 1024], F32)
        self.p_dn = c.psum("D_pdn", [128, 1024], F32)
        self.p_hs = c.psum("D_phs", [128, 512], F32)
        self.rs_d = c.dscr("D_rs_d", [1, 1024], F32)
        self.x4f_d = c.dscr("x4f_d", [TOW, D], F32, kind=out_kind)
        self.x3f_d = c.dram["x3f_d"] if "x3f_d" in c.dram else c.dscr("x3f_d", [TOW, D], F32)
        self.x3b_d = c.dram["x3b_d"] if "x3b_d" in c.dram else c.dscr("x3b_d", [TOW, D], BF16)
        c.sem("dP", "dONE", "dXT", "dRS", "dRSt", "dRSv", "dRSd", "dRSx", "dWL",
              "dMM", "dSIL", "dH", "dTG", "dLD", "dDN", "dFL", "dHS", "dX3R",
              "dX3c", "dHSe", "dLNV", "dLNS", "dX4", "dOUT", "dVS")

    def em_sync(self, e):
        sm, io = self.ctx.sems, self.io
        for name, dst in [("onescb_h", None), ("negf", self.negf),
                          ("identf", self.identf), ("dsc_rep", self.dsc),
                          ("gsc_t", self.gsc), ("usc_t", self.usc)]:
            if dst is None:
                continue
            e.dma_start(out=dst[:, :], in_=io[name]).then_inc(sm["dP"], 16)
        for dt_ in range(16):
            e.dma_start(out=self.xT[:, 1024 * dt_:1024 * dt_ + 1024],
                        in_=apm(self.x3b_d.ap(), 128 * dt_, [[D, 1024], [1, 128]]),
                        transpose=True).then_inc(sm["dXT"], 16)
        for t in range(NTT):
            if t >= 1:
                e.wait_ge(sm["dRS"], t)
            e.dma_start(out=self.x3res[:, :],
                        in_=apm(self.x3f_d.ap(), t * 128 * D, [[D, 128], [1, D]])
                        ).then_inc(sm["dX3R"], 16)
        e.wait_ge(sm["dRSv"], 1)
        e.dma_start(out=apm(self.rs_d.ap(), 0, [[128, 8], [1, 128]]),
                    in_=self.rstg[:, :]).then_inc(sm["dRSd"], 16)
        e.wait_ge(sm["dRSd"], 16)
        e.dma_start(out=self.rsrow[:, :],
                    in_=apm(self.rs_d.ap(), 0, [[1024, 1], [1, 1024]])
                    ).then_inc(sm["dRSd"], 16)
        for t in range(NTT):
            if t == 0:
                e.wait_ge(sm["dRS"], NTT)
            else:
                e.wait_ge(sm["dX3c"], t)
            e.dma_start(out=self.x3res[:, :],
                        in_=apm(self.x3f_d.ap(), t * 128 * D, [[D, 128], [1, D]])
                        ).then_inc(sm["dX3R"], 16)
            e.wait_ge(sm["dX4"], t + 1)
            e.dma_start(out=apm(self.x4f_d.ap(), t * 128 * D, [[D, 128], [1, D]]),
                        in_=self.x4f[:, :]).then_inc(sm["dOUT"], 16)
        if DBG_D:
            e.dma_start(out=self.ctx.dram["dbgD_rsx"].ap(), in_=self.rsx[:, :]).then_inc(sm["dOUT"], 16)
            e.dma_start(out=self.ctx.dram["dbgD_hs"].ap(), in_=self.hs[:, :]).then_inc(sm["dOUT"], 16)
            e.dma_start(out=self.ctx.dram["dbgD_h8"].ap(), in_=self.h8[:, :]).then_inc(sm["dOUT"], 16)
            e.wait_ge(sm["dOUT"], 16 * NTT + 48)
        else:
            e.wait_ge(sm["dOUT"], 16 * NTT)

    def em_gpsimd(self, e):
        sm, io = self.ctx.sems, self.io
        e.dma_start(out=self.onescb[:, :], in_=io["onescb_h"]).then_inc(sm["dONE"], 16)
        for seg in range(DSEG):
            for etl in range(DETS):
                etg = seg * DETS + etl
                for w, nm in enumerate(["uw_t", "gw_t"]):
                    li = 2 * etg + w
                    if li >= 2:
                        e.wait_ge(sm["dMM"], li - 1)
                    e.dma_start(out=self.win[li % 2][:, :],
                                in_=apm(io[nm], 128 * etg,
                                        [[DE, 128], [128 * DE, 16], [1, 128]])
                                ).then_inc(sm["dWL"], 16)
            for etl in range(DETS):
                eg = seg * DETS + etl
                if seg >= 1:
                    e.wait_ge(sm["dDN"], 16 * seg)
                e.dma_start(out=self.wdn[etl][:, :],
                            in_=apm(io["dw_t"], 128 * eg * D, [[D, 128], [1, D]])
                            ).then_inc(sm["dLD"], 16)

    def em_tensor(self, e):
        sm = self.ctx.sems
        e.wait_ge(sm["dP"], 16 * 5)
        e.wait_ge(sm["dRS"], NTT)
        e.transpose(self.p_hs[0:8, 0:128], self.rcol[:, 0:8],
                    self.identf[:, :]).then_inc(sm["dRSt"], 1)
        e.wait_ge(sm["dRSd"], 32)
        for ch in range(2):
            e.matmul(self.p_up[:, 512 * ch:512 * ch + 512], self.negf[0:1, 0:128],
                     self.rsrow[0:1, 512 * ch:512 * ch + 512],
                     start=True, stop=True).then_inc(sm["dRSx"], 1)
        e.wait_ge(sm["dXT"], 16 * 16)
        e.wait_ge(sm["dRSx"], 3)
        e.wait_ge(sm["dONE"], 16)
        for seg in range(DSEG):
            for etl in range(DETS):
                etg = seg * DETS + etl
                for w in range(2):
                    li = 2 * etg + w
                    e.wait_ge(sm["dWL"], 16 * (li + 1))
                    if etg >= 1:
                        e.wait_ge(sm["dTG"], 2 * etg if w == 0 else 2 * etg - 1)
                    pb = self.p_up if w == 0 else self.p_g
                    for c2 in range(2):
                        for dt_ in range(16):
                            mm = e.matmul(pb[:, 512 * c2:512 * c2 + 512],
                                          self.win[li % 2][:, 128 * dt_:128 * dt_ + 128],
                                          self.xT[:, 1024 * dt_ + 512 * c2:1024 * dt_ + 512 * c2 + 512],
                                          start=(dt_ == 0), stop=(dt_ == 15))
                    mm.then_inc(sm["dMM"], 1)
            e.wait_ge(sm["dH"], DETS * (seg + 1))
            e.wait_ge(sm["dLD"], 16 * DETS * (seg + 1))
            for half in range(2):
                for tt in range(NTT):
                    it = (seg * 2 + half) * NTT + tt
                    if it >= 1:
                        e.wait_ge(sm["dFL"], it)
                    for etl in range(DETS):
                        for c2 in range(2):
                            mm = e.matmul(self.p_dn[:, 512 * c2:512 * c2 + 512],
                                          self.h8[:, 1024 * etl + 128 * tt:1024 * etl + 128 * tt + 128],
                                          self.wdn[etl][:, 1024 * half + 512 * c2:1024 * half + 512 * c2 + 512],
                                          start=(etl == 0), stop=(etl == DETS - 1))
                    mm.then_inc(sm["dDN"], 1)
            if seg >= 1:
                e.wait_ge(sm["dHSe"], seg)
            for tt in range(NTT):
                for etl in range(DETS):
                    mm = e.matmul(self.p_hs[:, tt:tt + 1],
                                  self.h8[:, 1024 * etl + 128 * tt:1024 * etl + 128 * tt + 128],
                                  self.onescb[:, 0:1],
                                  start=(etl == 0), stop=(etl == DETS - 1))
                mm.then_inc(sm["dHS"], 1)

    def em_vector(self, e):
        sm = self.ctx.sems
        self._vsn = 0

        def W(ins):
            self._vsn += 1
            ins.then_inc(sm["dVS"], 1)
            e.wait_ge(sm["dVS"], self._vsn)
        e.memset(self.ln[:, 6:7], 1e-5)
        for t in range(NTT):
            e.wait_ge(sm["dX3R"], 16 * (t + 1))
            e.reduce_sum(self.rcol[:, t:t + 1], self.x3res[:, :], AX).then_inc(sm["dRS"], 1)
        e.wait_ge(sm["dRSt"], 1)
        e.tensor_copy(self.rstg[:, :], self.p_hs[0:8, 0:128]).then_inc(sm["dRSv"], 1)
        for ch in range(2):
            e.wait_ge(sm["dRSx"], ch + 1)
            cp = e.tensor_copy(self.rsx[:, 512 * ch:512 * ch + 512],
                               self.p_up[:, 512 * ch:512 * ch + 512])
        cp.then_inc(sm["dRSx"], 1)
        for seg in range(DSEG):
            for etl in range(DETS):
                etg = seg * DETS + etl
                e.wait_ge(sm["dMM"], 2 * etg + 2)
                if etl == 0 and seg >= 1:
                    e.wait_ge(sm["dDN"], 16 * seg)
                    e.wait_ge(sm["dHS"], NTT * seg)
                if etg >= 1:
                    e.wait_ge(sm["dSIL"], etg)
                e.tensor_add(self.tg[1][:, :], self.p_g[:, :],
                             self.rsx[:, :]).then_inc(sm["dTG"], 1)
                e.tensor_add(self.tg[0][:, :], self.p_up[:, :],
                             self.rsx[:, :]).then_inc(sm["dTG"], 1)
                e.wait_ge(sm["dSIL"], etg + 1)
                e.scalar_tensor_tensor(self.h8[:, 1024 * etl:1024 * etl + 1024],
                                       self.tg[0][:, :], self.usc[:, etg:etg + 1],
                                       self.silu[:, :], ALU.mult,
                                       ALU.mult).then_inc(sm["dH"], 1)
            for half in range(2):
                for tt in range(NTT):
                    it = (seg * 2 + half) * NTT + tt
                    e.wait_ge(sm["dDN"], it + 1)
                    dx = self.dacc[:, 2048 * tt + 1024 * half:2048 * tt + 1024 * half + 1024]
                    if seg == 0:
                        e.tensor_copy(dx, self.p_dn[:, :]).then_inc(sm["dFL"], 1)
                    else:
                        e.tensor_add(dx, dx, self.p_dn[:, :]).then_inc(sm["dFL"], 1)
            e.wait_ge(sm["dHS"], NTT * (seg + 1))
            if seg == 0:
                e.tensor_copy(self.hs[:, 0:8], self.p_hs[:, 0:8]).then_inc(sm["dHSe"], 1)
            else:
                e.tensor_add(self.hs[:, 0:8], self.hs[:, 0:8],
                             self.p_hs[:, 0:8]).then_inc(sm["dHSe"], 1)
        e.wait_ge(sm["dHSe"], DSEG)
        W(e.tensor_scalar_mul(self.hs[:, 0:8], self.hs[:, 0:8], -128.0))
        for t in range(NTT):
            e.wait_ge(sm["dX3R"], 16 * (NTT + t + 1))
            dx = self.dacc[:, 2048 * t:2048 * t + 2048]
            e.tensor_scalar(dx, dx, self.hs[:, t:t + 1], None, ALU.add)
            e.tensor_mul(dx, dx, self.dsc[:, :])
            e.tensor_add(dx, dx, self.x3res[:, :]).then_inc(sm["dX3c"], 1)
            W(e.reduce_sum(self.ln[:, 0:1], dx, AX))
            W(e.tensor_scalar_mul(self.ln[:, 1:2], self.ln[:, 0:1], 1.0 / D))
            e.tensor_scalar(dx, dx, self.ln[:, 1:2], None, ALU.subtract)
            if t >= 1:
                e.wait_ge(sm["dOUT"], 16 * t)
            e.tensor_mul(self.x4b[:, :], dx, dx)
            e.reduce_sum(self.ln[:, 2:3], self.x4b[:, :], AX).then_inc(sm["dLNV"], 1)
            e.wait_ge(sm["dLNS"], t + 1)
            W(e.reciprocal(self.ln[:, 4:5], self.ln[:, 3:4]))
            e.tensor_scalar(self.x4f[:, :], dx, self.ln[:, 4:5], None,
                            ALU.mult).then_inc(sm["dX4"], 1)

    def em_scalar(self, e):
        sm = self.ctx.sems
        e.wait_ge(sm["dP"], 16 * 5)
        for seg in range(DSEG):
            for etl in range(DETS):
                etg = seg * DETS + etl
                e.wait_ge(sm["dTG"], 2 * etg + 1)
                if etg >= 1:
                    e.wait_ge(sm["dH"], etg)
                e.activation(self.silu[:, :], self.tg[1][:, :], AF.Silu,
                             scale=self.gsc[:, etg:etg + 1]).then_inc(sm["dSIL"], 1)
        for t in range(NTT):
            e.wait_ge(sm["dLNV"], t + 1)
            e.activation(self.ln[:, 3:4], self.ln[:, 2:3], AF.Sqrt,
                         scale=1.0 / D, bias=self.ln[:, 6:7]).then_inc(sm["dLNS"], 1)


def _hostprep_D(inputs, core):
    io = {}
    io["uw_t"] = np.ascontiguousarray(inputs["L1_ffn_up_w"].T).astype(np.int32)
    io["gw_t"] = np.ascontiguousarray(inputs["L1_ffn_gate_w"].T).astype(np.int32)
    io["dw_t"] = np.ascontiguousarray(inputs["L1_ffn_down_w"].T).astype(np.int32)
    io["gsc_t"] = np.ascontiguousarray(
        inputs["L1_ffn_gate_scale"].astype(np.float32).reshape(64, 128).T)
    io["usc_t"] = np.ascontiguousarray(
        inputs["L1_ffn_up_scale"].astype(np.float32).reshape(64, 128).T)
    io["dsc_rep"] = np.ascontiguousarray(
        np.broadcast_to(inputs["L1_ffn_down_scale"].astype(np.float32), (128, D)))
    io["negf"] = np.full((1, 128), -128.0, np.float32)
    io["identf"] = np.eye(128, dtype=np.float32)
    io["onescb_h"] = np.ones((128, 1), np.float32)
    return io


def _declare_D(nc):
    io = {}
    decl = [("uw_t", [D, DE], I32), ("gw_t", [D, DE], I32), ("dw_t", [DE, D], I32),
            ("gsc_t", [128, 64], F32), ("usc_t", [128, 64], F32),
            ("dsc_rep", [128, D], F32), ("negf", [1, 128], F32),
            ("identf", [128, 128], F32), ("onescb_h", [128, 1], F32)]
    for n, sh, dt in decl:
        io[n] = nc.dram_tensor(n, sh, dt, kind="ExternalInput").ap()
    return io


def build_program_D():
    nc = bass.Bass("TRN2", target_bir_lowering=False)
    stack = ExitStack()
    ctx = Ctx(nc, stack)
    io = _declare_D(nc)
    ctx.dram["x3f_d"] = nc.dram_tensor("x3f_d", [TOW, D], F32, kind="ExternalInput")
    ctx.dram["x3b_d"] = nc.dram_tensor("x3b_d", [TOW, D], BF16, kind="ExternalInput")
    with stack:
        if DBG_D:
            ctx.dscr("dbgD_rsx", [128, 1024], F32, kind="ExternalOutput")
            ctx.dscr("dbgD_hs", [128, 8], F32, kind="ExternalOutput")
            ctx.dscr("dbgD_h8", [128, DETS * 1024], BF16, kind="ExternalOutput")
        ph = PhaseD(ctx, io, out_kind="ExternalOutput")
        with nc.Block() as block:
            @block.sync
            def _(e):
                ph.em_sync(e)

            @block.gpsimd
            def _(e):
                ph.em_gpsimd(e)

            @block.tensor
            def _(e):
                ph.em_tensor(e)

            @block.vector
            def _(e):
                ph.em_vector(e)

            @block.scalar
            def _(e):
                ph.em_scalar(e)
    return nc


_ETF = mybir.EngineType
_FENGS = [_ETF.PE, _ETF.DVE, _ETF.Activation, _ETF.SP, _ETF.Pool]

_DECL_A = ["x_own", "x_pref", "x_qext", "wq_t", "wkv_t", "wo_t", "v_emb0",
           "v_emb1", "tau", "maskm", "invi", "jp", "onesf", "E16f", "identf"]
_DECL_B = ["wu_t", "wg_t", "wd_t", "wu_sl", "wg_sl", "wd_sl", "sel3", "onesf"]
_DECL_C = ["wip_t", "wout_t", "convw", "convb", "acolin", "drow", "normw_rep",
           "trilU", "mBt", "mbias", "selw", "onesf", "identf", "identb"]
_DECL_D = ["uw_t", "gw_t", "dw_t", "gsc_t", "usc_t", "dsc_rep", "negf",
           "identf", "onescb_h"]
_DECL_SHAPES = {
    "x_own": ([TOW, D], F32), "x_pref": ([S, D], F32), "x_qext": ([1152, D], F32),
    "wq_t": ([128, 2048], F32), "wkv_t": ([128, 1024], F32), "wo_t": ([128, D], F32),
    "v_emb0": ([128, 1], F32), "v_emb1": ([128, 1], F32), "tau": ([1, 1], F32),
    "maskm": ([128, 5632], F32), "invi": ([1, TOW], F32), "jp": ([128, 1], F32),
    "onesf": ([1, 128], F32), "E16f": ([16, 128], F32), "identf": ([128, 128], F32),
    "wu_t": ([D, DE], F32), "wg_t": ([D, DE], F32), "wd_t": ([DE, D], F32),
    "wu_sl": ([128, 16384], F32), "wg_sl": ([128, 16384], F32),
    "wd_sl": ([128, 16384], F32), "sel3": ([3, 384], F32),
    "wip_t": ([D, 8512], F32), "wout_t": ([4096, D], F32),
    "convw": ([128, 136], F32), "convb": ([128, 34], F32),
    "acolin": ([64, 4], F32), "drow": ([1, 64], F32),
    "normw_rep": ([128, 4096], F32), "trilU": ([64, 64], BF16),
    "mBt": ([8, 8], F32), "mbias": ([64, 8], F32), "selw": ([128, 8], F32),
    "identb": ([128, 128], BF16),
    "uw_t": ([D, DE], I32), "gw_t": ([D, DE], I32), "dw_t": ([DE, D], I32),
    "gsc_t": ([128, 64], F32), "usc_t": ([128, 64], F32),
    "dsc_rep": ([128, D], F32), "negf": ([1, 128], F32),
    "onescb_h": ([128, 1], F32),
}


class _MockIns:
    __slots__ = ("rec",)

    def __init__(self, rec):
        self.rec = rec

    def then_inc(self, sem, k):
        self.rec[sem.num] = self.rec.get(sem.num, 0) + k
        return self


class _MockEng:
    def __init__(self, rec):
        self._rec = rec

    def __getattr__(self, name):
        rec = self._rec
        if name == "wait_ge":
            return lambda *a, **k: None
        return lambda *a, **k: _MockIns(rec)


class _EngOff:

    def __init__(self, e, sbase):
        self._e = e
        self._sbase = sbase

    def wait_ge(self, sem, n):
        return self._e.wait_ge(sem, n + self._sbase.get(sem.num, 0))

    def __getattr__(self, name):
        return getattr(self._e, name)


def _phase_inc_totals(ph):
    rec = {}
    saved = PhaseC.bar
    PhaseC.bar = lambda self: None
    try:
        for em in (ph.em_sync, ph.em_gpsimd, ph.em_tensor, ph.em_vector, ph.em_scalar):
            em(_MockEng(rec))
    finally:
        PhaseC.bar = saved
    return rec


class FusedCtx(Ctx):
    def __init__(self, nc, stack):
        super().__init__(nc, stack)
        self._phase_sems = []

    def sem(self, *names):
        for name in names:
            if name not in self.sems:
                h = self.nc.alloc_semaphore(name)
                self.sems[name] = h
                self._phase_sems.append(h)


def build_program_fused():
    nc = bass.Bass("TRN2", target_bir_lowering=False)
    ctx = FusedCtx(nc, None)
    io_all = {}
    for n in _DECL_A + _DECL_B + _DECL_C + _DECL_D:
        if n not in io_all:
            sh, dt = _DECL_SHAPES[n]
            io_all[n] = nc.dram_tensor(n, sh, dt, kind="ExternalInput").ap()
    specs = [
        (PhaseA, _DECL_A, "Internal"),
        (PhaseB, _DECL_B, "Internal"),
        (PhaseC, _DECL_C, "Internal"),
        (PhaseD, _DECL_D, "ExternalOutput"),
    ]
    sbase = {}
    for cls, names, okind in specs:
        pstack = ExitStack()
        ctx.stack = pstack
        ctx._phase_sems = []
        ctx.sems = {}
        pio = {n: io_all[n] for n in names}
        ph = cls(ctx, pio, out_kind=okind)
        rec = _phase_inc_totals(ph)
        with nc.Block() as block:
            @block.sync
            def _(e, ph=ph):
                ph.em_sync(_EngOff(e, sbase))
                nc.multi_engine_barrier(_FENGS)

            @block.gpsimd
            def _(e, ph=ph):
                ph.em_gpsimd(_EngOff(e, sbase))
                nc.multi_engine_barrier(_FENGS)

            @block.tensor
            def _(e, ph=ph):
                ph.em_tensor(_EngOff(e, sbase))
                nc.multi_engine_barrier(_FENGS)

            @block.vector
            def _(e, ph=ph):
                ph.em_vector(_EngOff(e, sbase))
                nc.multi_engine_barrier(_FENGS)

            @block.scalar
            def _(e, ph=ph):
                ph.em_scalar(_EngOff(e, sbase))
                nc.multi_engine_barrier(_FENGS)
        for num, tot in rec.items():
            sbase[num] = sbase.get(num, 0) + tot
        for h in ctx._phase_sems:
            nc.release_semaphore(h)
        pstack.close()
    return nc


def _hostprep_fused(inputs, core):
    im = {}
    for hp in (_hostprep_A, _hostprep_B, _hostprep_C, _hostprep_D):
        for k, v in hp(inputs, core).items():
            if k not in im:
                im[k] = v
    return im


def kernel(**inputs):
    nc = build_program_fused()
    in_maps = [_hostprep_fused(inputs, c) for c in range(NCORE)]
    res = run_bass_kernel_spmd(nc, in_maps, core_ids=list(range(NCORE)))
    out = np.zeros((B, S, D), np.float32)
    for c in range(NCORE):
        b, s = c // 4, c % 4
        out[b, s * 1024:(s + 1) * 1024] = np.asarray(res.results[c]["x4f_d"])
    return out


# version 03 (final)
# speedup vs baseline: 1.0091x; 1.0089x over previous
import dataclasses
import math
from contextlib import ExitStack
import numpy as np
import ml_dtypes
import concourse.bass as bass
import concourse.mybir as mybir
from concourse.bass_utils import run_bass_kernel_spmd

F32 = mybir.dt.float32
BF16 = mybir.dt.bfloat16
I32 = mybir.dt.int32
AF = mybir.ActivationFunctionType
ALU = mybir.AluOpType
AX = mybir.AxisListType.X
BF = ml_dtypes.bfloat16

B, S, D = 2, 4096, 2048
NCORE = 8
TOW = 1024
NTT = TOW // 128
WIN = 32
NJT = S // 128
NSWT = 32 + 9
STAGE = 5
import os as _os
SKIP = set((_os.environ.get('K_SKIP') or '').split(','))
DBG_A = bool(_os.environ.get('K_DBG_A'))
DBG_B = bool(_os.environ.get('K_DBG_B'))
DBG_D = bool(_os.environ.get('K_DBG_D'))
CCUT = int(_os.environ.get('K_CCUT') or 0)


class _CutDone(Exception):
    pass


def apm(ap, off=0, dims=None):
    return dataclasses.replace(ap, offset=ap.offset + off,
                               ap=[list(d) for d in dims] if dims is not None else ap.ap)


class Ctx:
    def __init__(self, nc, stack):
        self.nc = nc
        self.stack = stack
        self.sb = {}
        self.ps = {}
        self.dram = {}
        self.sems = {}

    def sem(self, *names):
        for name in names:
            if name not in self.sems:
                self.sems[name] = self.stack.enter_context(self.nc.semaphore(name))

    def sbuf(self, name, shape, dtype):
        self.sb[name] = self.stack.enter_context(self.nc.sbuf_tensor(name, shape, dtype))
        return self.sb[name]

    def sbuf_at(self, name, shape, dtype, host, off_bytes):
        base = self.nc.lookup_mloc(self.sb[host]).addr
        h = self.nc.alloc_sbuf_tensor_at(name, shape, dtype, offset=base + off_bytes)
        self.sb[name] = h
        return h

    def psum(self, name, shape, dtype=F32):
        self.ps[name] = self.stack.enter_context(self.nc.psum_tensor(name, shape, dtype))
        return self.ps[name]

    def dscr(self, name, shape, dtype, kind="Internal"):
        if name not in self.dram:
            self.dram[name] = self.nc.dram_tensor(name, shape, dtype, kind=kind)
        return self.dram[name]


class PhaseA:

    def __init__(self, ctx: Ctx, io, out_kind="Internal"):
        self.ctx = ctx
        self.io = io
        c = ctx
        self.xinb = [c.sbuf(f"A_xinb{i}", [128, D], BF16) for i in range(2)]
        self.xres = c.sbuf("A_xres", [128, D], F32)
        self.xT = c.sbuf("A_xT", [128, D], BF16)
        self.wkv = c.sbuf("A_wkv", [128, 16 * 64], BF16)
        self.wq = c.sbuf("A_wq", [128, 16 * 128], BF16)
        self.kfm = c.sbuf("A_kfm", [32, WIN - 1 + S], BF16)
        self.vfull = c.sbuf("A_vfull", [32, S + 128], BF16)
        self.qfm = c.sbuf("A_qfm", [128, 1152], BF16)
        self.vs = c.sbuf("A_vs", [128, NJT * 36], BF16)
        self.kw = [c.sbuf(f"A_kw{i}", [128, S], BF16) for i in range(2)]
        self.qw = [c.sbuf(f"A_qw{i}", [128, 4 * 512], BF16) for i in range(2)]
        self.ebias = c.sbuf("A_ebias", [128, NJT * 512], BF16)
        self.ebarg = [c.sbuf(f"A_ebarg{i}", [128, 512], BF16) for i in range(2)]
        self.Usb = c.sbuf("A_U", [128, 512], BF16)
        self.e1 = [c.sbuf(f"A_e1_{i}", [128, 512], BF16) for i in range(2)]
        self.e2 = [c.sbuf(f"A_e2_{i}", [128, 512], BF16) for i in range(2)]
        self.stg = c.sbuf("A_stg", [9, 512], F32)
        self.attn = c.sbuf("A_attn", [128, TOW], F32)
        self.attnb = c.sbuf("A_attnb", [128, TOW], BF16)
        self.den = c.sbuf("A_den", [16, TOW], F32)
        self.denb = c.sbuf("A_denb", [16, TOW], BF16)
        self.recip = c.sbuf("A_recip", [128, TOW], F32)
        self.wo = c.sbuf("A_wo", [128, D], BF16)
        self.wop = c.sbuf("A_wop", [128, D], BF16)
        self.v0c = c.sbuf("A_v0c", [128, 1], F32)
        self.v1c = c.sbuf("A_v1c", [128, 1], F32)
        self.v0cb = c.sbuf("A_v0cb", [128, 1], BF16)
        self.dvc = c.sbuf("A_dvc", [128, 1], F32)
        self.v0wo = c.sbuf("A_v0wo", [1, D], BF16)
        self.v0woe = c.sbuf("A_v0woe", [128, D], F32)
        self.km = c.sbuf("A_km", [128, 8], F32)
        self.sc1 = c.sbuf("A_sc1", [1, 8], F32)
        self.onesf = c.sbuf("A_onesf", [1, 128], F32)
        self.onesb = c.sbuf("A_onesb", [1, 128], BF16)
        self.E16 = c.sbuf("A_E16", [16, 128], BF16)
        self.maskm = c.sbuf("A_maskm", [128, 5632], BF16)
        self.invi = c.sbuf("A_invi", [1, TOW], F32)
        self.jga = c.sbuf("A_jga", [128, NJT], F32)
        self.jp = c.sbuf("A_jp", [128, 1], F32)
        self.identb = c.sbuf("A_identb", [128, 128], BF16)
        self.xr = c.sbuf("A_xr", [128, D], F32)
        self.ln = c.sbuf("A_ln", [128, 8], F32)
        self.x1f = c.sbuf("A_x1f", [128, D], F32)
        self.x1b = c.sbuf("A_x1b", [128, D], BF16)
        self.p_sc = [c.psum(f"A_psc{i}", [128, 512], F32) for i in range(2)]
        self.p_pv = c.psum("A_ppv", [9, 512], F32)
        self.p_tr = [c.psum(f"A_ptr{i}", [128, 1024], BF16) for i in range(2)]
        self.p_kv = c.psum("A_pkv", [64, 512], F32)
        self.p_q = c.psum("A_pq", [128, 512], F32)
        self.p_x = c.psum("A_px", [128, 512], F32)
        self.k_dram = c.dscr("A_kdram", [32, WIN - 1 + S], BF16)
        self.q_dram = c.dscr("A_qdram", [128, 1152], BF16)
        self.x1f_d = c.dscr("x1f_d", [TOW, D], F32, kind=out_kind)
        self.x1b_d = c.dscr("x1b_d", [TOW, D], BF16, kind=out_kind)
        c.sem("aP", "aXT", "aXC", "aTW", "aTE", "aPJ", "aAC", "aKDR", "aQDR",
              "aQWB", "aKWB", "aVT", "aVE", "aUE", "aEA", "aEE", "aSC", "aE1",
              "aE2", "aPV", "aSG", "aASM", "aDN", "aDXP", "aRC", "aAB",
              "aWOE", "aRCP", "aWO", "aLV", "aLNV", "aLNS", "aX1", "aOUT",
              "aPS1", "aPV1", "aKM", "aSW", "aPg", "aDBG", "aVSf")

    NPS = 6
    NPG = 7

    def em_sync(self, e):
        sm, io = self.ctx.sems, self.io
        for name, dst in [("invi", self.invi), ("jp", self.jp),
                          ("onesf", self.onesf), ("v_emb0", self.v0c),
                          ("v_emb1", self.v1c)]:
            e.dma_start(out=dst[:, :], in_=io[name]).then_inc(sm["aP"], 16)
        e.dma_start(out=self.sc1[0:1, 0:1], in_=io["tau"]).then_inc(sm["aP"], 16)
        if STAGE < 2:
            return
        e.wait_ge(sm["aAC"], 32)
        e.dma_start(out=self.k_dram.ap(), in_=self.kfm[:, :]).then_inc(sm["aKDR"], 16)
        e.wait_ge(sm["aAC"], NSWT)
        e.dma_start(out=self.q_dram.ap(), in_=self.qfm[:, :]).then_inc(sm["aQDR"], 16)
        if STAGE < 3:
            return
        e.wait_ge(sm["aQDR"], 16)
        e.wait_ge(sm["aKDR"], 16)
        for rb in range(2):
            for g4 in range(4):
                ng = rb * 4 + g4
                if ng > 0:
                    e.wait_ge(sm["aSC"], 128 * ng)
                for w in range(WIN):
                    half, wl = w // 16, w % 16
                    dst = apm(self.qw[half][:, :], (wl * 8) * (4 * 512),
                              [[4 * 512, 8], [512, 4], [1, 512]])
                    src = apm(self.q_dram.ap(), 97 + 512 * rb + w + (4 * g4) * 8 * 1152,
                              [[1152, 8], [8 * 1152, 4], [1, 512]])
                    e.dma_start(out=dst, in_=src).then_inc(sm["aQWB"], 16)
                for w in range(WIN):
                    half, wl = w // 16, w % 16
                    dst = apm(self.kw[half][:, :], (wl * 8) * S, [[S, 8], [1, S]])
                    src = apm(self.k_dram.ap(), g4 * 8 * (WIN - 1 + S) + w,
                              [[WIN - 1 + S, 8], [1, S]])
                    e.dma_start(out=dst, in_=src).then_inc(sm["aKWB"], 16)
        if DBG_A:
            e.wait_ge(sm["aAB"], 1)
            for nm, buf in [("dbg_attn", self.attn), ("dbg_attnb", self.attnb)]:
                e.dma_start(out=self.ctx.dram[nm].ap(), in_=buf[:, :]).then_inc(sm["aDBG"], 16)
            e.dma_start(out=self.ctx.dram["dbg_den"].ap(), in_=self.den[:, :]).then_inc(sm["aDBG"], 16)
            e.dma_start(out=self.ctx.dram["dbg_qfm"].ap(), in_=self.qfm[:, :]).then_inc(sm["aDBG"], 16)
            e.dma_start(out=self.ctx.dram["dbg_kfm"].ap(), in_=self.kfm[:, :]).then_inc(sm["aDBG"], 16)
            e.dma_start(out=self.ctx.dram["dbg_vs"].ap(), in_=self.vs[:, :]).then_inc(sm["aDBG"], 16)
            e.dma_start(out=self.ctx.dram["dbg_km"].ap(), in_=self.km[:, :]).then_inc(sm["aDBG"], 16)
            e.dma_start(out=self.ctx.dram["dbg_eb"].ap(), in_=self.ebias[:, :]).then_inc(sm["aDBG"], 16)
            e.wait_ge(sm["aDBG"], 16 * 8)
        if STAGE < 5:
            return
        for t in range(NTT):
            if t >= 1:
                e.wait_ge(sm["aXC"], t)
            e.dma_start(out=self.xres[:, :],
                        in_=apm(io["x_own"], t * 128 * D, [[D, 128], [1, D]])
                        ).then_inc(sm["aXT"], 16)
            if t >= 1:
                e.wait_ge(sm["aX1"], 2 * t)
                e.dma_start(out=apm(self.x1f_d.ap(), (t - 1) * 128 * D, [[D, 128], [1, D]]),
                            in_=self.x1f[:, :]).then_inc(sm["aOUT"], 16)
                e.dma_start(out=apm(self.x1b_d.ap(), (t - 1) * 128 * D, [[D, 128], [1, D]]),
                            in_=self.x1b[:, :]).then_inc(sm["aOUT"], 16)
        e.wait_ge(sm["aX1"], 2 * NTT)
        e.dma_start(out=apm(self.x1f_d.ap(), (NTT - 1) * 128 * D, [[D, 128], [1, D]]),
                    in_=self.x1f[:, :]).then_inc(sm["aOUT"], 16)
        e.dma_start(out=apm(self.x1b_d.ap(), (NTT - 1) * 128 * D, [[D, 128], [1, D]]),
                    in_=self.x1b[:, :]).then_inc(sm["aOUT"], 16)
        e.wait_ge(sm["aOUT"], 32 * NTT)

    def em_gpsimd(self, e):
        sm, io = self.ctx.sems, self.io
        e.dma_start(out=self.wkv[:, :], in_=io["wkv_t"]).then_inc(sm["aPg"], 16)
        e.dma_start(out=self.wq[:, :], in_=io["wq_t"]).then_inc(sm["aPg"], 16)
        e.dma_start(out=self.wo[:, :], in_=io["wo_t"]).then_inc(sm["aPg"], 16)
        e.dma_start(out=self.E16[:, :], in_=io["E16f"]).then_inc(sm["aPg"], 16)
        e.dma_start(out=self.identb[:, :], in_=io["identf"]).then_inc(sm["aPg"], 16)
        e.dma_start(out=self.maskm[:, :], in_=io["maskm"]).then_inc(sm["aPg"], 16)
        e.dma_start(out=self.onesb[:, :], in_=io["onesf"]).then_inc(sm["aPg"], 16)
        if STAGE < 2:
            return
        for g in range(NSWT):
            if g >= 2:
                e.wait_ge(sm["aTW"], 16 * (g - 1))
            if g < 32:
                srcap = apm(io["x_pref"], g * 128 * D, [[D, 128], [1, D]])
            else:
                srcap = apm(io["x_qext"], (g - 32) * 128 * D, [[D, 128], [1, D]])
            e.dma_start(out=self.xinb[g % 2][:, :], in_=srcap).then_inc(sm["aSW"], 16)
        if STAGE < 4:
            return
        for hx in range(32):
            rb, h = hx // 16, hx % 16
            e.wait_ge(sm["aSG"], hx + 1)
            e.dma_start(out=apm(self.attn[:, :], 8 * h * TOW + rb * 512,
                                [[TOW, 8], [1, 512]]),
                        in_=self.stg[0:8, :]).then_inc(sm["aASM"], 16)
            e.dma_start(out=apm(self.den[:, :], h * TOW + rb * 512,
                                [[TOW, 1], [1, 512]]),
                        in_=self.stg[8:9, :]).then_inc(sm["aASM"], 16)

    def em_tensor(self, e):
        sm = self.ctx.sems
        nc = self.ctx.nc
        if STAGE < 1:
            return
        e.wait_ge(sm["aP"], 16 * self.NPS)
        e.wait_ge(sm["aPg"], 16 * self.NPG)
        e.wait_ge(sm["aPV1"], 1)
        if 'expand' not in SKIP:
            for i, col in enumerate([4, 5, 1]):
                mm = e.matmul(self.p_x[:, i:i + 1], self.onesf[0:1, :],
                              self.sc1[0:1, col:col + 1], start=True, stop=True)
            mm.then_inc(sm["aWOE"], 1)
        else:
            nc.tensor.nop().then_inc(sm["aWOE"], 1)
        if STAGE < 2:
            return
        for t in range(NSWT):
            e.wait_ge(sm["aSW"], 16 * (t + 1))
            e.wait_ge(sm["aAC"], t)
            for dt in range(16):
                g = t * 16 + dt
                if g >= 2:
                    e.wait_ge(sm["aTE"], g - 1)
                e.transpose(self.p_tr[g % 2][:, 0:128],
                            self.xinb[t % 2][:, 128 * dt:128 * dt + 128],
                            self.identb[:, :]).then_inc(sm["aTW"], 1)
            e.wait_ge(sm["aTE"], 16 * (t + 1))
            if t < 32:
                for dt in range(16):
                    mm = e.matmul(self.p_kv[:, 0:128], self.wkv[:, 64 * dt:64 * dt + 64],
                                  self.xT[:, 128 * dt:128 * dt + 128],
                                  start=(dt == 0), stop=(dt == 15))
            else:
                for dt in range(16):
                    mm = e.matmul(self.p_q[:, 0:128], self.wq[:, 128 * dt:128 * dt + 128],
                                  self.xT[:, 128 * dt:128 * dt + 128],
                                  start=(dt == 0), stop=(dt == 15))
            mm.then_inc(sm["aPJ"], 1)
        if STAGE < 3:
            return
        e.wait_ge(sm["aAC"], NSWT)
        e.wait_ge(sm["aPV1"], 2)
        for jt in range(NJT):
            if jt >= 2:
                e.wait_ge(sm["aVE"], jt - 1)
            e.transpose(self.p_tr[jt % 2][:, 0:32],
                        self.vfull[:, 128 * jt + 1:128 * jt + 129],
                        self.identb[0:32, 0:32]).then_inc(sm["aVT"], 1)
        e.wait_ge(sm["aWO"], 1)
        for ch in range(4):
            if ch > 0:
                e.wait_ge(sm["aRCP"], ch)
            e.matmul(self.p_x[0:1, :], self.v0cb[:, 0:1],
                     self.wo[:, 512 * ch:512 * ch + 512], start=True,
                     stop=True).then_inc(sm["aWOE"], 1)
        for ch in range(4):
            e.wait_ge(sm["aRCP"], 4 + ch)
            e.matmul(self.p_x[:, :], self.onesb[0:1, :],
                     self.v0wo[0:1, 512 * ch:512 * ch + 512], start=True,
                     stop=True).then_inc(sm["aWOE"], 1)
        e.wait_ge(sm["aRCP"], 8)
        if STAGE < 4:
            return
        for rb in range(2):
            e.wait_ge(sm["aPV1"], 1)
            e.wait_ge(sm["aAC"], NSWT)
            if rb > 0:
                e.wait_ge(sm["aEA"], NJT * rb)
            e.matmul(self.p_q[:, :], self.onesf[0:1, :],
                     self.invi[0:1, 512 * rb:512 * rb + 512], start=True,
                     stop=True).then_inc(sm["aUE"], 1)
            for g4 in range(4):
                ng = rb * 4 + g4
                e.wait_ge(sm["aQWB"], 16 * WIN * (ng + 1))
                e.wait_ge(sm["aKWB"], 16 * WIN * (ng + 1))
                for hh in range(4):
                    h = 4 * g4 + hh
                    hx = rb * 16 + h
                    q0 = hx * NJT
                    e.wait_ge(sm["aSG"], hx)
                    e.wait_ge(sm["aVE"], NJT)
                    e.wait_ge(sm["aEE"], NJT * (rb + 1))
                    for jt in range(NJT):
                        q = q0 + jt
                        if q >= 2:
                            e.wait_ge(sm["aE1"], q - 1)
                        pb = self.p_sc[jt % 2]
                        e.matmul(pb[:, :], self.kw[0][:, 128 * jt:128 * jt + 128],
                                 self.qw[0][:, 512 * hh:512 * hh + 512],
                                 start=True, stop=False)
                        e.matmul(pb[:, :], self.kw[1][:, 128 * jt:128 * jt + 128],
                                 self.qw[1][:, 512 * hh:512 * hh + 512],
                                 start=False, stop=True).then_inc(sm["aSC"], 1)
                        if jt >= 1:
                            qq = q - 1
                            e.wait_ge(sm["aE2"], qq + 1)
                            jq = qq % NJT
                            e.matmul(self.p_pv[:, :],
                                     self.vs[:, 36 * jq + 9 * g4:36 * jq + 9 * g4 + 9],
                                     self.e2[qq % 2][:, :],
                                     start=(jq == 0), stop=False).then_inc(sm["aPV"], 1)
                    qq = q0 + NJT - 1
                    e.wait_ge(sm["aE2"], qq + 1)
                    e.matmul(self.p_pv[:, :],
                             self.vs[:, 36 * (NJT - 1) + 9 * g4:36 * (NJT - 1) + 9 * g4 + 9],
                             self.e2[qq % 2][:, :], start=False,
                             stop=True).then_inc(sm["aPV"], 1)
        if STAGE < 5:
            return
        e.wait_ge(sm["aDN"], 1)
        for ch in range(2):
            e.matmul(self.p_sc[ch][:, :], self.E16[:, :],
                     self.denb[:, 512 * ch:512 * ch + 512], start=True,
                     stop=True).then_inc(sm["aDXP"], 1)
        e.wait_ge(sm["aAB"], 1)
        for t in range(NTT):
            for ch in range(4):
                k = t * 4 + ch
                pb = self.p_x if ch % 2 == 0 else self.p_q
                if k >= 2:
                    e.wait_ge(sm["aLV"], k - 1)
                e.matmul(pb[:, :], self.attnb[:, 128 * t:128 * t + 128],
                         self.wop[:, 512 * ch:512 * ch + 512], start=True,
                         stop=True).then_inc(sm["aWO"], 1)

    def em_vector(self, e):
        sm = self.ctx.sems
        self._vsn = 0

        def W(ins):
            self._vsn += 1
            ins.then_inc(sm["aVSf"], 1)
            e.wait_ge(sm["aVSf"], self._vsn)
        if STAGE < 1:
            return
        e.wait_ge(sm["aP"], 16 * self.NPS)
        e.wait_ge(sm["aPg"], 16 * self.NPG)
        W(e.reciprocal(self.sc1[0:1, 1:2], self.sc1[0:1, 0:1]))
        W(e.tensor_scalar_mul(self.sc1[0:1, 6:7], self.sc1[0:1, 0:1], -1.0))
        W(e.tensor_max(self.sc1[0:1, 2:3], self.sc1[0:1, 0:1], self.sc1[0:1, 6:7]))
        W(e.reciprocal(self.sc1[0:1, 3:4], self.sc1[0:1, 2:3]))
        W(e.tensor_scalar_mul(self.sc1[0:1, 4:5], self.sc1[0:1, 3:4], -16.0))
        W(e.tensor_scalar_mul(self.sc1[0:1, 5:6], self.sc1[0:1, 1:2], 1.0 / 16.0))
        e.tensor_scalar(self.invi[0:1, :], self.invi[0:1, :], self.sc1[0:1, 1:2],
                        None, ALU.mult).then_inc(sm["aPV1"], 1)
        if 'jga' not in SKIP:
            for jt in range(NJT):
                e.tensor_scalar_add(self.jga[:, jt:jt + 1], self.jp[:, :], float(128 * jt))
        if 'memset' not in SKIP:
            e.memset(self.kfm[:, 0:WIN - 1], 0.0)
            e.memset(self.vfull[:, S:S + 128], 0.0)
            e.memset(self.vs[:, :], 1.0)
            e.memset(self.km[:, 3:4], 1e-5).then_inc(sm["aPV1"], 1)
        else:
            e.engine_nop().then_inc(sm["aPV1"], 1)
        e.wait_ge(sm["aWOE"], 1)
        if 'km' not in SKIP:
            e.tensor_copy(self.km[:, 0:3], self.p_x[:, 0:3]).then_inc(sm["aKM"], 1)
        else:
            e.engine_nop().then_inc(sm["aKM"], 1)
        W(e.tensor_sub(self.dvc[:, :], self.v1c[:, :], self.v0c[:, :]))
        e.tensor_copy(self.v0cb[:, :], self.v0c[:, :])
        e.tensor_scalar(self.wop[:, :], self.wo[:, :], self.dvc[:, 0:1], None,
                        ALU.mult).then_inc(sm["aWO"], 1)
        if STAGE < 3:
            return
        for jt in range(NJT):
            e.wait_ge(sm["aVT"], jt + 1)
            for g4 in range(4):
                cp = e.tensor_copy(self.vs[:, 36 * jt + 9 * g4:36 * jt + 9 * g4 + 8],
                                   self.p_tr[jt % 2][:, 8 * g4:8 * g4 + 8])
            cp.then_inc(sm["aVE"], 1)
        for ch in range(4):
            e.wait_ge(sm["aWOE"], 2 + ch)
            e.tensor_copy(self.v0wo[0:1, 512 * ch:512 * ch + 512],
                          self.p_x[0:1, :]).then_inc(sm["aRCP"], 1)
        for ch in range(4):
            e.wait_ge(sm["aWOE"], 6 + ch)
            e.tensor_copy(self.v0woe[:, 512 * ch:512 * ch + 512],
                          self.p_x[:, :]).then_inc(sm["aRCP"], 1)
        if STAGE < 4:
            return
        for rb in range(2):
            e.wait_ge(sm["aUE"], rb + 1)
            e.tensor_copy(self.Usb[:, :], self.p_q[:, :])
            for jt in range(NJT):
                idx = rb * NJT + jt
                if idx >= 2:
                    e.wait_ge(sm["aEE"], idx - 1)
                ofs = 4480 + 512 * rb - 128 * jt
                e.scalar_tensor_tensor(self.ebarg[idx % 2][:, :], self.Usb[:, :],
                                       self.jga[:, jt:jt + 1],
                                       self.maskm[:, ofs:ofs + 512], ALU.mult,
                                       ALU.add).then_inc(sm["aEA"], 1)
            for h in range(16):
                hx = rb * 16 + h
                q0 = hx * NJT
                for jt in range(NJT):
                    q = q0 + jt
                    e.wait_ge(sm["aE1"], q + 1)
                    if q >= 2:
                        e.wait_ge(sm["aPV"], q - 1)
                    e.tensor_mul(self.e2[q % 2][:, :], self.e1[q % 2][:, :],
                                 self.ebias[:, 512 * jt:512 * jt + 512]
                                 ).then_inc(sm["aE2"], 1)
                e.wait_ge(sm["aPV"], NJT * (hx + 1))
                e.wait_ge(sm["aASM"], 32 * hx)
                e.tensor_copy(self.stg[:, :], self.p_pv[:, :]).then_inc(sm["aSG"], 1)
        if STAGE < 5:
            return
        e.wait_ge(sm["aASM"], 32 * 32)
        e.tensor_scalar_max(self.den[:, :], self.den[:, :], 1e-30)
        e.tensor_copy(self.denb[:, :], self.den[:, :]).then_inc(sm["aDN"], 1)
        for ch in range(2):
            e.wait_ge(sm["aDXP"], ch + 1)
            e.reciprocal(self.recip[:, 512 * ch:512 * ch + 512], self.p_sc[ch][:, :])
        for ch in range(2):
            ab = e.tensor_mul(self.attnb[:, 512 * ch:512 * ch + 512],
                              self.attn[:, 512 * ch:512 * ch + 512],
                              self.recip[:, 512 * ch:512 * ch + 512])
        ab.then_inc(sm["aAB"], 1)
        for t in range(NTT):
            for ch in range(4):
                k = t * 4 + ch
                pb = self.p_x if ch % 2 == 0 else self.p_q
                e.wait_ge(sm["aWO"], 2 + k)
                e.tensor_add(self.xr[:, 512 * ch:512 * ch + 512], pb[:, :],
                             self.v0woe[:, 512 * ch:512 * ch + 512]
                             ).then_inc(sm["aLV"], 1)
            e.wait_ge(sm["aXT"], 16 * (t + 1))
            e.tensor_add(self.xr[:, :], self.xr[:, :],
                         self.xres[:, :]).then_inc(sm["aXC"], 1)
            W(e.reduce_sum(self.ln[:, 0:1], self.xr[:, :], AX))
            W(e.tensor_scalar_mul(self.ln[:, 1:2], self.ln[:, 0:1], 1.0 / D))
            e.tensor_scalar(self.xr[:, :], self.xr[:, :], self.ln[:, 1:2], None,
                            ALU.subtract)
            if t >= 1:
                e.wait_ge(sm["aOUT"], 32 * t)
            e.tensor_mul(self.x1b[:, :], self.xr[:, :], self.xr[:, :])
            e.reduce_sum(self.ln[:, 2:3], self.x1b[:, :], AX).then_inc(sm["aLNV"], 1)
            e.wait_ge(sm["aLNS"], t + 1)
            W(e.reciprocal(self.ln[:, 4:5], self.ln[:, 3:4]))
            e.tensor_scalar(self.x1f[:, :], self.xr[:, :], self.ln[:, 4:5], None,
                            ALU.mult).then_inc(sm["aX1"], 1)
            e.tensor_scalar(self.x1b[:, :], self.xr[:, :], self.ln[:, 4:5], None,
                            ALU.mult).then_inc(sm["aX1"], 1)

    def em_scalar(self, e):
        sm = self.ctx.sems
        nop = lambda: self.ctx.nc.scalar.nop()
        if STAGE < 1:
            return
        e.wait_ge(sm["aP"], 16 * self.NPS)
        e.wait_ge(sm["aPg"], 16 * self.NPG)

        e.wait_ge(sm["aKM"], 1)
        if STAGE < 2:
            return
        for t in range(NSWT):
            for dt in range(16):
                g = t * 16 + dt
                e.wait_ge(sm["aTW"], g + 1)
                if t >= 1 and dt == 0:
                    e.wait_ge(sm["aPJ"], t)
                e.copy(self.xT[:, 128 * dt:128 * dt + 128],
                       self.p_tr[g % 2][:, 0:128]).then_inc(sm["aTE"], 1)
            e.wait_ge(sm["aPJ"], t + 1)
            if t < 32:
                e.activation(self.kfm[:, WIN - 1 + 128 * t:WIN - 1 + 128 * t + 128],
                             self.p_kv[0:32, 0:128], AF.Tanh, scale=self.km[0:32, 2:3])
                e.activation(self.vfull[:, 128 * t:128 * t + 128],
                             self.p_kv[32:64, 0:128], AF.Sigmoid,
                             scale=self.km[32:64, 2:3]).then_inc(sm["aAC"], 1)
            else:
                tq = t - 32
                e.activation(self.qfm[:, 128 * tq:128 * tq + 128],
                             self.p_q[:, 0:128], AF.Tanh,
                             scale=self.km[:, 2:3]).then_inc(sm["aAC"], 1)
        if STAGE < 4:
            return
        for rb in range(2):
            for jt in range(NJT):
                idx = rb * NJT + jt
                e.wait_ge(sm["aEA"], idx + 1)
                e.activation(self.ebias[:, 512 * jt:512 * jt + 512],
                             self.ebarg[idx % 2][:, :], AF.Exp).then_inc(sm["aEE"], 1)
            for h in range(16):
                hx = rb * 16 + h
                q0 = hx * NJT
                for jt in range(NJT):
                    q = q0 + jt
                    e.wait_ge(sm["aSC"], q + 1)
                    if q >= 2:
                        e.wait_ge(sm["aE2"], q - 1)
                    e.activation(self.e1[q % 2][:, :], self.p_sc[jt % 2][:, :], AF.Exp,
                                 bias=self.km[:, 0:1],
                                 scale=self.km[:, 1:2]).then_inc(sm["aE1"], 1)
        if STAGE < 5:
            return
        for t in range(NTT):
            e.wait_ge(sm["aLNV"], t + 1)
            e.activation(self.ln[:, 3:4], self.ln[:, 2:3], AF.Sqrt,
                         scale=1.0 / D, bias=self.km[:, 3:4]).then_inc(sm["aLNS"], 1)


def _hostprep_A(inputs, core):
    beta, seg = core // 4, core % 4
    own0 = seg * TOW
    x = np.asarray(inputs["L0_attn_hidden_states"], dtype=np.float32)
    xb = x[beta]
    x_own = np.ascontiguousarray(xb[own0:own0 + TOW])
    x_pref = np.ascontiguousarray(xb.astype(BF))
    lo = own0 - 128
    x_qext = np.zeros((1152, D), BF)
    s0 = max(0, lo)
    x_qext[s0 - lo:, :] = xb[s0:own0 + TOW].astype(BF)
    wq = np.asarray(inputs["L0_attn_Wq"], np.float32)
    wk = np.asarray(inputs["L0_attn_Wk"], np.float32)
    wv = np.asarray(inputs["L0_attn_Wv"], np.float32)
    wo = np.asarray(inputs["L0_attn_Wo"], np.float32)
    wq_t = np.ascontiguousarray(wq.T.reshape(16, 128, 128).transpose(1, 0, 2).reshape(128, 2048))
    wkv_t = np.ascontiguousarray(np.concatenate([wk, wv], axis=0).T
                                 .reshape(16, 128, 64).transpose(1, 0, 2).reshape(128, 1024))
    wo_t = np.ascontiguousarray(wo.T)
    v0 = np.asarray(inputs["L0_attn_v_emb0"], np.float32).reshape(128, 1)
    v1 = np.asarray(inputs["L0_attn_v_emb1"], np.float32).reshape(128, 1)
    tau = np.asarray(inputs["L0_attn_tau"], np.float32).reshape(1, 1)
    jl = np.arange(128)[:, None]
    vv = np.arange(5632)[None, :]
    maskm = np.where(vv >= (4481 - own0) + jl, 0.0, -1e30).astype(np.float32)
    invi = (1.0 / (np.arange(own0, own0 + TOW, dtype=np.float64) + 1.0)).astype(np.float32).reshape(1, TOW)
    jp = np.arange(128, dtype=np.float32).reshape(128, 1)
    onesf = np.ones((1, 128), np.float32)
    E16f = np.zeros((16, 128), np.float32)
    for hh in range(16):
        E16f[hh, 8 * hh:8 * hh + 8] = 1.0
    identf = np.eye(128, dtype=np.float32)
    return dict(x_own=x_own, x_pref=x_pref, x_qext=x_qext, wq_t=wq_t, wkv_t=wkv_t,
                wo_t=wo_t, v_emb0=v0, v_emb1=v1, tau=tau, maskm=maskm, invi=invi,
                jp=jp, onesf=onesf, E16f=E16f, identf=identf)


def _declare_A(nc):
    io = {}
    decl = [("x_own", [TOW, D], F32), ("x_pref", [S, D], BF16), ("x_qext", [1152, D], BF16),
            ("wq_t", [128, 2048], F32), ("wkv_t", [128, 1024], F32), ("wo_t", [128, D], F32),
            ("v_emb0", [128, 1], F32), ("v_emb1", [128, 1], F32), ("tau", [1, 1], F32),
            ("maskm", [128, 5632], F32), ("invi", [1, TOW], F32), ("jp", [128, 1], F32),
            ("onesf", [1, 128], F32), ("E16f", [16, 128], F32), ("identf", [128, 128], F32)]
    for n, sh, dt in decl:
        io[n] = nc.dram_tensor(n, sh, dt, kind="ExternalInput").ap()
    return io


def build_program_A():
    nc = bass.Bass("TRN2", target_bir_lowering=False)
    stack = ExitStack()
    ctx = Ctx(nc, stack)
    io = _declare_A(nc)
    with stack:
        ph = PhaseA(ctx, io, out_kind="ExternalOutput")
        if DBG_A:
            ctx.dscr("dbg_attn", [128, TOW], F32, kind="ExternalOutput")
            ctx.dscr("dbg_attnb", [128, TOW], BF16, kind="ExternalOutput")
            ctx.dscr("dbg_den", [16, TOW], F32, kind="ExternalOutput")
            ctx.dscr("dbg_qfm", [128, 1152], BF16, kind="ExternalOutput")
            ctx.dscr("dbg_kfm", [32, WIN - 1 + S], BF16, kind="ExternalOutput")
            ctx.dscr("dbg_vs", [128, NJT * 36], BF16, kind="ExternalOutput")
            ctx.dscr("dbg_km", [128, 8], F32, kind="ExternalOutput")
            ctx.dscr("dbg_eb", [128, NJT * 512], BF16, kind="ExternalOutput")
        with nc.Block() as block:
            @block.sync
            def _(e):
                ph.em_sync(e)
                if STAGE >= 5:
                    e.wait_ge(ctx.sems["aOUT"], 32 * NTT)
                elif STAGE >= 2:
                    e.wait_ge(ctx.sems["aKDR"], 16)

            @block.gpsimd
            def _(e):
                ph.em_gpsimd(e)

            @block.tensor
            def _(e):
                ph.em_tensor(e)

            @block.vector
            def _(e):
                ph.em_vector(e)

            @block.scalar
            def _(e):
                ph.em_scalar(e)
    return nc


def run_phase_A(inputs):
    nc = build_program_A()
    in_maps = [_hostprep_A(inputs, c) for c in range(NCORE)]
    res = run_bass_kernel_spmd(nc, in_maps, core_ids=list(range(NCORE)))
    outs = []
    for c in range(NCORE):
        outs.append(np.asarray(res.results[c]["x1f_d"]))
    return outs


def kernel_multi(**inputs):
    cores = list(range(NCORE))
    resA = run_bass_kernel_spmd(build_program_A(),
                                [_hostprep_A(inputs, c) for c in cores],
                                core_ids=cores)
    imB = []
    for c in cores:
        im = _hostprep_B(inputs, c)
        im["x1f_d"] = np.asarray(resA.results[c]["x1f_d"])
        im["x1b_d"] = np.asarray(resA.results[c]["x1b_d"])
        imB.append(im)
    resB = run_bass_kernel_spmd(build_program_B(), imB, core_ids=cores)
    imC = []
    for c in cores:
        im = _hostprep_C(inputs, c)
        im["x2f_d"] = np.asarray(resB.results[c]["x2f_d"])
        im["x2b_d"] = np.asarray(resB.results[c]["x2b_d"])
        imC.append(im)
    resC = run_bass_kernel_spmd(build_program_C(), imC, core_ids=cores)
    imD = []
    for c in cores:
        im = _hostprep_D(inputs, c)
        im["x3f_d"] = np.asarray(resC.results[c]["x3f_d"])
        im["x3b_d"] = np.asarray(resC.results[c]["x3b_d"])
        imD.append(im)
    resD = run_bass_kernel_spmd(build_program_D(), imD, core_ids=cores)
    out = np.zeros((B, S, D), np.float32)
    for c in cores:
        b, s = c // 4, c % 4
        out[b, s * 1024:(s + 1) * 1024] = np.asarray(resD.results[c]["x4f_d"])
    return out


NSEG = 16
ETS = 4
DE = 8192


class PhaseB:

    def __init__(self, ctx: Ctx, io, out_kind="Internal"):
        self.ctx = ctx
        self.io = io
        c = ctx
        self.xT = c.sbuf("B_xT", [128, 16 * 1024], BF16)
        self.win = [c.sbuf(f"B_win{i}", [128, 2048], F32) for i in range(2)]
        self.sgn = c.sbuf("B_sgn", [128, 2048], BF16)
        self.q2 = [c.sbuf(f"B_q2_{i}", [128, 2048], BF16) for i in range(4)]
        self.h8 = c.sbuf("B_h8", [128, ETS * 1024], BF16)
        self.wdin = [c.sbuf(f"B_wdin{i}", [128, 2048], F32) for i in range(2)]
        self.sgd = c.sbuf("B_sgd", [128, 2048], BF16)
        self.q2d = [c.sbuf(f"B_q2d{i}", [128, 2048], BF16) for i in range(ETS)]
        self.dacc = c.sbuf("B_dacc", [128, NTT * 2048], F32)
        self.x1res = c.sbuf("B_x1res", [128, 2048], F32)
        self.x2f = c.sbuf("B_x2f", [128, 2048], F32)
        self.x2b = c.sbuf("B_x2b", [128, 2048], BF16)
        self.racc = c.sbuf("B_racc", [128, 64], F32)
        self.acc3 = c.sbuf("B_acc3", [128, 4], F32)
        self.tot3 = c.sbuf("B_tot3", [3, 8], F32)
        self.scC = c.sbuf("B_scC", [128, 12], F32)
        self.onesf = c.sbuf("B_onesf", [1, 128], F32)
        self.onescol = c.sbuf("B_onescol", [128, 1], F32)
        self.sel3 = c.sbuf("B_sel3", [3, 384], F32)
        self.silu = c.sbuf("B_silu", [128, 1024], BF16)
        self.ln = c.sbuf("B_ln", [128, 8], F32)
        self.p_up = c.psum("B_pup", [128, 1024], F32)
        self.p_g = c.psum("B_pg", [128, 1024], F32)
        self.p_dn = c.psum("B_pdn", [128, 2048], F32)
        self.part_d = c.dscr("B_part", [3, 1], F32)
        self.red_d = c.dscr("B_red", [3, 1], F32)
        self.x2f_d = c.dscr("x2f_d", [TOW, D], F32, kind=out_kind)
        self.x2b_d = c.dscr("x2b_d", [TOW, D], BF16, kind=out_kind)
        self.x1f_d = c.dram["x1f_d"] if "x1f_d" in c.dram else c.dscr("x1f_d", [TOW, D], F32)
        self.x1b_d = c.dram["x1b_d"] if "x1b_d" in c.dram else c.dscr("x1b_d", [TOW, D], BF16)
        c.sem("bSL", "bRD", "bACC", "bTOT", "bCC", "bCCL", "bSCL", "bXT", "bLU",
              "bSGp", "bQ", "bMM", "bSILs", "bGP", "bH", "bLD", "bSGd", "bQD",
              "bDN", "bFL", "bX1R", "bLNV", "bLNS", "bX2", "bOUT", "bVS", "bUP")

    def em_sync(self, e):
        sm, io = self.ctx.sems, self.io
        e.dma_start(out=self.onesf[:, :], in_=io["onesf"]).then_inc(sm["bSL"], 16)
        e.dma_start(out=self.sel3[:, :], in_=io["sel3"]).then_inc(sm["bSL"], 16)
        for m, nm in enumerate(["wu_sl", "wg_sl", "wd_sl"]):
            for st in range(16):
                idx = m * 16 + st
                if idx >= 2:
                    e.wait_ge(sm["bRD"], idx - 1)
                e.dma_start(out=self.win[idx % 2][:, 0:1024],
                            in_=apm(io[nm], st * 1024, [[16384, 128], [1, 1024]])
                            ).then_inc(sm["bSL"], 16)
        for dt in range(16):
            e.dma_start(out=self.xT[:, 1024 * dt:1024 * dt + 1024],
                        in_=apm(self.x1b_d.ap(), 128 * dt, [[D, 1024], [1, 128]]),
                        transpose=True).then_inc(sm["bXT"], 16)
        for seg in range(NSEG):
            for etl in range(ETS):
                etg = seg * ETS + etl
                for w, nm in enumerate(["wu_t", "wg_t"]):
                    li = 2 * etg + w
                    if li >= 2:
                        e.wait_ge(sm["bSGp"], li - 1)
                    else:
                        e.wait_ge(sm["bACC"], 1)
                    e.dma_start(out=self.win[li % 2][:, :],
                                in_=apm(io[nm], 128 * etg,
                                        [[DE, 128], [128 * DE, 16], [1, 128]])
                                ).then_inc(sm["bLU"], 16)
        for t in range(NTT):
            if t >= 1:
                e.wait_ge(sm["bX1R"], t)
            e.dma_start(out=self.x1res[:, :],
                        in_=apm(self.x1f_d.ap(), t * 128 * D, [[D, 128], [1, D]])
                        ).then_inc(sm["bXT"], 16)
            if t >= 1:
                e.wait_ge(sm["bX2"], 2 * t)
                e.dma_start(out=apm(self.x2f_d.ap(), (t - 1) * 128 * D, [[D, 128], [1, D]]),
                            in_=self.x2f[:, :]).then_inc(sm["bOUT"], 16)
                e.dma_start(out=apm(self.x2b_d.ap(), (t - 1) * 128 * D, [[D, 128], [1, D]]),
                            in_=self.x2b[:, :]).then_inc(sm["bOUT"], 16)
        e.wait_ge(sm["bX2"], 2 * NTT)
        e.dma_start(out=apm(self.x2f_d.ap(), (NTT - 1) * 128 * D, [[D, 128], [1, D]]),
                    in_=self.x2f[:, :]).then_inc(sm["bOUT"], 16)
        e.dma_start(out=apm(self.x2b_d.ap(), (NTT - 1) * 128 * D, [[D, 128], [1, D]]),
                    in_=self.x2b[:, :]).then_inc(sm["bOUT"], 16)
        if DBG_B:
            e.wait_ge(sm["bSCL"], 2)
            e.dma_start(out=self.ctx.dram["dbgB_scC"].ap(), in_=self.scC[:, :]).then_inc(sm["bVS"] if False else sm["bOUT"], 16)
            e.dma_start(out=self.ctx.dram["dbgB_tot"].ap(), in_=self.tot3[:, :]).then_inc(sm["bOUT"], 16)
            e.dma_start(out=self.ctx.dram["dbgB_acc"].ap(), in_=self.acc3[:, :]).then_inc(sm["bOUT"], 16)
            e.wait_ge(sm["bFL"], NSEG * NTT)
            e.dma_start(out=self.ctx.dram["dbgB_dacc"].ap(), in_=self.dacc[:, :]).then_inc(sm["bOUT"], 16)
        e.wait_ge(sm["bOUT"], 32 * NTT + (64 if DBG_B else 0))

    def em_gpsimd(self, e):
        sm, io = self.ctx.sems, self.io
        e.wait_ge(sm["bACC"], 2)
        e.dma_start(out=self.part_d.ap(), in_=self.tot3[:, 0:1]).then_inc(sm["bCC"], 16)
        e.wait_ge(sm["bCC"], 16)
        e.collective_compute("AllReduce", ALU.add, replica_groups=[list(range(8))],
                             ins=[self.part_d.ap()], outs=[self.red_d.ap()]
                             ).then_inc(sm["bCC"], 1)
        e.wait_ge(sm["bCC"], 17)
        e.dma_start(out=self.tot3[:, 0:1], in_=self.red_d.ap()).then_inc(sm["bCCL"], 16)
        for seg in range(NSEG):
            for etl in range(ETS):
                eg = seg * ETS + etl
                if eg >= 2:
                    e.wait_ge(sm["bSGd"], eg - 1)
                e.dma_start(out=self.wdin[eg % 2][:, :],
                            in_=apm(io["wd_t"], 128 * eg * D, [[D, 128], [1, D]])
                            ).then_inc(sm["bLD"], 16)

    def em_tensor(self, e):
        sm = self.ctx.sems
        e.wait_ge(sm["bACC"], 1)
        e.matmul(self.p_up[0:3, 0:1], self.acc3[:, 0:3], self.onescol[:, 0:1],
                 start=True, stop=True).then_inc(sm["bTOT"], 1)
        e.wait_ge(sm["bSCL"], 1)
        for i, (m, col) in enumerate([(0, 2), (1, 2), (2, 2), (0, 3), (1, 3), (2, 3)]):
            mm = e.matmul(self.p_g[:, i:i + 1], self.sel3[:, 128 * m:128 * m + 128],
                          self.tot3[:, col:col + 1], start=True, stop=True)
        mm.then_inc(sm["bTOT"], 1)
        e.wait_ge(sm["bXT"], 16 * 16)
        e.wait_ge(sm["bSCL"], 2)
        for seg in range(NSEG):
            for etl in range(ETS):
                etg = seg * ETS + etl
                for w in range(2):
                    li = 2 * etg + w
                    qi = li % 4
                    e.wait_ge(sm["bQ"], li + 1)
                    if w == 0 and etg >= 1:
                        e.wait_ge(sm["bSILs"], etg)
                    if w == 1 and etg >= 1:
                        e.wait_ge(sm["bH"], etg)
                    pb = self.p_up if w == 0 else self.p_g
                    for c2 in range(2):
                        for dt in range(16):
                            mm = e.matmul(pb[:, 512 * c2:512 * c2 + 512],
                                          self.q2[qi][:, 128 * dt:128 * dt + 128],
                                          self.xT[:, 1024 * dt + 512 * c2:1024 * dt + 512 * c2 + 512],
                                          start=(dt == 0), stop=(dt == 15))
                    mm.then_inc(sm["bMM"], 1)
            e.wait_ge(sm["bH"], ETS * (seg + 1))
            e.wait_ge(sm["bQD"], ETS * (seg + 1))
            for tt in range(NTT):
                it = seg * NTT + tt
                if it >= 1:
                    e.wait_ge(sm["bFL"], it)
                for etl in range(ETS):
                    for c4 in range(4):
                        mm = e.matmul(self.p_dn[:, 512 * c4:512 * c4 + 512],
                                      self.h8[:, 1024 * etl + 128 * tt:1024 * etl + 128 * tt + 128],
                                      self.q2d[etl][:, 512 * c4:512 * c4 + 512],
                                      start=(etl == 0), stop=(etl == ETS - 1))
                mm.then_inc(sm["bDN"], 1)

    def em_vector(self, e):
        sm = self.ctx.sems
        self._vsn = 0

        def W(ins):
            self._vsn += 1
            ins.then_inc(sm["bVS"], 1)
            e.wait_ge(sm["bVS"], self._vsn)
        e.memset(self.onescol[:, :], 1.0)
        e.memset(self.scC[:, 6:7], 1e-5)
        e.memset(self.scC[:, 7:8], -0.5)
        e.memset(self.scC[:, 8:9], 0.5)
        for idx in range(48):
            e.wait_ge(sm["bSL"], 16 * (idx + 3))
            e.reduce_sum(self.racc[:, idx:idx + 1], self.win[idx % 2][:, 0:1024], AX,
                         apply_absolute_value=True).then_inc(sm["bRD"], 1)
        for m in range(3):
            W(e.reduce_sum(self.acc3[:, m:m + 1], self.racc[:, 16 * m:16 * m + 16], AX))
        e.engine_nop().then_inc(sm["bACC"], 1)
        e.wait_ge(sm["bTOT"], 1)
        e.tensor_copy(self.tot3[:, 0:1], self.p_up[0:3, 0:1]).then_inc(sm["bACC"], 1)
        e.wait_ge(sm["bCCL"], 16)
        W(e.tensor_scalar(self.tot3[:, 1:2], self.tot3[:, 0:1],
                          1.0 / (2048.0 * 8192.0), 1e-8, ALU.mult, ALU.add))
        W(e.reciprocal(self.tot3[:, 2:3], self.tot3[:, 1:2]))
        W(e.tensor_scalar_mul(self.tot3[:, 3:4], self.tot3[:, 1:2], 0.5))
        e.engine_nop().then_inc(sm["bSCL"], 1)
        e.wait_ge(sm["bTOT"], 2)
        e.tensor_copy(self.scC[:, 0:6], self.p_g[:, 0:6])
        e.engine_nop().then_inc(sm["bSCL"], 1)
        for seg in range(NSEG):
            for etl in range(ETS):
                etg = seg * ETS + etl
                for w in range(2):
                    li = 2 * etg + w
                    qi = li % 4
                    e.wait_ge(sm["bSGp"], li + 1)
                    if li >= 4:
                        e.wait_ge(sm["bMM"], li - 3)
                    e.tensor_add(self.q2[qi][:, :], self.q2[qi][:, :],
                                 self.sgn[:, :]).then_inc(sm["bQ"], 1)
                e.wait_ge(sm["bMM"], 2 * etg + 2)
                e.wait_ge(sm["bSILs"], etg + 1)
                e.scalar_tensor_tensor(self.h8[:, 1024 * etl:1024 * etl + 1024],
                                       self.p_g[:, :], self.scC[:, 4:5],
                                       self.silu[:, :], ALU.mult,
                                       ALU.mult).then_inc(sm["bH"], 1)
            for etl in range(ETS):
                eg = seg * ETS + etl
                e.wait_ge(sm["bSGd"], eg + 1)
                if seg >= 1:
                    e.wait_ge(sm["bDN"], NTT * seg)
                e.tensor_add(self.q2d[etl][:, :], self.q2d[etl][:, :],
                             self.sgd[:, :]).then_inc(sm["bQD"], 1)
            for tt in range(NTT):
                it = seg * NTT + tt
                e.wait_ge(sm["bDN"], it + 1)
                if seg == 0:
                    e.tensor_copy(self.dacc[:, 2048 * tt:2048 * tt + 2048],
                                  self.p_dn[:, :]).then_inc(sm["bFL"], 1)
                else:
                    e.tensor_add(self.dacc[:, 2048 * tt:2048 * tt + 2048],
                                 self.dacc[:, 2048 * tt:2048 * tt + 2048],
                                 self.p_dn[:, :]).then_inc(sm["bFL"], 1)
        e.wait_ge(sm["bFL"], NSEG * NTT)
        for t in range(NTT):
            e.wait_ge(sm["bXT"], 16 * (16 + t + 1))
            dx = self.dacc[:, 2048 * t:2048 * t + 2048]
            e.scalar_tensor_tensor(dx, dx, self.scC[:, 5:6], self.x1res[:, :],
                                   ALU.mult, ALU.add).then_inc(sm["bX1R"], 1)
            W(e.reduce_sum(self.ln[:, 0:1], dx, AX))
            W(e.tensor_scalar_mul(self.ln[:, 1:2], self.ln[:, 0:1], 1.0 / D))
            e.tensor_scalar(dx, dx, self.ln[:, 1:2], None, ALU.subtract)
            if t >= 1:
                e.wait_ge(sm["bOUT"], 32 * t)
            e.tensor_mul(self.x2b[:, :], dx, dx)
            e.reduce_sum(self.ln[:, 2:3], self.x2b[:, :], AX).then_inc(sm["bLNV"], 1)
            e.wait_ge(sm["bLNS"], t + 1)
            W(e.reciprocal(self.ln[:, 4:5], self.ln[:, 3:4]))
            e.tensor_scalar(self.x2f[:, :], dx, self.ln[:, 4:5], None,
                            ALU.mult).then_inc(sm["bX2"], 1)
            e.tensor_scalar(self.x2b[:, :], dx, self.ln[:, 4:5], None,
                            ALU.mult).then_inc(sm["bX2"], 1)

    def em_scalar(self, e):
        sm = self.ctx.sems
        e.wait_ge(sm["bSCL"], 2)
        for seg in range(NSEG):
            for etl in range(ETS):
                etg = seg * ETS + etl
                for w in range(2):
                    li = 2 * etg + w
                    qi = li % 4
                    e.wait_ge(sm["bLU"], 16 * (li + 1))
                    if li >= 4:
                        e.wait_ge(sm["bMM"], li - 3)
                    if li >= 1:
                        e.wait_ge(sm["bQ"], li)
                    sc = self.scC[:, w:w + 1]
                    e.activation(self.q2[qi][:, :], self.win[li % 2][:, :], AF.Sign,
                                 scale=sc, bias=self.scC[:, 7:8])
                    e.activation(self.sgn[:, :], self.win[li % 2][:, :], AF.Sign,
                                 scale=sc, bias=self.scC[:, 8:9]).then_inc(sm["bSGp"], 1)
                e.wait_ge(sm["bMM"], 2 * etg + 1)
                if etg >= 1:
                    e.wait_ge(sm["bH"], etg)
                e.activation(self.silu[:, :], self.p_up[:, :], AF.Silu,
                             scale=self.scC[:, 3:4]).then_inc(sm["bSILs"], 1)
            for etl in range(ETS):
                eg = seg * ETS + etl
                e.wait_ge(sm["bLD"], 16 * (eg + 1))
                if seg >= 1:
                    e.wait_ge(sm["bDN"], NTT * seg)
                if eg >= 1:
                    e.wait_ge(sm["bQD"], eg)
                e.activation(self.q2d[etl][:, :], self.wdin[eg % 2][:, :], AF.Sign,
                             scale=self.scC[:, 2:3], bias=self.scC[:, 7:8])
                e.activation(self.sgd[:, :], self.wdin[eg % 2][:, :], AF.Sign,
                             scale=self.scC[:, 2:3],
                             bias=self.scC[:, 8:9]).then_inc(sm["bSGd"], 1)
        for t in range(NTT):
            e.wait_ge(sm["bLNV"], t + 1)
            e.activation(self.ln[:, 3:4], self.ln[:, 2:3], AF.Sqrt,
                         scale=1.0 / D, bias=self.scC[:, 6:7]).then_inc(sm["bLNS"], 1)


def _hostprep_B(inputs, core):
    wu = np.asarray(inputs["L0_ffn_W_up"], np.float32)
    wg = np.asarray(inputs["L0_ffn_W_gate"], np.float32)
    wd = np.asarray(inputs["L0_ffn_W_down"], np.float32)
    wu_t = np.ascontiguousarray(wu.T)
    wg_t = np.ascontiguousarray(wg.T)
    wd_t = np.ascontiguousarray(wd.T)
    def til(a):
        return np.ascontiguousarray(a.reshape(16, 128, 1024).transpose(1, 0, 2).reshape(128, 16384))
    wu_sl = til(wu_t[:, 1024 * core:1024 * core + 1024])
    wg_sl = til(wg_t[:, 1024 * core:1024 * core + 1024])
    wd_sl = np.ascontiguousarray(
        wd_t[1024 * core:1024 * core + 1024].reshape(8, 128, 2048).transpose(1, 0, 2).reshape(128, 16384))
    sel3 = np.zeros((3, 384), np.float32)
    for m in range(3):
        sel3[m, 128 * m:128 * m + 128] = 1.0
    return dict(wu_t=wu_t, wg_t=wg_t, wd_t=wd_t, wu_sl=wu_sl, wg_sl=wg_sl,
                wd_sl=wd_sl, sel3=sel3, onesf=np.ones((1, 128), np.float32))


def _declare_B(nc, standalone=True):
    io = {}
    decl = [("wu_t", [D, DE], F32), ("wg_t", [D, DE], F32), ("wd_t", [DE, D], F32),
            ("wu_sl", [128, 16384], F32), ("wg_sl", [128, 16384], F32),
            ("wd_sl", [128, 16384], F32), ("sel3", [3, 384], F32)]
    if standalone:
        decl += [("onesf", [1, 128], F32)]
    for n, sh, dt in decl:
        io[n] = nc.dram_tensor(n, sh, dt, kind="ExternalInput").ap()
    return io


def build_program_B():
    nc = bass.Bass("TRN2", target_bir_lowering=False)
    stack = ExitStack()
    ctx = Ctx(nc, stack)
    io = _declare_B(nc)
    ctx.dram["x1f_d"] = nc.dram_tensor("x1f_d", [TOW, D], F32, kind="ExternalInput")
    ctx.dram["x1b_d"] = nc.dram_tensor("x1b_d", [TOW, D], BF16, kind="ExternalInput")
    with stack:
        ph = PhaseB(ctx, io, out_kind="ExternalOutput")
        if DBG_B:
            ctx.dscr("dbgB_scC", [128, 12], F32, kind="ExternalOutput")
            ctx.dscr("dbgB_tot", [3, 8], F32, kind="ExternalOutput")
            ctx.dscr("dbgB_acc", [128, 4], F32, kind="ExternalOutput")
            ctx.dscr("dbgB_dacc", [128, NTT * 2048], F32, kind="ExternalOutput")
        with nc.Block() as block:
            @block.sync
            def _(e):
                ph.em_sync(e)

            @block.gpsimd
            def _(e):
                ph.em_gpsimd(e)

            @block.tensor
            def _(e):
                ph.em_tensor(e)

            @block.vector
            def _(e):
                ph.em_vector(e)

            @block.scalar
            def _(e):
                ph.em_scalar(e)
    return nc


NH = 64
HP = 64
NST = 128
CH = 64
NCH = 16
NMT = 67
MT_ORDER = [66, 64, 65] + list(range(32, 64)) + list(range(32))


class PhaseC:

    def __init__(self, ctx: Ctx, io, out_kind="Internal"):
        self.ctx = ctx
        self.io = io
        c = ctx
        self.xT = c.sbuf("C_xT", [128, 16 * 1024], BF16)
        self.win = [c.sbuf(f"C_win{i}", [128, 2048], BF16) for i in range(2)]
        self.zstg = [c.sbuf(f"C_zstg{i}", [128, 1024], BF16) for i in range(2)]
        self.xbc = [c.sbuf(f"C_xbc{i}", [128, 1032], BF16) for i in range(2)]
        self.xssm = c.sbuf("C_xssm", [128, 1024], BF16)
        self.Bfm = c.sbuf("C_Bfm", [128, 1024], BF16)
        self.Cfm = c.sbuf("C_Cfm", [128, 1024], BF16)
        self.btok = c.sbuf("C_btok", [64, 2048], BF16)
        self.xstg = c.sbuf("C_xstg", [128, 1024], BF16)
        self.dtraw = c.sbuf("C_dtraw", [64, 1024], F32)
        self.dt = c.sbuf("C_dt", [64, 1024], F32)
        self.acs = [c.sbuf(f"C_acs{i}", [64, 1024], F32) for i in range(2)]
        self.acsT = c.sbuf("C_acsT", [64, 1024], F32)
        self.dtT = c.sbuf("C_dtT", [64, 1024], F32)
        self.EA = c.sbuf("C_EA", [64, 1024], F32)
        self.dec = c.sbuf("C_dec", [64, 1024], BF16)
        self.decT = c.sbuf("C_decT", [64, 1024], BF16)
        self.plog = c.sbuf("C_plog", [64, 20], F32)
        self.hout = c.sbuf("C_hout", [128, 102], BF16)
        self.hgb = c.sbuf("C_hgb", [128, 816], BF16)
        self.hsel = c.sbuf("C_hsel", [128, 102], BF16)
        self.xch = c.sbuf("C_xch", [64, 4096], BF16)
        self.xdt = c.sbuf("C_xdt", [64, 4096], BF16)
        self.G = c.sbuf("C_G", [64, 64], BF16)
        self.argf = c.sbuf("C_argf", [64, 1024], F32)
        self.lgcol = c.sbuf("C_lgcol", [64, 1], F32)
        self.M = c.sbuf("C_M", [64, 4096], BF16)
        self.bdec = c.sbuf("C_bdec", [64, 1024], BF16)
        self.run = c.sbuf("C_run", [128, 4096], F32)
        self.runbf = c.sbuf("C_runbf", [128, 4096], BF16)
        self.csc = c.sbuf("C_csc", [128, 4096], BF16)
        self.ybuf = c.sbuf("C_ybuf", [64, 4096], F32)
        self.afl = [c.sbuf(f"C_afl{i}", [1, 1024], F32) for i in range(2)]
        self.afl2 = [c.sbuf(f"C_afl2{i}", [1, 1024], F32) for i in range(2)]
        self.gch = c.sbuf("C_gch", [128, 1024], F32)
        self.pde = c.sbuf("C_pde", [64, 16], F32)
        self.drow = c.sbuf("C_drow", [1, 64], F32)
        self.cdrow = c.sbuf("C_cdrow", [1, 64], F32)
        self.pdrow = c.sbuf("C_pdrow", [1, 64], F32)
        self.wrow = c.sbuf("C_wrow", [1, 64], F32)
        self.cacc = c.sbuf("C_cacc", [128, 1024], BF16)
        self.nwb = c.sbuf("C_nwb", [128, 4096], BF16)
        self.cde = c.sbuf("C_cde", [128, 64], F32)
        self.De = c.sbuf("C_De", [128, 64], F32)
        self.wexp = c.sbuf("C_wexp", [128, 64], F32)
        self.crow = c.sbuf("C_crow", [1, 64], F32)
        self.lgp = c.sbuf("C_lgp", [8, 64], F32)
        self.wT = c.sbuf("C_wT", [64, 8], F32)
        self.mbias = c.sbuf("C_mbias", [64, 8], F32)
        self.mBt = c.sbuf("C_mBt", [8, 8], F32)
        self.trilU = c.sbuf("C_trilU", [64, 64], BF16)
        self.acol = c.sbuf("C_acol", [64, 8], F32)
        self.convw = c.sbuf("C_convw", [128, 136], F32)
        self.convb = c.sbuf("C_convb", [128, 34], F32)
        self.onesf = c.sbuf("C_onesf", [1, 128], F32)
        self.identb = c.sbuf("C_identb", [128, 128], BF16)
        self.identf = c.sbuf("C_identf", [128, 128], F32)
        self.selw = c.sbuf("C_selw", [128, 8], F32)
        self.ln = c.sbuf("C_ln", [128, 8], F32)
        self.zt = c.sbuf_at("C_zt", [128, 4096], BF16, "C_xT", 0)
        self.zsil = c.sbuf_at("C_zsil", [128, 4096], BF16, "C_xT", 8192)
        self.yt = c.sbuf_at("C_yt", [128, 4096], F32, "C_xT", 16384)
        self.ynb = c.sbuf_at("C_ynb", [128, 4096], BF16, "C_M", 0)
        self.ynT = [c.sbuf_at(f"C_ynT{i}", [128, 1024], BF16, "C_xch", 2048 * i) for i in range(4)]
        self.wo = [c.sbuf_at(f"C_wo{i}", [128, 2048], BF16, "C_run", 4096 * i) for i in range(4)]
        self.ostg = c.sbuf_at("C_ostg", [128, 2048], F32, "C_xdt", 0)
        self.x2r = c.sbuf_at("C_x2r", [128, 2048], F32, "C_runbf", 0)
        self.x3f = c.sbuf_at("C_x3f", [128, 2048], F32, "C_ybuf", 0)
        self.x3b = c.sbuf_at("C_x3b", [128, 2048], BF16, "C_ybuf", 8192)
        self.p_a = c.psum("C_pa", [128, 512], F32)
        self.p_b = c.psum("C_pb", [128, 512], F32)
        self.p_G = c.psum("C_pG", [64, 512], F32)
        self.p_S = c.psum("C_pS", [128, 512], F32)
        self.p_x = c.psum("C_px", [128, 512], F32)
        self.p_t = [c.psum(f"C_pt{i}", [128, 512], BF16) for i in range(2)]
        self.p_o = c.psum("C_po", [128, 512], F32)
        self.x_d = c.dscr("C_x_d", [TOW, 4096], BF16)
        self.z_d = c.dscr("C_z_d", [4096, TOW], BF16)
        self.y_d = c.dscr("C_y_d", [TOW, 4096], F32)
        self.acs_d = c.dscr("C_acs_d", [64, 1024], F32)
        self.ea_d = c.dscr("C_ea_d", [64, 1024], F32)
        self.pde_d = c.dscr("C_pde_d", [64, 16], F32)
        self.hout_d = c.dscr("C_hout_d", [128, 102], BF16)
        self.hg_d = c.dscr("C_hg_d", [1024, 102], BF16)
        self.sum_d = c.dscr("C_sum_d", [128, 4096], F32)
        self.sumg_d = c.dscr("C_sumg_d", [1024, 4096], F32)
        self.lg_d = c.dscr("C_lg_d", [64, 1], F32)
        self.lgg_d = c.dscr("C_lgg_d", [512, 1], F32)
        self.wt_d = c.dscr("C_wt_d", [64, 8], F32)
        self.yn_d = c.dscr("C_yn_d", [TOW, 4096], BF16)
        self.o_d = c.dscr("C_o_d", [8 * TOW, D], F32)
        self.prt = c.sbuf_at("C_prt", [128, 2048], F32, "C_xT", 24576)
        self.x3f_d = c.dscr("x3f_d", [TOW, D], F32, kind=out_kind)
        self.x3b_d = c.dscr("x3b_d", [TOW, D], BF16, kind=out_kind)
        self.x2f_d = c.dram["x2f_d"] if "x2f_d" in c.dram else c.dscr("x2f_d", [TOW, D], F32)
        self.x2b_d = c.dram["x2b_d"] if "x2b_d" in c.dram else c.dscr("x2b_d", [TOW, D], BF16)
        c.sem("cP", "cPg", "cXT", "cHM", "cHL", "cHLd", "cHG", "cHS", "cWL",
              "cPJ", "cEVp", "cCVt", "cSILa", "cTR", "cTRe", "cXSD",
              "cZDd", "cTF", "cTFe", "cTB", "cTBe", "cACD", "cACSv", "cEAs",
              "cXC", "cAF", "cBAm", "cBAv", "cARGv", "cMe", "cMv", "cGm", "cGV",
              "cCD", "cCDm", "cCDE", "cEF", "cEFm", "cCSCv", "cBD", "cXDT",
              "cST", "cRU", "cYM", "cYE", "cYD", "cSU", "cAG", "cLG",
              "cWC", "cWE", "cWEd", "cWR", "cWXm", "cCW", "cGC", "cGA",
              "cPD", "cPDm", "cPDE", "cCSCc", "cCSC2v", "cEF2", "cEF2m",
              "cYB2", "cYM2", "cYE2", "cP2d", "cZT", "cYN", "cYND", "cWO",
              "cYNT", "cOM", "cOS", "cOA", "cXRl", "cLNS", "cX3", "cOUT",
              "cVS", "cDE", "cP5a", "cP5b", "cRBv", "cXRc")

    NPS = 11
    NPG = 1

    def bar(self):
        import concourse.mybir as _mb
        ET = _mb.EngineType
        self.ctx.nc.multi_engine_barrier([ET.PE, ET.DVE, ET.Activation, ET.SP, ET.Pool])
        self._bn = getattr(self, "_bn", 0) + 1
        if CCUT and self._bn >= CCUT:
            raise _CutDone()


    XSSM = list(range(32, 64))

    def em_sync(self, e):
        self._bn = 0
        try:
            self._em_sync(e)
        except _CutDone:
            pass

    def _em_sync(self, e):
        sm, io = self.ctx.sems, self.io
        for name, dst in [("onesf", self.onesf), ("identf", self.identf),
                          ("identb", self.identb), ("trilU", self.trilU),
                          ("convw", self.convw), ("convb", self.convb),
                          ("mBt", self.mBt), ("mbias", self.mbias),
                          ("selw", self.selw), ("drow", self.drow)]:
            e.dma_start(out=dst[:, :], in_=io[name]).then_inc(sm["cP"], 16)
        e.dma_start(out=self.acol[:, 0:4], in_=io["acolin"]).then_inc(sm["cP"], 16)
        for dt_ in range(16):
            e.dma_start(out=self.xT[:, 1024 * dt_:1024 * dt_ + 1024],
                        in_=apm(self.x2b_d.ap(), 128 * dt_, [[D, 1024], [1, 128]]),
                        transpose=True).then_inc(sm["cXT"], 16)
        e.wait_ge(sm["cP"], 16 * self.NPS)
        e.wait_ge(sm["cXT"], 16 * 16)
        self.bar()
        e.wait_ge(sm["cHL"], 34)
        e.dma_start(out=self.hout_d.ap(), in_=self.hout[:, :]).then_inc(sm["cHLd"], 16)
        e.wait_ge(sm["cHLd"], 16)
        self.bar()
        self.bar()
        for p in range(8):
            e.dma_start(out=self.hgb[:, 102 * p:102 * p + 102],
                        in_=apm(self.hg_d.ap(), 128 * p * 102, [[102, 128], [1, 102]])
                        ).then_inc(sm["cHS"], 16)
        for mi, m in enumerate(MT_ORDER):
            if m < 32:
                zi = mi - 35
                e.wait_ge(sm["cEVp"], 2 * (mi + 1))
                e.dma_start(out=apm(self.z_d.ap(), 128 * m * TOW, [[TOW, 128], [1, TOW]]),
                            in_=self.zstg[mi % 2][:, :]).then_inc(sm["cZDd"], 16)
            elif 32 <= m < 64:
                sxi = m - 32
                e.wait_ge(sm["cTRe"], 8 * (sxi + 1))
                e.dma_start(out=apm(self.x_d.ap(), 128 * sxi,
                                    [[4096, 128], [128 * 4096, 8], [1, 128]]),
                            in_=apm(self.xstg[:, :], 0, [[1024, 128], [128, 8], [1, 128]])
                            ).then_inc(sm["cXSD"], 16)
        self.bar()
        self.bar()
        self.bar()
        self.bar()
        self.bar()
        e.wait_ge(sm["cACSv"], 1)
        e.dma_start(out=self.acs_d.ap(), in_=self.acs[1][:, :]).then_inc(sm["cACD"], 16)
        e.wait_ge(sm["cEAs"], 1)
        e.dma_start(out=self.ea_d.ap(), in_=self.EA[:, :]).then_inc(sm["cACD"], 16)
        e.wait_ge(sm["cEAs"], 2)
        e.dma_start(out=self.pde_d.ap(), in_=self.pde[:, :]).then_inc(sm["cACD"], 16)
        e.wait_ge(sm["cACD"], 48)
        self.bar()
        for c in range(NCH):
            e.dma_start(out=self.xch[:, :],
                        in_=apm(self.x_d.ap(), 64 * c * 4096, [[4096, 64], [1, 4096]])
                        ).then_inc(sm["cXC"], 16)
            for q in range(4):
                aq = 4 * c + q
                if q >= 2:
                    e.wait_ge(sm["cBAm"], 2 * (aq - 1))
                e.dma_start(out=self.afl[aq % 2][:, :],
                            in_=apm(self.acs_d.ap(), 64 * c + 1024 * 16 * q,
                                    [[1024, 16], [1, 64]])).then_inc(sm["cAF"], 16)
            e.dma_start(out=self.cdrow[:, :],
                        in_=apm(self.ea_d.ap(), 64 * c + 63, [[1024, 64]]),
                        allow_slow_non_contiguous=True).then_inc(sm["cCD"], 16)
            for q in range(4):
                aq = 4 * c + q
                if q >= 2:
                    e.wait_ge(sm["cEFm"], 2 * (aq - 1))
                e.dma_start(out=self.afl2[aq % 2][:, :],
                            in_=apm(self.ea_d.ap(), 64 * c + 1024 * 16 * q,
                                    [[1024, 16], [1, 64]])).then_inc(sm["cEF"], 16)
            e.wait_ge(sm["cYE"], 8 * (c + 1))
            e.dma_start(out=apm(self.y_d.ap(), 64 * c * 4096, [[4096, 64], [1, 4096]]),
                        in_=self.ybuf[:, :]).then_inc(sm["cYD"], 16)
            self.bar()
        e.dma_start(out=self.sum_d.ap(), in_=self.run[:, :]).then_inc(sm["cSU"], 16)
        e.dma_start(out=self.lg_d.ap(), in_=self.lgcol[:, :]).then_inc(sm["cSU"], 16)
        e.wait_ge(sm["cSU"], 32)
        self.bar()
        self.bar()
        e.dma_start(out=self.lgp[:, :],
                    in_=apm(self.lgg_d.ap(), 0, [[64, 8], [1, 64]])).then_inc(sm["cLG"], 16)
        e.wait_ge(sm["cWE"], 2)
        e.dma_start(out=self.wt_d.ap(), in_=self.wT[:, :]).then_inc(sm["cWEd"], 16)
        e.wait_ge(sm["cWEd"], 16)
        self.bar()
        for p in range(8):
            if p >= 1:
                e.wait_ge(sm["cCW"], p)
            e.dma_start(out=self.wrow[:, :],
                        in_=apm(self.wt_d.ap(), p, [[8, 64]]),
                        allow_slow_non_contiguous=True).then_inc(sm["cWR"], 16)
            for q in range(4):
                gi = 4 * p + q
                if gi >= 1:
                    e.wait_ge(sm["cGA"], gi)
                e.dma_start(out=self.gch[:, :],
                            in_=apm(self.sumg_d.ap(), 128 * p * 4096 + 1024 * q,
                                    [[4096, 128], [1, 1024]])).then_inc(sm["cGC"], 16)
        self.bar()
        for c in range(NCH):
            e.dma_start(out=self.ybuf[:, :],
                        in_=apm(self.y_d.ap(), 64 * c * 4096, [[4096, 64], [1, 4096]])
                        ).then_inc(sm["cYB2"], 16)
            e.dma_start(out=self.pdrow[:, :],
                        in_=apm(self.pde_d.ap(), c, [[16, 64]]),
                        allow_slow_non_contiguous=True).then_inc(sm["cPD"], 16)
            for q in range(4):
                aq = 4 * c + q
                if q >= 2:
                    e.wait_ge(sm["cEF2m"], 2 * (aq - 1))
                e.dma_start(out=self.afl2[aq % 2][:, :],
                            in_=apm(self.ea_d.ap(), 64 * c + 1024 * 16 * q,
                                    [[1024, 16], [1, 64]])).then_inc(sm["cEF2"], 16)
            e.wait_ge(sm["cYE2"], 8 * (c + 1))
            e.dma_start(out=apm(self.y_d.ap(), 64 * c * 4096, [[4096, 64], [1, 4096]]),
                        in_=self.ybuf[:, :]).then_inc(sm["cP2d"], 16)
            self.bar()
        for t in range(NTT):
            e.dma_start(out=self.zt[:, :],
                        in_=apm(self.z_d.ap(), 128 * t, [[TOW, 4096], [1, 128]]),
                        transpose=True).then_inc(sm["cZT"], 16)
            e.dma_start(out=self.yt[:, :],
                        in_=apm(self.y_d.ap(), 128 * t * 4096, [[4096, 128], [1, 4096]])
                        ).then_inc(sm["cZT"], 16)
            e.wait_ge(sm["cYN"], t + 1)
            e.dma_start(out=apm(self.yn_d.ap(), 128 * t * 4096, [[4096, 128], [1, 4096]]),
                        in_=self.ynb[:, :]).then_inc(sm["cYND"], 16)
            e.wait_ge(sm["cYND"], 16 * (t + 1))
            self.bar()
        for dseg in range(8):
            for j in range(4):
                di = 4 * dseg + j
                if dseg >= 1:
                    e.wait_ge(sm["cOM"], 32 * dseg)
                e.dma_start(out=self.ynT[j][:, :],
                            in_=apm(self.yn_d.ap(), 128 * di, [[4096, TOW], [1, 128]]),
                            transpose=True).then_inc(sm["cYNT"], 16)
        self.bar()
        for t in range(NTT):
            e.dma_start(out=self.x2r[:, :],
                        in_=apm(self.x2f_d.ap(), 128 * t * D, [[D, 128], [1, D]])
                        ).then_inc(sm["cXRl"], 16)
            for p in range(8):
                pi = 8 * t + p
                buf = self.ostg if p % 2 == 0 else self.prt
                if p >= 2:
                    e.wait_ge(sm["cXRc"], pi - 1)
                e.dma_start(out=buf[:, :],
                            in_=apm(self.o_d.ap(), (p * TOW + 128 * t) * D,
                                    [[D, 128], [1, D]])).then_inc(sm["cXRl"], 16)
            e.wait_ge(sm["cX3"], 2 * (t + 1))
            e.dma_start(out=apm(self.x3f_d.ap(), t * 128 * D, [[D, 128], [1, D]]),
                        in_=self.x3f[:, :]).then_inc(sm["cOUT"], 16)
            e.dma_start(out=apm(self.x3b_d.ap(), t * 128 * D, [[D, 128], [1, D]]),
                        in_=self.x3b[:, :]).then_inc(sm["cOUT"], 16)
            e.wait_ge(sm["cOUT"], 32 * (t + 1))
            self.bar()

    def em_gpsimd(self, e):
        self._bn = 0
        try:
            self._em_gpsimd(e)
        except _CutDone:
            pass

    def _em_gpsimd(self, e):
        sm, io = self.ctx.sems, self.io
        e.dma_start(out=self.nwb[:, :], in_=io["normw_rep"]).then_inc(sm["cPg"], 16)
        e.wait_ge(sm["cPg"], 16)
        self.bar()
        for hi in range(34):
            if hi >= 2:
                e.wait_ge(sm["cHL"], hi - 1)
            m = 32 + hi
            e.dma_start(out=self.win[hi % 2][:, :],
                        in_=apm(io["wip_t"], 128 * m,
                                [[8512, 128], [128 * 8512, 16], [1, 128]])
                        ).then_inc(sm["cWL"], 16)
        self.bar()
        e.collective_compute("AllGather", ALU.bypass, replica_groups=[list(range(8))],
                             ins=[self.hout_d.ap()], outs=[self.hg_d.ap()]
                             ).then_inc(sm["cHG"], 1)
        e.wait_ge(sm["cHG"], 1)
        self.bar()
        for mi, m in enumerate(MT_ORDER):
            li = 34 + mi
            if mi >= 2:
                e.wait_ge(sm["cEVp"], 2 * (mi - 1))
            if m < 66:
                e.dma_start(out=self.win[li % 2][:, :],
                            in_=apm(io["wip_t"], 128 * m,
                                    [[8512, 128], [128 * 8512, 16], [1, 128]])
                            ).then_inc(sm["cWL"], 16)
            else:
                e.dma_start(out=apm(self.win[li % 2][:, :], 0,
                                    [[2048, 128], [64, 16], [1, 64]]),
                            in_=apm(io["wip_t"], 128 * m,
                                    [[8512, 128], [128 * 8512, 16], [1, 64]])
                            ).then_inc(sm["cWL"], 16)
        for b in range(6 + NCH + 1):
            self.bar()
        e.collective_compute("AllGather", ALU.bypass, replica_groups=[list(range(8))],
                             ins=[self.sum_d.ap()], outs=[self.sumg_d.ap()]
                             ).then_inc(sm["cAG"], 1)
        e.collective_compute("AllGather", ALU.bypass, replica_groups=[list(range(8))],
                             ins=[self.lg_d.ap()], outs=[self.lgg_d.ap()]
                             ).then_inc(sm["cAG"], 1)
        e.wait_ge(sm["cAG"], 2)
        self.bar()
        self.bar()
        self.bar()
        for c in range(NCH):
            self.bar()
        for t in range(NTT):
            self.bar()
        for dseg in range(8):
            for j in range(4):
                di = 4 * dseg + j
                if dseg >= 1:
                    e.wait_ge(sm["cOM"], 32 * dseg)
                e.dma_start(out=self.wo[j][:, :],
                            in_=apm(io["wout_t"], 128 * di * D, [[D, 128], [1, D]])
                            ).then_inc(sm["cWO"], 16)
            for tt in range(NTT):
                it = dseg * NTT + tt
                e.wait_ge(sm["cOS"], 4 * (it + 1))
                e.dma_start(out=apm(self.o_d.ap(), (dseg * TOW + 128 * tt) * D,
                                    [[D, 128], [1, D]]),
                            in_=self.ostg[:, :]).then_inc(sm["cOA"], 16)
        self.bar()
        for t in range(NTT):
            self.bar()

    def em_tensor(self, e):
        self._bn = 0
        try:
            self._em_tensor(e)
        except _CutDone:
            pass

    def _em_tensor(self, e):
        sm = self.ctx.sems
        self.bar()
        e.matmul(self.p_x[:, 0:64], self.onesf[0:1, :], self.drow[0:1, :],
                 start=True, stop=True).then_inc(sm["cDE"], 1)
        for hi in range(34):
            e.wait_ge(sm["cWL"], 16 * (hi + 1))
            if hi >= 2:
                e.wait_ge(sm["cHL"], hi - 1)
            pb = self.p_a if hi % 2 == 0 else self.p_b
            for dt_ in range(16):
                mm = e.matmul(pb[:, 0:3], self.win[hi % 2][:, 128 * dt_:128 * dt_ + 128],
                              self.xT[:, 1024 * dt_ + 1021:1024 * dt_ + 1024],
                              start=(dt_ == 0), stop=(dt_ == 15))
            mm.then_inc(sm["cHM"], 1)
        self.bar()
        self.bar()
        for mi, m in enumerate(MT_ORDER):
            li = 34 + mi
            e.wait_ge(sm["cWL"], 16 * (li + 1))
            for nc2 in range(2):
                pidx = 2 * mi + nc2
                pb = self.p_a if pidx % 2 == 0 else self.p_b
                if pidx >= 2:
                    e.wait_ge(sm["cEVp"], pidx - 1)
                for dt_ in range(16):
                    if m < 66:
                        mm = e.matmul(pb[:, :], self.win[li % 2][:, 128 * dt_:128 * dt_ + 128],
                                      self.xT[:, 1024 * dt_ + 512 * nc2:1024 * dt_ + 512 * nc2 + 512],
                                      start=(dt_ == 0), stop=(dt_ == 15))
                    else:
                        mm = e.matmul(pb[0:64, :], self.win[li % 2][:, 64 * dt_:64 * dt_ + 64],
                                      self.xT[:, 1024 * dt_ + 512 * nc2:1024 * dt_ + 512 * nc2 + 512],
                                      start=(dt_ == 0), stop=(dt_ == 15))
                mm.then_inc(sm["cPJ"], 1)
            if 32 <= m < 64:
                sxi = m - 32
                e.wait_ge(sm["cSILa"], sxi + 3)
                for tt in range(8):
                    g = 8 * sxi + tt
                    if g >= 2:
                        e.wait_ge(sm["cTRe"], g - 1)
                    e.transpose(self.p_t[g % 2][:, 0:128], self.xssm[:, 128 * tt:128 * tt + 128],
                                self.identb[:, :]).then_inc(sm["cTR"], 1)
        self.bar()
        self.bar()
        self.bar()
        self.bar()
        self.bar()
        for i in range(16):
            if i >= 1:
                e.wait_ge(sm["cTFe"], i)
            e.transpose(self.p_x[0:64, 0:64], self.acs[1][:, 64 * i:64 * i + 64],
                        self.identf[0:64, 0:64]).then_inc(sm["cTF"], 1)
        for i in range(16):
            e.wait_ge(sm["cTFe"], 16 + i)
            e.transpose(self.p_x[0:64, 0:64], self.dt[:, 64 * i:64 * i + 64],
                        self.identf[0:64, 0:64]).then_inc(sm["cTF"], 1)
        for i in range(16):
            if i >= 2:
                e.wait_ge(sm["cTBe"], i - 1)
            e.transpose(self.p_t[i % 2][0:64, 0:64], self.dec[:, 64 * i:64 * i + 64],
                        self.identb[0:64, 0:64]).then_inc(sm["cTB"], 1)
        for i in range(16):
            e.wait_ge(sm["cTBe"], 15 + i)
            e.transpose(self.p_t[i % 2][0:64, 0:128], self.Bfm[:, 64 * i:64 * i + 64],
                        self.identb[:, :]).then_inc(sm["cTB"], 1)
        self.bar()
        for c in range(NCH):
            e.matmul(self.p_G[:, 0:64], self.Bfm[:, 64 * c:64 * c + 64],
                     self.Cfm[:, 64 * c:64 * c + 64], start=True,
                     stop=True).then_inc(sm["cGm"], 1)
            for q in range(4):
                aq = 4 * c + q
                e.wait_ge(sm["cAF"], 16 * (aq + 1))
                if q >= 1:
                    e.wait_ge(sm["cBAv"], 8 * c + 2 * q)
                e.matmul(self.p_a[0:64, :], self.onesf[0:1, 0:64],
                         self.afl[aq % 2][0:1, 0:512], start=True,
                         stop=True).then_inc(sm["cBAm"], 1)
                e.matmul(self.p_b[0:64, :], self.onesf[0:1, 0:64],
                         self.afl[aq % 2][0:1, 512:1024], start=True,
                         stop=True).then_inc(sm["cBAm"], 1)
            e.wait_ge(sm["cCD"], 16 * (c + 1))
            if c >= 1:
                e.wait_ge(sm["cCDE"], c)
            e.matmul(self.p_x[:, 0:64], self.onesf[0:1, :], self.cdrow[0:1, :],
                     start=True, stop=True).then_inc(sm["cCDm"], 1)
            e.wait_ge(sm["cBAv"], 8 * (c + 1))
            for q in range(4):
                aq = 4 * c + q
                e.wait_ge(sm["cEF"], 16 * (aq + 1))
                if q >= 1:
                    e.wait_ge(sm["cCSCv"], 8 * c + 2 * q)
                e.matmul(self.p_a[:, :], self.onesf[0:1, :],
                         self.afl2[aq % 2][0:1, 0:512], start=True,
                         stop=True).then_inc(sm["cEFm"], 1)
                e.matmul(self.p_b[:, :], self.onesf[0:1, :],
                         self.afl2[aq % 2][0:1, 512:1024], start=True,
                         stop=True).then_inc(sm["cEFm"], 1)
            for hg in range(8):
                ih = 8 * c + hg
                e.wait_ge(sm["cBD"], ih + 1)
                e.wait_ge(sm["cXDT"], c + 1)
                if hg >= 1:
                    e.wait_ge(sm["cRU"], ih)
                for hl in range(8):
                    h = 8 * hg + hl
                    mm = e.matmul(self.p_S[:, 64 * hl:64 * hl + 64],
                                  self.bdec[:, 128 * hl:128 * hl + 128],
                                  self.xdt[:, 64 * h:64 * h + 64],
                                  start=True, stop=True)
                mm.then_inc(sm["cST"], 1)
            for hg in range(8):
                ih = 8 * c + hg
                e.wait_ge(sm["cMv"], 4 * (c + 1))
                e.wait_ge(sm["cCSCv"], 8 * (c + 1))
                e.wait_ge(sm["cRBv"], c + 1)
                e.wait_ge(sm["cGV"], c + 1)
                if hg >= 1:
                    e.wait_ge(sm["cYE"], ih)
                for hl in range(8):
                    h = 8 * hg + hl
                    e.matmul(self.p_G[:, 64 * hl:64 * hl + 64],
                             self.M[:, 64 * h:64 * h + 64],
                             self.xdt[:, 64 * h:64 * h + 64], start=True, stop=False)
                    mm = e.matmul(self.p_G[:, 64 * hl:64 * hl + 64],
                                  self.csc[:, 64 * h:64 * h + 64],
                                  self.runbf[:, 64 * h:64 * h + 64], start=False, stop=True)
                mm.then_inc(sm["cYM"], 1)
            self.bar()
        self.bar()
        self.bar()
        e.wait_ge(sm["cLG"], 16)
        e.matmul(self.p_x[0:64, 0:8], self.lgp[:, :], self.mBt[:, :],
                 start=True, stop=True).then_inc(sm["cWC"], 1)
        self.bar()
        for p in range(8):
            e.wait_ge(sm["cWR"], 16 * (p + 1))
            if p >= 1:
                e.wait_ge(sm["cCW"], p)
            e.matmul(self.p_x[:, 0:64], self.onesf[0:1, :], self.wrow[0:1, :],
                     start=True, stop=True).then_inc(sm["cWXm"], 1)
        self.bar()
        for c in range(NCH):
            e.wait_ge(sm["cPD"], 16 * (c + 1))
            if c >= 1:
                e.wait_ge(sm["cPDE"], c)
            e.matmul(self.p_x[:, 0:64], self.onesf[0:1, :], self.pdrow[0:1, :],
                     start=True, stop=True).then_inc(sm["cPDm"], 1)
            for q in range(4):
                aq = 4 * c + q
                e.wait_ge(sm["cEF2"], 16 * (aq + 1))
                if q >= 1:
                    e.wait_ge(sm["cCSC2v"], 8 * c + 2 * q)
                e.matmul(self.p_a[:, :], self.onesf[0:1, :],
                         self.afl2[aq % 2][0:1, 0:512], start=True,
                         stop=True).then_inc(sm["cEF2m"], 1)
                e.matmul(self.p_b[:, :], self.onesf[0:1, :],
                         self.afl2[aq % 2][0:1, 512:1024], start=True,
                         stop=True).then_inc(sm["cEF2m"], 1)
            for hg in range(8):
                ih = 8 * c + hg
                e.wait_ge(sm["cCSC2v"], 8 * (c + 1))
                e.wait_ge(sm["cCSCc"], c + 1)
                e.wait_ge(sm["cYB2"], 16 * (c + 1))
                if hg >= 1:
                    e.wait_ge(sm["cYE2"], ih)
                for hl in range(8):
                    h = 8 * hg + hl
                    mm = e.matmul(self.p_G[:, 64 * hl:64 * hl + 64],
                                  self.csc[:, 64 * h:64 * h + 64],
                                  self.runbf[:, 64 * h:64 * h + 64],
                                  start=True, stop=True)
                mm.then_inc(sm["cYM2"], 1)
            self.bar()
        for t in range(NTT):
            self.bar()
        for dseg in range(8):
            e.wait_ge(sm["cWO"], 64 * (dseg + 1))
            e.wait_ge(sm["cYNT"], 64 * (dseg + 1))
            for tt in range(NTT):
                it = dseg * NTT + tt
                for q4 in range(4):
                    pq = 4 * it + q4
                    po = self.p_o if q4 % 2 == 0 else self.p_S
                    if pq >= 2:
                        e.wait_ge(sm["cOS"], pq - 1)
                    for j in range(4):
                        mm = e.matmul(po[:, 0:512],
                                      self.ynT[j][:, 128 * tt:128 * tt + 128],
                                      self.wo[j][:, 512 * q4:512 * q4 + 512],
                                      start=(j == 0), stop=(j == 3))
                    mm.then_inc(sm["cOM"], 1)
        self.bar()
        for t in range(NTT):
            self.bar()

    def em_vector(self, e):
        self._bn = 0
        try:
            self._em_vector(e)
        except _CutDone:
            pass

    def _em_vector(self, e):
        sm = self.ctx.sems
        self._vsn = 0

        def W(ins):
            self._vsn += 1
            ins.then_inc(sm["cVS"], 1)
            e.wait_ge(sm["cVS"], self._vsn)
        e.memset(self.ln[:, 6:7], 1e-5)
        self.bar()
        e.wait_ge(sm["cDE"], 1)
        W(e.tensor_copy(self.De[:, :], self.p_x[:, 0:64]))
        e.memset(self.run[:, :], 0.0)
        self.bar()
        self.bar()
        e.wait_ge(sm["cHS"], 128)
        W(e.tensor_scalar(self.hsel[:, :], self.hgb[:, 0:102], self.selw[:, 0:1],
                          None, ALU.mult))
        for p in range(1, 8):
            W(e.scalar_tensor_tensor(self.hsel[:, :], self.hgb[:, 102 * p:102 * p + 102],
                                     self.selw[:, p:p + 1], self.hsel[:, :],
                                     ALU.mult, ALU.add))
        xbi = 0
        for mi, m in enumerate(MT_ORDER):
            if not (32 <= m < 66):
                continue
            ct = (m - 32) if m < 64 else (32 if m == 64 else 33)
            e.wait_ge(sm["cEVp"], 2 * (mi + 1))
            if xbi >= 1:
                e.wait_ge(sm["cSILa"], xbi)
            W(e.tensor_copy(self.xbc[mi % 2][:, 1:4], self.hsel[:, 3 * ct:3 * ct + 3]))
            b = self.xbc[mi % 2]
            e.tensor_scalar(self.cacc[:, :], b[:, 1:1025], self.convw[:, 4 * ct:4 * ct + 1],
                            None, ALU.mult)
            for k in range(1, 4):
                ins = e.scalar_tensor_tensor(self.cacc[:, :], b[:, 1 + k:1025 + k],
                                             self.convw[:, 4 * ct + k:4 * ct + k + 1],
                                             self.cacc[:, :], ALU.mult, ALU.add)
            ins.then_inc(sm["cCVt"], 1)
            xbi += 1
        self.bar()
        e.tensor_scalar(self.dtraw[:, :], self.dtraw[:, :], self.acol[:, 1:2],
                        None, ALU.add)
        self.bar()
        self.bar()
        e.tensor_scalar_max(self.dt[:, :], self.dtraw[:, :], 0.0)
        e.tensor_add(self.dt[:, :], self.dt[:, :], self.acs[0][:, :])
        e.tensor_scalar(self.acs[1][:, :], self.dt[:, :], self.acol[:, 2:3], -1.0,
                        ALU.mult, ALU.mult)
        srcb, dstb = 1, 0
        for k in [1, 2, 4, 8, 16, 32]:
            for c in range(NCH):
                c0 = 64 * c
                e.tensor_copy(self.acs[dstb][:, c0:c0 + k], self.acs[srcb][:, c0:c0 + k])
                ins = e.tensor_add(self.acs[dstb][:, c0 + k:c0 + 64],
                                   self.acs[srcb][:, c0 + k:c0 + 64],
                                   self.acs[srcb][:, c0:c0 + 64 - k])
            W(ins)
            srcb, dstb = dstb, srcb
        W(e.memset(self.plog[:, 0:1], 0.0))
        for c in range(NCH):
            W(e.tensor_add(self.plog[:, c + 1:c + 2], self.plog[:, c:c + 1],
                           self.acs[1][:, 64 * c + 63:64 * c + 64]))
        lastap = apm(self.acs[1][:, :], 63, [[1024, 64], [64, 16]])
        W(e.reduce_sum(self.lgcol[:, :], lastap, AX))
        bc = apm(self.acs[1][:, :], 63, [[1024, 64], [64, 16], [0, 64]])
        e.tensor_tensor(self.dtraw[:, :], bc, self.acs[1][:, :],
                        ALU.subtract).then_inc(sm["cACSv"], 1)
        self.bar()
        self.bar()
        self.bar()
        for c in range(NCH):
            e.tensor_copy(self.runbf[:, :], self.run[:, :]).then_inc(sm["cRBv"], 1)
            e.wait_ge(sm["cGm"], c + 1)
            e.tensor_mul(self.G[:, :], self.p_G[0:64, 0:64], self.trilU[:, :]
                         ).then_inc(sm["cGV"], 1)
            e.wait_ge(sm["cGV"], c + 1)
            e.wait_ge(sm["cXC"], 16 * (c + 1))
            dtap = dataclasses.replace(self.dtT[:, :], offset=self.dtT[:, :].offset + 64 * c,
                                       ap=[[1024, 64], [1, 64], [0, 64]])
            e.tensor_tensor(self.xdt[:, :], self.xch[:, :], dtap,
                            ALU.mult).then_inc(sm["cXDT"], 1)
            for q in range(4):
                aq = 4 * c + q
                e.wait_ge(sm["cBAm"], 2 * (aq + 1))
                for piece, pb in [(0, self.p_a), (1, self.p_b)]:
                    hoff = 16 * q + 8 * piece
                    aap = dataclasses.replace(self.acsT[:, :],
                                              offset=self.acsT[:, :].offset + 64 * c + hoff,
                                              ap=[[1024, 64], [1, 8], [0, 64]])
                    ins = e.tensor_tensor(self.argf[:, 512 * piece:512 * piece + 512],
                                          pb[0:64, :], aap, ALU.subtract)
                    ins.then_inc(sm["cBAv"], 1)
                e.tensor_scalar_min(self.argf[:, :], self.argf[:, :],
                                    0.0).then_inc(sm["cARGv"], 1)
                e.wait_ge(sm["cMe"], aq + 1)
                gap = dataclasses.replace(self.G[:, :], ap=[[64, 64], [0, 16], [1, 64]])
                e.tensor_mul(self.M[:, 1024 * q:1024 * q + 1024],
                             self.M[:, 1024 * q:1024 * q + 1024], gap
                             ).then_inc(sm["cMv"], 1)
            e.wait_ge(sm["cCDm"], c + 1)
            e.tensor_copy(self.cde[:, :], self.p_x[:, 0:64]).then_inc(sm["cCDE"], 1)
            e.wait_ge(sm["cCDE"], c + 1)
            for q in range(4):
                aq = 4 * c + q
                e.wait_ge(sm["cEFm"], 2 * (aq + 1))
                cap = dataclasses.replace(self.Cfm[:, 64 * c:64 * c + 64],
                                          ap=[[1024, 128], [0, 8], [1, 64]])
                for piece, pb in [(0, self.p_a), (1, self.p_b)]:
                    ins = e.tensor_tensor(self.csc[:, 1024 * q + 512 * piece:1024 * q + 512 * piece + 512],
                                          cap, pb[:, :], ALU.mult)
                    ins.then_inc(sm["cCSCv"], 1)
            for hg in range(8):
                ih = 8 * c + hg
                btap = dataclasses.replace(self.btok[:, :],
                                           offset=self.btok[:, :].offset + 128 * c,
                                           ap=[[2048, 64], [0, 8], [1, 128]])
                dcap = dataclasses.replace(self.decT[:, :],
                                           offset=self.decT[:, :].offset + 64 * c + 8 * hg,
                                           ap=[[1024, 64], [1, 8], [0, 128]])
                e.tensor_tensor(self.bdec[:, :], btap, dcap, ALU.mult).then_inc(sm["cBD"], 1)
            for hg in range(8):
                ih = 8 * c + hg
                e.wait_ge(sm["cST"], ih + 1)
                cdap = dataclasses.replace(self.cde[:, :], offset=self.cde[:, :].offset + 8 * hg,
                                           ap=[[64, 128], [1, 8], [0, 64]])
                e.tensor_tensor(self.run[:, 512 * hg:512 * hg + 512],
                                self.run[:, 512 * hg:512 * hg + 512], cdap, ALU.mult)
                e.tensor_add(self.run[:, 512 * hg:512 * hg + 512],
                             self.run[:, 512 * hg:512 * hg + 512],
                             self.p_S[:, :]).then_inc(sm["cRU"], 1)
            for hg in range(8):
                ih = 8 * c + hg
                e.wait_ge(sm["cYM"], ih + 1)
                dap = dataclasses.replace(self.De[:, :], offset=self.De[:, :].offset + 8 * hg,
                                          ap=[[64, 64], [1, 8], [0, 64]])
                e.tensor_tensor(self.ybuf[:, 512 * hg:512 * hg + 512],
                                self.xch[:, 512 * hg:512 * hg + 512], dap, ALU.mult)
                e.tensor_add(self.ybuf[:, 512 * hg:512 * hg + 512],
                             self.ybuf[:, 512 * hg:512 * hg + 512],
                             self.p_G[:, :]).then_inc(sm["cYE"], 1)
            self.bar()
        self.bar()
        self.bar()
        e.wait_ge(sm["cWC"], 1)
        W(e.tensor_copy(self.wT[:, :], self.p_x[0:64, 0:8]))
        e.tensor_add(self.wT[:, :], self.wT[:, :], self.mbias[:, :]).then_inc(sm["cWE"], 1)
        self.bar()
        e.memset(self.run[:, :], 0.0)
        for p in range(8):
            e.wait_ge(sm["cWXm"], p + 1)
            e.tensor_copy(self.wexp[:, :], self.p_x[:, 0:64]).then_inc(sm["cCW"], 1)
            e.wait_ge(sm["cCW"], p + 1)
            for q in range(4):
                gi = 4 * p + q
                e.wait_ge(sm["cGC"], 16 * (gi + 1))
                wap = dataclasses.replace(self.wexp[:, :],
                                          offset=self.wexp[:, :].offset + 16 * q,
                                          ap=[[64, 128], [1, 16], [0, 64]])
                e.tensor_mul(self.gch[:, :], self.gch[:, :], wap)
                e.tensor_add(self.run[:, 1024 * q:1024 * q + 1024],
                             self.run[:, 1024 * q:1024 * q + 1024],
                             self.gch[:, :]).then_inc(sm["cGA"], 1)
        self.bar()
        for c in range(NCH):
            e.wait_ge(sm["cPDm"], c + 1)
            e.tensor_copy(self.cde[:, :], self.p_x[:, 0:64]).then_inc(sm["cPDE"], 1)
            e.wait_ge(sm["cPDE"], c + 1)
            cdap = dataclasses.replace(self.cde[:, :], ap=[[64, 128], [1, 64], [0, 64]])
            e.tensor_tensor(self.runbf[:, :], self.run[:, :], cdap,
                            ALU.mult).then_inc(sm["cCSCc"], 1)
            for q in range(4):
                aq = 4 * c + q
                e.wait_ge(sm["cEF2m"], 2 * (aq + 1))
                cap = dataclasses.replace(self.Cfm[:, 64 * c:64 * c + 64],
                                          ap=[[1024, 128], [0, 8], [1, 64]])
                for piece, pb in [(0, self.p_a), (1, self.p_b)]:
                    ins = e.tensor_tensor(self.csc[:, 1024 * q + 512 * piece:1024 * q + 512 * piece + 512],
                                          cap, pb[:, :], ALU.mult)
                    ins.then_inc(sm["cCSC2v"], 1)
            for hg in range(8):
                ih = 8 * c + hg
                e.wait_ge(sm["cYM2"], ih + 1)
                e.tensor_add(self.ybuf[:, 512 * hg:512 * hg + 512],
                             self.ybuf[:, 512 * hg:512 * hg + 512],
                             self.p_G[:, :]).then_inc(sm["cYE2"], 1)
            self.bar()
        for t in range(NTT):
            e.wait_ge(sm["cP5a"], t + 1)
            e.tensor_mul(self.yt[:, :], self.yt[:, :], self.zsil[:, :])
            e.tensor_mul(self.ynb[:, :], self.yt[:, :], self.yt[:, :])
            e.reduce_sum(self.ln[:, 2:3], self.ynb[:, :], AX).then_inc(sm["cP5b"], 1)
            e.wait_ge(sm["cLNS"], t + 1)
            W(e.reciprocal(self.ln[:, 4:5], self.ln[:, 3:4]))
            e.scalar_tensor_tensor(self.ynb[:, :], self.yt[:, :], self.ln[:, 4:5],
                                   self.nwb[:, :], ALU.mult,
                                   ALU.mult).then_inc(sm["cYN"], 1)
            self.bar()
        for dseg in range(8):
            for tt in range(NTT):
                it = dseg * NTT + tt
                for q4 in range(4):
                    pq = 4 * it + q4
                    po = self.p_o if q4 % 2 == 0 else self.p_S
                    e.wait_ge(sm["cOM"], pq + 1)
                    if it >= 1 and q4 == 0:
                        e.wait_ge(sm["cOA"], 16 * it)
                    e.tensor_copy(self.ostg[:, 512 * q4:512 * q4 + 512],
                                  po[:, 0:512]).then_inc(sm["cOS"], 1)
        self.bar()
        for t in range(NTT):
            dx = self.x2r[:, :]
            for p in range(8):
                pi = 8 * t + p
                buf = self.ostg if p % 2 == 0 else self.prt
                e.wait_ge(sm["cXRl"], 16 * (9 * t + p + 2))
                e.tensor_add(dx, dx, buf[:, :]).then_inc(sm["cXRc"], 1)
            W(e.reduce_sum(self.ln[:, 0:1], dx, AX))
            W(e.tensor_scalar_mul(self.ln[:, 1:2], self.ln[:, 0:1], 1.0 / D))
            e.tensor_scalar(dx, dx, self.ln[:, 1:2], None, ALU.subtract)
            e.tensor_mul(self.x3b[:, :], dx, dx)
            e.reduce_sum(self.ln[:, 2:3], self.x3b[:, :], AX).then_inc(sm["cP5b"], 1)
            e.wait_ge(sm["cLNS"], NTT + t + 1)
            W(e.reciprocal(self.ln[:, 4:5], self.ln[:, 3:4]))
            e.tensor_scalar(self.x3f[:, :], dx, self.ln[:, 4:5], None,
                            ALU.mult).then_inc(sm["cX3"], 1)
            e.tensor_scalar(self.x3b[:, :], dx, self.ln[:, 4:5], None,
                            ALU.mult).then_inc(sm["cX3"], 1)
            self.bar()

    def em_scalar(self, e):
        self._bn = 0
        try:
            self._em_scalar(e)
        except _CutDone:
            pass

    def _em_scalar(self, e):
        sm = self.ctx.sems
        self.bar()
        for hi in range(34):
            e.wait_ge(sm["cHM"], hi + 1)
            e.copy(self.hout[:, 3 * hi:3 * hi + 3],
                   (self.p_a if hi % 2 == 0 else self.p_b)[:, 0:3]).then_inc(sm["cHL"], 1)
        self.bar()
        self.bar()
        xbi = 0
        sxi = 0
        zi = 0
        for mi, m in enumerate(MT_ORDER):
            for nc2 in range(2):
                pidx = 2 * mi + nc2
                pb = self.p_a if pidx % 2 == 0 else self.p_b
                e.wait_ge(sm["cPJ"], pidx + 1)
                if m == 66:
                    e.copy(self.dtraw[:, 512 * nc2:512 * nc2 + 512],
                           pb[0:64, :]).then_inc(sm["cEVp"], 1)
                elif m >= 32:
                    if xbi >= 2 and nc2 == 0:
                        e.wait_ge(sm["cCVt"], xbi - 1)
                    e.copy(self.xbc[mi % 2][:, 4 + 512 * nc2:4 + 512 * nc2 + 512],
                           pb[:, :]).then_inc(sm["cEVp"], 1)
                else:
                    if zi >= 2 and nc2 == 0:
                        e.wait_ge(sm["cZDd"], 16 * (zi - 1))
                    e.copy(self.zstg[mi % 2][:, 512 * nc2:512 * nc2 + 512],
                           pb[:, :]).then_inc(sm["cEVp"], 1)
            if 32 <= m < 66:
                ct = (m - 32) if m < 64 else (32 if m == 64 else 33)
                e.wait_ge(sm["cCVt"], xbi + 1)
                if m < 64:
                    if sxi >= 1:
                        e.wait_ge(sm["cTR"], 8 * sxi)
                    dst = self.xssm[:, :]
                elif m == 64:
                    dst = self.Bfm[:, :]
                else:
                    dst = self.Cfm[:, :]
                e.activation(dst, self.cacc[:, :], AF.Silu,
                             bias=self.convb[:, ct:ct + 1]).then_inc(sm["cSILa"], 1)
                if m < 64:
                    for tt in range(8):
                        g = 8 * sxi + tt
                        e.wait_ge(sm["cTR"], g + 1)
                        e.copy(self.xstg[:, 128 * tt:128 * tt + 128],
                               self.p_t[g % 2][:, 0:128]).then_inc(sm["cTRe"], 1)
                    sxi += 1
                xbi += 1
            if m < 32:
                zi += 1
        self.bar()
        self.bar()
        e.activation(self.acs[1][:, :], self.dtraw[:, :], AF.Abs)
        e.activation(self.EA[:, :], self.acs[1][:, :], AF.Exp, scale=-1.0)
        e.activation(self.acs[0][:, :], self.EA[:, :], AF.Ln,
                     bias=self.acol[:, 3:4])
        e.activation(self.acol[:, 2:3], self.acol[:, 0:1], AF.Exp)
        self.bar()
        self.bar()
        e.activation(self.EA[:, :], self.acs[1][:, :], AF.Exp).then_inc(sm["cEAs"], 1)
        e.activation(self.dec[:, :], self.dtraw[:, :], AF.Exp)
        e.activation(self.pde[:, :], self.plog[:, 0:16], AF.Exp).then_inc(sm["cEAs"], 1)
        self.bar()
        for i in range(16):
            e.wait_ge(sm["cTF"], i + 1)
            e.copy(self.acsT[:, 64 * i:64 * i + 64], self.p_x[0:64, 0:64]).then_inc(sm["cTFe"], 1)
        for i in range(16):
            e.wait_ge(sm["cTF"], 16 + i + 1)
            e.copy(self.dtT[:, 64 * i:64 * i + 64], self.p_x[0:64, 0:64]).then_inc(sm["cTFe"], 1)
        for i in range(16):
            e.wait_ge(sm["cTB"], i + 1)
            e.copy(self.decT[:, 64 * i:64 * i + 64], self.p_t[i % 2][0:64, 0:64]).then_inc(sm["cTBe"], 1)
        for i in range(16):
            e.wait_ge(sm["cTB"], 16 + i + 1)
            e.copy(self.btok[:, 128 * i:128 * i + 128],
                   self.p_t[i % 2][0:64, 0:128]).then_inc(sm["cTBe"], 1)
        self.bar()
        for c in range(NCH):
            for q in range(4):
                aq = 4 * c + q
                e.wait_ge(sm["cARGv"], aq + 1)
                e.activation(self.M[:, 1024 * q:1024 * q + 1024], self.argf[:, :],
                             AF.Exp).then_inc(sm["cMe"], 1)
            self.bar()
        self.bar()
        self.bar()
        e.wait_ge(sm["cWE"], 1)
        e.activation(self.wT[:, :], self.wT[:, :], AF.Exp).then_inc(sm["cWE"], 1)
        self.bar()
        self.bar()
        for c in range(NCH):
            self.bar()
        for t in range(NTT):
            e.wait_ge(sm["cZT"], 32 * (t + 1))
            e.activation(self.zsil[:, :], self.zt[:, :], AF.Silu).then_inc(sm["cP5a"], 1)
            e.wait_ge(sm["cP5b"], t + 1)
            e.activation(self.ln[:, 3:4], self.ln[:, 2:3], AF.Sqrt,
                         scale=1.0 / 4096.0, bias=self.ln[:, 6:7]).then_inc(sm["cLNS"], 1)
            self.bar()
        self.bar()
        for t in range(NTT):
            e.wait_ge(sm["cP5b"], NTT + t + 1)
            e.activation(self.ln[:, 3:4], self.ln[:, 2:3], AF.Sqrt,
                         scale=1.0 / D, bias=self.ln[:, 6:7]).then_inc(sm["cLNS"], 1)
            self.bar()


def _hostprep_C(inputs, core):
    wip = np.asarray(inputs["L1_attn_in_proj_w"], np.float32)
    wip_t = np.ascontiguousarray(wip.T.astype(BF))
    wout = np.asarray(inputs["L1_attn_out_proj_w"], np.float32)
    wout_t = np.ascontiguousarray(wout.T.astype(BF))
    cw = np.asarray(inputs["L1_attn_conv_w"], np.float32).reshape(4352, 4)
    convw = np.ascontiguousarray(cw.reshape(34, 128, 4).transpose(1, 0, 2).reshape(128, 136))
    cb = np.asarray(inputs["L1_attn_conv_b"], np.float32)
    convb = np.ascontiguousarray(cb.reshape(34, 128).T)
    acolin = np.zeros((64, 4), np.float32)
    acolin[:, 0] = np.asarray(inputs["L1_attn_A_log"], np.float32)
    acolin[:, 1] = np.asarray(inputs["L1_attn_dt_bias"], np.float32)
    acolin[:, 3] = 1.0
    drow = np.asarray(inputs["L1_attn_D"], np.float32).reshape(1, 64)
    normw_rep = np.ascontiguousarray(
        np.broadcast_to(np.asarray(inputs["L1_attn_norm_w"], np.float32)[None, :], (128, 4096)))
    trilU = np.triu(np.ones((64, 64), np.float32)).astype(BF)
    beta, sig = core // 4, core % 4
    mBt = np.zeros((8, 8), np.float32)
    mbias = np.full((64, 8), -1e30, np.float32)
    for p in range(8):
        pb_, ps = p // 4, p % 4
        if pb_ == beta and ps < sig:
            mbias[:, p] = 0.0
            for qq in range(8):
                qb, qs = qq // 4, qq % 4
                if qb == beta and ps < qs < sig:
                    mBt[qq, p] = 1.0
    selw = np.zeros((128, 8), np.float32)
    if sig > 0:
        selw[:, core - 1] = 1.0
    return dict(wip_t=wip_t, wout_t=wout_t, convw=convw, convb=convb,
                acolin=acolin, drow=drow, normw_rep=normw_rep,
                trilU=np.asarray(trilU), mBt=mBt, mbias=mbias, selw=selw,
                onesf=np.ones((1, 128), np.float32),
                identf=np.eye(128, dtype=np.float32),
                identb=np.eye(128, dtype=BF))


def _declare_C(nc):
    io = {}
    decl = [("wip_t", [D, 8512], BF16), ("wout_t", [4096, D], BF16),
            ("convw", [128, 136], F32), ("convb", [128, 34], F32),
            ("acolin", [64, 4], F32), ("drow", [1, 64], F32),
            ("normw_rep", [128, 4096], F32), ("trilU", [64, 64], BF16),
            ("mBt", [8, 8], F32), ("mbias", [64, 8], F32), ("selw", [128, 8], F32),
            ("onesf", [1, 128], F32), ("identf", [128, 128], F32),
            ("identb", [128, 128], BF16)]
    for n, sh, dt in decl:
        io[n] = nc.dram_tensor(n, sh, dt, kind="ExternalInput").ap()
    return io


def build_program_C():
    nc = bass.Bass("TRN2", target_bir_lowering=False)
    stack = ExitStack()
    ctx = Ctx(nc, stack)
    io = _declare_C(nc)
    ctx.dram["x2f_d"] = nc.dram_tensor("x2f_d", [TOW, D], F32, kind="ExternalInput")
    ctx.dram["x2b_d"] = nc.dram_tensor("x2b_d", [TOW, D], BF16, kind="ExternalInput")
    with stack:
        ph = PhaseC(ctx, io, out_kind="ExternalOutput")
        with nc.Block() as block:
            @block.sync
            def _(e):
                ph.em_sync(e)

            @block.gpsimd
            def _(e):
                ph.em_gpsimd(e)

            @block.tensor
            def _(e):
                ph.em_tensor(e)

            @block.vector
            def _(e):
                ph.em_vector(e)

            @block.scalar
            def _(e):
                ph.em_scalar(e)
    return nc



DSEG = 8
DETS = 8


class PhaseD:

    def __init__(self, ctx: Ctx, io, out_kind="Internal"):
        self.ctx = ctx
        self.io = io
        c = ctx
        self.xT = c.sbuf("D_xT", [128, 16 * 1024], BF16)
        self.win = [c.sbuf(f"D_win{i}", [128, 2048], BF16) for i in range(2)]
        self.tg = [c.sbuf(f"D_tg{i}", [128, 1024], F32) for i in range(2)]
        self.silu = c.sbuf("D_silu", [128, 1024], BF16)
        self.h8 = c.sbuf("D_h8", [128, DETS * 1024], BF16)
        self.wdn = [c.sbuf(f"D_wdn{i}", [128, 2048], BF16) for i in range(DETS)]
        self.dacc = c.sbuf("D_dacc", [128, NTT * 2048], F32)
        self.rsx = c.sbuf("D_rsx", [128, 1024], F32)
        self.rsrow = c.sbuf("D_rsrow", [1, 1024], F32)
        self.rstg = c.sbuf("D_rstg", [8, 128], F32)
        self.rcol = c.sbuf("D_rcol", [128, 8], F32)
        self.hs = c.sbuf("D_hs", [128, 8], F32)
        self.dsc = c.sbuf("D_dsc", [128, 2048], F32)
        self.gsc = c.sbuf("D_gsc", [128, 64], F32)
        self.usc = c.sbuf("D_usc", [128, 64], F32)
        self.x3res = c.sbuf("D_x3res", [128, 2048], F32)
        self.x4f = c.sbuf("D_x4f", [128, 2048], F32)
        self.x4b = c.sbuf("D_x4b", [128, 2048], BF16)
        self.negf = c.sbuf("D_negf", [1, 128], F32)
        self.onescb = c.sbuf("D_onescb", [128, 1], BF16)
        self.identf = c.sbuf("D_identf", [128, 128], F32)
        self.ln = c.sbuf("D_ln", [128, 8], F32)
        self.p_up = c.psum("D_pup", [128, 1024], F32)
        self.p_g = c.psum("D_pg", [128, 1024], F32)
        self.p_dn = c.psum("D_pdn", [128, 1024], F32)
        self.p_hs = c.psum("D_phs", [128, 512], F32)
        self.rs_d = c.dscr("D_rs_d", [1, 1024], F32)
        self.x4f_d = c.dscr("x4f_d", [TOW, D], F32, kind=out_kind)
        self.x3f_d = c.dram["x3f_d"] if "x3f_d" in c.dram else c.dscr("x3f_d", [TOW, D], F32)
        self.x3b_d = c.dram["x3b_d"] if "x3b_d" in c.dram else c.dscr("x3b_d", [TOW, D], BF16)
        c.sem("dP", "dONE", "dXT", "dRS", "dRSt", "dRSv", "dRSd", "dRSx", "dWL",
              "dMM", "dSIL", "dH", "dTG", "dLD", "dDN", "dFL", "dHS", "dX3R",
              "dX3c", "dHSe", "dLNV", "dLNS", "dX4", "dOUT", "dVS")

    def em_sync(self, e):
        sm, io = self.ctx.sems, self.io
        for name, dst in [("onescb_h", None), ("negf", self.negf),
                          ("identf", self.identf), ("dsc_rep", self.dsc),
                          ("gsc_t", self.gsc), ("usc_t", self.usc)]:
            if dst is None:
                continue
            e.dma_start(out=dst[:, :], in_=io[name]).then_inc(sm["dP"], 16)
        for dt_ in range(16):
            e.dma_start(out=self.xT[:, 1024 * dt_:1024 * dt_ + 1024],
                        in_=apm(self.x3b_d.ap(), 128 * dt_, [[D, 1024], [1, 128]]),
                        transpose=True).then_inc(sm["dXT"], 16)
        for t in range(NTT):
            if t >= 1:
                e.wait_ge(sm["dRS"], t)
            e.dma_start(out=self.x3res[:, :],
                        in_=apm(self.x3f_d.ap(), t * 128 * D, [[D, 128], [1, D]])
                        ).then_inc(sm["dX3R"], 16)
        e.wait_ge(sm["dRSv"], 1)
        e.dma_start(out=apm(self.rs_d.ap(), 0, [[128, 8], [1, 128]]),
                    in_=self.rstg[:, :]).then_inc(sm["dRSd"], 16)
        e.wait_ge(sm["dRSd"], 16)
        e.dma_start(out=self.rsrow[:, :],
                    in_=apm(self.rs_d.ap(), 0, [[1024, 1], [1, 1024]])
                    ).then_inc(sm["dRSd"], 16)
        for seg in range(DSEG):
            for etl in range(DETS):
                etg = seg * DETS + etl
                for w, nm in enumerate(["uw_t", "gw_t"]):
                    li = 2 * etg + w
                    if li >= 2:
                        e.wait_ge(sm["dMM"], li - 1)
                    e.dma_start(out=self.win[li % 2][:, :],
                                in_=apm(io[nm], 128 * etg,
                                        [[DE, 128], [128 * DE, 16], [1, 128]])
                                ).then_inc(sm["dWL"], 16)
        for t in range(NTT):
            if t == 0:
                e.wait_ge(sm["dRS"], NTT)
            else:
                e.wait_ge(sm["dX3c"], t)
            e.dma_start(out=self.x3res[:, :],
                        in_=apm(self.x3f_d.ap(), t * 128 * D, [[D, 128], [1, D]])
                        ).then_inc(sm["dX3R"], 16)
            e.wait_ge(sm["dX4"], t + 1)
            e.dma_start(out=apm(self.x4f_d.ap(), t * 128 * D, [[D, 128], [1, D]]),
                        in_=self.x4f[:, :]).then_inc(sm["dOUT"], 16)
        if DBG_D:
            e.dma_start(out=self.ctx.dram["dbgD_rsx"].ap(), in_=self.rsx[:, :]).then_inc(sm["dOUT"], 16)
            e.dma_start(out=self.ctx.dram["dbgD_hs"].ap(), in_=self.hs[:, :]).then_inc(sm["dOUT"], 16)
            e.dma_start(out=self.ctx.dram["dbgD_h8"].ap(), in_=self.h8[:, :]).then_inc(sm["dOUT"], 16)
            e.wait_ge(sm["dOUT"], 16 * NTT + 48)
        else:
            e.wait_ge(sm["dOUT"], 16 * NTT)

    def em_gpsimd(self, e):
        sm, io = self.ctx.sems, self.io
        e.dma_start(out=self.onescb[:, :], in_=io["onescb_h"]).then_inc(sm["dONE"], 16)
        for seg in range(DSEG):
            for etl in range(DETS):
                eg = seg * DETS + etl
                if seg >= 1:
                    e.wait_ge(sm["dDN"], 16 * seg)
                e.dma_start(out=self.wdn[etl][:, :],
                            in_=apm(io["dw_t"], 128 * eg * D, [[D, 128], [1, D]])
                            ).then_inc(sm["dLD"], 16)

    def em_tensor(self, e):
        sm = self.ctx.sems
        e.wait_ge(sm["dP"], 16 * 5)
        e.wait_ge(sm["dRS"], NTT)
        e.transpose(self.p_hs[0:8, 0:128], self.rcol[:, 0:8],
                    self.identf[:, :]).then_inc(sm["dRSt"], 1)
        e.wait_ge(sm["dRSd"], 32)
        for ch in range(2):
            e.matmul(self.p_up[:, 512 * ch:512 * ch + 512], self.negf[0:1, 0:128],
                     self.rsrow[0:1, 512 * ch:512 * ch + 512],
                     start=True, stop=True).then_inc(sm["dRSx"], 1)
        e.wait_ge(sm["dXT"], 16 * 16)
        e.wait_ge(sm["dRSx"], 3)
        e.wait_ge(sm["dONE"], 16)
        for seg in range(DSEG):
            for etl in range(DETS):
                etg = seg * DETS + etl
                for w in range(2):
                    li = 2 * etg + w
                    e.wait_ge(sm["dWL"], 16 * (li + 1))
                    if etg >= 1:
                        e.wait_ge(sm["dTG"], 2 * etg if w == 0 else 2 * etg - 1)
                    pb = self.p_up if w == 0 else self.p_g
                    for c2 in range(2):
                        for dt_ in range(16):
                            mm = e.matmul(pb[:, 512 * c2:512 * c2 + 512],
                                          self.win[li % 2][:, 128 * dt_:128 * dt_ + 128],
                                          self.xT[:, 1024 * dt_ + 512 * c2:1024 * dt_ + 512 * c2 + 512],
                                          start=(dt_ == 0), stop=(dt_ == 15))
                    mm.then_inc(sm["dMM"], 1)
            e.wait_ge(sm["dH"], DETS * (seg + 1))
            e.wait_ge(sm["dLD"], 16 * DETS * (seg + 1))
            for half in range(2):
                for tt in range(NTT):
                    it = (seg * 2 + half) * NTT + tt
                    if it >= 1:
                        e.wait_ge(sm["dFL"], it)
                    for etl in range(DETS):
                        for c2 in range(2):
                            mm = e.matmul(self.p_dn[:, 512 * c2:512 * c2 + 512],
                                          self.h8[:, 1024 * etl + 128 * tt:1024 * etl + 128 * tt + 128],
                                          self.wdn[etl][:, 1024 * half + 512 * c2:1024 * half + 512 * c2 + 512],
                                          start=(etl == 0), stop=(etl == DETS - 1))
                    mm.then_inc(sm["dDN"], 1)
            if seg >= 1:
                e.wait_ge(sm["dHSe"], seg)
            for tt in range(NTT):
                for etl in range(DETS):
                    mm = e.matmul(self.p_hs[:, tt:tt + 1],
                                  self.h8[:, 1024 * etl + 128 * tt:1024 * etl + 128 * tt + 128],
                                  self.onescb[:, 0:1],
                                  start=(etl == 0), stop=(etl == DETS - 1))
                mm.then_inc(sm["dHS"], 1)

    def em_vector(self, e):
        sm = self.ctx.sems
        self._vsn = 0

        def W(ins):
            self._vsn += 1
            ins.then_inc(sm["dVS"], 1)
            e.wait_ge(sm["dVS"], self._vsn)
        e.memset(self.ln[:, 6:7], 1e-5)
        for t in range(NTT):
            e.wait_ge(sm["dX3R"], 16 * (t + 1))
            e.reduce_sum(self.rcol[:, t:t + 1], self.x3res[:, :], AX).then_inc(sm["dRS"], 1)
        e.wait_ge(sm["dRSt"], 1)
        e.tensor_copy(self.rstg[:, :], self.p_hs[0:8, 0:128]).then_inc(sm["dRSv"], 1)
        for ch in range(2):
            e.wait_ge(sm["dRSx"], ch + 1)
            cp = e.tensor_copy(self.rsx[:, 512 * ch:512 * ch + 512],
                               self.p_up[:, 512 * ch:512 * ch + 512])
        cp.then_inc(sm["dRSx"], 1)
        for seg in range(DSEG):
            for etl in range(DETS):
                etg = seg * DETS + etl
                e.wait_ge(sm["dMM"], 2 * etg + 2)
                if etl == 0 and seg >= 1:
                    e.wait_ge(sm["dDN"], 16 * seg)
                    e.wait_ge(sm["dHS"], NTT * seg)
                if etg >= 1:
                    e.wait_ge(sm["dSIL"], etg)
                e.tensor_add(self.tg[1][:, :], self.p_g[:, :],
                             self.rsx[:, :]).then_inc(sm["dTG"], 1)
                e.tensor_add(self.tg[0][:, :], self.p_up[:, :],
                             self.rsx[:, :]).then_inc(sm["dTG"], 1)
                e.wait_ge(sm["dSIL"], etg + 1)
                e.scalar_tensor_tensor(self.h8[:, 1024 * etl:1024 * etl + 1024],
                                       self.tg[0][:, :], self.usc[:, etg:etg + 1],
                                       self.silu[:, :], ALU.mult,
                                       ALU.mult).then_inc(sm["dH"], 1)
            for half in range(2):
                for tt in range(NTT):
                    it = (seg * 2 + half) * NTT + tt
                    e.wait_ge(sm["dDN"], it + 1)
                    dx = self.dacc[:, 2048 * tt + 1024 * half:2048 * tt + 1024 * half + 1024]
                    if seg == 0:
                        e.tensor_copy(dx, self.p_dn[:, :]).then_inc(sm["dFL"], 1)
                    else:
                        e.tensor_add(dx, dx, self.p_dn[:, :]).then_inc(sm["dFL"], 1)
            e.wait_ge(sm["dHS"], NTT * (seg + 1))
            if seg == 0:
                e.tensor_copy(self.hs[:, 0:8], self.p_hs[:, 0:8]).then_inc(sm["dHSe"], 1)
            else:
                e.tensor_add(self.hs[:, 0:8], self.hs[:, 0:8],
                             self.p_hs[:, 0:8]).then_inc(sm["dHSe"], 1)
        e.wait_ge(sm["dHSe"], DSEG)
        W(e.tensor_scalar_mul(self.hs[:, 0:8], self.hs[:, 0:8], -128.0))
        for t in range(NTT):
            e.wait_ge(sm["dX3R"], 16 * (NTT + t + 1))
            dx = self.dacc[:, 2048 * t:2048 * t + 2048]
            e.tensor_scalar(dx, dx, self.hs[:, t:t + 1], None, ALU.add)
            e.tensor_mul(dx, dx, self.dsc[:, :])
            e.tensor_add(dx, dx, self.x3res[:, :]).then_inc(sm["dX3c"], 1)
            W(e.reduce_sum(self.ln[:, 0:1], dx, AX))
            W(e.tensor_scalar_mul(self.ln[:, 1:2], self.ln[:, 0:1], 1.0 / D))
            e.tensor_scalar(dx, dx, self.ln[:, 1:2], None, ALU.subtract)
            if t >= 1:
                e.wait_ge(sm["dOUT"], 16 * t)
            e.tensor_mul(self.x4b[:, :], dx, dx)
            e.reduce_sum(self.ln[:, 2:3], self.x4b[:, :], AX).then_inc(sm["dLNV"], 1)
            e.wait_ge(sm["dLNS"], t + 1)
            W(e.reciprocal(self.ln[:, 4:5], self.ln[:, 3:4]))
            e.tensor_scalar(self.x4f[:, :], dx, self.ln[:, 4:5], None,
                            ALU.mult).then_inc(sm["dX4"], 1)

    def em_scalar(self, e):
        sm = self.ctx.sems
        e.wait_ge(sm["dP"], 16 * 5)
        for seg in range(DSEG):
            for etl in range(DETS):
                etg = seg * DETS + etl
                e.wait_ge(sm["dTG"], 2 * etg + 1)
                if etg >= 1:
                    e.wait_ge(sm["dH"], etg)
                e.activation(self.silu[:, :], self.tg[1][:, :], AF.Silu,
                             scale=self.gsc[:, etg:etg + 1]).then_inc(sm["dSIL"], 1)
        for t in range(NTT):
            e.wait_ge(sm["dLNV"], t + 1)
            e.activation(self.ln[:, 3:4], self.ln[:, 2:3], AF.Sqrt,
                         scale=1.0 / D, bias=self.ln[:, 6:7]).then_inc(sm["dLNS"], 1)


def _hostprep_D(inputs, core):
    io = {}
    io["uw_t"] = np.ascontiguousarray(inputs["L1_ffn_up_w"].T.astype(BF))
    io["gw_t"] = np.ascontiguousarray(inputs["L1_ffn_gate_w"].T.astype(BF))
    io["dw_t"] = np.ascontiguousarray(inputs["L1_ffn_down_w"].T.astype(BF))
    io["gsc_t"] = np.ascontiguousarray(
        inputs["L1_ffn_gate_scale"].astype(np.float32).reshape(64, 128).T)
    io["usc_t"] = np.ascontiguousarray(
        inputs["L1_ffn_up_scale"].astype(np.float32).reshape(64, 128).T)
    io["dsc_rep"] = np.ascontiguousarray(
        np.broadcast_to(inputs["L1_ffn_down_scale"].astype(np.float32), (128, D)))
    io["negf"] = np.full((1, 128), -128.0, np.float32)
    io["identf"] = np.eye(128, dtype=np.float32)
    io["onescb_h"] = np.ones((128, 1), np.float32)
    return io


def _declare_D(nc):
    io = {}
    decl = [("uw_t", [D, DE], BF16), ("gw_t", [D, DE], BF16), ("dw_t", [DE, D], BF16),
            ("gsc_t", [128, 64], F32), ("usc_t", [128, 64], F32),
            ("dsc_rep", [128, D], F32), ("negf", [1, 128], F32),
            ("identf", [128, 128], F32), ("onescb_h", [128, 1], F32)]
    for n, sh, dt in decl:
        io[n] = nc.dram_tensor(n, sh, dt, kind="ExternalInput").ap()
    return io


def build_program_D():
    nc = bass.Bass("TRN2", target_bir_lowering=False)
    stack = ExitStack()
    ctx = Ctx(nc, stack)
    io = _declare_D(nc)
    ctx.dram["x3f_d"] = nc.dram_tensor("x3f_d", [TOW, D], F32, kind="ExternalInput")
    ctx.dram["x3b_d"] = nc.dram_tensor("x3b_d", [TOW, D], BF16, kind="ExternalInput")
    with stack:
        if DBG_D:
            ctx.dscr("dbgD_rsx", [128, 1024], F32, kind="ExternalOutput")
            ctx.dscr("dbgD_hs", [128, 8], F32, kind="ExternalOutput")
            ctx.dscr("dbgD_h8", [128, DETS * 1024], BF16, kind="ExternalOutput")
        ph = PhaseD(ctx, io, out_kind="ExternalOutput")
        with nc.Block() as block:
            @block.sync
            def _(e):
                ph.em_sync(e)

            @block.gpsimd
            def _(e):
                ph.em_gpsimd(e)

            @block.tensor
            def _(e):
                ph.em_tensor(e)

            @block.vector
            def _(e):
                ph.em_vector(e)

            @block.scalar
            def _(e):
                ph.em_scalar(e)
    return nc


_ETF = mybir.EngineType
_FENGS = [_ETF.PE, _ETF.DVE, _ETF.Activation, _ETF.SP, _ETF.Pool]

_DECL_A = ["x_own", "x_pref", "x_qext", "wq_t", "wkv_t", "wo_t", "v_emb0",
           "v_emb1", "tau", "maskm", "invi", "jp", "onesf", "E16f", "identf"]
_DECL_B = ["wu_t", "wg_t", "wd_t", "wu_sl", "wg_sl", "wd_sl", "sel3", "onesf"]
_DECL_C = ["wip_t", "wout_t", "convw", "convb", "acolin", "drow", "normw_rep",
           "trilU", "mBt", "mbias", "selw", "onesf", "identf", "identb"]
_DECL_D = ["uw_t", "gw_t", "dw_t", "gsc_t", "usc_t", "dsc_rep", "negf",
           "identf", "onescb_h"]
_DECL_SHAPES = {
    "x_own": ([TOW, D], F32), "x_pref": ([S, D], BF16), "x_qext": ([1152, D], BF16),
    "wq_t": ([128, 2048], F32), "wkv_t": ([128, 1024], F32), "wo_t": ([128, D], F32),
    "v_emb0": ([128, 1], F32), "v_emb1": ([128, 1], F32), "tau": ([1, 1], F32),
    "maskm": ([128, 5632], F32), "invi": ([1, TOW], F32), "jp": ([128, 1], F32),
    "onesf": ([1, 128], F32), "E16f": ([16, 128], F32), "identf": ([128, 128], F32),
    "wu_t": ([D, DE], F32), "wg_t": ([D, DE], F32), "wd_t": ([DE, D], F32),
    "wu_sl": ([128, 16384], F32), "wg_sl": ([128, 16384], F32),
    "wd_sl": ([128, 16384], F32), "sel3": ([3, 384], F32),
    "wip_t": ([D, 8512], BF16), "wout_t": ([4096, D], BF16),
    "convw": ([128, 136], F32), "convb": ([128, 34], F32),
    "acolin": ([64, 4], F32), "drow": ([1, 64], F32),
    "normw_rep": ([128, 4096], F32), "trilU": ([64, 64], BF16),
    "mBt": ([8, 8], F32), "mbias": ([64, 8], F32), "selw": ([128, 8], F32),
    "identb": ([128, 128], BF16),
    "uw_t": ([D, DE], BF16), "gw_t": ([D, DE], BF16), "dw_t": ([DE, D], BF16),
    "gsc_t": ([128, 64], F32), "usc_t": ([128, 64], F32),
    "dsc_rep": ([128, D], F32), "negf": ([1, 128], F32),
    "onescb_h": ([128, 1], F32),
}


class _MockIns:
    __slots__ = ("rec",)

    def __init__(self, rec):
        self.rec = rec

    def then_inc(self, sem, k):
        self.rec[sem.num] = self.rec.get(sem.num, 0) + k
        return self


class _MockEng:
    def __init__(self, rec):
        self._rec = rec

    def __getattr__(self, name):
        rec = self._rec
        if name == "wait_ge":
            return lambda *a, **k: None
        return lambda *a, **k: _MockIns(rec)


class _EngOff:

    def __init__(self, e, sbase):
        self._e = e
        self._sbase = sbase

    def wait_ge(self, sem, n):
        return self._e.wait_ge(sem, n + self._sbase.get(sem.num, 0))

    def __getattr__(self, name):
        return getattr(self._e, name)


def _phase_inc_totals(ph):
    rec = {}
    saved = PhaseC.bar
    PhaseC.bar = lambda self: None
    try:
        for em in (ph.em_sync, ph.em_gpsimd, ph.em_tensor, ph.em_vector, ph.em_scalar):
            em(_MockEng(rec))
    finally:
        PhaseC.bar = saved
    return rec


class FusedCtx(Ctx):
    def __init__(self, nc, stack):
        super().__init__(nc, stack)
        self._phase_sems = []

    def sem(self, *names):
        for name in names:
            if name not in self.sems:
                h = self.nc.alloc_semaphore(name)
                self.sems[name] = h
                self._phase_sems.append(h)


def build_program_fused():
    nc = bass.Bass("TRN2", target_bir_lowering=False)
    ctx = FusedCtx(nc, None)
    io_all = {}
    for n in _DECL_A + _DECL_B + _DECL_C + _DECL_D:
        if n not in io_all:
            sh, dt = _DECL_SHAPES[n]
            io_all[n] = nc.dram_tensor(n, sh, dt, kind="ExternalInput").ap()
    specs = [
        (PhaseA, _DECL_A, "Internal"),
        (PhaseB, _DECL_B, "Internal"),
        (PhaseC, _DECL_C, "Internal"),
        (PhaseD, _DECL_D, "ExternalOutput"),
    ]
    sbase = {}
    for cls, names, okind in specs:
        pstack = ExitStack()
        ctx.stack = pstack
        ctx._phase_sems = []
        ctx.sems = {}
        pio = {n: io_all[n] for n in names}
        ph = cls(ctx, pio, out_kind=okind)
        rec = _phase_inc_totals(ph)
        with nc.Block() as block:
            @block.sync
            def _(e, ph=ph):
                ph.em_sync(_EngOff(e, sbase))
                nc.multi_engine_barrier(_FENGS)

            @block.gpsimd
            def _(e, ph=ph):
                ph.em_gpsimd(_EngOff(e, sbase))
                nc.multi_engine_barrier(_FENGS)

            @block.tensor
            def _(e, ph=ph):
                ph.em_tensor(_EngOff(e, sbase))
                nc.multi_engine_barrier(_FENGS)

            @block.vector
            def _(e, ph=ph):
                ph.em_vector(_EngOff(e, sbase))
                nc.multi_engine_barrier(_FENGS)

            @block.scalar
            def _(e, ph=ph):
                ph.em_scalar(_EngOff(e, sbase))
                nc.multi_engine_barrier(_FENGS)
        for num, tot in rec.items():
            sbase[num] = sbase.get(num, 0) + tot
        for h in ctx._phase_sems:
            nc.release_semaphore(h)
        pstack.close()
    return nc


def _hostprep_fused(inputs, core):
    im = {}
    for hp in (_hostprep_A, _hostprep_B, _hostprep_C, _hostprep_D):
        for k, v in hp(inputs, core).items():
            if k not in im:
                im[k] = v
    return im


def kernel(**inputs):
    nc = build_program_fused()
    in_maps = [_hostprep_fused(inputs, c) for c in range(NCORE)]
    res = run_bass_kernel_spmd(nc, in_maps, core_ids=list(range(NCORE)))
    out = np.zeros((B, S, D), np.float32)
    for c in range(NCORE):
        b, s = c // 4, c % 4
        out[b, s * 1024:(s + 1) * 1024] = np.asarray(res.results[c]["x4f_d"])
    return out
